# version 02
# baseline (speedup 1.0000x reference)
import numpy as np
import ml_dtypes
from contextlib import ExitStack

import concourse.bass as bass
import concourse.mybir as mybir
from concourse.bass_utils import run_bass_kernel_spmd

F32 = mybir.dt.float32
BF16 = mybir.dt.bfloat16
AF = mybir.ActivationFunctionType
ALU = mybir.AluOpType
BF = ml_dtypes.bfloat16

E_DIM = 2048
H = 16
D = 128
INTER = 8192
EPS = 1e-5

ENGINES = ("sync", "scalar", "vector", "tensor", "gpsimd")
DMA_QUEUES = ("sync", "gpsimd")


class _Op:
    __slots__ = ("eng", "fn", "is_dma", "deps", "inc_needed", "sem_val", "sem_key", "idx")

    def __init__(self, eng, fn, is_dma, idx):
        self.eng = eng
        self.fn = fn
        self.is_dma = is_dma
        self.deps = set()
        self.inc_needed = False
        self.sem_val = None
        self.idx = idx


class Prog:

    def __init__(self, name="prog"):
        self.nc = bass.Bass("TRN2", target_bir_lowering=False)
        self.stack = ExitStack()
        self.ops = []
        self.last_writer = {}
        self.readers = {}
        self.names = set()
        self._barrier = ()
        self._scopes = []
        self._dma_recent = {}
        self.prefix = ""

    from contextlib import contextmanager as _cm

    @_cm
    def scope(self):
        st = ExitStack()
        self._scopes.append(st)
        try:
            yield
        finally:
            self._scopes.pop()
            st.close()
            self.barrier()

    def barrier(self):
        last = {}
        for op in self.ops:
            if not op.is_dma:
                last[op.eng] = op
        bar = set(last.values())
        for q, lst in self._dma_recent.items():
            bar.update(lst)
        self._barrier = tuple(bar)

    def inp(self, name, shape, dtype):
        return self.nc.dram_tensor(name, list(shape), dtype, kind="ExternalInput").ap()

    def outp(self, name, shape, dtype):
        return self.nc.dram_tensor(name, list(shape), dtype, kind="ExternalOutput").ap()

    def dram(self, name, shape, dtype):
        return self.nc.dram_tensor(self.prefix + name, list(shape), dtype, kind="Internal").ap()

    def sb(self, name, shape, dtype):
        name = self.prefix + name
        assert name not in self.names
        self.names.add(name)
        st = self._scopes[-1] if self._scopes else self.stack
        return st.enter_context(self.nc.sbuf_tensor(name, list(shape), dtype))

    def ps(self, name, shape, dtype=F32):
        name = self.prefix + name
        assert name not in self.names
        self.names.add(name)
        st = self._scopes[-1] if self._scopes else self.stack
        return st.enter_context(self.nc.psum_tensor(name, list(shape), dtype))

    def exec(self, eng, fn, reads=(), writes=(), is_dma=False):
        op = _Op(eng, fn, is_dma, len(self.ops))
        deps = set()
        for t in reads:
            w = self.last_writer.get(self._key(t))
            if w is not None:
                deps.add(w)
        for t in writes:
            k = self._key(t)
            deps.update(self.readers.get(k, ()))
            w = self.last_writer.get(k)
            if w is not None:
                deps.add(w)
        deps.update(self._barrier)
        op.deps = {d for d in deps if d is not op}
        self.ops.append(op)
        for t in reads:
            self.readers.setdefault(self._key(t), []).append(op)
        for t in writes:
            k = self._key(t)
            self.last_writer[k] = op
            self.readers[k] = []
        return op

    def dma(self, queue, out, in_, reads=(), writes=(), **kw):
        fn = lambda e, o=out, i=in_, kw=kw: e.dma_start(out=o, in_=i, **kw)
        reads = list(reads) + [in_]
        writes = list(writes) + [out]
        op = self.exec(queue, fn, reads=reads, writes=writes, is_dma=True)
        rec = self._dma_recent.setdefault(queue, [])
        rec.append(op)
        if len(rec) > self.DMA_SLOTS:
            rec.pop(0)
        return op

    @staticmethod
    def _key(t):
        n = getattr(t, "name", None)
        if n is None and hasattr(t, "tensor"):
            n = t.tensor.name
        return n if n is not None else t

    DMA_SLOTS = 8

    def emit_and_finalize(self):
        def _needs_sem(d, op):
            if d.eng == "tensor" and op.eng == "tensor" and not d.is_dma and not op.is_dma:
                return False
            return True

        for op in self.ops:
            for d in op.deps:
                if _needs_sem(d, op):
                    d.inc_needed = True
        for op in self.ops:
            if op.is_dma:
                op.inc_needed = True

        sems = {}
        counters = {}
        slot_rr = {}
        slot_prev = {}
        for op in self.ops:
            if not op.inc_needed:
                continue
            if op.is_dma:
                q = op.eng
                s = slot_rr.get(q, 0)
                slot_rr[q] = (s + 1) % self.DMA_SLOTS
                key = (q, "d", s)
                prev = slot_prev.get(key)
                if prev is not None:
                    op.deps.add(prev)
                slot_prev[key] = op
            else:
                key = (op.eng, "c", 0)
            if key not in sems:
                sems[key] = self.stack.enter_context(
                    self.nc.semaphore(f"s_{key[0]}_{key[1]}{key[2]}")
                )
            step = 16 if op.is_dma else 1
            counters[key] = counters.get(key, 0) + step
            op.sem_val = counters[key]
            op.sem_key = key
        self._sems = sems

        streams = {e: [] for e in ENGINES}
        floors = {e: {} for e in ENGINES}
        for op in self.ops:
            waits = {}
            fl = floors[op.eng]
            for d in op.deps:
                if not _needs_sem(d, op):
                    continue
                key = d.sem_key
                if fl.get(key, 0) >= d.sem_val:
                    continue
                waits[key] = max(waits.get(key, 0), d.sem_val)
            for key, val in waits.items():
                fl[key] = val
            streams[op.eng].append((op, sorted(waits.items(), key=lambda kv: str(kv[0]))))

        block = self.stack.enter_context(self.nc.Block())
        for e in ENGINES:
            if not streams[e]:
                continue

            def body(eng, ops=streams[e]):
                for op, waits in ops:
                    for (key, val) in waits:
                        eng.wait_ge(sems[key], val)
                    inst = op.fn(eng)
                    if op.inc_needed:
                        inst.then_inc(sems[op.sem_key], 16 if op.is_dma else 1)

            getattr(block, e)(body)
        self.stack.close()
        self.nc.finalize()
        return self.nc

    def fence(self, eng="gpsimd"):
        pend = set()
        for k, w in self.last_writer.items():
            pend.add(w)
        for k, rs in self.readers.items():
            pend.update(rs)

        def fn(e):
            return e.nop()

        op = _Op(eng, fn, False, len(self.ops))
        op.deps = {d for d in pend if d is not op}
        self.ops.append(op)
        return op


def cdiv(a, b):
    return (a + b - 1) // b


def prep_host(w, T):
    import math
    pr = {}
    pr["identbf"] = np.eye(128).astype(BF)
    pr["identf32"] = np.eye(128, dtype=np.float32)
    pr["h128bf"] = sylvester(128).astype(BF)
    pr["ones128f"] = np.ones((1, 128), np.float32)
    pr["onesbf"] = np.ones((1, 128)).astype(BF)
    pr["constpm"] = np.tile(np.array([[EPS, -0.5]], np.float32), (128, 1))
    freq = (1.0 / 1024.0) ** np.linspace(0.0, 1.0, 32, dtype=np.float32)
    freq = np.concatenate([freq, np.zeros(32, np.float32)])
    t = np.arange(T, dtype=np.float32)
    theta = t[:, None] * freq[None, :]
    pr["rotc"] = np.ascontiguousarray(np.cos(theta).astype(np.float32).reshape(T // 128, 128, 64))
    pr["rots"] = np.ascontiguousarray(np.sin(theta).astype(np.float32).reshape(T // 128, 128, 64))
    start = 2.0 ** (-2.0 ** (-(math.log2(H) - 3)))
    slopes0 = np.array([start * start ** i for i in range(H)], np.float64)
    r = np.exp(-slopes0)
    i = np.arange(128, dtype=np.float64)
    pr["qdec"] = (r[None, :] ** (i[:, None] + 1.0)).astype(np.float32)
    pr["kdec"] = (r[None, :] ** (127.0 - i[:, None])).astype(np.float32)
    pr["rC"] = np.tile((r ** 128.0).astype(np.float32)[None, :], (128, 1))
    diff = i[:, None] - i[None, :]
    dmaskT = np.zeros((H, 128, 128), np.float32)
    for h in range(H):
        dm = np.where(diff >= 0, r[h] ** np.maximum(diff, 0), 0.0)
        dmaskT[h] = dm.T.astype(np.float32)
    pr["dmaskT"] = dmaskT
    j64 = np.arange(64, dtype=np.float32)
    dT = j64[None, :] - j64[:, None]
    pr["diffT64"] = np.where(dT >= 0, dT, 1e6).astype(np.float32)
    pr["negiota64"] = np.tile(-j64[:, None], (1, H)).astype(np.float32)
    pr["negrev64"] = np.tile(-(64.0 - j64)[:, None], (1, H)).astype(np.float32)
    pr["Wq"] = w["L0_attn_Wq"]
    pr["Wk"] = w["L0_attn_Wk"]
    pr["Wv"] = w["L0_attn_Wv"]
    pr["Wo"] = w["L0_attn_Wo"]
    pr["rms_w_row"] = np.ascontiguousarray(w["L0_attn_rms_w"][None, :])
    pr["W_upT"] = np.ascontiguousarray(w["L0_ffn_W_up"].T)
    pr["W_gateT"] = np.ascontiguousarray(w["L0_ffn_W_gate"].T)
    pr["W_downT"] = np.ascontiguousarray(w["L0_ffn_W_down"].T)
    pr["su_ug_pm"] = np.ascontiguousarray(w["L0_ffn_SU_ug"].reshape(16, 128).T)
    pr["su_dp_pm"] = np.ascontiguousarray(w["L0_ffn_SU_dp"].reshape(64, 128).T)
    pr["wsc_up_pm"] = np.ascontiguousarray(w["L0_ffn_Wscale_ug"][:INTER].reshape(64, 128).T)
    pr["wsc_gate_pm"] = np.ascontiguousarray(w["L0_ffn_Wscale_ug"][INTER:].reshape(64, 128).T)
    pr["wsc_dp_pm"] = np.ascontiguousarray(w["L0_ffn_Wscale_dp"].reshape(16, 128).T)
    pr["Wqkv"] = w["L1_attn_Wqkv"]
    pr["bqkv_pm"] = np.ascontiguousarray(w["L1_attn_bqkv"].reshape(48, 128).T)
    pr["Wout"] = w["L1_attn_Wout"]
    pr["bout_row"] = np.ascontiguousarray(w["L1_attn_bout"][None, :])
    pr["slopes_row"] = np.ascontiguousarray(w["L1_attn_slopes"][None, :])
    fprime = np.arange(INTER)
    perm = 4 * (fprime % 2048) + fprime // 2048
    pr["w_gate_p"] = np.ascontiguousarray(w["L1_ffn_w_gate"][:, perm])
    pr["w_up_p"] = np.ascontiguousarray(w["L1_ffn_w_up"][:, perm])
    pr["w_down_p"] = np.ascontiguousarray(w["L1_ffn_w_down"][perm, :])
    return pr


def build_mega(T):
    p = Prog("mega")
    ext = {}
    for name, shape, dt in [
        ("x", [T, E_DIM], F32), ("identbf", [128, 128], BF16), ("constpm", [128, 2], F32),
        ("rotc", [T // 128, 128, 64], F32), ("rots", [T // 128, 128, 64], F32),
        ("Wq", [E_DIM, E_DIM], F32), ("Wk", [E_DIM, E_DIM], F32), ("Wv", [E_DIM, E_DIM], F32),
        ("Wo", [E_DIM, E_DIM], F32), ("rms_w_row", [1, E_DIM], F32),
        ("dmaskT", [H, 128, 128], F32), ("qdec", [128, H], F32), ("kdec", [128, H], F32),
        ("rC", [128, H], F32), ("ones128f", [1, 128], F32), ("onesbf", [1, 128], BF16),
        ("W_upT", [E_DIM, INTER], F32), ("W_gateT", [E_DIM, INTER], F32),
        ("W_downT", [INTER, E_DIM], F32),
        ("su_ug_pm", [128, 16], F32), ("su_dp_pm", [128, 64], F32),
        ("wsc_up_pm", [128, 64], F32), ("wsc_gate_pm", [128, 64], F32),
        ("wsc_dp_pm", [128, 16], F32),
        ("h128bf", [128, 128], BF16), ("identf32", [128, 128], F32),
        ("Wqkv", [E_DIM, 3 * E_DIM], F32), ("bqkv_pm", [128, 48], F32),
        ("Wout", [E_DIM, E_DIM], F32), ("bout_row", [1, E_DIM], F32),
        ("slopes_row", [1, H], F32), ("diffT64", [64, 64], F32),
        ("negiota64", [64, H], F32), ("negrev64", [64, H], F32),
        ("w_gate_p", [E_DIM, INTER], F32), ("w_up_p", [E_DIM, INTER], F32),
        ("w_down_p", [INTER, E_DIM], F32),
    ]:
        ext[name] = p.inp(name, shape, dt)
    out = p.outp("out_tm", [T, E_DIM], F32)
    d_xn0 = p.dram("xn0_a", [E_DIM, T], BF16)
    d_q0 = p.dram("q0_a", [E_DIM, T], BF16)
    d_k0 = p.dram("k0_a", [E_DIM, T], BF16)
    d_k0t = p.dram("k0t_a", [T, E_DIM], BF16)
    d_v0 = p.dram("v0_a", [T, E_DIM], BF16)
    d_attn0 = p.dram("attn0_a", [T, E_DIM], F32)
    d_ffn0 = p.dram("ffn0_a", [T, E_DIM], F32)
    d_x1 = p.dram("x1_a", [T, E_DIM], F32)
    d_xn1 = p.dram("xn1_a", [E_DIM, T], BF16)
    d_q1 = p.dram("q1_a", [E_DIM, T], BF16)
    d_k1 = p.dram("k1_a", [E_DIM, T], BF16)
    d_k1t = p.dram("k1t_a", [T, E_DIM], BF16)
    d_v1 = p.dram("v1_a", [T, E_DIM], BF16)
    d_attn1 = p.dram("attn1_a", [T, E_DIM], F32)
    d_ffn1 = p.dram("ffn1_a", [T, E_DIM], F32)

    with p.scope():
        build_ln(T, p=p, io={"x": ext["x"], "xn_fm": d_xn0,
                             "identbf": ext["identbf"], "constpm": ext["constpm"]})
    with p.scope():
        build_qkv0(T, p=p, io={"xn_fm": d_xn0, "Wq": ext["Wq"], "Wk": ext["Wk"],
                               "Wv": ext["Wv"], "rotc": ext["rotc"], "rots": ext["rots"],
                               "identbf": ext["identbf"], "qr_fm": d_q0, "kr_fm": d_k0,
                               "kr_tm": d_k0t, "v_tm": d_v0})
    with p.scope():
        build_attn0(T, p=p, io={"qr_fm": d_q0, "kr_fm": d_k0, "kr_tm": d_k0t, "v_tm": d_v0,
                                "Wo": ext["Wo"], "rms_w_row": ext["rms_w_row"],
                                "dmaskT": ext["dmaskT"], "qdec": ext["qdec"],
                                "kdec": ext["kdec"], "rC": ext["rC"],
                                "identbf": ext["identbf"], "ones128f": ext["ones128f"],
                                "constpm": ext["constpm"], "attn0_tm": d_attn0})
    with p.scope():
        build_ffn0(T, p=p, io={"xn_fm": d_xn0, "W_upT": ext["W_upT"],
                               "W_gateT": ext["W_gateT"], "W_downT": ext["W_downT"],
                               "su_ug_pm": ext["su_ug_pm"], "su_dp_pm": ext["su_dp_pm"],
                               "wsc_up_pm": ext["wsc_up_pm"], "wsc_gate_pm": ext["wsc_gate_pm"],
                               "wsc_dp_pm": ext["wsc_dp_pm"], "h128bf": ext["h128bf"],
                               "identf32": ext["identf32"], "ffn0_tm": d_ffn0})
    with p.scope():
        p.prefix = "r0_"
        build_resln(T, with_ln=True, p=p, io={"x": ext["x"], "add0": d_attn0,
                                              "add1": d_ffn0, "x1_tm": d_x1,
                                              "xn_fm": d_xn1, "identbf": ext["identbf"],
                                              "constpm": ext["constpm"]})
        p.prefix = ""
    with p.scope():
        build_qkv1(T, p=p, io={"xn_fm": d_xn1, "Wqkv": ext["Wqkv"],
                               "bqkv_pm": ext["bqkv_pm"], "identbf": ext["identbf"],
                               "q1_fm": d_q1, "k1_fm": d_k1, "k1_tm": d_k1t, "v1_tm": d_v1})
    with p.scope():
        build_attn1(T, p=p, io={"q1_fm": d_q1, "k1_fm": d_k1, "k1_tm": d_k1t, "v1_tm": d_v1,
                                "Wout": ext["Wout"], "bout_row": ext["bout_row"],
                                "slopes_row": ext["slopes_row"], "diffT64": ext["diffT64"],
                                "negiota64": ext["negiota64"], "negrev64": ext["negrev64"],
                                "ones128f": ext["ones128f"], "onesbf": ext["onesbf"],
                                "identbf": ext["identbf"], "attn1_tm": d_attn1})
    with p.scope():
        build_ffn1(T, p=p, io={"xn_fm": d_xn1, "w_gate_p": ext["w_gate_p"],
                               "w_up_p": ext["w_up_p"], "w_down_p": ext["w_down_p"],
                               "identf32": ext["identf32"], "ffn1_tm": d_ffn1})
    with p.scope():
        p.prefix = "r1_"
        build_resln(T, with_ln=False, p=p, io={"x": d_x1, "add0": d_attn1,
                                               "add1": d_ffn1, "x1_tm": out})
        p.prefix = ""
    p.fence()
    return p.emit_and_finalize()


def mega_in_map(pr, x_b):
    m = {k: pr[k] for k in (
        "identbf", "constpm", "rotc", "rots", "Wq", "Wk", "Wv", "Wo", "rms_w_row",
        "dmaskT", "qdec", "kdec", "rC", "ones128f", "onesbf", "W_upT", "W_gateT",
        "W_downT", "su_ug_pm", "su_dp_pm", "wsc_up_pm", "wsc_gate_pm", "wsc_dp_pm",
        "h128bf", "identf32", "Wqkv", "bqkv_pm", "Wout", "bout_row", "slopes_row",
        "diffT64", "negiota64", "negrev64", "w_gate_p", "w_up_p", "w_down_p")}
    m["x"] = np.ascontiguousarray(x_b)
    return m


_NC_CACHE = {}


def _get_nc(key, builder, T):
    if key not in _NC_CACHE:
        _NC_CACHE[key] = builder(T)
    return _NC_CACHE[key]


def forward(x_full, w, T, run):
    B = x_full.shape[0]
    pr = prep_host(w, T)

    def rep(d):
        return [d] * B

    def percore(key_arrs, shared):
        return [dict(shared, **ka) for ka in key_arrs]

    nc = _get_nc(("ln0", T), build_ln, T)
    outs = run(nc, percore([{"x": np.ascontiguousarray(x_full[b])} for b in range(B)],
                           {"identbf": pr["identbf"], "constpm": pr["constpm"]}))
    xn0 = [o["xn_fm"] for o in outs]
    nc = _get_nc(("qkv0", T), build_qkv0, T)
    outs = run(nc, percore([{"xn_fm": xn0[b]} for b in range(B)],
                           {"Wq": pr["Wq"], "Wk": pr["Wk"], "Wv": pr["Wv"],
                            "rotc": pr["rotc"], "rots": pr["rots"], "identbf": pr["identbf"]}))
    q0, k0, k0t, v0 = ([o[k] for o in outs] for k in ("qr_fm", "kr_fm", "kr_tm", "v_tm"))
    nc = _get_nc(("attn0", T), build_attn0, T)
    outs = run(nc, percore(
        [{"qr_fm": q0[b], "kr_fm": k0[b], "kr_tm": k0t[b], "v_tm": v0[b]} for b in range(B)],
        {"Wo": pr["Wo"], "rms_w_row": pr["rms_w_row"], "dmaskT": pr["dmaskT"],
         "qdec": pr["qdec"], "kdec": pr["kdec"], "rC": pr["rC"],
         "identbf": pr["identbf"], "ones128f": pr["ones128f"], "constpm": pr["constpm"]}))
    attn0 = [o["attn0_tm"] for o in outs]
    nc = _get_nc(("ffn0", T), build_ffn0, T)
    outs = run(nc, percore([{"xn_fm": xn0[b]} for b in range(B)],
                           {"W_upT": pr["W_upT"], "W_gateT": pr["W_gateT"],
                            "W_downT": pr["W_downT"], "su_ug_pm": pr["su_ug_pm"],
                            "su_dp_pm": pr["su_dp_pm"], "wsc_up_pm": pr["wsc_up_pm"],
                            "wsc_gate_pm": pr["wsc_gate_pm"], "wsc_dp_pm": pr["wsc_dp_pm"],
                            "h128bf": pr["h128bf"], "identf32": pr["identf32"]}))
    ffn0 = [o["ffn0_tm"] for o in outs]
    nc = _get_nc(("resln", T), lambda TT: build_resln(TT, with_ln=True), T)
    outs = run(nc, percore(
        [{"x": np.ascontiguousarray(x_full[b]), "add0": attn0[b], "add1": ffn0[b]}
         for b in range(B)],
        {"identbf": pr["identbf"], "constpm": pr["constpm"]}))
    x1 = [o["x1_tm"] for o in outs]
    xn1 = [o["xn_fm"] for o in outs]
    nc = _get_nc(("qkv1", T), build_qkv1, T)
    outs = run(nc, percore([{"xn_fm": xn1[b]} for b in range(B)],
                           {"Wqkv": pr["Wqkv"], "bqkv_pm": pr["bqkv_pm"],
                            "identbf": pr["identbf"]}))
    q1, k1, k1t, v1 = ([o[k] for o in outs] for k in ("q1_fm", "k1_fm", "k1_tm", "v1_tm"))
    nc = _get_nc(("attn1", T), build_attn1, T)
    outs = run(nc, percore(
        [{"q1_fm": q1[b], "k1_fm": k1[b], "k1_tm": k1t[b], "v1_tm": v1[b]} for b in range(B)],
        {"Wout": pr["Wout"], "bout_row": pr["bout_row"], "slopes_row": pr["slopes_row"],
         "diffT64": pr["diffT64"], "negiota64": pr["negiota64"], "negrev64": pr["negrev64"],
         "ones128f": pr["ones128f"], "onesbf": pr["onesbf"], "identbf": pr["identbf"]}))
    attn1 = [o["attn1_tm"] for o in outs]
    nc = _get_nc(("ffn1", T), build_ffn1, T)
    outs = run(nc, percore([{"xn_fm": xn1[b]} for b in range(B)],
                           {"w_gate_p": pr["w_gate_p"], "w_up_p": pr["w_up_p"],
                            "w_down_p": pr["w_down_p"], "identf32": pr["identf32"]}))
    ffn1 = [o["ffn1_tm"] for o in outs]
    nc = _get_nc(("res1", T), lambda TT: build_resln(TT, with_ln=False), T)
    outs = run(nc, [{"x": x1[b], "add0": attn1[b], "add1": ffn1[b]} for b in range(B)])
    return np.stack([o["x1_tm"] for o in outs], axis=0)


def kernel(**inputs):
    x = np.asarray(inputs["L0_attn_x"], np.float32)
    T = x.shape[1]
    B = x.shape[0]
    pr = prep_host(inputs, T)
    nc = _get_nc(("mega", T), build_mega, T)
    in_maps = [mega_in_map(pr, x[b]) for b in range(B)]
    res = run_bass_kernel_spmd(nc, in_maps, core_ids=list(range(B)))
    return np.stack([r["out_tm"] for r in res.results], axis=0)


def kernel_multilaunch(**inputs):
    x = np.asarray(inputs["L0_attn_x"], np.float32)
    T = x.shape[1]

    def run(nc, in_maps):
        res = run_bass_kernel_spmd(nc, in_maps, core_ids=list(range(len(in_maps))))
        return res.results

    return forward(x, inputs, T, run)


def sylvester(n):
    Hm = np.array([[1.0]], np.float32)
    while Hm.shape[0] < n:
        Hm = np.block([[Hm, Hm], [Hm, -Hm]])
    return Hm.astype(np.float32)


def load_weight_bf16(p, wsb, w_dram, rows, cols, name=None, ecs_per_dma=8):
    nech = rows // 128
    wre = w_dram.rearrange("(ec pp) c -> pp ec c", pp=128)
    for e0 in range(0, nech, ecs_per_dma):
        e1 = min(e0 + ecs_per_dma, nech)
        p.dma("gpsimd", wsb[:, e0:e1, :], wre[:, e0:e1, :], writes=[wsb])


def _butterfly(p, n, cur, nxt, final_out=None, final_slice=None, width=None, tag=""):
    import math as _m
    stages = int(_m.log2(n))
    bufs = [cur, nxt]
    for s in range(stages):
        h = 1 << s
        src = bufs[s % 2]
        dst = bufs[(s + 1) % 2]
        last = (s == stages - 1) and final_out is not None
        for blk in range(n // (2 * h)):
            for i in range(h):
                ia = blk * 2 * h + i
                ib = ia + h
                for sign, oi in ((ALU.add, ia), (ALU.subtract, ib)):
                    if last:
                        oap = final_out[:, oi, final_slice[0]:final_slice[1]]
                        ow = final_out
                    else:
                        oap = dst[:, oi, :]
                        ow = dst
                    p.exec("vector", lambda e, src=src, ia=ia, ib=ib, oap=oap, sign=sign:
                           e.tensor_tensor(oap, src[:, ia, :], src[:, ib, :], sign),
                           reads=[src], writes=[ow])
    return bufs[stages % 2]


def build_qkv1(T, p=None, io=None):
    standalone = p is None
    if standalone:
        p = Prog("qkv1")
    _oldpfx = p.prefix
    if not standalone:
        p.prefix = "qkv1_" + p.prefix
    def _inp(name, shape, dtype):
        if io is not None and name in io:
            return io[name]
        return _inp(name, shape, dtype)
    def _outp(name, shape, dtype):
        if io is not None and name in io:
            return io[name]
        return _outp(name, shape, dtype)
    xn_fm = _inp("xn_fm", [E_DIM, T], BF16)
    Wqkv = _inp("Wqkv", [E_DIM, 3 * E_DIM], F32)
    bq_pm = _inp("bqkv_pm", [128, 48], F32)
    identbf = _inp("identbf", [128, 128], BF16)
    q_fm = _outp("q1_fm", [E_DIM, T], BF16)
    k_fm = _outp("k1_fm", [E_DIM, T], BF16)
    k_tm = _outp("k1_tm", [T, E_DIM], BF16)
    v_tm = _outp("v1_tm", [T, E_DIM], BF16)
    nech = E_DIM // 128
    TS = min(512, T)
    nts = T // TS
    nci_ = TS // 128

    idb = p.sb("idb", [128, 128], BF16)
    p.dma("sync", idb[:], identbf, writes=[idb])
    bq = p.sb("bq", [128, 48], F32)
    p.dma("sync", bq[:], bq_pm, writes=[bq])
    xnsb = p.sb("xnsb", [128, nech, T], BF16)
    xre = xn_fm.rearrange("(ec pp) t -> pp ec t", pp=128)
    for e0 in range(0, nech, 4):
        p.dma("sync", xnsb[:, e0:e0 + 4, :], xre[:, e0:e0 + 4, :], writes=[xnsb])
    wre = Wqkv.rearrange("(ec pp) o -> pp ec o", pp=128)
    q_re = q_fm.rearrange("(ec pp) t -> ec pp t", pp=128)
    k_re = k_fm.rearrange("(ec pp) t -> ec pp t", pp=128)

    wt = [p.sb(f"wt{i}", [128, nech, 128], BF16) for i in range(2)]
    ebuf = p.sb("ebuf", [128, TS], BF16)
    ttile = p.sb("ttile", [128, 128], BF16)
    ps_m = [p.ps(f"ps_m{i}", [128, TS]) for i in range(2)]
    ps_t = [p.ps(f"ps_t{i}", [128, 128], BF16) for i in range(2)]

    for oc in range(48):
        w = wt[oc % 2]
        for e0 in range(0, nech, 8):
            p.dma("gpsimd", w[:, e0:e0 + 8, :], wre[:, e0:e0 + 8, oc * 128:(oc + 1) * 128],
                  writes=[w])
        for ts in range(nts):
            pm = ps_m[ts % 2]
            for ec in range(nech):
                p.exec("tensor", lambda e, ec=ec, ts=ts, w=w, pm=pm: e.matmul(
                    pm[:], w[:, ec, :], xnsb[:, ec, ts * TS:(ts + 1) * TS],
                    start=(ec == 0), stop=(ec == nech - 1)),
                    reads=[w, xnsb], writes=[pm])
            p.exec("vector", lambda e, oc=oc, pm=pm: e.tensor_scalar_add(
                ebuf[:], pm[:], bq[:, oc:oc + 1]), reads=[pm, bq], writes=[ebuf])
            if oc < 16:
                p.dma("gpsimd", q_re[oc, :, ts * TS:(ts + 1) * TS], ebuf[:], reads=[ebuf])
            elif oc < 32:
                p.dma("gpsimd", k_re[oc - 16, :, ts * TS:(ts + 1) * TS], ebuf[:], reads=[ebuf])
            if oc >= 16:
                tmout = k_tm if oc < 32 else v_tm
                for ci in range(nci_):
                    pt = ps_t[ci % 2]
                    p.exec("tensor", lambda e, ci=ci, pt=pt: e.transpose(
                        pt[:], ebuf[:, ci * 128:(ci + 1) * 128], idb[:]),
                        reads=[ebuf, idb], writes=[pt])
                    p.exec("vector", lambda e, pt=pt: e.tensor_copy(ttile[:], pt[:]),
                           reads=[pt], writes=[ttile])
                    r0 = ts * TS + ci * 128
                    p.dma("sync", tmout[r0:r0 + 128, (oc % 16) * 128:(oc % 16 + 1) * 128],
                          ttile[:], reads=[ttile])
    p.prefix = _oldpfx
    if standalone:
        p.fence()
        return p.emit_and_finalize()
    return None


def build_attn1(T, p=None, io=None):
    standalone = p is None
    if standalone:
        p = Prog("attn1")
    _oldpfx = p.prefix
    if not standalone:
        p.prefix = "attn1_" + p.prefix
    def _inp(name, shape, dtype):
        if io is not None and name in io:
            return io[name]
        return _inp(name, shape, dtype)
    def _outp(name, shape, dtype):
        if io is not None and name in io:
            return io[name]
        return _outp(name, shape, dtype)
    q_fm = _inp("q1_fm", [E_DIM, T], BF16)
    k_fm = _inp("k1_fm", [E_DIM, T], BF16)
    k_tm = _inp("k1_tm", [T, E_DIM], BF16)
    v_tm = _inp("v1_tm", [T, E_DIM], BF16)
    Wout = _inp("Wout", [E_DIM, E_DIM], F32)
    bout_row = _inp("bout_row", [1, E_DIM], F32)
    slopes_row = _inp("slopes_row", [1, H], F32)
    diffT = _inp("diffT64", [64, 64], F32)
    negiota = _inp("negiota64", [64, H], F32)
    negrev = _inp("negrev64", [64, H], F32)
    ones128f = _inp("ones128f", [1, 128], F32)
    onesbf = _inp("onesbf", [1, 128], BF16)
    identbf = _inp("identbf", [128, 128], BF16)
    attn_out = _outp("attn1_tm", [T, E_DIM], F32)
    BLK = 64
    nblk = T // BLK
    nech = E_DIM // 128

    idb = p.sb("idb", [128, 128], BF16)
    onb = p.sb("onb", [1, 128], BF16)
    onf = p.sb("onf", [1, 128], F32)
    slp = p.sb("slp", [1, H], F32)
    dfT = p.sb("dfT", [64, 64], F32)
    nio = p.sb("nio", [64, H], F32)
    nrv = p.sb("nrv", [64, H], F32)
    boutb = p.sb("boutb", [1, E_DIM], BF16)
    wsb = p.sb("wsb", [128, nech, E_DIM], BF16)
    for t, src in ((idb, identbf), (onb, onesbf), (onf, ones128f), (slp, slopes_row),
                   (dfT, diffT), (nio, negiota), (nrv, negrev)):
        p.dma("sync", t[:], src, writes=[t])
    p.dma("gpsimd", boutb[:], bout_row, writes=[boutb])
    load_weight_bf16(p, wsb, Wout, E_DIM, E_DIM)

    srep = p.sb("srep", [128, H], F32)
    nsrep = p.sb("nsrep", [128, H], F32)
    qd1 = p.sb("qd1", [64, H], F32)
    kd1 = p.sb("kd1", [64, H], F32)
    bd1 = p.sb("bd1", [128, H], F32)
    dT = p.sb("dT", [64, H, 64], F32)
    tmp16 = p.sb("tmp16", [128, H], F32)
    with p.scope():
        ps_r = p.ps("ps_r", [128, H])
        p.exec("tensor", lambda e: e.matmul(ps_r[:], onf[:], slp[:], start=True, stop=True),
               reads=[onf, slp], writes=[ps_r])
        p.exec("vector", lambda e: e.tensor_copy(srep[:], ps_r[:]), reads=[ps_r], writes=[srep])
    p.exec("vector", lambda e: e.tensor_scalar_mul(nsrep[:], srep[:], -1.0),
           reads=[srep], writes=[nsrep])
    p.exec("vector", lambda e: e.tensor_mul(tmp16[0:64, :], nio[:], srep[0:64, :]),
           reads=[nio, srep], writes=[tmp16])
    p.exec("scalar", lambda e: e.activation(qd1[:], tmp16[0:64, :], AF.Exp),
           reads=[tmp16], writes=[qd1])
    p.exec("vector", lambda e: e.tensor_mul(tmp16[0:64, :], nrv[:], srep[0:64, :]),
           reads=[nio, srep, qd1], writes=[tmp16])
    p.exec("scalar", lambda e: e.activation(kd1[:], tmp16[0:64, :], AF.Exp),
           reads=[tmp16], writes=[kd1])
    p.exec("vector", lambda e: e.tensor_scalar_mul(tmp16[:], srep[:], -64.0),
           reads=[srep, kd1], writes=[tmp16])
    p.exec("scalar", lambda e: e.activation(bd1[:], tmp16[:], AF.Exp),
           reads=[tmp16], writes=[bd1])
    for h in range(H):
        p.exec("scalar", lambda e, h=h: e.activation(
            dT[:, h, :], dfT[:], AF.Exp, scale=nsrep[0:64, h:h + 1]),
            reads=[dfT, nsrep], writes=[dT])

    qre = q_fm.rearrange("(h d) t -> d h t", d=128)
    kre = k_fm.rearrange("(h d) t -> d h t", d=128)

    ktm = [p.sb(f"ktm{i}", [64, E_DIM], BF16) for i in range(2)]
    vtm = [p.sb(f"vtm{i}", [64, E_DIM], BF16) for i in range(2)]
    qfm = [p.sb(f"qfm{i}", [128, H, 64], BF16) for i in range(2)]
    kfm = [p.sb(f"kfm{i}", [128, H, 64], BF16) for i in range(2)]
    kv32 = [[p.sb(f"kv32_{h}_{i}", [128, 128], F32) for i in range(2)] for h in range(H)]
    kvbf = [p.sb(f"kvbf_{h}", [128, 128], BF16) for h in range(H)]
    kdt = [p.sb(f"kdt{i}", [64, 128], BF16) for i in range(2)]
    smb = [p.sb(f"smb{i}", [64, 64], BF16) for i in range(2)]
    tmpo = [p.sb(f"tmpo{i}", [64, 128], F32) for i in range(2)]
    attn_b = p.sb("attn_b", [64, E_DIM], F32)
    anb = p.sb("anb", [64, E_DIM], BF16)
    afm = p.sb("afm", [128, nech, 64], BF16)
    outc = p.sb("outc", [64, E_DIM], F32)

    ps_st = p.ps("ps_st", [64, 64])
    ps_o = p.ps("ps_o", [64, 128])
    ps_i = p.ps("ps_i", [64, 128])
    ps_kv = p.ps("ps_kv", [128, 128])
    pstr = [p.ps(f"pstr{i}", [128, 64], BF16) for i in range(2)]
    ps_w = [p.ps(f"ps_w{i}", [64, 512]) for i in range(2)]

    for h in range(H):
        p.exec("vector", lambda e, h=h: e.memset(kv32[h][0][:], 0.0), writes=[kv32[h][0]])
        p.exec("gpsimd", lambda e, h=h: e.memset(kvbf[h][:], 0.0), writes=[kvbf[h]])

    for c in range(nblk):
        b = c % 2
        kt, vt, qf, kf = ktm[b], vtm[b], qfm[b], kfm[b]
        p.dma("sync", kt[:], k_tm[c * 64:(c + 1) * 64, :], writes=[kt])
        p.dma("sync", vt[:], v_tm[c * 64:(c + 1) * 64, :], writes=[vt])
        for h in range(H):
            p.dma("sync", qf[:, h, :], qre[:, h, c * 64:(c + 1) * 64], writes=[qf])
            p.dma("sync", kf[:, h, :], kre[:, h, c * 64:(c + 1) * 64], writes=[kf])
        for h in range(H):
            hb = h % 2
            kvp = kv32[h][c % 2]
            kvn = kv32[h][(c + 1) % 2]
            p.exec("vector", lambda e, h=h, hb=hb, kt=kt: e.tensor_scalar_mul(
                kdt[hb][:], kt[:, h * 128:(h + 1) * 128], kd1[:, h:h + 1]),
                reads=[kt, kd1], writes=[kdt[hb]])
            p.exec("tensor", lambda e, h=h, qf=qf, kf=kf: e.matmul(
                ps_st[:], kf[:, h, :], qf[:, h, :], start=True, stop=True),
                reads=[kf, qf], writes=[ps_st])
            p.exec("vector", lambda e, h=h, hb=hb: e.tensor_tensor(
                smb[hb][:], ps_st[:], dT[:, h, :], ALU.mult),
                reads=[ps_st, dT], writes=[smb[hb]])
            p.exec("tensor", lambda e, h=h, hb=hb, vt=vt: e.matmul(
                ps_o[:], smb[hb][:], vt[:, h * 128:(h + 1) * 128], start=True, stop=True),
                reads=[smb[hb], vt], writes=[ps_o])
            p.exec("tensor", lambda e, h=h, qf=qf: e.matmul(
                ps_i[:], qf[:, h, :], kvbf[h][:], start=True, stop=True),
                reads=[qf, kvbf[h]], writes=[ps_i])
            p.exec("scalar", lambda e, hb=hb: e.copy(tmpo[hb][:], ps_o[:]),
                   reads=[ps_o], writes=[tmpo[hb]])
            p.exec("vector", lambda e, h=h, hb=hb: e.scalar_tensor_tensor(
                attn_b[:, h * 128:(h + 1) * 128], ps_i[:], qd1[:, h:h + 1], tmpo[hb][:],
                ALU.mult, ALU.add),
                reads=[ps_i, qd1, tmpo[hb]], writes=[attn_b])
            p.exec("tensor", lambda e, h=h, hb=hb, vt=vt: e.matmul(
                ps_kv[:], kdt[hb][:], vt[:, h * 128:(h + 1) * 128], start=True, stop=True),
                reads=[kdt[hb], vt], writes=[ps_kv])
            p.exec("vector", lambda e, h=h, kvp=kvp, kvn=kvn: e.scalar_tensor_tensor(
                kvn[:], kvp[:], bd1[:, h:h + 1], ps_kv[:], ALU.mult, ALU.add),
                reads=[kvp, bd1, ps_kv], writes=[kvn])
            p.exec("scalar", lambda e, h=h, kvn=kvn: e.copy(kvbf[h][:], kvn[:]),
                   reads=[kvn], writes=[kvbf[h]])
        p.exec("vector", lambda e: e.tensor_copy(anb[:], attn_b[:]),
               reads=[attn_b], writes=[anb])
        for ec in range(nech):
            pb = pstr[ec % 2]
            p.exec("tensor", lambda e, ec=ec, pb=pb: e.transpose(
                pb[:], anb[:, ec * 128:(ec + 1) * 128], idb[0:64, 0:64]),
                reads=[anb, idb], writes=[pb])
            p.exec("vector", lambda e, ec=ec, pb=pb: e.tensor_copy(afm[:, ec, :], pb[:]),
                   reads=[pb], writes=[afm])
        for sl in range(4):
            pw = ps_w[sl % 2]
            for e_ in range(nech):
                p.exec("tensor", lambda en, sl=sl, e_=e_, pw=pw: en.matmul(
                    pw[:], afm[:, e_, :], wsb[:, e_, sl * 512:(sl + 1) * 512],
                    start=(e_ == 0), stop=False),
                    reads=[afm, wsb], writes=[pw])
            p.exec("tensor", lambda en, sl=sl, pw=pw: en.matmul(
                pw[:], onb[:, 0:64], boutb[:, sl * 512:(sl + 1) * 512],
                start=False, stop=True),
                reads=[onb, boutb], writes=[pw])
            p.exec("scalar", lambda en, sl=sl, pw=pw: en.copy(
                outc[:, sl * 512:(sl + 1) * 512], pw[:]), reads=[pw], writes=[outc])
        p.dma("gpsimd", attn_out[c * 64:(c + 1) * 64, :], outc[:], reads=[outc])
    p.prefix = _oldpfx
    if standalone:
        p.fence()
        return p.emit_and_finalize()
    return None


def build_ffn0(T, p=None, io=None):
    standalone = p is None
    if standalone:
        p = Prog("ffn0")
    _oldpfx = p.prefix
    if not standalone:
        p.prefix = "ffn0_" + p.prefix
    def _inp(name, shape, dtype):
        if io is not None and name in io:
            return io[name]
        return _inp(name, shape, dtype)
    def _outp(name, shape, dtype):
        if io is not None and name in io:
            return io[name]
        return _outp(name, shape, dtype)
    xn_fm = _inp("xn_fm", [E_DIM, T], BF16)
    W_upT = _inp("W_upT", [E_DIM, INTER], F32)
    W_gateT = _inp("W_gateT", [E_DIM, INTER], F32)
    W_downT = _inp("W_downT", [INTER, E_DIM], F32)
    su_ug = _inp("su_ug_pm", [128, 16], F32)
    su_dp = _inp("su_dp_pm", [128, 64], F32)
    wsc_up = _inp("wsc_up_pm", [128, 64], F32)
    wsc_gate = _inp("wsc_gate_pm", [128, 64], F32)
    wsc_dp = _inp("wsc_dp_pm", [128, 16], F32)
    h128in = _inp("h128bf", [128, 128], BF16)
    identf = _inp("identf32", [128, 128], F32)
    ffn_out = _outp("ffn0_tm", [T, E_DIM], F32)

    u_dram = p.dram("u_scr", [INTER, T], BF16)
    w_dram = p.dram("w_scr", [INTER, T], BF16)

    nech = E_DIM // 128
    nfch = INTER // 128

    h128sb = p.sb("h128sb", [128, 128], BF16)
    idf = p.sb("idf", [128, 128], F32)
    sug = p.sb("sug", [128, 16], F32)
    sud = p.sb("sud", [128, 64], F32)
    wup_s = p.sb("wup_s", [128, 64], F32)
    wgt_s = p.sb("wgt_s", [128, 64], F32)
    wdp_s = p.sb("wdp_s", [128, 16], F32)
    upscsu = p.sb("upscsu", [128, 64], F32)
    wdps = p.sb("wdps", [128, 16], F32)
    ybf = p.sb("ybf", [128, nech, T], BF16)
    for t, src in ((h128sb, h128in), (idf, identf), (sug, su_ug), (sud, su_dp),
                   (wup_s, wsc_up), (wgt_s, wsc_gate), (wdp_s, wsc_dp)):
        p.dma("sync", t[:], src, writes=[t])
    p.exec("vector", lambda e: e.tensor_mul(upscsu[:], wup_s[:], sud[:]),
           reads=[wup_s, sud], writes=[upscsu])
    p.exec("vector", lambda e: e.tensor_scalar_mul(wdps[:], wdp_s[:], float(1.0 / np.sqrt(INTER))),
           reads=[wdp_s], writes=[wdps])

    xre = xn_fm.rearrange("(ec pp) t -> pp ec t", pp=128)
    wupre = W_upT.rearrange("(ec pp) f -> pp ec f", pp=128)
    wgtre = W_gateT.rearrange("(ec pp) f -> pp ec f", pp=128)
    wdnre = W_downT.rearrange("(fc pp) g -> pp fc g", pp=128)
    u_re = u_dram.rearrange("(fc pp) t -> pp fc t", pp=128)
    w_re = w_dram.rearrange("(fc pp) t -> pp fc t", pp=128)

    with p.scope():
        suh = p.sb("suh", [128, 16, 128], BF16)
        for ec in range(nech):
            p.exec("vector", lambda e, ec=ec: e.tensor_scalar_mul(
                suh[:, ec, :], h128sb[:], sug[:, ec:ec + 1]),
                reads=[h128sb, sug], writes=[suh])
        TS = min(512, T)
        nts = T // TS
        xnb = [p.sb(f"xnb{i}", [128, nech, TS], BF16) for i in range(2)]
        yf = p.sb("yf", [128, nech, TS], F32)
        yg = p.sb("yg", [128, nech, TS], F32)
        ps_a = [p.ps(f"ps_a{i}", [128, TS]) for i in range(2)]
        for tsl in range(nts):
            xb = xnb[tsl % 2]
            for e0 in range(0, nech, 8):
                p.dma("sync", xb[:, e0:e0 + 8, :], xre[:, e0:e0 + 8, tsl * TS:(tsl + 1) * TS],
                      writes=[xb])
            for ec in range(nech):
                pa = ps_a[ec % 2]
                p.exec("tensor", lambda e, ec=ec, xb=xb, pa=pa: e.matmul(
                    pa[:], suh[:, ec, :], xb[:, ec, :], start=True, stop=True),
                    reads=[suh, xb], writes=[pa])
                p.exec("vector", lambda e, ec=ec, pa=pa: e.tensor_scalar_mul(
                    yf[:, ec, :], pa[:], float(1.0 / np.sqrt(E_DIM))),
                    reads=[pa], writes=[yf])
            _butterfly(p, nech, yf, yg, final_out=ybf,
                       final_slice=(tsl * TS, (tsl + 1) * TS))

    with p.scope():
        TS = min(512, T)
        nts = T // TS
        wut = [p.sb(f"wut{i}", [128, nech, 128], BF16) for i in range(2)]
        wgt = [p.sb(f"wgt{i}", [128, nech, 128], BF16) for i in range(2)]
        sg = p.sb("sg", [128, TS], F32)
        ut = p.sb("ut", [128, TS], BF16)
        ps_up = p.ps("ps_up", [128, TS])
        ps_gt = p.ps("ps_gt", [128, TS])
        for fi in range(nfch):
            wu, wg = wut[fi % 2], wgt[fi % 2]
            for e0 in range(0, nech, 8):
                p.dma("gpsimd", wu[:, e0:e0 + 8, :],
                      wupre[:, e0:e0 + 8, fi * 128:(fi + 1) * 128], writes=[wu])
                p.dma("gpsimd", wg[:, e0:e0 + 8, :],
                      wgtre[:, e0:e0 + 8, fi * 128:(fi + 1) * 128], writes=[wg])
            for tsl in range(nts):
                for ec in range(nech):
                    p.exec("tensor", lambda e, ec=ec, tsl=tsl, wu=wu: e.matmul(
                        ps_up[:], wu[:, ec, :], ybf[:, ec, tsl * TS:(tsl + 1) * TS],
                        start=(ec == 0), stop=(ec == nech - 1)),
                        reads=[wu, ybf], writes=[ps_up])
                    p.exec("tensor", lambda e, ec=ec, tsl=tsl, wg=wg: e.matmul(
                        ps_gt[:], wg[:, ec, :], ybf[:, ec, tsl * TS:(tsl + 1) * TS],
                        start=(ec == 0), stop=(ec == nech - 1)),
                        reads=[wg, ybf], writes=[ps_gt])
                p.exec("scalar", lambda e, fi=fi: e.activation(
                    sg[:], ps_gt[:], AF.Silu, scale=wgt_s[:, fi:fi + 1]),
                    reads=[ps_gt, wgt_s], writes=[sg])
                p.exec("vector", lambda e, fi=fi: e.scalar_tensor_tensor(
                    ut[:], ps_up[:], upscsu[:, fi:fi + 1], sg[:], ALU.mult, ALU.mult),
                    reads=[ps_up, upscsu, sg], writes=[ut])
                p.dma("sync", u_re[:, fi, tsl * TS:(tsl + 1) * TS], ut[:], reads=[ut])

    with p.scope():
        TC = min(256, T)
        ntc = T // TC
        ub = p.sb("ub", [128, nfch, TC], BF16)
        vb = p.sb("vb", [128, nfch, TC], F32)
        wt = p.sb("wt", [128, nfch, TC], BF16)
        ps_c = [p.ps(f"ps_c{i}", [128, TC]) for i in range(2)]
        for tc in range(ntc):
            for f0 in range(0, nfch, 8):
                p.dma("sync", ub[:, f0:f0 + 8, :], u_re[:, f0:f0 + 8, tc * TC:(tc + 1) * TC],
                      writes=[ub])
            for fc in range(nfch):
                pc = ps_c[fc % 2]
                p.exec("tensor", lambda e, fc=fc, pc=pc: e.matmul(
                    pc[:], h128sb[:], ub[:, fc, :], start=True, stop=True),
                    reads=[h128sb, ub], writes=[pc])
                p.exec("vector", lambda e, fc=fc, pc=pc: e.tensor_copy(vb[:, fc, :], pc[:]),
                       reads=[pc], writes=[vb])
            _butterfly(p, nfch, vb, ub, final_out=wt, final_slice=(0, TC))
            for f0 in range(0, nfch, 8):
                p.dma("gpsimd", w_re[:, f0:f0 + 8, tc * TC:(tc + 1) * TC], wt[:, f0:f0 + 8, :],
                      reads=[wt])

    with p.scope():
        TQ = min(512, T)
        ntq = T // TQ
        nci = TQ // 128
        wq = p.sb("wq", [128, nfch, TQ], BF16)
        wdt = [p.sb(f"wdt{i}", [128, 128], BF16) for i in range(2)]
        dtile = p.sb("dtile", [128, TQ], F32)
        obuf = [p.sb(f"obuf{i}", [128, E_DIM], F32) for i in range(nci)]
        ps_d = p.ps("ps_d", [128, TQ])
        ps_t = [p.ps(f"ps_t{i}", [128, 128]) for i in range(2)]
        for tq in range(ntq):
            for f0 in range(0, nfch, 8):
                p.dma("sync", wq[:, f0:f0 + 8, :], w_re[:, f0:f0 + 8, tq * TQ:(tq + 1) * TQ],
                      writes=[wq])
            for g in range(nech):
                for fc in range(nfch):
                    wd = wdt[fc % 2]
                    p.dma("gpsimd", wd[:], wdnre[:, fc, g * 128:(g + 1) * 128], writes=[wd])
                    p.exec("tensor", lambda e, fc=fc, wd=wd: e.matmul(
                        ps_d[:], wd[:], wq[:, fc, :], start=(fc == 0), stop=(fc == nfch - 1)),
                        reads=[wd, wq], writes=[ps_d])
                p.exec("vector", lambda e, g=g: e.tensor_scalar_mul(
                    dtile[:], ps_d[:], wdps[:, g:g + 1]),
                    reads=[ps_d, wdps], writes=[dtile])
                for ci in range(nci):
                    pt = ps_t[ci % 2]
                    p.exec("tensor", lambda e, ci=ci, pt=pt: e.transpose(
                        pt[:], dtile[:, ci * 128:(ci + 1) * 128], idf[:]),
                        reads=[dtile, idf], writes=[pt])
                    p.exec("vector", lambda e, ci=ci, g=g, pt=pt: e.tensor_copy(
                        obuf[ci][:, g * 128:(g + 1) * 128], pt[:]),
                        reads=[pt], writes=[obuf[ci]])
            for ci in range(nci):
                p.dma("gpsimd", ffn_out[tq * TQ + ci * 128: tq * TQ + (ci + 1) * 128, :],
                      obuf[ci][:], reads=[obuf[ci]])
    p.prefix = _oldpfx
    if standalone:
        p.fence()
        return p.emit_and_finalize()
    return None


def build_ffn1(T, p=None, io=None):
    standalone = p is None
    if standalone:
        p = Prog("ffn1")
    _oldpfx = p.prefix
    if not standalone:
        p.prefix = "ffn1_" + p.prefix
    def _inp(name, shape, dtype):
        if io is not None and name in io:
            return io[name]
        return _inp(name, shape, dtype)
    def _outp(name, shape, dtype):
        if io is not None and name in io:
            return io[name]
        return _outp(name, shape, dtype)
    xn_fm = _inp("xn_fm", [E_DIM, T], BF16)
    w_gate = _inp("w_gate_p", [E_DIM, INTER], F32)
    w_up = _inp("w_up_p", [E_DIM, INTER], F32)
    w_down = _inp("w_down_p", [INTER, E_DIM], F32)
    identf = _inp("identf32", [128, 128], F32)
    ffn_out = _outp("ffn1_tm", [T, E_DIM], F32)

    h_dram = p.dram("h_scr", [INTER, T], F32)
    hs_dram = p.dram("hs_scr", [INTER, T], BF16)
    nech = E_DIM // 128
    nfch = INTER // 128

    idf = p.sb("idf", [128, 128], F32)
    p.dma("sync", idf[:], identf, writes=[idf])
    xnsb = p.sb("xnsb", [128, nech, T], BF16)
    xre = xn_fm.rearrange("(ec pp) t -> pp ec t", pp=128)
    for e0 in range(0, nech, 4):
        p.dma("sync", xnsb[:, e0:e0 + 4, :], xre[:, e0:e0 + 4, :], writes=[xnsb])
    wgre = w_gate.rearrange("(ec pp) f -> pp ec f", pp=128)
    wure = w_up.rearrange("(ec pp) f -> pp ec f", pp=128)
    wdre = w_down.rearrange("(fc pp) g -> pp fc g", pp=128)
    h_re = h_dram.rearrange("(fc pp) t -> pp fc t", pp=128)
    hs_re = hs_dram.rearrange("(fc pp) t -> pp fc t", pp=128)

    with p.scope():
        TS = min(512, T)
        nts = T // TS
        wut = [p.sb(f"wut{i}", [128, nech, 128], BF16) for i in range(2)]
        wgt = [p.sb(f"wgt{i}", [128, nech, 128], BF16) for i in range(2)]
        sg = p.sb("sg", [128, TS], F32)
        ht = p.sb("ht", [128, TS], F32)
        ps_up = p.ps("ps_up", [128, TS])
        ps_gt = p.ps("ps_gt", [128, TS])
        for fi in range(nfch):
            wu, wg = wut[fi % 2], wgt[fi % 2]
            for e0 in range(0, nech, 8):
                p.dma("gpsimd", wu[:, e0:e0 + 8, :],
                      wure[:, e0:e0 + 8, fi * 128:(fi + 1) * 128], writes=[wu])
                p.dma("gpsimd", wg[:, e0:e0 + 8, :],
                      wgre[:, e0:e0 + 8, fi * 128:(fi + 1) * 128], writes=[wg])
            for tsl in range(nts):
                for ec in range(nech):
                    p.exec("tensor", lambda e, ec=ec, tsl=tsl, wu=wu, TS=TS: e.matmul(
                        ps_up[:], wu[:, ec, :], xnsb[:, ec, tsl * TS:(tsl + 1) * TS],
                        start=(ec == 0), stop=(ec == nech - 1)),
                        reads=[wu, xnsb], writes=[ps_up])
                    p.exec("tensor", lambda e, ec=ec, tsl=tsl, wg=wg, TS=TS: e.matmul(
                        ps_gt[:], wg[:, ec, :], xnsb[:, ec, tsl * TS:(tsl + 1) * TS],
                        start=(ec == 0), stop=(ec == nech - 1)),
                        reads=[wg, xnsb], writes=[ps_gt])
                p.exec("scalar", lambda e: e.activation(sg[:], ps_gt[:], AF.Silu),
                       reads=[ps_gt], writes=[sg])
                p.exec("vector", lambda e: e.tensor_mul(ht[:], ps_up[:], sg[:]),
                       reads=[ps_up, sg], writes=[ht])
                p.dma("sync", h_re[:, fi, tsl * TS:(tsl + 1) * TS], ht[:], reads=[ht])

    with p.scope():
        TS = min(512, T)
        nts = T // TS
        h4 = p.sb("h4", [128, 4, TS], F32)
        a4 = p.sb("a4", [128, 4, TS], F32)
        b6 = p.sb("b6", [128, 6, TS], F32)
        t2 = p.sb("t2", [128, 2, TS], F32)
        hs4 = p.sb("hs4", [128, 4, TS], BF16)
        pairs = [(0, 1), (0, 2), (0, 3), (1, 2), (1, 3), (2, 3)]
        for tsl in range(nts):
            for grp in range(16):
                fcs = [grp, grp + 16, grp + 32, grp + 48]
                for m, fc in enumerate(fcs):
                    p.dma("sync", h4[:, m, :], h_re[:, fc, tsl * TS:(tsl + 1) * TS],
                          writes=[h4])
                for m in range(4):
                    p.exec("scalar", lambda e, m=m: e.activation(
                        a4[:, m, :], h4[:, m, :], AF.Abs), reads=[h4], writes=[a4])
                for bi, (i, j) in enumerate(pairs):
                    p.exec("vector", lambda e, bi=bi, i=i, j=j: e.tensor_tensor(
                        b6[:, bi, :], a4[:, i, :], a4[:, j, :], ALU.is_ge),
                        reads=[a4], writes=[b6])
                p.exec("vector", lambda e: e.tensor_add(t2[:, 0, :], b6[:, 0, :], b6[:, 1, :]),
                       reads=[b6], writes=[t2])
                p.exec("vector", lambda e: e.tensor_add(t2[:, 0, :], t2[:, 0, :], b6[:, 2, :]),
                       reads=[b6, t2], writes=[t2])
                p.exec("vector", lambda e: e.scalar_tensor_tensor(
                    hs4[:, 0, :], t2[:, 0, :], 1.5, h4[:, 0, :], ALU.is_ge, ALU.mult),
                    reads=[t2, h4], writes=[hs4])
                p.exec("vector", lambda e: e.tensor_add(t2[:, 1, :], b6[:, 3, :], b6[:, 4, :]),
                       reads=[b6], writes=[t2])
                p.exec("vector", lambda e: e.tensor_sub(t2[:, 1, :], t2[:, 1, :], b6[:, 0, :]),
                       reads=[b6, t2], writes=[t2])
                p.exec("vector", lambda e: e.scalar_tensor_tensor(
                    hs4[:, 1, :], t2[:, 1, :], 0.5, h4[:, 1, :], ALU.is_ge, ALU.mult),
                    reads=[t2, h4], writes=[hs4])
                p.exec("vector", lambda e: e.tensor_sub(t2[:, 0, :], b6[:, 5, :], b6[:, 1, :]),
                       reads=[b6], writes=[t2])
                p.exec("vector", lambda e: e.tensor_sub(t2[:, 0, :], t2[:, 0, :], b6[:, 3, :]),
                       reads=[b6, t2], writes=[t2])
                p.exec("vector", lambda e: e.scalar_tensor_tensor(
                    hs4[:, 2, :], t2[:, 0, :], -0.5, h4[:, 2, :], ALU.is_ge, ALU.mult),
                    reads=[t2, h4], writes=[hs4])
                p.exec("vector", lambda e: e.tensor_add(t2[:, 1, :], b6[:, 2, :], b6[:, 4, :]),
                       reads=[b6], writes=[t2])
                p.exec("vector", lambda e: e.tensor_add(t2[:, 1, :], t2[:, 1, :], b6[:, 5, :]),
                       reads=[b6, t2], writes=[t2])
                p.exec("vector", lambda e: e.scalar_tensor_tensor(
                    hs4[:, 3, :], t2[:, 1, :], 1.5, h4[:, 3, :], ALU.is_le, ALU.mult),
                    reads=[t2, h4], writes=[hs4])
                for m, fc in enumerate(fcs):
                    p.dma("gpsimd", hs_re[:, fc, tsl * TS:(tsl + 1) * TS], hs4[:, m, :],
                          reads=[hs4])

    with p.scope():
        TQ = min(512, T)
        ntq = T // TQ
        nci = TQ // 128
        wq = p.sb("wq", [128, nfch, TQ], BF16)
        wdt = [p.sb(f"wdt{i}", [128, 128], BF16) for i in range(2)]
        dtile = p.sb("dtile", [128, TQ], F32)
        obuf = [p.sb(f"obuf{i}", [128, E_DIM], F32) for i in range(nci)]
        ps_d = p.ps("ps_d", [128, TQ])
        ps_t = [p.ps(f"ps_t{i}", [128, 128]) for i in range(2)]
        for tq in range(ntq):
            for f0 in range(0, nfch, 8):
                p.dma("sync", wq[:, f0:f0 + 8, :], hs_re[:, f0:f0 + 8, tq * TQ:(tq + 1) * TQ],
                      writes=[wq])
            for g in range(nech):
                for fc in range(nfch):
                    wd = wdt[fc % 2]
                    p.dma("gpsimd", wd[:], wdre[:, fc, g * 128:(g + 1) * 128], writes=[wd])
                    p.exec("tensor", lambda e, fc=fc, wd=wd: e.matmul(
                        ps_d[:], wd[:], wq[:, fc, :], start=(fc == 0), stop=(fc == nfch - 1)),
                        reads=[wd, wq], writes=[ps_d])
                p.exec("vector", lambda e: e.tensor_copy(dtile[:], ps_d[:]),
                       reads=[ps_d], writes=[dtile])
                for ci in range(nci):
                    pt = ps_t[ci % 2]
                    p.exec("tensor", lambda e, ci=ci, pt=pt: e.transpose(
                        pt[:], dtile[:, ci * 128:(ci + 1) * 128], idf[:]),
                        reads=[dtile, idf], writes=[pt])
                    p.exec("vector", lambda e, ci=ci, g=g, pt=pt: e.tensor_copy(
                        obuf[ci][:, g * 128:(g + 1) * 128], pt[:]),
                        reads=[pt], writes=[obuf[ci]])
            for ci in range(nci):
                p.dma("gpsimd", ffn_out[tq * TQ + ci * 128: tq * TQ + (ci + 1) * 128, :],
                      obuf[ci][:], reads=[obuf[ci]])
    p.prefix = _oldpfx
    if standalone:
        p.fence()
        return p.emit_and_finalize()
    return None


def build_resln(T, with_ln=True, nadd=2, p=None, io=None):
    standalone = p is None
    if standalone:
        p = Prog("resln")
    _oldpfx = p.prefix
    if not standalone:
        p.prefix = "resln_" + p.prefix
    def _inp(name, shape, dtype):
        if io is not None and name in io:
            return io[name]
        return _inp(name, shape, dtype)
    def _outp(name, shape, dtype):
        if io is not None and name in io:
            return io[name]
        return _outp(name, shape, dtype)
    x = _inp("x", [T, E_DIM], F32)
    adds = [_inp(f"add{i}", [T, E_DIM], F32) for i in range(nadd)]
    x1_tm = _outp("x1_tm", [T, E_DIM], F32)
    if with_ln:
        identbf = _inp("identbf", [128, 128], BF16)
        constpm = _inp("constpm", [128, 2], F32)
        xn_fm = _outp("xn_fm", [E_DIM, T], BF16)
        idb = p.sb("idb", [128, 128], BF16)
        cp = p.sb("cp", [128, 2], F32)
        p.dma("sync", idb[:], identbf, writes=[idb])
        p.dma("sync", cp[:], constpm, writes=[cp])
        xn_re = xn_fm.rearrange("(ec pp) t -> ec pp t", pp=128)
    nch = T // 128
    nech = E_DIM // 128

    xin = [p.sb(f"xin{i}", [128, E_DIM], F32) for i in range(2)]
    ain = [p.sb(f"ain{i}", [128, E_DIM], F32) for i in range(2)]
    bin_ = [p.sb(f"bin{i}", [128, E_DIM], F32) for i in range(2)]
    x1b = [p.sb(f"x1b{i}", [128, E_DIM], F32) for i in range(2)]
    if with_ln:
        xsq = p.sb("xsq", [128, E_DIM], F32)
        stats = p.sb("stats", [128, 8], F32)
        xn_tm = p.sb("xn_tm", [128, E_DIM], BF16)
        tbuf = [p.sb(f"tbuf{i}", [128, nech, 128], BF16) for i in range(2)]
        pstr = [p.ps(f"pstr{i}", [128, 128], BF16) for i in range(2)]

    for c in range(nch):
        b = c % 2
        xb, ab, bb, x1 = xin[b], ain[b], bin_[b], x1b[b]
        p.dma("sync", xb[:], x[c * 128:(c + 1) * 128, :], writes=[xb])
        p.dma("sync", ab[:], adds[0][c * 128:(c + 1) * 128, :], writes=[ab])
        if nadd > 1:
            p.dma("sync", bb[:], adds[1][c * 128:(c + 1) * 128, :], writes=[bb])
        p.exec("vector", lambda e, xb=xb, ab=ab, x1=x1: e.tensor_add(x1[:], xb[:], ab[:]),
               reads=[xb, ab], writes=[x1])
        if nadd > 1:
            p.exec("vector", lambda e, bb=bb, x1=x1: e.tensor_add(x1[:], x1[:], bb[:]),
                   reads=[bb, x1], writes=[x1])
        p.dma("gpsimd", x1_tm[c * 128:(c + 1) * 128, :], x1[:], reads=[x1])
        if not with_ln:
            continue
        p.exec("scalar", lambda e, x1=x1: e.square(xsq[:], x1[:]), reads=[x1], writes=[xsq])
        p.exec("vector", lambda e, x1=x1: e.reduce_sum(stats[:, 0:1], x1[:], axis=mybir.AxisListType.X),
               reads=[x1], writes=[stats])
        p.exec("vector", lambda e: e.reduce_sum(stats[:, 1:2], xsq[:], axis=mybir.AxisListType.X),
               reads=[xsq], writes=[stats])
        p.exec("vector", lambda e: e.tensor_scalar_mul(stats[:, 2:3], stats[:, 0:1], -1.0 / E_DIM),
               reads=[stats], writes=[stats])
        p.exec("vector", lambda e: e.tensor_scalar_mul(stats[:, 3:4], stats[:, 1:2], 1.0 / E_DIM),
               reads=[stats], writes=[stats])
        p.exec("vector", lambda e: e.tensor_mul(stats[:, 4:5], stats[:, 2:3], stats[:, 2:3]),
               reads=[stats], writes=[stats])
        p.exec("vector", lambda e: e.tensor_sub(stats[:, 5:6], stats[:, 3:4], stats[:, 4:5]),
               reads=[stats], writes=[stats])
        p.exec("scalar", lambda e: e.activation(stats[:, 6:7], stats[:, 5:6], AF.Ln, bias=cp[:, 0:1]),
               reads=[stats, cp], writes=[stats])
        p.exec("scalar", lambda e: e.activation(stats[:, 7:8], stats[:, 6:7], AF.Exp, scale=cp[:, 1:2]),
               reads=[stats, cp], writes=[stats])
        p.exec("vector", lambda e: e.tensor_mul(stats[:, 4:5], stats[:, 2:3], stats[:, 7:8]),
               reads=[stats], writes=[stats])
        p.exec("vector", lambda e, x1=x1: e.tensor_scalar(
            xn_tm[:], x1[:], stats[:, 7:8], stats[:, 4:5], ALU.mult, ALU.add),
            reads=[x1, stats], writes=[xn_tm])
        tb = tbuf[b]
        for ec in range(nech):
            pb = pstr[ec % 2]
            p.exec("tensor", lambda e, ec=ec, pb=pb: e.transpose(
                pb[:], xn_tm[:, ec * 128:(ec + 1) * 128], idb[:]),
                reads=[xn_tm, idb], writes=[pb])
            p.exec("vector", lambda e, ec=ec, pb=pb, tb=tb: e.tensor_copy(tb[:, ec, :], pb[:]),
                   reads=[pb], writes=[tb])
            p.dma("gpsimd", xn_re[ec, :, c * 128:(c + 1) * 128], tb[:, ec, :],
                  reads=[tb])
    p.prefix = _oldpfx
    if standalone:
        p.fence()
        return p.emit_and_finalize()
    return None


def build_attn0(T, p=None, io=None):
    standalone = p is None
    if standalone:
        p = Prog("attn0")
    _oldpfx = p.prefix
    if not standalone:
        p.prefix = "attn0_" + p.prefix
    def _inp(name, shape, dtype):
        if io is not None and name in io:
            return io[name]
        return _inp(name, shape, dtype)
    def _outp(name, shape, dtype):
        if io is not None and name in io:
            return io[name]
        return _outp(name, shape, dtype)
    qr_fm = _inp("qr_fm", [E_DIM, T], BF16)
    kr_fm = _inp("kr_fm", [E_DIM, T], BF16)
    kr_tm = _inp("kr_tm", [T, E_DIM], BF16)
    v_tm = _inp("v_tm", [T, E_DIM], BF16)
    Wo = _inp("Wo", [E_DIM, E_DIM], F32)
    rms_w_row = _inp("rms_w_row", [1, E_DIM], F32)
    dmaskT = _inp("dmaskT", [H, 128, 128], F32)
    qdec = _inp("qdec", [128, H], F32)
    kdec = _inp("kdec", [128, H], F32)
    rC = _inp("rC", [128, H], F32)
    identbf = _inp("identbf", [128, 128], BF16)
    ones128f = _inp("ones128f", [1, 128], F32)
    constpm = _inp("constpm", [128, 2], F32)
    attn_out = _outp("attn0_tm", [T, E_DIM], F32)
    nch = T // 128
    nech = E_DIM // 128

    idb = p.sb("idb", [128, 128], BF16)
    cp = p.sb("cp", [128, 2], F32)
    wsb = p.sb("wsb", [128, nech, E_DIM], BF16)
    dm = p.sb("dm", [128, H, 128], F32)
    qd = p.sb("qd", [128, H], F32)
    kd = p.sb("kd", [128, H], F32)
    rc = p.sb("rc", [128, H], F32)
    ones_sb = p.sb("ones_sb", [1, 128], F32)
    rmsrow = p.sb("rmsrow", [1, E_DIM], F32)
    rmsrep = p.sb("rmsrep", [128, E_DIM], F32)

    p.dma("sync", idb[:], identbf, writes=[idb])
    p.dma("sync", cp[:], constpm, writes=[cp])
    for h_ in range(H):
        p.dma("sync", dm[:, h_, :], dmaskT[h_], writes=[dm])
    p.dma("sync", qd[:], qdec, writes=[qd])
    p.dma("sync", kd[:], kdec, writes=[kd])
    p.dma("sync", rc[:], rC, writes=[rc])
    p.dma("sync", ones_sb[:], ones128f, writes=[ones_sb])
    p.dma("sync", rmsrow[:], rms_w_row, writes=[rmsrow])
    load_weight_bf16(p, wsb, Wo, E_DIM, E_DIM)

    qre = qr_fm.rearrange("(h d) t -> d h t", d=128)
    kre = kr_fm.rearrange("(h d) t -> d h t", d=128)

    ktm = [p.sb(f"ktm{i}", [128, E_DIM], BF16) for i in range(2)]
    vtm = [p.sb(f"vtm{i}", [128, E_DIM], BF16) for i in range(2)]
    qfm = [p.sb(f"qfm{i}", [128, H, 128], BF16) for i in range(2)]
    kfm = [p.sb(f"kfm{i}", [128, H, 128], BF16) for i in range(2)]
    kv32 = [[p.sb(f"kv32_{h}_{i}", [128, 128], F32) for i in range(2)] for h in range(H)]
    kvbf = [p.sb(f"kvbf_{h}", [128, 128], BF16) for h in range(H)]
    kdt = [p.sb(f"kdt{i}", [128, 128], BF16) for i in range(2)]
    smb = [p.sb(f"smb{i}", [128, 128], BF16) for i in range(2)]
    tmpo = [p.sb(f"tmpo{i}", [128, 128], F32) for i in range(2)]
    attn_tm = p.sb("attn_tm", [128, E_DIM], F32)
    asq = p.sb("asq", [128, E_DIM], F32)
    st8 = p.sb("st8", [128, 8], F32)
    anb = p.sb("anb", [128, E_DIM], BF16)
    afm = p.sb("afm", [128, nech, 128], BF16)
    outc = p.sb("outc", [128, E_DIM], F32)

    ps_st = p.ps("ps_st", [128, 128])
    ps_o = p.ps("ps_o", [128, 128])
    ps_i = p.ps("ps_i", [128, 128])
    ps_kv = p.ps("ps_kv", [128, 128])
    pstr = [p.ps(f"pstr{i}", [128, 128], BF16) for i in range(2)]
    ps_w = [p.ps(f"ps_w{i}", [128, 512]) for i in range(2)]

    for sl in range(4):
        pw = ps_w[sl % 2]
        p.exec("tensor", lambda e, sl=sl, pw=pw: e.matmul(
            pw[:], ones_sb[:], rmsrow[:, sl * 512:(sl + 1) * 512], start=True, stop=True),
            reads=[ones_sb, rmsrow], writes=[pw])
        p.exec("vector", lambda e, sl=sl, pw=pw: e.tensor_copy(
            rmsrep[:, sl * 512:(sl + 1) * 512], pw[:]), reads=[pw], writes=[rmsrep])

    for h in range(H):
        p.exec("vector", lambda e, h=h: e.memset(kv32[h][0][:], 0.0), writes=[kv32[h][0]])
        p.exec("gpsimd", lambda e, h=h: e.memset(kvbf[h][:], 0.0), writes=[kvbf[h]])

    for c in range(nch):
        b = c % 2
        kt, vt, qf, kf = ktm[b], vtm[b], qfm[b], kfm[b]
        p.dma("sync", kt[:], kr_tm[c * 128:(c + 1) * 128, :], writes=[kt])
        p.dma("sync", vt[:], v_tm[c * 128:(c + 1) * 128, :], writes=[vt])
        for h in range(H):
            p.dma("sync", qf[:, h, :], qre[:, h, c * 128:(c + 1) * 128], writes=[qf])
            p.dma("sync", kf[:, h, :], kre[:, h, c * 128:(c + 1) * 128], writes=[kf])
        for h in range(H):
            hb = h % 2
            kvp = kv32[h][c % 2]
            kvn = kv32[h][(c + 1) % 2]
            p.exec("vector", lambda e, h=h, hb=hb, kt=kt: e.tensor_scalar_mul(
                kdt[hb][:], kt[:, h * 128:(h + 1) * 128], kd[:, h:h + 1]),
                reads=[kt, kd], writes=[kdt[hb]])
            p.exec("tensor", lambda e, h=h, qf=qf, kf=kf: e.matmul(
                ps_st[:], kf[:, h, :], qf[:, h, :], start=True, stop=True),
                reads=[kf, qf], writes=[ps_st])
            p.exec("vector", lambda e, h=h, hb=hb: e.tensor_tensor(
                smb[hb][:], ps_st[:], dm[:, h, :], ALU.mult),
                reads=[ps_st, dm], writes=[smb[hb]])
            p.exec("tensor", lambda e, h=h, hb=hb, vt=vt: e.matmul(
                ps_o[:], smb[hb][:], vt[:, h * 128:(h + 1) * 128], start=True, stop=True),
                reads=[smb[hb], vt], writes=[ps_o])
            p.exec("tensor", lambda e, h=h, qf=qf: e.matmul(
                ps_i[:], qf[:, h, :], kvbf[h][:], start=True, stop=True),
                reads=[qf, kvbf[h]], writes=[ps_i])
            p.exec("scalar", lambda e, hb=hb: e.copy(tmpo[hb][:], ps_o[:]),
                   reads=[ps_o], writes=[tmpo[hb]])
            p.exec("vector", lambda e, h=h, hb=hb: e.scalar_tensor_tensor(
                attn_tm[:, h * 128:(h + 1) * 128], ps_i[:], qd[:, h:h + 1], tmpo[hb][:],
                ALU.mult, ALU.add),
                reads=[ps_i, qd, tmpo[hb]], writes=[attn_tm])
            p.exec("tensor", lambda e, h=h, hb=hb, vt=vt: e.matmul(
                ps_kv[:], kdt[hb][:], vt[:, h * 128:(h + 1) * 128], start=True, stop=True),
                reads=[kdt[hb], vt], writes=[ps_kv])
            p.exec("vector", lambda e, h=h, kvp=kvp, kvn=kvn: e.scalar_tensor_tensor(
                kvn[:], kvp[:], rc[:, h:h + 1], ps_kv[:], ALU.mult, ALU.add),
                reads=[kvp, rc, ps_kv], writes=[kvn])
            p.exec("scalar", lambda e, h=h, kvn=kvn: e.copy(kvbf[h][:], kvn[:]),
                   reads=[kvn], writes=[kvbf[h]])
        p.exec("scalar", lambda e: e.square(asq[:], attn_tm[:]), reads=[attn_tm], writes=[asq])
        p.exec("vector", lambda e: e.reduce_sum(st8[:, 0:1], asq[:], axis=mybir.AxisListType.X),
               reads=[asq], writes=[st8])
        p.exec("vector", lambda e: e.tensor_scalar_mul(st8[:, 1:2], st8[:, 0:1], 1.0 / E_DIM),
               reads=[st8], writes=[st8])
        p.exec("scalar", lambda e: e.activation(st8[:, 2:3], st8[:, 1:2], AF.Ln, bias=cp[:, 0:1]),
               reads=[st8, cp], writes=[st8])
        p.exec("scalar", lambda e: e.activation(st8[:, 3:4], st8[:, 2:3], AF.Exp, scale=cp[:, 1:2]),
               reads=[st8, cp], writes=[st8])
        p.exec("vector", lambda e: e.scalar_tensor_tensor(
            anb[:], attn_tm[:], st8[:, 3:4], rmsrep[:], ALU.mult, ALU.mult),
            reads=[attn_tm, st8, rmsrep], writes=[anb])
        for ec in range(nech):
            pb = pstr[ec % 2]
            p.exec("tensor", lambda e, ec=ec, pb=pb: e.transpose(
                pb[:], anb[:, ec * 128:(ec + 1) * 128], idb[:]),
                reads=[anb, idb], writes=[pb])
            p.exec("vector", lambda e, ec=ec, pb=pb: e.tensor_copy(afm[:, ec, :], pb[:]),
                   reads=[pb], writes=[afm])
        for sl in range(4):
            pw = ps_w[sl % 2]
            for e_ in range(nech):
                p.exec("tensor", lambda en, sl=sl, e_=e_, pw=pw: en.matmul(
                    pw[:], afm[:, e_, :], wsb[:, e_, sl * 512:(sl + 1) * 512],
                    start=(e_ == 0), stop=(e_ == nech - 1)),
                    reads=[afm, wsb], writes=[pw])
            p.exec("scalar", lambda en, sl=sl, pw=pw: en.copy(
                outc[:, sl * 512:(sl + 1) * 512], pw[:]), reads=[pw], writes=[outc])
        p.dma("gpsimd", attn_out[c * 128:(c + 1) * 128, :], outc[:], reads=[outc])
    p.prefix = _oldpfx
    if standalone:
        p.fence()
        return p.emit_and_finalize()
    return None


def build_qkv0(T, p=None, io=None):
    standalone = p is None
    if standalone:
        p = Prog("qkv0")
    _oldpfx = p.prefix
    if not standalone:
        p.prefix = "qkv0_" + p.prefix
    def _inp(name, shape, dtype):
        if io is not None and name in io:
            return io[name]
        return _inp(name, shape, dtype)
    def _outp(name, shape, dtype):
        if io is not None and name in io:
            return io[name]
        return _outp(name, shape, dtype)
    xn_fm = _inp("xn_fm", [E_DIM, T], BF16)
    Ws = {n: _inp(n, [E_DIM, E_DIM], F32) for n in ("Wq", "Wk", "Wv")}
    rotc = _inp("rotc", [T // 128, 128, 64], F32)
    rots = _inp("rots", [T // 128, 128, 64], F32)
    identbf = _inp("identbf", [128, 128], BF16)
    outs = {
        "q": (_outp("qr_fm", [E_DIM, T], BF16), None),
        "k": (_outp("kr_fm", [E_DIM, T], BF16), _outp("kr_tm", [T, E_DIM], BF16)),
        "v": (None, _outp("v_tm", [T, E_DIM], BF16)),
    }
    nch = T // 128
    nech = E_DIM // 128

    idb = p.sb("idb", [128, 128], BF16)
    p.dma("sync", idb[:], identbf, writes=[idb])
    wsb = p.sb("wsb", [128, nech, E_DIM], BF16)
    xnc = [p.sb(f"xnc{i}", [128, nech, 128], BF16) for i in range(2)]
    pf32 = p.sb("pf32", [128, E_DIM], F32)
    tmpA = p.sb("tmpA", [128, H, 64], F32)
    tmpB = p.sb("tmpB", [128, H, 64], F32)
    ptm = p.sb("ptm", [128, E_DIM], BF16)
    tbuf = [p.sb(f"tbuf{i}", [128, nech, 128], BF16) for i in range(2)]
    cosc = p.sb("cosc", [128, 64], F32)
    sinc = p.sb("sinc", [128, 64], F32)
    psa = [p.ps(f"psa{i}", [128, 512]) for i in range(4)]
    pstr = [p.ps(f"pstr{i}", [128, 128], BF16) for i in range(2)]

    xre = xn_fm.rearrange("(ec pp) t -> pp ec t", pp=128)

    for tname in ("q", "k", "v"):
        W = Ws["W" + tname]
        fm_out, tm_out = outs[tname]
        load_weight_bf16(p, wsb, W, E_DIM, E_DIM)
        fm_re = fm_out.rearrange("(ec pp) t -> ec pp t", pp=128) if fm_out is not None else None
        for c in range(nch):
            xb = xnc[c % 2]
            for e0 in range(0, nech, 8):
                p.dma("sync", xb[:, e0:e0 + 8, :], xre[:, e0:e0 + 8, c * 128:(c + 1) * 128],
                      writes=[xb])
            for e in range(nech):
                for sl in range(4):
                    p.exec("tensor", lambda en, e=e, sl=sl, xb=xb: en.matmul(
                        psa[sl][:], xb[:, e, :], wsb[:, e, sl * 512:(sl + 1) * 512],
                        start=(e == 0), stop=(e == nech - 1)),
                        reads=[xb, wsb], writes=[psa[sl]])
            if tname == "v":
                for sl in range(4):
                    p.exec("vector", lambda en, sl=sl: en.tensor_copy(
                        ptm[:, sl * 512:(sl + 1) * 512], psa[sl][:]),
                        reads=[psa[sl]], writes=[ptm])
            else:
                for sl in range(4):
                    p.exec("vector", lambda en, sl=sl: en.tensor_copy(
                        pf32[:, sl * 512:(sl + 1) * 512], psa[sl][:]),
                        reads=[psa[sl]], writes=[pf32])
                p.dma("sync", cosc[:], rotc[c], writes=[cosc])
                p.dma("sync", sinc[:], rots[c], writes=[sinc])
                v4 = pf32.ap().rearrange("pp (h two d) -> pp h two d", h=H, two=2)
                o4 = ptm.ap().rearrange("pp (h two d) -> pp h two d", h=H, two=2)
                cb = cosc.ap().rearrange("pp (o d) -> pp o d", o=1).broadcast_to([128, H, 64])
                sb_ = sinc.ap().rearrange("pp (o d) -> pp o d", o=1).broadcast_to([128, H, 64])
                x1 = v4[:, :, 0, :]
                x2 = v4[:, :, 1, :]
                p.exec("vector", lambda en, x1=x1, cb=cb: en.tensor_tensor(tmpA.ap(), x1, cb, ALU.mult),
                       reads=[pf32, cosc], writes=[tmpA])
                p.exec("vector", lambda en, x2=x2, sb_=sb_: en.tensor_tensor(tmpB.ap(), x2, sb_, ALU.mult),
                       reads=[pf32, sinc], writes=[tmpB])
                p.exec("vector", lambda en, o4=o4: en.tensor_tensor(
                    o4[:, :, 0, :], tmpA.ap(), tmpB.ap(), ALU.add),
                    reads=[tmpA, tmpB], writes=[ptm])
                p.exec("vector", lambda en, x2=x2, cb=cb: en.tensor_tensor(tmpA.ap(), x2, cb, ALU.mult),
                       reads=[pf32, cosc], writes=[tmpA])
                p.exec("vector", lambda en, x1=x1, sb_=sb_: en.tensor_tensor(tmpB.ap(), x1, sb_, ALU.mult),
                       reads=[pf32, sinc], writes=[tmpB])
                p.exec("vector", lambda en, o4=o4: en.tensor_tensor(
                    o4[:, :, 1, :], tmpA.ap(), tmpB.ap(), ALU.subtract),
                    reads=[tmpA, tmpB], writes=[ptm])
            if tm_out is not None:
                p.dma("gpsimd", tm_out[c * 128:(c + 1) * 128, :], ptm[:], reads=[ptm])
            if fm_out is not None:
                tb = tbuf[c % 2]
                for ec in range(nech):
                    pb = pstr[ec % 2]
                    p.exec("tensor", lambda en, ec=ec, pb=pb: en.transpose(
                        pb[:], ptm[:, ec * 128:(ec + 1) * 128], idb[:]),
                        reads=[ptm, idb], writes=[pb])
                    p.exec("vector", lambda en, ec=ec, pb=pb, tb=tb: en.tensor_copy(
                        tb[:, ec, :], pb[:]), reads=[pb], writes=[tb])
                    p.dma("gpsimd", fm_re[ec, :, c * 128:(c + 1) * 128], tb[:, ec, :],
                          reads=[tb])
    p.prefix = _oldpfx
    if standalone:
        p.fence()
        return p.emit_and_finalize()
    return None
def build_ln(T, xname="x", outname="xn_fm", p=None, io=None):
    standalone = p is None
    if standalone:
        p = Prog("ln")
    _oldpfx = p.prefix
    if not standalone:
        p.prefix = "ln_" + p.prefix
    def _inp(name, shape, dtype):
        if io is not None and name in io:
            return io[name]
        return _inp(name, shape, dtype)
    def _outp(name, shape, dtype):
        if io is not None and name in io:
            return io[name]
        return _outp(name, shape, dtype)
    x = _inp(xname, [T, E_DIM], F32)
    xn_fm = _outp(outname, [E_DIM, T], BF16)
    identbf = _inp("identbf", [128, 128], BF16)
    constpm = _inp("constpm", [128, 2], F32)
    nch = T // 128
    nech = E_DIM // 128

    xin = [p.sb(f"xin{i}", [128, E_DIM], F32) for i in range(2)]
    xsq = p.sb("xsq", [128, E_DIM], F32)
    stats = p.sb("stats", [128, 8], F32)
    xn_tm = p.sb("xn_tm", [128, E_DIM], BF16)
    tbuf = [p.sb(f"tbuf{i}", [128, nech, 128], BF16) for i in range(2)]
    idb = p.sb("idb", [128, 128], BF16)
    pstr = [p.ps(f"pstr{i}", [128, 128], BF16) for i in range(2)]

    cp = p.sb("cp", [128, 2], F32)
    p.dma("sync", cp[:], constpm, writes=[cp])
    p.dma("sync", idb[:], identbf, writes=[idb])
    xn_re = xn_fm.rearrange("(ec pp) t -> ec pp t", pp=128)

    for c in range(nch):
        b = c % 2
        xb = xin[b]
        p.dma("sync", xb[:], x[c * 128:(c + 1) * 128, :], writes=[xb])
        p.exec("scalar", lambda e, i=xb: e.square(xsq[:], i[:]), reads=[xb], writes=[xsq])
        p.exec("vector", lambda e, i=xb: e.reduce_sum(stats[:, 0:1], i[:], axis=mybir.AxisListType.X),
               reads=[xb], writes=[stats])
        p.exec("vector", lambda e: e.reduce_sum(stats[:, 1:2], xsq[:], axis=mybir.AxisListType.X),
               reads=[xsq], writes=[stats])
        p.exec("vector", lambda e: e.tensor_scalar_mul(stats[:, 2:3], stats[:, 0:1], -1.0 / E_DIM),
               reads=[stats], writes=[stats])
        p.exec("vector", lambda e: e.tensor_scalar_mul(stats[:, 3:4], stats[:, 1:2], 1.0 / E_DIM),
               reads=[stats], writes=[stats])
        p.exec("vector", lambda e: e.tensor_mul(stats[:, 4:5], stats[:, 2:3], stats[:, 2:3]),
               reads=[stats], writes=[stats])
        p.exec("vector", lambda e: e.tensor_sub(stats[:, 5:6], stats[:, 3:4], stats[:, 4:5]),
               reads=[stats], writes=[stats])
        p.exec("scalar", lambda e: e.activation(stats[:, 6:7], stats[:, 5:6], AF.Ln, bias=cp[:, 0:1]),
               reads=[stats, cp], writes=[stats])
        p.exec("scalar", lambda e: e.activation(stats[:, 7:8], stats[:, 6:7], AF.Exp, scale=cp[:, 1:2]),
               reads=[stats, cp], writes=[stats])
        p.exec("vector", lambda e: e.tensor_mul(stats[:, 4:5], stats[:, 2:3], stats[:, 7:8]),
               reads=[stats], writes=[stats])
        p.exec("vector", lambda e, i=xb: e.tensor_scalar(
            xn_tm[:], i[:], stats[:, 7:8], stats[:, 4:5], ALU.mult, ALU.add),
            reads=[xb, stats], writes=[xn_tm])
        tb = tbuf[b]
        for ec in range(nech):
            pb = pstr[ec % 2]
            p.exec("tensor", lambda e, ec=ec, pb=pb: e.transpose(
                pb[:], xn_tm[:, ec * 128:(ec + 1) * 128], idb[:]),
                reads=[xn_tm, idb], writes=[pb])
            p.exec("vector", lambda e, ec=ec, pb=pb, tb=tb: e.tensor_copy(tb[:, ec, :], pb[:]),
                   reads=[pb], writes=[tb])
            p.dma("gpsimd", xn_re[ec, :, c * 128:(c + 1) * 128], tb[:, ec, :],
                  reads=[tb])
    p.prefix = _oldpfx
    if standalone:
        p.fence()
        return p.emit_and_finalize()
    return None


# version 03
# speedup vs baseline: 1.5631x; 1.5631x over previous
import numpy as np
import ml_dtypes
from contextlib import ExitStack

import concourse.bass as bass
import concourse.mybir as mybir
from concourse.bass_utils import run_bass_kernel_spmd

F32 = mybir.dt.float32
BF16 = mybir.dt.bfloat16
AF = mybir.ActivationFunctionType
ALU = mybir.AluOpType
BF = ml_dtypes.bfloat16

E_DIM = 2048
H = 16
D = 128
INTER = 8192
EPS = 1e-5

ENGINES = ("sync", "scalar", "vector", "tensor", "gpsimd")
DMA_QUEUES = ("sync", "gpsimd")


class _Op:
    __slots__ = ("eng", "fn", "is_dma", "deps", "inc_needed", "sem_val", "sem_key", "idx")

    def __init__(self, eng, fn, is_dma, idx):
        self.eng = eng
        self.fn = fn
        self.is_dma = is_dma
        self.deps = set()
        self.inc_needed = False
        self.sem_val = None
        self.idx = idx


class Prog:

    def __init__(self, name="prog"):
        self.nc = bass.Bass("TRN2", target_bir_lowering=False)
        self.stack = ExitStack()
        self.ops = []
        self.last_writer = {}
        self.readers = {}
        self.names = set()
        self._barrier = ()
        self._scopes = []
        self._dma_recent = {}
        self.prefix = ""

    from contextlib import contextmanager as _cm

    @_cm
    def scope(self):
        st = ExitStack()
        self._scopes.append(st)
        try:
            yield
        finally:
            self._scopes.pop()
            st.close()
            self.barrier()

    def barrier(self):
        last = {}
        for op in self.ops:
            if not op.is_dma:
                last[op.eng] = op
        bar = set(last.values())
        for q, lst in self._dma_recent.items():
            bar.update(lst)
        self._barrier = tuple(bar)

    def inp(self, name, shape, dtype):
        return self.nc.dram_tensor(name, list(shape), dtype, kind="ExternalInput").ap()

    def outp(self, name, shape, dtype):
        return self.nc.dram_tensor(name, list(shape), dtype, kind="ExternalOutput").ap()

    def dram(self, name, shape, dtype):
        return self.nc.dram_tensor(self.prefix + name, list(shape), dtype, kind="Internal").ap()

    def sb(self, name, shape, dtype):
        name = self.prefix + name
        assert name not in self.names
        self.names.add(name)
        st = self._scopes[-1] if self._scopes else self.stack
        return st.enter_context(self.nc.sbuf_tensor(name, list(shape), dtype))

    def ps(self, name, shape, dtype=F32):
        name = self.prefix + name
        assert name not in self.names
        self.names.add(name)
        st = self._scopes[-1] if self._scopes else self.stack
        return st.enter_context(self.nc.psum_tensor(name, list(shape), dtype))

    def exec(self, eng, fn, reads=(), writes=(), is_dma=False):
        op = _Op(eng, fn, is_dma, len(self.ops))
        deps = set()
        for t in reads:
            w = self.last_writer.get(self._key(t))
            if w is not None:
                deps.add(w)
        for t in writes:
            k = self._key(t)
            deps.update(self.readers.get(k, ()))
            w = self.last_writer.get(k)
            if w is not None:
                deps.add(w)
        deps.update(self._barrier)
        op.deps = {d for d in deps if d is not op}
        self.ops.append(op)
        for t in reads:
            self.readers.setdefault(self._key(t), []).append(op)
        for t in writes:
            k = self._key(t)
            self.last_writer[k] = op
            self.readers[k] = []
        return op

    def dma(self, queue, out, in_, reads=(), writes=(), **kw):
        fn = lambda e, o=out, i=in_, kw=kw: e.dma_start(out=o, in_=i, **kw)
        reads = list(reads) + [in_]
        writes = list(writes) + [out]
        op = self.exec(queue, fn, reads=reads, writes=writes, is_dma=True)
        rec = self._dma_recent.setdefault(queue, [])
        rec.append(op)
        if len(rec) > self.DMA_SLOTS:
            rec.pop(0)
        return op

    @staticmethod
    def _key(t):
        n = getattr(t, "name", None)
        if n is None and hasattr(t, "tensor"):
            n = t.tensor.name
        return n if n is not None else t

    DMA_SLOTS = 8

    def emit_and_finalize(self):
        def _needs_sem(d, op):
            if d.eng == "tensor" and op.eng == "tensor" and not d.is_dma and not op.is_dma:
                return False
            return True

        for op in self.ops:
            for d in op.deps:
                if _needs_sem(d, op):
                    d.inc_needed = True
        for op in self.ops:
            if op.is_dma:
                op.inc_needed = True

        sems = {}
        counters = {}
        slot_rr = {}
        slot_prev = {}
        for op in self.ops:
            if not op.inc_needed:
                continue
            if op.is_dma:
                q = op.eng
                s = slot_rr.get(q, 0)
                slot_rr[q] = (s + 1) % self.DMA_SLOTS
                key = (q, "d", s)
                prev = slot_prev.get(key)
                if prev is not None:
                    op.deps.add(prev)
                slot_prev[key] = op
            else:
                key = (op.eng, "c", 0)
            if key not in sems:
                sems[key] = self.stack.enter_context(
                    self.nc.semaphore(f"s_{key[0]}_{key[1]}{key[2]}")
                )
            step = 16 if op.is_dma else 1
            counters[key] = counters.get(key, 0) + step
            op.sem_val = counters[key]
            op.sem_key = key
        self._sems = sems

        streams = {e: [] for e in ENGINES}
        floors = {e: {} for e in ENGINES}
        for op in self.ops:
            waits = {}
            fl = floors[op.eng]
            for d in op.deps:
                if not _needs_sem(d, op):
                    continue
                key = d.sem_key
                if fl.get(key, 0) >= d.sem_val:
                    continue
                waits[key] = max(waits.get(key, 0), d.sem_val)
            for key, val in waits.items():
                fl[key] = val
            streams[op.eng].append((op, sorted(waits.items(), key=lambda kv: str(kv[0]))))

        block = self.stack.enter_context(self.nc.Block())
        for e in ENGINES:
            if not streams[e]:
                continue

            def body(eng, ops=streams[e]):
                for op, waits in ops:
                    for (key, val) in waits:
                        eng.wait_ge(sems[key], val)
                    inst = op.fn(eng)
                    if op.inc_needed:
                        inst.then_inc(sems[op.sem_key], 16 if op.is_dma else 1)

            getattr(block, e)(body)
        self.stack.close()
        self.nc.finalize()
        return self.nc

    def fence(self, eng="gpsimd"):
        pend = set()
        for k, w in self.last_writer.items():
            pend.add(w)
        for k, rs in self.readers.items():
            pend.update(rs)

        def fn(e):
            return e.nop()

        op = _Op(eng, fn, False, len(self.ops))
        op.deps = {d for d in pend if d is not op}
        self.ops.append(op)
        return op


def cdiv(a, b):
    return (a + b - 1) // b


def prep_host(w, T):
    import math
    pr = {}
    pr["identbf"] = np.eye(128).astype(BF)
    pr["identf32"] = np.eye(128, dtype=np.float32)
    pr["h128bf"] = sylvester(128).astype(BF)
    pr["ones128f"] = np.ones((1, 128), np.float32)
    pr["onesbf"] = np.ones((1, 128)).astype(BF)
    pr["constpm"] = np.tile(np.array([[EPS, -0.5]], np.float32), (128, 1))
    freq = (1.0 / 1024.0) ** np.linspace(0.0, 1.0, 32, dtype=np.float32)
    freq = np.concatenate([freq, np.zeros(32, np.float32)])
    t = np.arange(T, dtype=np.float32)
    theta = t[:, None] * freq[None, :]
    pr["rotc"] = np.ascontiguousarray(np.cos(theta).astype(np.float32).reshape(T // 128, 128, 64))
    pr["rots"] = np.ascontiguousarray(np.sin(theta).astype(np.float32).reshape(T // 128, 128, 64))
    start = 2.0 ** (-2.0 ** (-(math.log2(H) - 3)))
    slopes0 = np.array([start * start ** i for i in range(H)], np.float64)
    r = np.exp(-slopes0)
    i = np.arange(128, dtype=np.float64)
    pr["qdec"] = (r[None, :] ** (i[:, None] + 1.0)).astype(np.float32)
    pr["kdec"] = (r[None, :] ** (127.0 - i[:, None])).astype(np.float32)
    pr["rC"] = np.tile((r ** 128.0).astype(np.float32)[None, :], (128, 1))
    diff = i[:, None] - i[None, :]
    dmaskT = np.zeros((H, 128, 128), np.float32)
    for h in range(H):
        dm = np.where(diff >= 0, r[h] ** np.maximum(diff, 0), 0.0)
        dmaskT[h] = dm.T.astype(np.float32)
    pr["dmaskT"] = dmaskT
    j64 = np.arange(64, dtype=np.float32)
    dT = j64[None, :] - j64[:, None]
    pr["diffT64"] = np.where(dT >= 0, dT, 1e6).astype(np.float32)
    pr["negiota64"] = np.tile(-j64[:, None], (1, H)).astype(np.float32)
    pr["negrev64"] = np.tile(-(64.0 - j64)[:, None], (1, H)).astype(np.float32)
    pr["Wq"] = w["L0_attn_Wq"]
    pr["Wk"] = w["L0_attn_Wk"]
    pr["Wv"] = w["L0_attn_Wv"]
    pr["Wo"] = w["L0_attn_Wo"]
    pr["rms_w_row"] = np.ascontiguousarray(w["L0_attn_rms_w"][None, :])
    pr["W_upT"] = np.ascontiguousarray(w["L0_ffn_W_up"].T)
    pr["W_gateT"] = np.ascontiguousarray(w["L0_ffn_W_gate"].T)
    pr["W_downT"] = np.ascontiguousarray(w["L0_ffn_W_down"].T)
    pr["su_ug_pm"] = np.ascontiguousarray(w["L0_ffn_SU_ug"].reshape(16, 128).T)
    pr["su_dp_pm"] = np.ascontiguousarray(w["L0_ffn_SU_dp"].reshape(64, 128).T)
    pr["wsc_up_pm"] = np.ascontiguousarray(w["L0_ffn_Wscale_ug"][:INTER].reshape(64, 128).T)
    pr["wsc_gate_pm"] = np.ascontiguousarray(w["L0_ffn_Wscale_ug"][INTER:].reshape(64, 128).T)
    pr["wsc_dp_pm"] = np.ascontiguousarray(w["L0_ffn_Wscale_dp"].reshape(16, 128).T)
    pr["Wqkv"] = w["L1_attn_Wqkv"]
    pr["bqkv_pm"] = np.ascontiguousarray(w["L1_attn_bqkv"].reshape(48, 128).T)
    pr["Wout"] = w["L1_attn_Wout"]
    pr["bout_row"] = np.ascontiguousarray(w["L1_attn_bout"][None, :])
    pr["slopes_row"] = np.ascontiguousarray(w["L1_attn_slopes"][None, :])
    fprime = np.arange(INTER)
    perm = 4 * (fprime % 2048) + fprime // 2048
    pr["w_gate_p"] = np.ascontiguousarray(w["L1_ffn_w_gate"][:, perm])
    pr["w_up_p"] = np.ascontiguousarray(w["L1_ffn_w_up"][:, perm])
    pr["w_down_p"] = np.ascontiguousarray(w["L1_ffn_w_down"][perm, :])
    return pr


def build_mega(T):
    p = Prog("mega")
    ext = {}
    for name, shape, dt in [
        ("x", [T, E_DIM], F32), ("identbf", [128, 128], BF16), ("constpm", [128, 2], F32),
        ("rotc", [T // 128, 128, 64], F32), ("rots", [T // 128, 128, 64], F32),
        ("Wq", [E_DIM, E_DIM], F32), ("Wk", [E_DIM, E_DIM], F32), ("Wv", [E_DIM, E_DIM], F32),
        ("Wo", [E_DIM, E_DIM], F32), ("rms_w_row", [1, E_DIM], F32),
        ("dmaskT", [H, 128, 128], F32), ("qdec", [128, H], F32), ("kdec", [128, H], F32),
        ("rC", [128, H], F32), ("ones128f", [1, 128], F32), ("onesbf", [1, 128], BF16),
        ("W_upT", [E_DIM, INTER], F32), ("W_gateT", [E_DIM, INTER], F32),
        ("W_downT", [INTER, E_DIM], F32),
        ("su_ug_pm", [128, 16], F32), ("su_dp_pm", [128, 64], F32),
        ("wsc_up_pm", [128, 64], F32), ("wsc_gate_pm", [128, 64], F32),
        ("wsc_dp_pm", [128, 16], F32),
        ("h128bf", [128, 128], BF16), ("identf32", [128, 128], F32),
        ("Wqkv", [E_DIM, 3 * E_DIM], F32), ("bqkv_pm", [128, 48], F32),
        ("Wout", [E_DIM, E_DIM], F32), ("bout_row", [1, E_DIM], F32),
        ("slopes_row", [1, H], F32), ("diffT64", [64, 64], F32),
        ("negiota64", [64, H], F32), ("negrev64", [64, H], F32),
        ("w_gate_p", [E_DIM, INTER], F32), ("w_up_p", [E_DIM, INTER], F32),
        ("w_down_p", [INTER, E_DIM], F32),
    ]:
        ext[name] = p.inp(name, shape, dt)
    out = p.outp("out_tm", [T, E_DIM], F32)
    d_xn0 = p.dram("xn0_a", [E_DIM, T], BF16)
    d_q0 = p.dram("q0_a", [E_DIM, T], BF16)
    d_k0 = p.dram("k0_a", [E_DIM, T], BF16)
    d_k0t = p.dram("k0t_a", [T, E_DIM], BF16)
    d_v0 = p.dram("v0_a", [T, E_DIM], BF16)
    d_attn0 = p.dram("attn0_a", [T, E_DIM], F32)
    d_ffn0 = p.dram("ffn0_a", [T, E_DIM], F32)
    d_x1 = p.dram("x1_a", [T, E_DIM], F32)
    d_xn1 = p.dram("xn1_a", [E_DIM, T], BF16)
    d_q1 = p.dram("q1_a", [E_DIM, T], BF16)
    d_k1 = p.dram("k1_a", [E_DIM, T], BF16)
    d_k1t = p.dram("k1t_a", [T, E_DIM], BF16)
    d_v1 = p.dram("v1_a", [T, E_DIM], BF16)
    d_attn1 = p.dram("attn1_a", [T, E_DIM], F32)
    d_ffn1 = p.dram("ffn1_a", [T, E_DIM], F32)

    with p.scope():
        build_ln(T, p=p, io={"x": ext["x"], "xn_fm": d_xn0,
                             "identbf": ext["identbf"], "constpm": ext["constpm"]})
    with p.scope():
        build_qkv0(T, p=p, io={"xn_fm": d_xn0, "Wq": ext["Wq"], "Wk": ext["Wk"],
                               "Wv": ext["Wv"], "rotc": ext["rotc"], "rots": ext["rots"],
                               "identbf": ext["identbf"], "qr_fm": d_q0, "kr_fm": d_k0,
                               "kr_tm": d_k0t, "v_tm": d_v0})
    with p.scope():
        build_attn0(T, p=p, io={"qr_fm": d_q0, "kr_fm": d_k0, "kr_tm": d_k0t, "v_tm": d_v0,
                                "Wo": ext["Wo"], "rms_w_row": ext["rms_w_row"],
                                "dmaskT": ext["dmaskT"], "qdec": ext["qdec"],
                                "kdec": ext["kdec"], "rC": ext["rC"],
                                "identbf": ext["identbf"], "ones128f": ext["ones128f"],
                                "constpm": ext["constpm"], "attn0_tm": d_attn0})
    with p.scope():
        build_ffn0(T, p=p, io={"xn_fm": d_xn0, "W_upT": ext["W_upT"],
                               "W_gateT": ext["W_gateT"], "W_downT": ext["W_downT"],
                               "su_ug_pm": ext["su_ug_pm"], "su_dp_pm": ext["su_dp_pm"],
                               "wsc_up_pm": ext["wsc_up_pm"], "wsc_gate_pm": ext["wsc_gate_pm"],
                               "wsc_dp_pm": ext["wsc_dp_pm"], "h128bf": ext["h128bf"],
                               "identf32": ext["identf32"], "ffn0_tm": d_ffn0})
    with p.scope():
        p.prefix = "r0_"
        build_resln(T, with_ln=True, p=p, io={"x": ext["x"], "add0": d_attn0,
                                              "add1": d_ffn0, "x1_tm": d_x1,
                                              "xn_fm": d_xn1, "identbf": ext["identbf"],
                                              "constpm": ext["constpm"]})
        p.prefix = ""
    with p.scope():
        build_qkv1(T, p=p, io={"xn_fm": d_xn1, "Wqkv": ext["Wqkv"],
                               "bqkv_pm": ext["bqkv_pm"], "identbf": ext["identbf"],
                               "q1_fm": d_q1, "k1_fm": d_k1, "k1_tm": d_k1t, "v1_tm": d_v1})
    with p.scope():
        build_attn1(T, p=p, io={"q1_fm": d_q1, "k1_fm": d_k1, "k1_tm": d_k1t, "v1_tm": d_v1,
                                "Wout": ext["Wout"], "bout_row": ext["bout_row"],
                                "slopes_row": ext["slopes_row"], "diffT64": ext["diffT64"],
                                "negiota64": ext["negiota64"], "negrev64": ext["negrev64"],
                                "ones128f": ext["ones128f"], "onesbf": ext["onesbf"],
                                "identbf": ext["identbf"], "attn1_tm": d_attn1})
    with p.scope():
        build_ffn1(T, p=p, io={"xn_fm": d_xn1, "w_gate_p": ext["w_gate_p"],
                               "w_up_p": ext["w_up_p"], "w_down_p": ext["w_down_p"],
                               "identf32": ext["identf32"], "ffn1_tm": d_ffn1})
    with p.scope():
        p.prefix = "r1_"
        build_resln(T, with_ln=False, p=p, io={"x": d_x1, "add0": d_attn1,
                                               "add1": d_ffn1, "x1_tm": out})
        p.prefix = ""
    p.fence()
    return p.emit_and_finalize()


def mega_in_map(pr, x_b):
    m = {k: pr[k] for k in (
        "identbf", "constpm", "rotc", "rots", "Wq", "Wk", "Wv", "Wo", "rms_w_row",
        "dmaskT", "qdec", "kdec", "rC", "ones128f", "onesbf", "W_upT", "W_gateT",
        "W_downT", "su_ug_pm", "su_dp_pm", "wsc_up_pm", "wsc_gate_pm", "wsc_dp_pm",
        "h128bf", "identf32", "Wqkv", "bqkv_pm", "Wout", "bout_row", "slopes_row",
        "diffT64", "negiota64", "negrev64", "w_gate_p", "w_up_p", "w_down_p")}
    m["x"] = np.ascontiguousarray(x_b)
    return m


_NC_CACHE = {}


def _get_nc(key, builder, T):
    if key not in _NC_CACHE:
        _NC_CACHE[key] = builder(T)
    return _NC_CACHE[key]


def forward(x_full, w, T, run):
    B = x_full.shape[0]
    pr = prep_host(w, T)

    def rep(d):
        return [d] * B

    def percore(key_arrs, shared):
        return [dict(shared, **ka) for ka in key_arrs]

    nc = _get_nc(("ln0", T), build_ln, T)
    outs = run(nc, percore([{"x": np.ascontiguousarray(x_full[b])} for b in range(B)],
                           {"identbf": pr["identbf"], "constpm": pr["constpm"]}))
    xn0 = [o["xn_fm"] for o in outs]
    nc = _get_nc(("qkv0", T), build_qkv0, T)
    outs = run(nc, percore([{"xn_fm": xn0[b]} for b in range(B)],
                           {"Wq": pr["Wq"], "Wk": pr["Wk"], "Wv": pr["Wv"],
                            "rotc": pr["rotc"], "rots": pr["rots"], "identbf": pr["identbf"]}))
    q0, k0, k0t, v0 = ([o[k] for o in outs] for k in ("qr_fm", "kr_fm", "kr_tm", "v_tm"))
    nc = _get_nc(("attn0", T), build_attn0, T)
    outs = run(nc, percore(
        [{"qr_fm": q0[b], "kr_fm": k0[b], "kr_tm": k0t[b], "v_tm": v0[b]} for b in range(B)],
        {"Wo": pr["Wo"], "rms_w_row": pr["rms_w_row"], "dmaskT": pr["dmaskT"],
         "qdec": pr["qdec"], "kdec": pr["kdec"], "rC": pr["rC"],
         "identbf": pr["identbf"], "ones128f": pr["ones128f"], "constpm": pr["constpm"]}))
    attn0 = [o["attn0_tm"] for o in outs]
    nc = _get_nc(("ffn0", T), build_ffn0, T)
    outs = run(nc, percore([{"xn_fm": xn0[b]} for b in range(B)],
                           {"W_upT": pr["W_upT"], "W_gateT": pr["W_gateT"],
                            "W_downT": pr["W_downT"], "su_ug_pm": pr["su_ug_pm"],
                            "su_dp_pm": pr["su_dp_pm"], "wsc_up_pm": pr["wsc_up_pm"],
                            "wsc_gate_pm": pr["wsc_gate_pm"], "wsc_dp_pm": pr["wsc_dp_pm"],
                            "h128bf": pr["h128bf"], "identf32": pr["identf32"]}))
    ffn0 = [o["ffn0_tm"] for o in outs]
    nc = _get_nc(("resln", T), lambda TT: build_resln(TT, with_ln=True), T)
    outs = run(nc, percore(
        [{"x": np.ascontiguousarray(x_full[b]), "add0": attn0[b], "add1": ffn0[b]}
         for b in range(B)],
        {"identbf": pr["identbf"], "constpm": pr["constpm"]}))
    x1 = [o["x1_tm"] for o in outs]
    xn1 = [o["xn_fm"] for o in outs]
    nc = _get_nc(("qkv1", T), build_qkv1, T)
    outs = run(nc, percore([{"xn_fm": xn1[b]} for b in range(B)],
                           {"Wqkv": pr["Wqkv"], "bqkv_pm": pr["bqkv_pm"],
                            "identbf": pr["identbf"]}))
    q1, k1, k1t, v1 = ([o[k] for o in outs] for k in ("q1_fm", "k1_fm", "k1_tm", "v1_tm"))
    nc = _get_nc(("attn1", T), build_attn1, T)
    outs = run(nc, percore(
        [{"q1_fm": q1[b], "k1_fm": k1[b], "k1_tm": k1t[b], "v1_tm": v1[b]} for b in range(B)],
        {"Wout": pr["Wout"], "bout_row": pr["bout_row"], "slopes_row": pr["slopes_row"],
         "diffT64": pr["diffT64"], "negiota64": pr["negiota64"], "negrev64": pr["negrev64"],
         "ones128f": pr["ones128f"], "onesbf": pr["onesbf"], "identbf": pr["identbf"]}))
    attn1 = [o["attn1_tm"] for o in outs]
    nc = _get_nc(("ffn1", T), build_ffn1, T)
    outs = run(nc, percore([{"xn_fm": xn1[b]} for b in range(B)],
                           {"w_gate_p": pr["w_gate_p"], "w_up_p": pr["w_up_p"],
                            "w_down_p": pr["w_down_p"], "identf32": pr["identf32"]}))
    ffn1 = [o["ffn1_tm"] for o in outs]
    nc = _get_nc(("res1", T), lambda TT: build_resln(TT, with_ln=False), T)
    outs = run(nc, [{"x": x1[b], "add0": attn1[b], "add1": ffn1[b]} for b in range(B)])
    return np.stack([o["x1_tm"] for o in outs], axis=0)


def kernel(**inputs):
    x = np.asarray(inputs["L0_attn_x"], np.float32)
    T = x.shape[1]
    B = x.shape[0]
    pr = prep_host(inputs, T)
    nc = _get_nc(("mega", T), build_mega, T)
    in_maps = [mega_in_map(pr, x[b]) for b in range(B)]
    res = run_bass_kernel_spmd(nc, in_maps, core_ids=list(range(B)))
    return np.stack([r["out_tm"] for r in res.results], axis=0)


def kernel_multilaunch(**inputs):
    x = np.asarray(inputs["L0_attn_x"], np.float32)
    T = x.shape[1]

    def run(nc, in_maps):
        res = run_bass_kernel_spmd(nc, in_maps, core_ids=list(range(len(in_maps))))
        return res.results

    return forward(x, inputs, T, run)


def sylvester(n):
    Hm = np.array([[1.0]], np.float32)
    while Hm.shape[0] < n:
        Hm = np.block([[Hm, Hm], [Hm, -Hm]])
    return Hm.astype(np.float32)


def load_weight_bf16(p, wsb, w_dram, rows, cols, name=None, ecs_per_dma=8):
    nech = rows // 128
    wre = w_dram.rearrange("(ec pp) c -> pp ec c", pp=128)
    for e0 in range(0, nech, ecs_per_dma):
        e1 = min(e0 + ecs_per_dma, nech)
        p.dma("gpsimd", wsb[:, e0:e1, :], wre[:, e0:e1, :], writes=[wsb])


def _butterfly(p, n, cur, nxt, final_out=None, final_slice=None, width=None, tag=""):
    import math as _m
    stages = int(_m.log2(n))
    bufs = [cur, nxt]
    for s in range(stages):
        h = 1 << s
        src = bufs[s % 2]
        dst = bufs[(s + 1) % 2]
        last = (s == stages - 1) and final_out is not None
        for blk in range(n // (2 * h)):
            for i in range(h):
                ia = blk * 2 * h + i
                ib = ia + h
                for sign, oi in ((ALU.add, ia), (ALU.subtract, ib)):
                    if last:
                        oap = final_out[:, oi, final_slice[0]:final_slice[1]]
                        ow = final_out
                    else:
                        oap = dst[:, oi, :]
                        ow = dst
                    p.exec("vector", lambda e, src=src, ia=ia, ib=ib, oap=oap, sign=sign:
                           e.tensor_tensor(oap, src[:, ia, :], src[:, ib, :], sign),
                           reads=[src], writes=[ow])
    return bufs[stages % 2]


def build_qkv1(T, p=None, io=None):
    standalone = p is None
    if standalone:
        p = Prog("qkv1")
    _oldpfx = p.prefix
    if not standalone:
        p.prefix = "qkv1_" + p.prefix
    def _inp(name, shape, dtype):
        if io is not None and name in io:
            return io[name]
        return _inp(name, shape, dtype)
    def _outp(name, shape, dtype):
        if io is not None and name in io:
            return io[name]
        return _outp(name, shape, dtype)
    xn_fm = _inp("xn_fm", [E_DIM, T], BF16)
    Wqkv = _inp("Wqkv", [E_DIM, 3 * E_DIM], F32)
    bq_pm = _inp("bqkv_pm", [128, 48], F32)
    identbf = _inp("identbf", [128, 128], BF16)
    q_fm = _outp("q1_fm", [E_DIM, T], BF16)
    k_fm = _outp("k1_fm", [E_DIM, T], BF16)
    k_tm = _outp("k1_tm", [T, E_DIM], BF16)
    v_tm = _outp("v1_tm", [T, E_DIM], BF16)
    nech = E_DIM // 128
    TS = min(512, T)
    nts = T // TS
    nci_ = TS // 128

    idb = p.sb("idb", [128, 128], BF16)
    p.dma("sync", idb[:], identbf, writes=[idb])
    bq = p.sb("bq", [128, 48], F32)
    p.dma("sync", bq[:], bq_pm, writes=[bq])
    xnsb = p.sb("xnsb", [128, nech, T], BF16)
    xre = xn_fm.rearrange("(ec pp) t -> pp ec t", pp=128)
    for e0 in range(0, nech, 4):
        p.dma("sync", xnsb[:, e0:e0 + 4, :], xre[:, e0:e0 + 4, :], writes=[xnsb])
    wre = Wqkv.rearrange("(ec pp) o -> pp ec o", pp=128)
    q_re = q_fm.rearrange("(ec pp) t -> ec pp t", pp=128)
    k_re = k_fm.rearrange("(ec pp) t -> ec pp t", pp=128)

    wt = [p.sb(f"wt{i}", [128, nech, 128], BF16) for i in range(2)]
    ebuf_ = [p.sb(f"ebuf{i}", [128, TS], BF16) for i in range(2)]
    ttile_ = [p.sb(f"ttile{i}", [128, 128], BF16) for i in range(2)]
    ps_m = [p.ps(f"ps_m{i}", [128, TS]) for i in range(2)]
    ps_t = [p.ps(f"ps_t{i}", [128, 128], BF16) for i in range(2)]

    for oc in range(48):
        w = wt[oc % 2]
        for e0 in range(0, nech, 8):
            p.dma("gpsimd", w[:, e0:e0 + 8, :], wre[:, e0:e0 + 8, oc * 128:(oc + 1) * 128],
                  writes=[w])
        for ts in range(nts):
            pm = ps_m[ts % 2]
            ebuf = ebuf_[(oc * nts + ts) % 2]
            for ec in range(nech):
                p.exec("tensor", lambda e, ec=ec, ts=ts, w=w, pm=pm: e.matmul(
                    pm[:], w[:, ec, :], xnsb[:, ec, ts * TS:(ts + 1) * TS],
                    start=(ec == 0), stop=(ec == nech - 1)),
                    reads=[w, xnsb], writes=[pm])
            p.exec("vector", lambda e, oc=oc, pm=pm, ebuf=ebuf: e.tensor_scalar_add(
                ebuf[:], pm[:], bq[:, oc:oc + 1]), reads=[pm, bq], writes=[ebuf])
            if oc < 16:
                p.dma("gpsimd", q_re[oc, :, ts * TS:(ts + 1) * TS], ebuf[:], reads=[ebuf])
            elif oc < 32:
                p.dma("gpsimd", k_re[oc - 16, :, ts * TS:(ts + 1) * TS], ebuf[:], reads=[ebuf])
            if oc >= 16:
                tmout = k_tm if oc < 32 else v_tm
                for ci in range(nci_):
                    pt = ps_t[ci % 2]
                    ttile = ttile_[ci % 2]
                    p.exec("tensor", lambda e, ci=ci, pt=pt, ebuf=ebuf: e.transpose(
                        pt[:], ebuf[:, ci * 128:(ci + 1) * 128], idb[:]),
                        reads=[ebuf, idb], writes=[pt])
                    p.exec("vector", lambda e, pt=pt, ttile=ttile: e.tensor_copy(ttile[:], pt[:]),
                           reads=[pt], writes=[ttile])
                    r0 = ts * TS + ci * 128
                    p.dma("sync", tmout[r0:r0 + 128, (oc % 16) * 128:(oc % 16 + 1) * 128],
                          ttile[:], reads=[ttile])
    p.prefix = _oldpfx
    if standalone:
        p.fence()
        return p.emit_and_finalize()
    return None


def build_attn1(T, p=None, io=None):
    standalone = p is None
    if standalone:
        p = Prog("attn1")
    _oldpfx = p.prefix
    if not standalone:
        p.prefix = "attn1_" + p.prefix
    def _inp(name, shape, dtype):
        if io is not None and name in io:
            return io[name]
        return _inp(name, shape, dtype)
    def _outp(name, shape, dtype):
        if io is not None and name in io:
            return io[name]
        return _outp(name, shape, dtype)
    q_fm = _inp("q1_fm", [E_DIM, T], BF16)
    k_fm = _inp("k1_fm", [E_DIM, T], BF16)
    k_tm = _inp("k1_tm", [T, E_DIM], BF16)
    v_tm = _inp("v1_tm", [T, E_DIM], BF16)
    Wout = _inp("Wout", [E_DIM, E_DIM], F32)
    bout_row = _inp("bout_row", [1, E_DIM], F32)
    slopes_row = _inp("slopes_row", [1, H], F32)
    diffT = _inp("diffT64", [64, 64], F32)
    negiota = _inp("negiota64", [64, H], F32)
    negrev = _inp("negrev64", [64, H], F32)
    ones128f = _inp("ones128f", [1, 128], F32)
    onesbf = _inp("onesbf", [1, 128], BF16)
    identbf = _inp("identbf", [128, 128], BF16)
    attn_out = _outp("attn1_tm", [T, E_DIM], F32)
    BLK = 64
    nblk = T // BLK
    nech = E_DIM // 128

    idb = p.sb("idb", [128, 128], BF16)
    onb = p.sb("onb", [1, 128], BF16)
    onf = p.sb("onf", [1, 128], F32)
    slp = p.sb("slp", [1, H], F32)
    dfT = p.sb("dfT", [64, 64], F32)
    nio = p.sb("nio", [64, H], F32)
    nrv = p.sb("nrv", [64, H], F32)
    boutb = p.sb("boutb", [1, E_DIM], BF16)
    wsb = p.sb("wsb", [128, nech, E_DIM], BF16)
    for t, src in ((idb, identbf), (onb, onesbf), (onf, ones128f), (slp, slopes_row),
                   (dfT, diffT), (nio, negiota), (nrv, negrev)):
        p.dma("sync", t[:], src, writes=[t])
    p.dma("gpsimd", boutb[:], bout_row, writes=[boutb])
    load_weight_bf16(p, wsb, Wout, E_DIM, E_DIM)

    srep = p.sb("srep", [128, H], F32)
    nsrep = p.sb("nsrep", [128, H], F32)
    qd1 = p.sb("qd1", [64, H], F32)
    kd1 = p.sb("kd1", [64, H], F32)
    bd1 = p.sb("bd1", [128, H], F32)
    dT = p.sb("dT", [64, H, 64], F32)
    tmp16 = p.sb("tmp16", [128, H], F32)
    with p.scope():
        ps_r = p.ps("ps_r", [128, H])
        p.exec("tensor", lambda e: e.matmul(ps_r[:], onf[:], slp[:], start=True, stop=True),
               reads=[onf, slp], writes=[ps_r])
        p.exec("vector", lambda e: e.tensor_copy(srep[:], ps_r[:]), reads=[ps_r], writes=[srep])
    p.exec("vector", lambda e: e.tensor_scalar_mul(nsrep[:], srep[:], -1.0),
           reads=[srep], writes=[nsrep])
    p.exec("vector", lambda e: e.tensor_mul(tmp16[0:64, :], nio[:], srep[0:64, :]),
           reads=[nio, srep], writes=[tmp16])
    p.exec("scalar", lambda e: e.activation(qd1[:], tmp16[0:64, :], AF.Exp),
           reads=[tmp16], writes=[qd1])
    p.exec("vector", lambda e: e.tensor_mul(tmp16[0:64, :], nrv[:], srep[0:64, :]),
           reads=[nio, srep, qd1], writes=[tmp16])
    p.exec("scalar", lambda e: e.activation(kd1[:], tmp16[0:64, :], AF.Exp),
           reads=[tmp16], writes=[kd1])
    p.exec("vector", lambda e: e.tensor_scalar_mul(tmp16[:], srep[:], -64.0),
           reads=[srep, kd1], writes=[tmp16])
    p.exec("scalar", lambda e: e.activation(bd1[:], tmp16[:], AF.Exp),
           reads=[tmp16], writes=[bd1])
    for h in range(H):
        p.exec("scalar", lambda e, h=h: e.activation(
            dT[:, h, :], dfT[:], AF.Exp, scale=nsrep[0:64, h:h + 1]),
            reads=[dfT, nsrep], writes=[dT])

    qre = q_fm.rearrange("(h d) t -> d h t", d=128)
    kre = k_fm.rearrange("(h d) t -> d h t", d=128)

    ktm = [p.sb(f"ktm{i}", [64, E_DIM], BF16) for i in range(2)]
    vtm = [p.sb(f"vtm{i}", [64, E_DIM], BF16) for i in range(2)]
    qfm = [p.sb(f"qfm{i}", [128, H, 64], BF16) for i in range(2)]
    kfm = [p.sb(f"kfm{i}", [128, H, 64], BF16) for i in range(2)]
    kv32 = [[p.sb(f"kv32_{h}_{i}", [128, 128], F32) for i in range(2)] for h in range(H)]
    kvbf = [p.sb(f"kvbf_{h}", [128, 128], BF16) for h in range(H)]
    kdt = [p.sb(f"kdt{i}", [64, 128], BF16) for i in range(2)]
    smb = [p.sb(f"smb{i}", [64, 64], BF16) for i in range(2)]
    tmpo = [p.sb(f"tmpo{i}", [64, 128], F32) for i in range(2)]
    attn_b = p.sb("attn_b", [64, E_DIM], F32)
    anb = p.sb("anb", [64, E_DIM], BF16)
    afm = p.sb("afm", [128, nech, 64], BF16)
    outc = p.sb("outc", [64, E_DIM], F32)

    ps_st = p.ps("ps_st", [64, 64])
    ps_o = p.ps("ps_o", [64, 128])
    ps_i = p.ps("ps_i", [64, 128])
    ps_kv = p.ps("ps_kv", [128, 128])
    pstr = [p.ps(f"pstr{i}", [128, 64], BF16) for i in range(2)]
    ps_w = [p.ps(f"ps_w{i}", [64, 512]) for i in range(2)]

    for h in range(H):
        p.exec("vector", lambda e, h=h: e.memset(kv32[h][0][:], 0.0), writes=[kv32[h][0]])
        p.exec("gpsimd", lambda e, h=h: e.memset(kvbf[h][:], 0.0), writes=[kvbf[h]])

    for c in range(nblk):
        b = c % 2
        kt, vt, qf, kf = ktm[b], vtm[b], qfm[b], kfm[b]
        p.dma("sync", kt[:], k_tm[c * 64:(c + 1) * 64, :], writes=[kt])
        p.dma("sync", vt[:], v_tm[c * 64:(c + 1) * 64, :], writes=[vt])
        for h in range(H):
            p.dma("sync", qf[:, h, :], qre[:, h, c * 64:(c + 1) * 64], writes=[qf])
            p.dma("sync", kf[:, h, :], kre[:, h, c * 64:(c + 1) * 64], writes=[kf])
        for h in range(H):
            hb = h % 2
            kvp = kv32[h][c % 2]
            kvn = kv32[h][(c + 1) % 2]
            p.exec("vector", lambda e, h=h, hb=hb, kt=kt: e.tensor_scalar_mul(
                kdt[hb][:], kt[:, h * 128:(h + 1) * 128], kd1[:, h:h + 1]),
                reads=[kt, kd1], writes=[kdt[hb]])
            p.exec("tensor", lambda e, h=h, qf=qf, kf=kf: e.matmul(
                ps_st[:], kf[:, h, :], qf[:, h, :], start=True, stop=True),
                reads=[kf, qf], writes=[ps_st])
            p.exec("vector", lambda e, h=h, hb=hb: e.tensor_tensor(
                smb[hb][:], ps_st[:], dT[:, h, :], ALU.mult),
                reads=[ps_st, dT], writes=[smb[hb]])
            p.exec("tensor", lambda e, h=h, hb=hb, vt=vt: e.matmul(
                ps_o[:], smb[hb][:], vt[:, h * 128:(h + 1) * 128], start=True, stop=True),
                reads=[smb[hb], vt], writes=[ps_o])
            p.exec("tensor", lambda e, h=h, qf=qf: e.matmul(
                ps_i[:], qf[:, h, :], kvbf[h][:], start=True, stop=True),
                reads=[qf, kvbf[h]], writes=[ps_i])
            p.exec("scalar", lambda e, hb=hb: e.copy(tmpo[hb][:], ps_o[:]),
                   reads=[ps_o], writes=[tmpo[hb]])
            p.exec("vector", lambda e, h=h, hb=hb: e.scalar_tensor_tensor(
                attn_b[:, h * 128:(h + 1) * 128], ps_i[:], qd1[:, h:h + 1], tmpo[hb][:],
                ALU.mult, ALU.add),
                reads=[ps_i, qd1, tmpo[hb]], writes=[attn_b])
            p.exec("tensor", lambda e, h=h, hb=hb, vt=vt: e.matmul(
                ps_kv[:], kdt[hb][:], vt[:, h * 128:(h + 1) * 128], start=True, stop=True),
                reads=[kdt[hb], vt], writes=[ps_kv])
            p.exec("vector", lambda e, h=h, kvp=kvp, kvn=kvn: e.scalar_tensor_tensor(
                kvn[:], kvp[:], bd1[:, h:h + 1], ps_kv[:], ALU.mult, ALU.add),
                reads=[kvp, bd1, ps_kv], writes=[kvn])
            p.exec("scalar", lambda e, h=h, kvn=kvn: e.copy(kvbf[h][:], kvn[:]),
                   reads=[kvn], writes=[kvbf[h]])
        p.exec("vector", lambda e: e.tensor_copy(anb[:], attn_b[:]),
               reads=[attn_b], writes=[anb])
        for ec in range(nech):
            pb = pstr[ec % 2]
            p.exec("tensor", lambda e, ec=ec, pb=pb: e.transpose(
                pb[:], anb[:, ec * 128:(ec + 1) * 128], idb[0:64, 0:64]),
                reads=[anb, idb], writes=[pb])
            p.exec("vector", lambda e, ec=ec, pb=pb: e.tensor_copy(afm[:, ec, :], pb[:]),
                   reads=[pb], writes=[afm])
        for sl in range(4):
            pw = ps_w[sl % 2]
            for e_ in range(nech):
                p.exec("tensor", lambda en, sl=sl, e_=e_, pw=pw: en.matmul(
                    pw[:], afm[:, e_, :], wsb[:, e_, sl * 512:(sl + 1) * 512],
                    start=(e_ == 0), stop=False),
                    reads=[afm, wsb], writes=[pw])
            p.exec("tensor", lambda en, sl=sl, pw=pw: en.matmul(
                pw[:], onb[:, 0:64], boutb[:, sl * 512:(sl + 1) * 512],
                start=False, stop=True),
                reads=[onb, boutb], writes=[pw])
            p.exec("scalar", lambda en, sl=sl, pw=pw: en.copy(
                outc[:, sl * 512:(sl + 1) * 512], pw[:]), reads=[pw], writes=[outc])
        p.dma("gpsimd", attn_out[c * 64:(c + 1) * 64, :], outc[:], reads=[outc])
    p.prefix = _oldpfx
    if standalone:
        p.fence()
        return p.emit_and_finalize()
    return None


def build_ffn0(T, p=None, io=None):
    standalone = p is None
    if standalone:
        p = Prog("ffn0")
    _oldpfx = p.prefix
    if not standalone:
        p.prefix = "ffn0_" + p.prefix
    def _inp(name, shape, dtype):
        if io is not None and name in io:
            return io[name]
        return _inp(name, shape, dtype)
    def _outp(name, shape, dtype):
        if io is not None and name in io:
            return io[name]
        return _outp(name, shape, dtype)
    xn_fm = _inp("xn_fm", [E_DIM, T], BF16)
    W_upT = _inp("W_upT", [E_DIM, INTER], F32)
    W_gateT = _inp("W_gateT", [E_DIM, INTER], F32)
    W_downT = _inp("W_downT", [INTER, E_DIM], F32)
    su_ug = _inp("su_ug_pm", [128, 16], F32)
    su_dp = _inp("su_dp_pm", [128, 64], F32)
    wsc_up = _inp("wsc_up_pm", [128, 64], F32)
    wsc_gate = _inp("wsc_gate_pm", [128, 64], F32)
    wsc_dp = _inp("wsc_dp_pm", [128, 16], F32)
    h128in = _inp("h128bf", [128, 128], BF16)
    identf = _inp("identf32", [128, 128], F32)
    ffn_out = _outp("ffn0_tm", [T, E_DIM], F32)

    u_dram = p.dram("u_scr", [INTER, T], BF16)
    w_dram = p.dram("w_scr", [INTER, T], BF16)

    nech = E_DIM // 128
    nfch = INTER // 128

    h128sb = p.sb("h128sb", [128, 128], BF16)
    idf = p.sb("idf", [128, 128], F32)
    sug = p.sb("sug", [128, 16], F32)
    sud = p.sb("sud", [128, 64], F32)
    wup_s = p.sb("wup_s", [128, 64], F32)
    wgt_s = p.sb("wgt_s", [128, 64], F32)
    wdp_s = p.sb("wdp_s", [128, 16], F32)
    upscsu = p.sb("upscsu", [128, 64], F32)
    wdps = p.sb("wdps", [128, 16], F32)
    ybf = p.sb("ybf", [128, nech, T], BF16)
    for t, src in ((h128sb, h128in), (idf, identf), (sug, su_ug), (sud, su_dp),
                   (wup_s, wsc_up), (wgt_s, wsc_gate), (wdp_s, wsc_dp)):
        p.dma("sync", t[:], src, writes=[t])
    p.exec("vector", lambda e: e.tensor_mul(upscsu[:], wup_s[:], sud[:]),
           reads=[wup_s, sud], writes=[upscsu])
    p.exec("vector", lambda e: e.tensor_scalar_mul(wdps[:], wdp_s[:], float(1.0 / np.sqrt(INTER))),
           reads=[wdp_s], writes=[wdps])

    xre = xn_fm.rearrange("(ec pp) t -> pp ec t", pp=128)
    wupre = W_upT.rearrange("(ec pp) f -> pp ec f", pp=128)
    wgtre = W_gateT.rearrange("(ec pp) f -> pp ec f", pp=128)
    wdnre = W_downT.rearrange("(fc pp) g -> pp fc g", pp=128)
    u_re = u_dram.rearrange("(fc pp) t -> pp fc t", pp=128)
    w_re = w_dram.rearrange("(fc pp) t -> pp fc t", pp=128)

    with p.scope():
        suh = p.sb("suh", [128, 16, 128], BF16)
        for ec in range(nech):
            p.exec("vector", lambda e, ec=ec: e.tensor_scalar_mul(
                suh[:, ec, :], h128sb[:], sug[:, ec:ec + 1]),
                reads=[h128sb, sug], writes=[suh])
        TS = min(512, T)
        nts = T // TS
        xnb = [p.sb(f"xnb{i}", [128, nech, TS], BF16) for i in range(2)]
        yf = p.sb("yf", [128, nech, TS], F32)
        yg = p.sb("yg", [128, nech, TS], F32)
        ps_a = [p.ps(f"ps_a{i}", [128, TS]) for i in range(2)]
        for tsl in range(nts):
            xb = xnb[tsl % 2]
            for e0 in range(0, nech, 8):
                p.dma("sync", xb[:, e0:e0 + 8, :], xre[:, e0:e0 + 8, tsl * TS:(tsl + 1) * TS],
                      writes=[xb])
            for ec in range(nech):
                pa = ps_a[ec % 2]
                p.exec("tensor", lambda e, ec=ec, xb=xb, pa=pa: e.matmul(
                    pa[:], suh[:, ec, :], xb[:, ec, :], start=True, stop=True),
                    reads=[suh, xb], writes=[pa])
                p.exec("vector", lambda e, ec=ec, pa=pa: e.tensor_scalar_mul(
                    yf[:, ec, :], pa[:], float(1.0 / np.sqrt(E_DIM))),
                    reads=[pa], writes=[yf])
            _butterfly(p, nech, yf, yg, final_out=ybf,
                       final_slice=(tsl * TS, (tsl + 1) * TS))

    with p.scope():
        TS = min(512, T)
        nts = T // TS
        wut = [p.sb(f"wut{i}", [128, nech, 128], BF16) for i in range(2)]
        wgt = [p.sb(f"wgt{i}", [128, nech, 128], BF16) for i in range(2)]
        sg = p.sb("sg", [128, TS], F32)
        ut = p.sb("ut", [128, TS], BF16)
        ps_up = p.ps("ps_up", [128, TS])
        ps_gt = p.ps("ps_gt", [128, TS])
        for fi in range(nfch):
            wu, wg = wut[fi % 2], wgt[fi % 2]
            for e0 in range(0, nech, 8):
                p.dma("gpsimd", wu[:, e0:e0 + 8, :],
                      wupre[:, e0:e0 + 8, fi * 128:(fi + 1) * 128], writes=[wu])
                p.dma("gpsimd", wg[:, e0:e0 + 8, :],
                      wgtre[:, e0:e0 + 8, fi * 128:(fi + 1) * 128], writes=[wg])
            for tsl in range(nts):
                for ec in range(nech):
                    p.exec("tensor", lambda e, ec=ec, tsl=tsl, wu=wu: e.matmul(
                        ps_up[:], wu[:, ec, :], ybf[:, ec, tsl * TS:(tsl + 1) * TS],
                        start=(ec == 0), stop=(ec == nech - 1)),
                        reads=[wu, ybf], writes=[ps_up])
                    p.exec("tensor", lambda e, ec=ec, tsl=tsl, wg=wg: e.matmul(
                        ps_gt[:], wg[:, ec, :], ybf[:, ec, tsl * TS:(tsl + 1) * TS],
                        start=(ec == 0), stop=(ec == nech - 1)),
                        reads=[wg, ybf], writes=[ps_gt])
                p.exec("scalar", lambda e, fi=fi: e.activation(
                    sg[:], ps_gt[:], AF.Silu, scale=wgt_s[:, fi:fi + 1]),
                    reads=[ps_gt, wgt_s], writes=[sg])
                p.exec("vector", lambda e, fi=fi: e.scalar_tensor_tensor(
                    ut[:], ps_up[:], upscsu[:, fi:fi + 1], sg[:], ALU.mult, ALU.mult),
                    reads=[ps_up, upscsu, sg], writes=[ut])
                p.dma("sync", u_re[:, fi, tsl * TS:(tsl + 1) * TS], ut[:], reads=[ut])

    with p.scope():
        TC = min(256, T)
        ntc = T // TC
        ub = p.sb("ub", [128, nfch, TC], BF16)
        vb = p.sb("vb", [128, nfch, TC], F32)
        wt = p.sb("wt", [128, nfch, TC], BF16)
        ps_c = [p.ps(f"ps_c{i}", [128, TC]) for i in range(2)]
        for tc in range(ntc):
            for f0 in range(0, nfch, 8):
                p.dma("sync", ub[:, f0:f0 + 8, :], u_re[:, f0:f0 + 8, tc * TC:(tc + 1) * TC],
                      writes=[ub])
            for fc in range(nfch):
                pc = ps_c[fc % 2]
                p.exec("tensor", lambda e, fc=fc, pc=pc: e.matmul(
                    pc[:], h128sb[:], ub[:, fc, :], start=True, stop=True),
                    reads=[h128sb, ub], writes=[pc])
                p.exec("vector", lambda e, fc=fc, pc=pc: e.tensor_copy(vb[:, fc, :], pc[:]),
                       reads=[pc], writes=[vb])
            _butterfly(p, nfch, vb, ub, final_out=wt, final_slice=(0, TC))
            for f0 in range(0, nfch, 8):
                p.dma("gpsimd", w_re[:, f0:f0 + 8, tc * TC:(tc + 1) * TC], wt[:, f0:f0 + 8, :],
                      reads=[wt])

    with p.scope():
        TQ = min(512, T)
        ntq = T // TQ
        nci = TQ // 128
        wq = p.sb("wq", [128, nfch, TQ], BF16)
        wdt = [p.sb(f"wdt{i}", [128, 8, 128], BF16) for i in range(4)]
        dtile = [p.sb(f"dtile{i}", [128, TQ], F32) for i in range(2)]
        obuf = [p.sb(f"obuf{i}", [128, E_DIM], F32) for i in range(nci)]
        ps_d = [p.ps(f"ps_d{i}", [128, TQ]) for i in range(2)]
        ps_t = [p.ps(f"ps_t{i}", [128, 128]) for i in range(2)]
        nbat = 0
        for tq in range(ntq):
            for f0 in range(0, nfch, 8):
                p.dma("sync", wq[:, f0:f0 + 8, :], w_re[:, f0:f0 + 8, tq * TQ:(tq + 1) * TQ],
                      writes=[wq])
            for g in range(nech):
                pd = ps_d[g % 2]
                dt = dtile[g % 2]
                for f0 in range(0, nfch, 8):
                    wd = wdt[nbat % 4]
                    nbat += 1
                    p.dma("gpsimd", wd[:], wdnre[:, f0:f0 + 8, g * 128:(g + 1) * 128],
                          writes=[wd])
                    for k in range(8):
                        fc = f0 + k
                        p.exec("tensor", lambda e, fc=fc, k=k, wd=wd, pd=pd: e.matmul(
                            pd[:], wd[:, k, :], wq[:, fc, :],
                            start=(fc == 0), stop=(fc == nfch - 1)),
                            reads=[wd, wq], writes=[pd])
                p.exec("vector", lambda e, g=g, pd=pd, dt=dt: e.tensor_scalar_mul(
                    dt[:], pd[:], wdps[:, g:g + 1]),
                    reads=[pd, wdps], writes=[dt])
                for ci in range(nci):
                    pt = ps_t[ci % 2]
                    p.exec("tensor", lambda e, ci=ci, pt=pt, dt=dt: e.transpose(
                        pt[:], dt[:, ci * 128:(ci + 1) * 128], idf[:]),
                        reads=[dt, idf], writes=[pt])
                    p.exec("vector", lambda e, ci=ci, g=g, pt=pt: e.tensor_copy(
                        obuf[ci][:, g * 128:(g + 1) * 128], pt[:]),
                        reads=[pt], writes=[obuf[ci]])
            for ci in range(nci):
                p.dma("gpsimd", ffn_out[tq * TQ + ci * 128: tq * TQ + (ci + 1) * 128, :],
                      obuf[ci][:], reads=[obuf[ci]])
    p.prefix = _oldpfx
    if standalone:
        p.fence()
        return p.emit_and_finalize()
    return None


def build_ffn1(T, p=None, io=None):
    standalone = p is None
    if standalone:
        p = Prog("ffn1")
    _oldpfx = p.prefix
    if not standalone:
        p.prefix = "ffn1_" + p.prefix
    def _inp(name, shape, dtype):
        if io is not None and name in io:
            return io[name]
        return _inp(name, shape, dtype)
    def _outp(name, shape, dtype):
        if io is not None and name in io:
            return io[name]
        return _outp(name, shape, dtype)
    xn_fm = _inp("xn_fm", [E_DIM, T], BF16)
    w_gate = _inp("w_gate_p", [E_DIM, INTER], F32)
    w_up = _inp("w_up_p", [E_DIM, INTER], F32)
    w_down = _inp("w_down_p", [INTER, E_DIM], F32)
    identf = _inp("identf32", [128, 128], F32)
    ffn_out = _outp("ffn1_tm", [T, E_DIM], F32)

    h_dram = p.dram("h_scr", [INTER, T], F32)
    hs_dram = p.dram("hs_scr", [INTER, T], BF16)
    nech = E_DIM // 128
    nfch = INTER // 128

    idf = p.sb("idf", [128, 128], F32)
    p.dma("sync", idf[:], identf, writes=[idf])
    xnsb = p.sb("xnsb", [128, nech, T], BF16)
    xre = xn_fm.rearrange("(ec pp) t -> pp ec t", pp=128)
    for e0 in range(0, nech, 4):
        p.dma("sync", xnsb[:, e0:e0 + 4, :], xre[:, e0:e0 + 4, :], writes=[xnsb])
    wgre = w_gate.rearrange("(ec pp) f -> pp ec f", pp=128)
    wure = w_up.rearrange("(ec pp) f -> pp ec f", pp=128)
    wdre = w_down.rearrange("(fc pp) g -> pp fc g", pp=128)
    h_re = h_dram.rearrange("(fc pp) t -> pp fc t", pp=128)
    hs_re = hs_dram.rearrange("(fc pp) t -> pp fc t", pp=128)

    with p.scope():
        TS = min(512, T)
        nts = T // TS
        wut = [p.sb(f"wut{i}", [128, nech, 128], BF16) for i in range(2)]
        wgt = [p.sb(f"wgt{i}", [128, nech, 128], BF16) for i in range(2)]
        sg = p.sb("sg", [128, TS], F32)
        ht = p.sb("ht", [128, TS], F32)
        ps_up = p.ps("ps_up", [128, TS])
        ps_gt = p.ps("ps_gt", [128, TS])
        for fi in range(nfch):
            wu, wg = wut[fi % 2], wgt[fi % 2]
            for e0 in range(0, nech, 8):
                p.dma("gpsimd", wu[:, e0:e0 + 8, :],
                      wure[:, e0:e0 + 8, fi * 128:(fi + 1) * 128], writes=[wu])
                p.dma("gpsimd", wg[:, e0:e0 + 8, :],
                      wgre[:, e0:e0 + 8, fi * 128:(fi + 1) * 128], writes=[wg])
            for tsl in range(nts):
                for ec in range(nech):
                    p.exec("tensor", lambda e, ec=ec, tsl=tsl, wu=wu, TS=TS: e.matmul(
                        ps_up[:], wu[:, ec, :], xnsb[:, ec, tsl * TS:(tsl + 1) * TS],
                        start=(ec == 0), stop=(ec == nech - 1)),
                        reads=[wu, xnsb], writes=[ps_up])
                    p.exec("tensor", lambda e, ec=ec, tsl=tsl, wg=wg, TS=TS: e.matmul(
                        ps_gt[:], wg[:, ec, :], xnsb[:, ec, tsl * TS:(tsl + 1) * TS],
                        start=(ec == 0), stop=(ec == nech - 1)),
                        reads=[wg, xnsb], writes=[ps_gt])
                p.exec("scalar", lambda e: e.activation(sg[:], ps_gt[:], AF.Silu),
                       reads=[ps_gt], writes=[sg])
                p.exec("vector", lambda e: e.tensor_mul(ht[:], ps_up[:], sg[:]),
                       reads=[ps_up, sg], writes=[ht])
                p.dma("sync", h_re[:, fi, tsl * TS:(tsl + 1) * TS], ht[:], reads=[ht])

    with p.scope():
        TS = min(512, T)
        nts = T // TS
        h4 = p.sb("h4", [128, 4, TS], F32)
        a4 = p.sb("a4", [128, 4, TS], F32)
        b6 = p.sb("b6", [128, 6, TS], F32)
        t2 = p.sb("t2", [128, 2, TS], F32)
        hs4 = p.sb("hs4", [128, 4, TS], BF16)
        pairs = [(0, 1), (0, 2), (0, 3), (1, 2), (1, 3), (2, 3)]
        for tsl in range(nts):
            for grp in range(16):
                fcs = [grp, grp + 16, grp + 32, grp + 48]
                for m, fc in enumerate(fcs):
                    p.dma("sync", h4[:, m, :], h_re[:, fc, tsl * TS:(tsl + 1) * TS],
                          writes=[h4])
                for m in range(4):
                    p.exec("scalar", lambda e, m=m: e.activation(
                        a4[:, m, :], h4[:, m, :], AF.Abs), reads=[h4], writes=[a4])
                for bi, (i, j) in enumerate(pairs):
                    p.exec("vector", lambda e, bi=bi, i=i, j=j: e.tensor_tensor(
                        b6[:, bi, :], a4[:, i, :], a4[:, j, :], ALU.is_ge),
                        reads=[a4], writes=[b6])
                p.exec("vector", lambda e: e.tensor_add(t2[:, 0, :], b6[:, 0, :], b6[:, 1, :]),
                       reads=[b6], writes=[t2])
                p.exec("vector", lambda e: e.tensor_add(t2[:, 0, :], t2[:, 0, :], b6[:, 2, :]),
                       reads=[b6, t2], writes=[t2])
                p.exec("vector", lambda e: e.scalar_tensor_tensor(
                    hs4[:, 0, :], t2[:, 0, :], 1.5, h4[:, 0, :], ALU.is_ge, ALU.mult),
                    reads=[t2, h4], writes=[hs4])
                p.exec("vector", lambda e: e.tensor_add(t2[:, 1, :], b6[:, 3, :], b6[:, 4, :]),
                       reads=[b6], writes=[t2])
                p.exec("vector", lambda e: e.tensor_sub(t2[:, 1, :], t2[:, 1, :], b6[:, 0, :]),
                       reads=[b6, t2], writes=[t2])
                p.exec("vector", lambda e: e.scalar_tensor_tensor(
                    hs4[:, 1, :], t2[:, 1, :], 0.5, h4[:, 1, :], ALU.is_ge, ALU.mult),
                    reads=[t2, h4], writes=[hs4])
                p.exec("vector", lambda e: e.tensor_sub(t2[:, 0, :], b6[:, 5, :], b6[:, 1, :]),
                       reads=[b6], writes=[t2])
                p.exec("vector", lambda e: e.tensor_sub(t2[:, 0, :], t2[:, 0, :], b6[:, 3, :]),
                       reads=[b6, t2], writes=[t2])
                p.exec("vector", lambda e: e.scalar_tensor_tensor(
                    hs4[:, 2, :], t2[:, 0, :], -0.5, h4[:, 2, :], ALU.is_ge, ALU.mult),
                    reads=[t2, h4], writes=[hs4])
                p.exec("vector", lambda e: e.tensor_add(t2[:, 1, :], b6[:, 2, :], b6[:, 4, :]),
                       reads=[b6], writes=[t2])
                p.exec("vector", lambda e: e.tensor_add(t2[:, 1, :], t2[:, 1, :], b6[:, 5, :]),
                       reads=[b6, t2], writes=[t2])
                p.exec("vector", lambda e: e.scalar_tensor_tensor(
                    hs4[:, 3, :], t2[:, 1, :], 1.5, h4[:, 3, :], ALU.is_le, ALU.mult),
                    reads=[t2, h4], writes=[hs4])
                for m, fc in enumerate(fcs):
                    p.dma("gpsimd", hs_re[:, fc, tsl * TS:(tsl + 1) * TS], hs4[:, m, :],
                          reads=[hs4])

    with p.scope():
        TQ = min(512, T)
        ntq = T // TQ
        nci = TQ // 128
        wq = p.sb("wq", [128, nfch, TQ], BF16)
        wdt = [p.sb(f"wdt{i}", [128, 8, 128], BF16) for i in range(4)]
        dtile = [p.sb(f"dtile{i}", [128, TQ], F32) for i in range(2)]
        obuf = [p.sb(f"obuf{i}", [128, E_DIM], F32) for i in range(nci)]
        ps_d = [p.ps(f"ps_d{i}", [128, TQ]) for i in range(2)]
        ps_t = [p.ps(f"ps_t{i}", [128, 128]) for i in range(2)]
        nbat = 0
        for tq in range(ntq):
            for f0 in range(0, nfch, 8):
                p.dma("sync", wq[:, f0:f0 + 8, :], hs_re[:, f0:f0 + 8, tq * TQ:(tq + 1) * TQ],
                      writes=[wq])
            for g in range(nech):
                pd = ps_d[g % 2]
                dt = dtile[g % 2]
                for f0 in range(0, nfch, 8):
                    wd = wdt[nbat % 4]
                    nbat += 1
                    p.dma("gpsimd", wd[:], wdre[:, f0:f0 + 8, g * 128:(g + 1) * 128],
                          writes=[wd])
                    for k in range(8):
                        fc = f0 + k
                        p.exec("tensor", lambda e, fc=fc, k=k, wd=wd, pd=pd: e.matmul(
                            pd[:], wd[:, k, :], wq[:, fc, :],
                            start=(fc == 0), stop=(fc == nfch - 1)),
                            reads=[wd, wq], writes=[pd])
                p.exec("vector", lambda e, pd=pd, dt=dt: e.tensor_copy(dt[:], pd[:]),
                       reads=[pd], writes=[dt])
                for ci in range(nci):
                    pt = ps_t[ci % 2]
                    p.exec("tensor", lambda e, ci=ci, pt=pt, dt=dt: e.transpose(
                        pt[:], dt[:, ci * 128:(ci + 1) * 128], idf[:]),
                        reads=[dt, idf], writes=[pt])
                    p.exec("vector", lambda e, ci=ci, g=g, pt=pt: e.tensor_copy(
                        obuf[ci][:, g * 128:(g + 1) * 128], pt[:]),
                        reads=[pt], writes=[obuf[ci]])
            for ci in range(nci):
                p.dma("gpsimd", ffn_out[tq * TQ + ci * 128: tq * TQ + (ci + 1) * 128, :],
                      obuf[ci][:], reads=[obuf[ci]])
    p.prefix = _oldpfx
    if standalone:
        p.fence()
        return p.emit_and_finalize()
    return None


def build_resln(T, with_ln=True, nadd=2, p=None, io=None):
    standalone = p is None
    if standalone:
        p = Prog("resln")
    _oldpfx = p.prefix
    if not standalone:
        p.prefix = "resln_" + p.prefix
    def _inp(name, shape, dtype):
        if io is not None and name in io:
            return io[name]
        return _inp(name, shape, dtype)
    def _outp(name, shape, dtype):
        if io is not None and name in io:
            return io[name]
        return _outp(name, shape, dtype)
    x = _inp("x", [T, E_DIM], F32)
    adds = [_inp(f"add{i}", [T, E_DIM], F32) for i in range(nadd)]
    x1_tm = _outp("x1_tm", [T, E_DIM], F32)
    if with_ln:
        identbf = _inp("identbf", [128, 128], BF16)
        constpm = _inp("constpm", [128, 2], F32)
        xn_fm = _outp("xn_fm", [E_DIM, T], BF16)
        idb = p.sb("idb", [128, 128], BF16)
        cp = p.sb("cp", [128, 2], F32)
        p.dma("sync", idb[:], identbf, writes=[idb])
        p.dma("sync", cp[:], constpm, writes=[cp])
        xn_re = xn_fm.rearrange("(ec pp) t -> ec pp t", pp=128)
    nch = T // 128
    nech = E_DIM // 128

    xin = [p.sb(f"xin{i}", [128, E_DIM], F32) for i in range(2)]
    ain = [p.sb(f"ain{i}", [128, E_DIM], F32) for i in range(2)]
    bin_ = [p.sb(f"bin{i}", [128, E_DIM], F32) for i in range(2)]
    x1b = [p.sb(f"x1b{i}", [128, E_DIM], F32) for i in range(2)]
    if with_ln:
        xsq = p.sb("xsq", [128, E_DIM], F32)
        stats = p.sb("stats", [128, 8], F32)
        xn_tm = p.sb("xn_tm", [128, E_DIM], BF16)
        tbuf = [p.sb(f"tbuf{i}", [128, nech, 128], BF16) for i in range(2)]
        pstr = [p.ps(f"pstr{i}", [128, 128], BF16) for i in range(2)]

    for c in range(nch):
        b = c % 2
        xb, ab, bb, x1 = xin[b], ain[b], bin_[b], x1b[b]
        p.dma("sync", xb[:], x[c * 128:(c + 1) * 128, :], writes=[xb])
        p.dma("sync", ab[:], adds[0][c * 128:(c + 1) * 128, :], writes=[ab])
        if nadd > 1:
            p.dma("sync", bb[:], adds[1][c * 128:(c + 1) * 128, :], writes=[bb])
        p.exec("vector", lambda e, xb=xb, ab=ab, x1=x1: e.tensor_add(x1[:], xb[:], ab[:]),
               reads=[xb, ab], writes=[x1])
        if nadd > 1:
            p.exec("vector", lambda e, bb=bb, x1=x1: e.tensor_add(x1[:], x1[:], bb[:]),
                   reads=[bb, x1], writes=[x1])
        p.dma("gpsimd", x1_tm[c * 128:(c + 1) * 128, :], x1[:], reads=[x1])
        if not with_ln:
            continue
        p.exec("scalar", lambda e, x1=x1: e.square(xsq[:], x1[:]), reads=[x1], writes=[xsq])
        p.exec("vector", lambda e, x1=x1: e.reduce_sum(stats[:, 0:1], x1[:], axis=mybir.AxisListType.X),
               reads=[x1], writes=[stats])
        p.exec("vector", lambda e: e.reduce_sum(stats[:, 1:2], xsq[:], axis=mybir.AxisListType.X),
               reads=[xsq], writes=[stats])
        p.exec("vector", lambda e: e.tensor_scalar_mul(stats[:, 2:3], stats[:, 0:1], -1.0 / E_DIM),
               reads=[stats], writes=[stats])
        p.exec("vector", lambda e: e.tensor_scalar_mul(stats[:, 3:4], stats[:, 1:2], 1.0 / E_DIM),
               reads=[stats], writes=[stats])
        p.exec("vector", lambda e: e.tensor_mul(stats[:, 4:5], stats[:, 2:3], stats[:, 2:3]),
               reads=[stats], writes=[stats])
        p.exec("vector", lambda e: e.tensor_sub(stats[:, 5:6], stats[:, 3:4], stats[:, 4:5]),
               reads=[stats], writes=[stats])
        p.exec("scalar", lambda e: e.activation(stats[:, 6:7], stats[:, 5:6], AF.Ln, bias=cp[:, 0:1]),
               reads=[stats, cp], writes=[stats])
        p.exec("scalar", lambda e: e.activation(stats[:, 7:8], stats[:, 6:7], AF.Exp, scale=cp[:, 1:2]),
               reads=[stats, cp], writes=[stats])
        p.exec("vector", lambda e: e.tensor_mul(stats[:, 4:5], stats[:, 2:3], stats[:, 7:8]),
               reads=[stats], writes=[stats])
        p.exec("vector", lambda e, x1=x1: e.tensor_scalar(
            xn_tm[:], x1[:], stats[:, 7:8], stats[:, 4:5], ALU.mult, ALU.add),
            reads=[x1, stats], writes=[xn_tm])
        tb = tbuf[b]
        for ec in range(nech):
            pb = pstr[ec % 2]
            p.exec("tensor", lambda e, ec=ec, pb=pb: e.transpose(
                pb[:], xn_tm[:, ec * 128:(ec + 1) * 128], idb[:]),
                reads=[xn_tm, idb], writes=[pb])
            p.exec("vector", lambda e, ec=ec, pb=pb, tb=tb: e.tensor_copy(tb[:, ec, :], pb[:]),
                   reads=[pb], writes=[tb])
            p.dma("gpsimd", xn_re[ec, :, c * 128:(c + 1) * 128], tb[:, ec, :],
                  reads=[tb])
    p.prefix = _oldpfx
    if standalone:
        p.fence()
        return p.emit_and_finalize()
    return None


def build_attn0(T, p=None, io=None):
    standalone = p is None
    if standalone:
        p = Prog("attn0")
    _oldpfx = p.prefix
    if not standalone:
        p.prefix = "attn0_" + p.prefix
    def _inp(name, shape, dtype):
        if io is not None and name in io:
            return io[name]
        return _inp(name, shape, dtype)
    def _outp(name, shape, dtype):
        if io is not None and name in io:
            return io[name]
        return _outp(name, shape, dtype)
    qr_fm = _inp("qr_fm", [E_DIM, T], BF16)
    kr_fm = _inp("kr_fm", [E_DIM, T], BF16)
    kr_tm = _inp("kr_tm", [T, E_DIM], BF16)
    v_tm = _inp("v_tm", [T, E_DIM], BF16)
    Wo = _inp("Wo", [E_DIM, E_DIM], F32)
    rms_w_row = _inp("rms_w_row", [1, E_DIM], F32)
    dmaskT = _inp("dmaskT", [H, 128, 128], F32)
    qdec = _inp("qdec", [128, H], F32)
    kdec = _inp("kdec", [128, H], F32)
    rC = _inp("rC", [128, H], F32)
    identbf = _inp("identbf", [128, 128], BF16)
    ones128f = _inp("ones128f", [1, 128], F32)
    constpm = _inp("constpm", [128, 2], F32)
    attn_out = _outp("attn0_tm", [T, E_DIM], F32)
    nch = T // 128
    nech = E_DIM // 128

    idb = p.sb("idb", [128, 128], BF16)
    cp = p.sb("cp", [128, 2], F32)
    wsb = p.sb("wsb", [128, nech, E_DIM], BF16)
    dm = p.sb("dm", [128, H, 128], F32)
    qd = p.sb("qd", [128, H], F32)
    kd = p.sb("kd", [128, H], F32)
    rc = p.sb("rc", [128, H], F32)
    ones_sb = p.sb("ones_sb", [1, 128], F32)
    rmsrow = p.sb("rmsrow", [1, E_DIM], F32)
    rmsrep = p.sb("rmsrep", [128, E_DIM], F32)

    p.dma("sync", idb[:], identbf, writes=[idb])
    p.dma("sync", cp[:], constpm, writes=[cp])
    for h_ in range(H):
        p.dma("sync", dm[:, h_, :], dmaskT[h_], writes=[dm])
    p.dma("sync", qd[:], qdec, writes=[qd])
    p.dma("sync", kd[:], kdec, writes=[kd])
    p.dma("sync", rc[:], rC, writes=[rc])
    p.dma("sync", ones_sb[:], ones128f, writes=[ones_sb])
    p.dma("sync", rmsrow[:], rms_w_row, writes=[rmsrow])
    load_weight_bf16(p, wsb, Wo, E_DIM, E_DIM)

    qre = qr_fm.rearrange("(h d) t -> d h t", d=128)
    kre = kr_fm.rearrange("(h d) t -> d h t", d=128)

    ktm = [p.sb(f"ktm{i}", [128, E_DIM], BF16) for i in range(2)]
    vtm = [p.sb(f"vtm{i}", [128, E_DIM], BF16) for i in range(2)]
    qfm = [p.sb(f"qfm{i}", [128, H, 128], BF16) for i in range(2)]
    kfm = [p.sb(f"kfm{i}", [128, H, 128], BF16) for i in range(2)]
    kv32 = [[p.sb(f"kv32_{h}_{i}", [128, 128], F32) for i in range(2)] for h in range(H)]
    kvbf = [p.sb(f"kvbf_{h}", [128, 128], BF16) for h in range(H)]
    kdt = [p.sb(f"kdt{i}", [128, 128], BF16) for i in range(2)]
    smb = [p.sb(f"smb{i}", [128, 128], BF16) for i in range(2)]
    tmpo = [p.sb(f"tmpo{i}", [128, 128], F32) for i in range(2)]
    attn_tm = p.sb("attn_tm", [128, E_DIM], F32)
    asq = p.sb("asq", [128, E_DIM], F32)
    st8 = p.sb("st8", [128, 8], F32)
    anb = p.sb("anb", [128, E_DIM], BF16)
    afm = p.sb("afm", [128, nech, 128], BF16)
    outc = p.sb("outc", [128, E_DIM], F32)

    ps_st = p.ps("ps_st", [128, 128])
    ps_o = p.ps("ps_o", [128, 128])
    ps_i = p.ps("ps_i", [128, 128])
    ps_kv = p.ps("ps_kv", [128, 128])
    pstr = [p.ps(f"pstr{i}", [128, 128], BF16) for i in range(2)]
    ps_w = [p.ps(f"ps_w{i}", [128, 512]) for i in range(2)]

    for sl in range(4):
        pw = ps_w[sl % 2]
        p.exec("tensor", lambda e, sl=sl, pw=pw: e.matmul(
            pw[:], ones_sb[:], rmsrow[:, sl * 512:(sl + 1) * 512], start=True, stop=True),
            reads=[ones_sb, rmsrow], writes=[pw])
        p.exec("vector", lambda e, sl=sl, pw=pw: e.tensor_copy(
            rmsrep[:, sl * 512:(sl + 1) * 512], pw[:]), reads=[pw], writes=[rmsrep])

    for h in range(H):
        p.exec("vector", lambda e, h=h: e.memset(kv32[h][0][:], 0.0), writes=[kv32[h][0]])
        p.exec("gpsimd", lambda e, h=h: e.memset(kvbf[h][:], 0.0), writes=[kvbf[h]])

    for c in range(nch):
        b = c % 2
        kt, vt, qf, kf = ktm[b], vtm[b], qfm[b], kfm[b]
        p.dma("sync", kt[:], kr_tm[c * 128:(c + 1) * 128, :], writes=[kt])
        p.dma("sync", vt[:], v_tm[c * 128:(c + 1) * 128, :], writes=[vt])
        for h in range(H):
            p.dma("sync", qf[:, h, :], qre[:, h, c * 128:(c + 1) * 128], writes=[qf])
            p.dma("sync", kf[:, h, :], kre[:, h, c * 128:(c + 1) * 128], writes=[kf])
        for h in range(H):
            hb = h % 2
            kvp = kv32[h][c % 2]
            kvn = kv32[h][(c + 1) % 2]
            p.exec("vector", lambda e, h=h, hb=hb, kt=kt: e.tensor_scalar_mul(
                kdt[hb][:], kt[:, h * 128:(h + 1) * 128], kd[:, h:h + 1]),
                reads=[kt, kd], writes=[kdt[hb]])
            p.exec("tensor", lambda e, h=h, qf=qf, kf=kf: e.matmul(
                ps_st[:], kf[:, h, :], qf[:, h, :], start=True, stop=True),
                reads=[kf, qf], writes=[ps_st])
            p.exec("vector", lambda e, h=h, hb=hb: e.tensor_tensor(
                smb[hb][:], ps_st[:], dm[:, h, :], ALU.mult),
                reads=[ps_st, dm], writes=[smb[hb]])
            p.exec("tensor", lambda e, h=h, hb=hb, vt=vt: e.matmul(
                ps_o[:], smb[hb][:], vt[:, h * 128:(h + 1) * 128], start=True, stop=True),
                reads=[smb[hb], vt], writes=[ps_o])
            p.exec("tensor", lambda e, h=h, qf=qf: e.matmul(
                ps_i[:], qf[:, h, :], kvbf[h][:], start=True, stop=True),
                reads=[qf, kvbf[h]], writes=[ps_i])
            p.exec("scalar", lambda e, hb=hb: e.copy(tmpo[hb][:], ps_o[:]),
                   reads=[ps_o], writes=[tmpo[hb]])
            p.exec("vector", lambda e, h=h, hb=hb: e.scalar_tensor_tensor(
                attn_tm[:, h * 128:(h + 1) * 128], ps_i[:], qd[:, h:h + 1], tmpo[hb][:],
                ALU.mult, ALU.add),
                reads=[ps_i, qd, tmpo[hb]], writes=[attn_tm])
            p.exec("tensor", lambda e, h=h, hb=hb, vt=vt: e.matmul(
                ps_kv[:], kdt[hb][:], vt[:, h * 128:(h + 1) * 128], start=True, stop=True),
                reads=[kdt[hb], vt], writes=[ps_kv])
            p.exec("vector", lambda e, h=h, kvp=kvp, kvn=kvn: e.scalar_tensor_tensor(
                kvn[:], kvp[:], rc[:, h:h + 1], ps_kv[:], ALU.mult, ALU.add),
                reads=[kvp, rc, ps_kv], writes=[kvn])
            p.exec("scalar", lambda e, h=h, kvn=kvn: e.copy(kvbf[h][:], kvn[:]),
                   reads=[kvn], writes=[kvbf[h]])
        p.exec("scalar", lambda e: e.square(asq[:], attn_tm[:]), reads=[attn_tm], writes=[asq])
        p.exec("vector", lambda e: e.reduce_sum(st8[:, 0:1], asq[:], axis=mybir.AxisListType.X),
               reads=[asq], writes=[st8])
        p.exec("vector", lambda e: e.tensor_scalar_mul(st8[:, 1:2], st8[:, 0:1], 1.0 / E_DIM),
               reads=[st8], writes=[st8])
        p.exec("scalar", lambda e: e.activation(st8[:, 2:3], st8[:, 1:2], AF.Ln, bias=cp[:, 0:1]),
               reads=[st8, cp], writes=[st8])
        p.exec("scalar", lambda e: e.activation(st8[:, 3:4], st8[:, 2:3], AF.Exp, scale=cp[:, 1:2]),
               reads=[st8, cp], writes=[st8])
        p.exec("vector", lambda e: e.scalar_tensor_tensor(
            anb[:], attn_tm[:], st8[:, 3:4], rmsrep[:], ALU.mult, ALU.mult),
            reads=[attn_tm, st8, rmsrep], writes=[anb])
        for ec in range(nech):
            pb = pstr[ec % 2]
            p.exec("tensor", lambda e, ec=ec, pb=pb: e.transpose(
                pb[:], anb[:, ec * 128:(ec + 1) * 128], idb[:]),
                reads=[anb, idb], writes=[pb])
            p.exec("vector", lambda e, ec=ec, pb=pb: e.tensor_copy(afm[:, ec, :], pb[:]),
                   reads=[pb], writes=[afm])
        for sl in range(4):
            pw = ps_w[sl % 2]
            for e_ in range(nech):
                p.exec("tensor", lambda en, sl=sl, e_=e_, pw=pw: en.matmul(
                    pw[:], afm[:, e_, :], wsb[:, e_, sl * 512:(sl + 1) * 512],
                    start=(e_ == 0), stop=(e_ == nech - 1)),
                    reads=[afm, wsb], writes=[pw])
            p.exec("scalar", lambda en, sl=sl, pw=pw: en.copy(
                outc[:, sl * 512:(sl + 1) * 512], pw[:]), reads=[pw], writes=[outc])
        p.dma("gpsimd", attn_out[c * 128:(c + 1) * 128, :], outc[:], reads=[outc])
    p.prefix = _oldpfx
    if standalone:
        p.fence()
        return p.emit_and_finalize()
    return None


def build_qkv0(T, p=None, io=None):
    standalone = p is None
    if standalone:
        p = Prog("qkv0")
    _oldpfx = p.prefix
    if not standalone:
        p.prefix = "qkv0_" + p.prefix
    def _inp(name, shape, dtype):
        if io is not None and name in io:
            return io[name]
        return _inp(name, shape, dtype)
    def _outp(name, shape, dtype):
        if io is not None and name in io:
            return io[name]
        return _outp(name, shape, dtype)
    xn_fm = _inp("xn_fm", [E_DIM, T], BF16)
    Ws = {n: _inp(n, [E_DIM, E_DIM], F32) for n in ("Wq", "Wk", "Wv")}
    rotc = _inp("rotc", [T // 128, 128, 64], F32)
    rots = _inp("rots", [T // 128, 128, 64], F32)
    identbf = _inp("identbf", [128, 128], BF16)
    outs = {
        "q": (_outp("qr_fm", [E_DIM, T], BF16), None),
        "k": (_outp("kr_fm", [E_DIM, T], BF16), _outp("kr_tm", [T, E_DIM], BF16)),
        "v": (None, _outp("v_tm", [T, E_DIM], BF16)),
    }
    nch = T // 128
    nech = E_DIM // 128

    idb = p.sb("idb", [128, 128], BF16)
    p.dma("sync", idb[:], identbf, writes=[idb])
    wsb = p.sb("wsb", [128, nech, E_DIM], BF16)
    xnc = [p.sb(f"xnc{i}", [128, nech, 128], BF16) for i in range(2)]
    pf32_ = [p.sb(f"pf32_{i}", [128, E_DIM], F32) for i in range(2)]
    tmpA_ = [p.sb(f"tmpA_{i}", [128, H, 64], F32) for i in range(2)]
    tmpB_ = [p.sb(f"tmpB_{i}", [128, H, 64], F32) for i in range(2)]
    ptm_ = [p.sb(f"ptm_{i}", [128, E_DIM], BF16) for i in range(2)]
    tbuf = [p.sb(f"tbuf{i}", [128, nech, 128], BF16) for i in range(2)]
    cosc = p.sb("cosc", [128, 64], F32)
    sinc = p.sb("sinc", [128, 64], F32)
    psa = [p.ps(f"psa{i}", [128, 512]) for i in range(4)]
    pstr = [p.ps(f"pstr{i}", [128, 128], BF16) for i in range(2)]

    xre = xn_fm.rearrange("(ec pp) t -> pp ec t", pp=128)

    for tname in ("q", "k", "v"):
        W = Ws["W" + tname]
        fm_out, tm_out = outs[tname]
        load_weight_bf16(p, wsb, W, E_DIM, E_DIM)
        fm_re = fm_out.rearrange("(ec pp) t -> ec pp t", pp=128) if fm_out is not None else None
        for c in range(nch):
            xb = xnc[c % 2]
            pf32 = pf32_[c % 2]
            tmpA = tmpA_[c % 2]
            tmpB = tmpB_[c % 2]
            ptm = ptm_[c % 2]
            for e0 in range(0, nech, 8):
                p.dma("sync", xb[:, e0:e0 + 8, :], xre[:, e0:e0 + 8, c * 128:(c + 1) * 128],
                      writes=[xb])
            for e in range(nech):
                for sl in range(4):
                    p.exec("tensor", lambda en, e=e, sl=sl, xb=xb: en.matmul(
                        psa[sl][:], xb[:, e, :], wsb[:, e, sl * 512:(sl + 1) * 512],
                        start=(e == 0), stop=(e == nech - 1)),
                        reads=[xb, wsb], writes=[psa[sl]])
            if tname == "v":
                for sl in range(4):
                    p.exec("vector", lambda en, sl=sl, ptm=ptm: en.tensor_copy(
                        ptm[:, sl * 512:(sl + 1) * 512], psa[sl][:]),
                        reads=[psa[sl]], writes=[ptm])
            else:
                for sl in range(4):
                    p.exec("vector", lambda en, sl=sl, pf32=pf32: en.tensor_copy(
                        pf32[:, sl * 512:(sl + 1) * 512], psa[sl][:]),
                        reads=[psa[sl]], writes=[pf32])
                p.dma("sync", cosc[:], rotc[c], writes=[cosc])
                p.dma("sync", sinc[:], rots[c], writes=[sinc])
                v4 = pf32.ap().rearrange("pp (h two d) -> pp h two d", h=H, two=2)
                o4 = ptm.ap().rearrange("pp (h two d) -> pp h two d", h=H, two=2)
                cb = cosc.ap().rearrange("pp (o d) -> pp o d", o=1).broadcast_to([128, H, 64])
                sb_ = sinc.ap().rearrange("pp (o d) -> pp o d", o=1).broadcast_to([128, H, 64])
                x1 = v4[:, :, 0, :]
                x2 = v4[:, :, 1, :]
                p.exec("vector", lambda en, x1=x1, cb=cb, tmpA=tmpA: en.tensor_tensor(tmpA.ap(), x1, cb, ALU.mult),
                       reads=[pf32, cosc], writes=[tmpA])
                p.exec("vector", lambda en, x2=x2, sb_=sb_, tmpB=tmpB: en.tensor_tensor(tmpB.ap(), x2, sb_, ALU.mult),
                       reads=[pf32, sinc], writes=[tmpB])
                p.exec("vector", lambda en, o4=o4, tmpA=tmpA, tmpB=tmpB: en.tensor_tensor(
                    o4[:, :, 0, :], tmpA.ap(), tmpB.ap(), ALU.add),
                    reads=[tmpA, tmpB], writes=[ptm])
                p.exec("vector", lambda en, x2=x2, cb=cb, tmpA=tmpA: en.tensor_tensor(tmpA.ap(), x2, cb, ALU.mult),
                       reads=[pf32, cosc], writes=[tmpA])
                p.exec("vector", lambda en, x1=x1, sb_=sb_, tmpB=tmpB: en.tensor_tensor(tmpB.ap(), x1, sb_, ALU.mult),
                       reads=[pf32, sinc], writes=[tmpB])
                p.exec("vector", lambda en, o4=o4, tmpA=tmpA, tmpB=tmpB: en.tensor_tensor(
                    o4[:, :, 1, :], tmpA.ap(), tmpB.ap(), ALU.subtract),
                    reads=[tmpA, tmpB], writes=[ptm])
            if tm_out is not None:
                p.dma("gpsimd", tm_out[c * 128:(c + 1) * 128, :], ptm[:], reads=[ptm])
            if fm_out is not None:
                tb = tbuf[c % 2]
                for ec in range(nech):
                    pb = pstr[ec % 2]
                    p.exec("tensor", lambda en, ec=ec, pb=pb, ptm=ptm: en.transpose(
                        pb[:], ptm[:, ec * 128:(ec + 1) * 128], idb[:]),
                        reads=[ptm, idb], writes=[pb])
                    p.exec("vector", lambda en, ec=ec, pb=pb, tb=tb: en.tensor_copy(
                        tb[:, ec, :], pb[:]), reads=[pb], writes=[tb])
                    p.dma("gpsimd", fm_re[ec, :, c * 128:(c + 1) * 128], tb[:, ec, :],
                          reads=[tb])
    p.prefix = _oldpfx
    if standalone:
        p.fence()
        return p.emit_and_finalize()
    return None
def build_ln(T, xname="x", outname="xn_fm", p=None, io=None):
    standalone = p is None
    if standalone:
        p = Prog("ln")
    _oldpfx = p.prefix
    if not standalone:
        p.prefix = "ln_" + p.prefix
    def _inp(name, shape, dtype):
        if io is not None and name in io:
            return io[name]
        return _inp(name, shape, dtype)
    def _outp(name, shape, dtype):
        if io is not None and name in io:
            return io[name]
        return _outp(name, shape, dtype)
    x = _inp(xname, [T, E_DIM], F32)
    xn_fm = _outp(outname, [E_DIM, T], BF16)
    identbf = _inp("identbf", [128, 128], BF16)
    constpm = _inp("constpm", [128, 2], F32)
    nch = T // 128
    nech = E_DIM // 128

    xin = [p.sb(f"xin{i}", [128, E_DIM], F32) for i in range(2)]
    xsq = p.sb("xsq", [128, E_DIM], F32)
    stats = p.sb("stats", [128, 8], F32)
    xn_tm = p.sb("xn_tm", [128, E_DIM], BF16)
    tbuf = [p.sb(f"tbuf{i}", [128, nech, 128], BF16) for i in range(2)]
    idb = p.sb("idb", [128, 128], BF16)
    pstr = [p.ps(f"pstr{i}", [128, 128], BF16) for i in range(2)]

    cp = p.sb("cp", [128, 2], F32)
    p.dma("sync", cp[:], constpm, writes=[cp])
    p.dma("sync", idb[:], identbf, writes=[idb])
    xn_re = xn_fm.rearrange("(ec pp) t -> ec pp t", pp=128)

    for c in range(nch):
        b = c % 2
        xb = xin[b]
        p.dma("sync", xb[:], x[c * 128:(c + 1) * 128, :], writes=[xb])
        p.exec("scalar", lambda e, i=xb: e.square(xsq[:], i[:]), reads=[xb], writes=[xsq])
        p.exec("vector", lambda e, i=xb: e.reduce_sum(stats[:, 0:1], i[:], axis=mybir.AxisListType.X),
               reads=[xb], writes=[stats])
        p.exec("vector", lambda e: e.reduce_sum(stats[:, 1:2], xsq[:], axis=mybir.AxisListType.X),
               reads=[xsq], writes=[stats])
        p.exec("vector", lambda e: e.tensor_scalar_mul(stats[:, 2:3], stats[:, 0:1], -1.0 / E_DIM),
               reads=[stats], writes=[stats])
        p.exec("vector", lambda e: e.tensor_scalar_mul(stats[:, 3:4], stats[:, 1:2], 1.0 / E_DIM),
               reads=[stats], writes=[stats])
        p.exec("vector", lambda e: e.tensor_mul(stats[:, 4:5], stats[:, 2:3], stats[:, 2:3]),
               reads=[stats], writes=[stats])
        p.exec("vector", lambda e: e.tensor_sub(stats[:, 5:6], stats[:, 3:4], stats[:, 4:5]),
               reads=[stats], writes=[stats])
        p.exec("scalar", lambda e: e.activation(stats[:, 6:7], stats[:, 5:6], AF.Ln, bias=cp[:, 0:1]),
               reads=[stats, cp], writes=[stats])
        p.exec("scalar", lambda e: e.activation(stats[:, 7:8], stats[:, 6:7], AF.Exp, scale=cp[:, 1:2]),
               reads=[stats, cp], writes=[stats])
        p.exec("vector", lambda e: e.tensor_mul(stats[:, 4:5], stats[:, 2:3], stats[:, 7:8]),
               reads=[stats], writes=[stats])
        p.exec("vector", lambda e, i=xb: e.tensor_scalar(
            xn_tm[:], i[:], stats[:, 7:8], stats[:, 4:5], ALU.mult, ALU.add),
            reads=[xb, stats], writes=[xn_tm])
        tb = tbuf[b]
        for ec in range(nech):
            pb = pstr[ec % 2]
            p.exec("tensor", lambda e, ec=ec, pb=pb: e.transpose(
                pb[:], xn_tm[:, ec * 128:(ec + 1) * 128], idb[:]),
                reads=[xn_tm, idb], writes=[pb])
            p.exec("vector", lambda e, ec=ec, pb=pb, tb=tb: e.tensor_copy(tb[:, ec, :], pb[:]),
                   reads=[pb], writes=[tb])
            p.dma("gpsimd", xn_re[ec, :, c * 128:(c + 1) * 128], tb[:, ec, :],
                  reads=[tb])
    p.prefix = _oldpfx
    if standalone:
        p.fence()
        return p.emit_and_finalize()
    return None


# version 05
# speedup vs baseline: 1.5982x; 1.0225x over previous
import numpy as np
import ml_dtypes
from contextlib import ExitStack

import concourse.bass as bass
import concourse.mybir as mybir
from concourse.bass_utils import run_bass_kernel_spmd

F32 = mybir.dt.float32
BF16 = mybir.dt.bfloat16
AF = mybir.ActivationFunctionType
ALU = mybir.AluOpType
BF = ml_dtypes.bfloat16

E_DIM = 2048
H = 16
D = 128
INTER = 8192
EPS = 1e-5

ENGINES = ("sync", "scalar", "vector", "tensor", "gpsimd")
DMA_QUEUES = ("sync", "gpsimd")


class _Op:
    __slots__ = ("eng", "fn", "is_dma", "deps", "inc_needed", "sem_val", "sem_key", "idx")

    def __init__(self, eng, fn, is_dma, idx):
        self.eng = eng
        self.fn = fn
        self.is_dma = is_dma
        self.deps = set()
        self.inc_needed = False
        self.sem_val = None
        self.idx = idx


class Prog:

    def __init__(self, name="prog"):
        self.nc = bass.Bass("TRN2", target_bir_lowering=False)
        self.stack = ExitStack()
        self.ops = []
        self.last_writer = {}
        self.readers = {}
        self.names = set()
        self._barrier = ()
        self._scopes = []
        self._dma_recent = {}
        self.prefix = ""

    from contextlib import contextmanager as _cm

    @_cm
    def scope(self):
        st = ExitStack()
        self._scopes.append(st)
        try:
            yield
        finally:
            self._scopes.pop()
            st.close()
            self.barrier()

    def barrier(self):
        last = {}
        for op in self.ops:
            if not op.is_dma:
                last[op.eng] = op
        bar = set(last.values())
        for q, lst in self._dma_recent.items():
            bar.update(lst)
        self._barrier = tuple(bar)

    def inp(self, name, shape, dtype):
        return self.nc.dram_tensor(name, list(shape), dtype, kind="ExternalInput").ap()

    def outp(self, name, shape, dtype):
        return self.nc.dram_tensor(name, list(shape), dtype, kind="ExternalOutput").ap()

    def dram(self, name, shape, dtype):
        return self.nc.dram_tensor(self.prefix + name, list(shape), dtype, kind="Internal").ap()

    def sb(self, name, shape, dtype):
        name = self.prefix + name
        assert name not in self.names
        self.names.add(name)
        st = self._scopes[-1] if self._scopes else self.stack
        return st.enter_context(self.nc.sbuf_tensor(name, list(shape), dtype))

    def ps(self, name, shape, dtype=F32):
        name = self.prefix + name
        assert name not in self.names
        self.names.add(name)
        st = self._scopes[-1] if self._scopes else self.stack
        return st.enter_context(self.nc.psum_tensor(name, list(shape), dtype))

    def exec(self, eng, fn, reads=(), writes=(), is_dma=False):
        op = _Op(eng, fn, is_dma, len(self.ops))
        deps = set()
        for t in reads:
            w = self.last_writer.get(self._key(t))
            if w is not None:
                deps.add(w)
        for t in writes:
            k = self._key(t)
            deps.update(self.readers.get(k, ()))
            w = self.last_writer.get(k)
            if w is not None:
                deps.add(w)
        deps.update(self._barrier)
        op.deps = {d for d in deps if d is not op}
        self.ops.append(op)
        for t in reads:
            self.readers.setdefault(self._key(t), []).append(op)
        for t in writes:
            k = self._key(t)
            self.last_writer[k] = op
            self.readers[k] = []
        return op

    def dma(self, queue, out, in_, reads=(), writes=(), **kw):
        fn = lambda e, o=out, i=in_, kw=kw: e.dma_start(out=o, in_=i, **kw)
        reads = list(reads) + [in_]
        writes = list(writes) + [out]
        op = self.exec(queue, fn, reads=reads, writes=writes, is_dma=True)
        rec = self._dma_recent.setdefault(queue, [])
        rec.append(op)
        if len(rec) > self.DMA_SLOTS:
            rec.pop(0)
        return op

    @staticmethod
    def _key(t):
        n = getattr(t, "name", None)
        if n is None and hasattr(t, "tensor"):
            n = t.tensor.name
        return n if n is not None else t

    DMA_SLOTS = 8

    def emit_and_finalize(self):
        def _needs_sem(d, op):
            if d.eng == "tensor" and op.eng == "tensor" and not d.is_dma and not op.is_dma:
                return False
            return True

        for op in self.ops:
            for d in op.deps:
                if _needs_sem(d, op):
                    d.inc_needed = True
        for op in self.ops:
            if op.is_dma:
                op.inc_needed = True

        sems = {}
        counters = {}
        slot_rr = {}
        slot_prev = {}
        for op in self.ops:
            if not op.inc_needed:
                continue
            if op.is_dma:
                q = op.eng
                s = slot_rr.get(q, 0)
                slot_rr[q] = (s + 1) % self.DMA_SLOTS
                key = (q, "d", s)
                prev = slot_prev.get(key)
                if prev is not None:
                    op.deps.add(prev)
                slot_prev[key] = op
            else:
                key = (op.eng, "c", 0)
            if key not in sems:
                sems[key] = self.stack.enter_context(
                    self.nc.semaphore(f"s_{key[0]}_{key[1]}{key[2]}")
                )
            step = 16 if op.is_dma else 1
            counters[key] = counters.get(key, 0) + step
            op.sem_val = counters[key]
            op.sem_key = key
        self._sems = sems

        streams = {e: [] for e in ENGINES}
        floors = {e: {} for e in ENGINES}
        for op in self.ops:
            waits = {}
            fl = floors[op.eng]
            for d in op.deps:
                if not _needs_sem(d, op):
                    continue
                key = d.sem_key
                if fl.get(key, 0) >= d.sem_val:
                    continue
                waits[key] = max(waits.get(key, 0), d.sem_val)
            for key, val in waits.items():
                fl[key] = val
            streams[op.eng].append((op, sorted(waits.items(), key=lambda kv: str(kv[0]))))

        block = self.stack.enter_context(self.nc.Block())
        for e in ENGINES:
            if not streams[e]:
                continue

            def body(eng, ops=streams[e]):
                for op, waits in ops:
                    for (key, val) in waits:
                        eng.wait_ge(sems[key], val)
                    inst = op.fn(eng)
                    if op.inc_needed:
                        inst.then_inc(sems[op.sem_key], 16 if op.is_dma else 1)

            getattr(block, e)(body)
        self.stack.close()
        self.nc.finalize()
        return self.nc

    def fence(self, eng="gpsimd"):
        pend = set()
        for k, w in self.last_writer.items():
            pend.add(w)
        for k, rs in self.readers.items():
            pend.update(rs)

        def fn(e):
            return e.nop()

        op = _Op(eng, fn, False, len(self.ops))
        op.deps = {d for d in pend if d is not op}
        self.ops.append(op)
        return op


def cdiv(a, b):
    return (a + b - 1) // b


def prep_host(w, T):
    import math
    pr = {}
    pr["identbf"] = np.eye(128).astype(BF)
    pr["identf32"] = np.eye(128, dtype=np.float32)
    pr["h128bf"] = sylvester(128).astype(BF)
    pr["ones128f"] = np.ones((1, 128), np.float32)
    pr["onesbf"] = np.ones((1, 128)).astype(BF)
    pr["constpm"] = np.tile(np.array([[EPS, -0.5]], np.float32), (128, 1))
    freq = (1.0 / 1024.0) ** np.linspace(0.0, 1.0, 32, dtype=np.float32)
    freq = np.concatenate([freq, np.zeros(32, np.float32)])
    t = np.arange(T, dtype=np.float32)
    theta = t[:, None] * freq[None, :]
    pr["rotc"] = np.ascontiguousarray(np.cos(theta).astype(np.float32).reshape(T // 128, 128, 64))
    pr["rots"] = np.ascontiguousarray(np.sin(theta).astype(np.float32).reshape(T // 128, 128, 64))
    start = 2.0 ** (-2.0 ** (-(math.log2(H) - 3)))
    slopes0 = np.array([start * start ** i for i in range(H)], np.float64)
    r = np.exp(-slopes0)
    i = np.arange(128, dtype=np.float64)
    pr["qdec"] = (r[None, :] ** (i[:, None] + 1.0)).astype(np.float32)
    pr["kdec"] = (r[None, :] ** (127.0 - i[:, None])).astype(np.float32)
    pr["rC"] = np.tile((r ** 128.0).astype(np.float32)[None, :], (128, 1))
    diff = i[:, None] - i[None, :]
    dmaskT = np.zeros((H, 128, 128), np.float32)
    for h in range(H):
        dm = np.where(diff >= 0, r[h] ** np.maximum(diff, 0), 0.0)
        dmaskT[h] = dm.T.astype(np.float32)
    pr["dmaskT"] = dmaskT
    j64 = np.arange(64, dtype=np.float32)
    dT = j64[None, :] - j64[:, None]
    pr["diffT64"] = np.where(dT >= 0, dT, 1e6).astype(np.float32)
    pr["negiota64"] = np.tile(-j64[:, None], (1, H)).astype(np.float32)
    pr["negrev64"] = np.tile(-(64.0 - j64)[:, None], (1, H)).astype(np.float32)
    pr["Wq"] = w["L0_attn_Wq"]
    pr["Wk"] = w["L0_attn_Wk"]
    pr["Wv"] = w["L0_attn_Wv"]
    pr["Wo"] = w["L0_attn_Wo"]
    pr["rms_w_row"] = np.ascontiguousarray(w["L0_attn_rms_w"][None, :])
    pr["W_upT"] = np.ascontiguousarray(w["L0_ffn_W_up"].T)
    pr["W_gateT"] = np.ascontiguousarray(w["L0_ffn_W_gate"].T)
    pr["W_downT"] = np.ascontiguousarray(w["L0_ffn_W_down"].T)
    pr["su_ug_pm"] = np.ascontiguousarray(w["L0_ffn_SU_ug"].reshape(16, 128).T)
    pr["su_dp_pm"] = np.ascontiguousarray(w["L0_ffn_SU_dp"].reshape(64, 128).T)
    pr["wsc_up_pm"] = np.ascontiguousarray(w["L0_ffn_Wscale_ug"][:INTER].reshape(64, 128).T)
    pr["wsc_gate_pm"] = np.ascontiguousarray(w["L0_ffn_Wscale_ug"][INTER:].reshape(64, 128).T)
    pr["wsc_dp_pm"] = np.ascontiguousarray(w["L0_ffn_Wscale_dp"].reshape(16, 128).T)
    pr["Wqkv"] = w["L1_attn_Wqkv"]
    pr["bqkv_pm"] = np.ascontiguousarray(w["L1_attn_bqkv"].reshape(48, 128).T)
    pr["Wout"] = w["L1_attn_Wout"]
    pr["bout_row"] = np.ascontiguousarray(w["L1_attn_bout"][None, :])
    pr["slopes_row"] = np.ascontiguousarray(w["L1_attn_slopes"][None, :])
    fprime = np.arange(INTER)
    perm = 4 * (fprime % 2048) + fprime // 2048
    pr["w_gate_p"] = np.ascontiguousarray(w["L1_ffn_w_gate"][:, perm])
    pr["w_up_p"] = np.ascontiguousarray(w["L1_ffn_w_up"][:, perm])
    pr["w_down_p"] = np.ascontiguousarray(w["L1_ffn_w_down"][perm, :])
    return pr


def build_mega(T):
    p = Prog("mega")
    ext = {}
    for name, shape, dt in [
        ("x", [T, E_DIM], F32), ("identbf", [128, 128], BF16), ("constpm", [128, 2], F32),
        ("rotc", [T // 128, 128, 64], F32), ("rots", [T // 128, 128, 64], F32),
        ("Wq", [E_DIM, E_DIM], F32), ("Wk", [E_DIM, E_DIM], F32), ("Wv", [E_DIM, E_DIM], F32),
        ("Wo", [E_DIM, E_DIM], F32), ("rms_w_row", [1, E_DIM], F32),
        ("dmaskT", [H, 128, 128], F32), ("qdec", [128, H], F32), ("kdec", [128, H], F32),
        ("rC", [128, H], F32), ("ones128f", [1, 128], F32), ("onesbf", [1, 128], BF16),
        ("W_upT", [E_DIM, INTER], F32), ("W_gateT", [E_DIM, INTER], F32),
        ("W_downT", [INTER, E_DIM], F32),
        ("su_ug_pm", [128, 16], F32), ("su_dp_pm", [128, 64], F32),
        ("wsc_up_pm", [128, 64], F32), ("wsc_gate_pm", [128, 64], F32),
        ("wsc_dp_pm", [128, 16], F32),
        ("h128bf", [128, 128], BF16), ("identf32", [128, 128], F32),
        ("Wqkv", [E_DIM, 3 * E_DIM], F32), ("bqkv_pm", [128, 48], F32),
        ("Wout", [E_DIM, E_DIM], F32), ("bout_row", [1, E_DIM], F32),
        ("slopes_row", [1, H], F32), ("diffT64", [64, 64], F32),
        ("negiota64", [64, H], F32), ("negrev64", [64, H], F32),
        ("w_gate_p", [E_DIM, INTER], F32), ("w_up_p", [E_DIM, INTER], F32),
        ("w_down_p", [INTER, E_DIM], F32),
    ]:
        ext[name] = p.inp(name, shape, dt)
    out = p.outp("out_tm", [T, E_DIM], F32)
    d_xn0 = p.dram("xn0_a", [E_DIM, T], BF16)
    d_q0 = p.dram("q0_a", [E_DIM, T], BF16)
    d_k0 = p.dram("k0_a", [E_DIM, T], BF16)
    d_k0t = p.dram("k0t_a", [T, E_DIM], BF16)
    d_v0 = p.dram("v0_a", [T, E_DIM], BF16)
    d_attn0 = p.dram("attn0_a", [T, E_DIM], F32)
    d_ffn0 = p.dram("ffn0_a", [T, E_DIM], F32)
    d_x1 = p.dram("x1_a", [T, E_DIM], F32)
    d_xn1 = p.dram("xn1_a", [E_DIM, T], BF16)
    d_q1 = p.dram("q1_a", [E_DIM, T], BF16)
    d_k1 = p.dram("k1_a", [E_DIM, T], BF16)
    d_k1t = p.dram("k1t_a", [T, E_DIM], BF16)
    d_v1 = p.dram("v1_a", [T, E_DIM], BF16)
    d_attn1 = p.dram("attn1_a", [T, E_DIM], F32)
    d_ffn1 = p.dram("ffn1_a", [T, E_DIM], F32)

    with p.scope():
        build_ln(T, p=p, io={"x": ext["x"], "xn_fm": d_xn0,
                             "identbf": ext["identbf"], "constpm": ext["constpm"]})
    with p.scope():
        build_qkv0(T, p=p, io={"xn_fm": d_xn0, "Wq": ext["Wq"], "Wk": ext["Wk"],
                               "Wv": ext["Wv"], "rotc": ext["rotc"], "rots": ext["rots"],
                               "identbf": ext["identbf"], "qr_fm": d_q0, "kr_fm": d_k0,
                               "kr_tm": d_k0t, "v_tm": d_v0})
    with p.scope():
        build_attn0(T, p=p, io={"qr_fm": d_q0, "kr_fm": d_k0, "kr_tm": d_k0t, "v_tm": d_v0,
                                "Wo": ext["Wo"], "rms_w_row": ext["rms_w_row"],
                                "dmaskT": ext["dmaskT"], "qdec": ext["qdec"],
                                "kdec": ext["kdec"], "rC": ext["rC"],
                                "identbf": ext["identbf"], "ones128f": ext["ones128f"],
                                "constpm": ext["constpm"], "attn0_tm": d_attn0})
    with p.scope():
        build_ffn0(T, p=p, io={"xn_fm": d_xn0, "W_upT": ext["W_upT"],
                               "W_gateT": ext["W_gateT"], "W_downT": ext["W_downT"],
                               "su_ug_pm": ext["su_ug_pm"], "su_dp_pm": ext["su_dp_pm"],
                               "wsc_up_pm": ext["wsc_up_pm"], "wsc_gate_pm": ext["wsc_gate_pm"],
                               "wsc_dp_pm": ext["wsc_dp_pm"], "h128bf": ext["h128bf"],
                               "identf32": ext["identf32"], "ffn0_tm": d_ffn0})
    with p.scope():
        p.prefix = "r0_"
        build_resln(T, with_ln=True, p=p, io={"x": ext["x"], "add0": d_attn0,
                                              "add1": d_ffn0, "x1_tm": d_x1,
                                              "xn_fm": d_xn1, "identbf": ext["identbf"],
                                              "constpm": ext["constpm"]})
        p.prefix = ""
    with p.scope():
        build_qkv1(T, p=p, io={"xn_fm": d_xn1, "Wqkv": ext["Wqkv"],
                               "bqkv_pm": ext["bqkv_pm"], "identbf": ext["identbf"],
                               "q1_fm": d_q1, "k1_fm": d_k1, "k1_tm": d_k1t, "v1_tm": d_v1})
    with p.scope():
        build_attn1(T, p=p, io={"q1_fm": d_q1, "k1_fm": d_k1, "k1_tm": d_k1t, "v1_tm": d_v1,
                                "Wout": ext["Wout"], "bout_row": ext["bout_row"],
                                "slopes_row": ext["slopes_row"], "diffT64": ext["diffT64"],
                                "negiota64": ext["negiota64"], "negrev64": ext["negrev64"],
                                "ones128f": ext["ones128f"], "onesbf": ext["onesbf"],
                                "identbf": ext["identbf"], "attn1_tm": d_attn1})
    with p.scope():
        build_ffn1(T, p=p, io={"xn_fm": d_xn1, "w_gate_p": ext["w_gate_p"],
                               "w_up_p": ext["w_up_p"], "w_down_p": ext["w_down_p"],
                               "identf32": ext["identf32"], "ffn1_tm": d_ffn1})
    with p.scope():
        p.prefix = "r1_"
        build_resln(T, with_ln=False, p=p, io={"x": d_x1, "add0": d_attn1,
                                               "add1": d_ffn1, "x1_tm": out})
        p.prefix = ""
    p.fence()
    return p.emit_and_finalize()


def mega_in_map(pr, x_b):
    m = {k: pr[k] for k in (
        "identbf", "constpm", "rotc", "rots", "Wq", "Wk", "Wv", "Wo", "rms_w_row",
        "dmaskT", "qdec", "kdec", "rC", "ones128f", "onesbf", "W_upT", "W_gateT",
        "W_downT", "su_ug_pm", "su_dp_pm", "wsc_up_pm", "wsc_gate_pm", "wsc_dp_pm",
        "h128bf", "identf32", "Wqkv", "bqkv_pm", "Wout", "bout_row", "slopes_row",
        "diffT64", "negiota64", "negrev64", "w_gate_p", "w_up_p", "w_down_p")}
    m["x"] = np.ascontiguousarray(x_b)
    return m


_NC_CACHE = {}


def _get_nc(key, builder, T):
    if key not in _NC_CACHE:
        _NC_CACHE[key] = builder(T)
    return _NC_CACHE[key]


def forward(x_full, w, T, run):
    B = x_full.shape[0]
    pr = prep_host(w, T)

    def rep(d):
        return [d] * B

    def percore(key_arrs, shared):
        return [dict(shared, **ka) for ka in key_arrs]

    nc = _get_nc(("ln0", T), build_ln, T)
    outs = run(nc, percore([{"x": np.ascontiguousarray(x_full[b])} for b in range(B)],
                           {"identbf": pr["identbf"], "constpm": pr["constpm"]}))
    xn0 = [o["xn_fm"] for o in outs]
    nc = _get_nc(("qkv0", T), build_qkv0, T)
    outs = run(nc, percore([{"xn_fm": xn0[b]} for b in range(B)],
                           {"Wq": pr["Wq"], "Wk": pr["Wk"], "Wv": pr["Wv"],
                            "rotc": pr["rotc"], "rots": pr["rots"], "identbf": pr["identbf"]}))
    q0, k0, k0t, v0 = ([o[k] for o in outs] for k in ("qr_fm", "kr_fm", "kr_tm", "v_tm"))
    nc = _get_nc(("attn0", T), build_attn0, T)
    outs = run(nc, percore(
        [{"qr_fm": q0[b], "kr_fm": k0[b], "kr_tm": k0t[b], "v_tm": v0[b]} for b in range(B)],
        {"Wo": pr["Wo"], "rms_w_row": pr["rms_w_row"], "dmaskT": pr["dmaskT"],
         "qdec": pr["qdec"], "kdec": pr["kdec"], "rC": pr["rC"],
         "identbf": pr["identbf"], "ones128f": pr["ones128f"], "constpm": pr["constpm"]}))
    attn0 = [o["attn0_tm"] for o in outs]
    nc = _get_nc(("ffn0", T), build_ffn0, T)
    outs = run(nc, percore([{"xn_fm": xn0[b]} for b in range(B)],
                           {"W_upT": pr["W_upT"], "W_gateT": pr["W_gateT"],
                            "W_downT": pr["W_downT"], "su_ug_pm": pr["su_ug_pm"],
                            "su_dp_pm": pr["su_dp_pm"], "wsc_up_pm": pr["wsc_up_pm"],
                            "wsc_gate_pm": pr["wsc_gate_pm"], "wsc_dp_pm": pr["wsc_dp_pm"],
                            "h128bf": pr["h128bf"], "identf32": pr["identf32"]}))
    ffn0 = [o["ffn0_tm"] for o in outs]
    nc = _get_nc(("resln", T), lambda TT: build_resln(TT, with_ln=True), T)
    outs = run(nc, percore(
        [{"x": np.ascontiguousarray(x_full[b]), "add0": attn0[b], "add1": ffn0[b]}
         for b in range(B)],
        {"identbf": pr["identbf"], "constpm": pr["constpm"]}))
    x1 = [o["x1_tm"] for o in outs]
    xn1 = [o["xn_fm"] for o in outs]
    nc = _get_nc(("qkv1", T), build_qkv1, T)
    outs = run(nc, percore([{"xn_fm": xn1[b]} for b in range(B)],
                           {"Wqkv": pr["Wqkv"], "bqkv_pm": pr["bqkv_pm"],
                            "identbf": pr["identbf"]}))
    q1, k1, k1t, v1 = ([o[k] for o in outs] for k in ("q1_fm", "k1_fm", "k1_tm", "v1_tm"))
    nc = _get_nc(("attn1", T), build_attn1, T)
    outs = run(nc, percore(
        [{"q1_fm": q1[b], "k1_fm": k1[b], "k1_tm": k1t[b], "v1_tm": v1[b]} for b in range(B)],
        {"Wout": pr["Wout"], "bout_row": pr["bout_row"], "slopes_row": pr["slopes_row"],
         "diffT64": pr["diffT64"], "negiota64": pr["negiota64"], "negrev64": pr["negrev64"],
         "ones128f": pr["ones128f"], "onesbf": pr["onesbf"], "identbf": pr["identbf"]}))
    attn1 = [o["attn1_tm"] for o in outs]
    nc = _get_nc(("ffn1", T), build_ffn1, T)
    outs = run(nc, percore([{"xn_fm": xn1[b]} for b in range(B)],
                           {"w_gate_p": pr["w_gate_p"], "w_up_p": pr["w_up_p"],
                            "w_down_p": pr["w_down_p"], "identf32": pr["identf32"]}))
    ffn1 = [o["ffn1_tm"] for o in outs]
    nc = _get_nc(("res1", T), lambda TT: build_resln(TT, with_ln=False), T)
    outs = run(nc, [{"x": x1[b], "add0": attn1[b], "add1": ffn1[b]} for b in range(B)])
    return np.stack([o["x1_tm"] for o in outs], axis=0)


def kernel(**inputs):
    x = np.asarray(inputs["L0_attn_x"], np.float32)
    T = x.shape[1]
    B = x.shape[0]
    pr = prep_host(inputs, T)
    nc = _get_nc(("mega", T), build_mega, T)
    in_maps = [mega_in_map(pr, x[b]) for b in range(B)]
    res = run_bass_kernel_spmd(nc, in_maps, core_ids=list(range(B)))
    return np.stack([r["out_tm"] for r in res.results], axis=0)


def kernel_multilaunch(**inputs):
    x = np.asarray(inputs["L0_attn_x"], np.float32)
    T = x.shape[1]

    def run(nc, in_maps):
        res = run_bass_kernel_spmd(nc, in_maps, core_ids=list(range(len(in_maps))))
        return res.results

    return forward(x, inputs, T, run)


def sylvester(n):
    Hm = np.array([[1.0]], np.float32)
    while Hm.shape[0] < n:
        Hm = np.block([[Hm, Hm], [Hm, -Hm]])
    return Hm.astype(np.float32)


def load_weight_bf16(p, wsb, w_dram, rows, cols, name=None, ecs_per_dma=8):
    nech = rows // 128
    wre = w_dram.rearrange("(ec pp) c -> pp ec c", pp=128)
    for e0 in range(0, nech, ecs_per_dma):
        e1 = min(e0 + ecs_per_dma, nech)
        p.dma("gpsimd", wsb[:, e0:e1, :], wre[:, e0:e1, :], writes=[wsb])


def _key_of(t):
    return Prog._key(t)


def _butterfly(p, n, cur, nxt, final_out=None, final_slice=None, width=None, tag=""):
    import math as _m
    stages = int(_m.log2(n))
    bufs = [cur, nxt]
    kc, kn, kf = _key_of(cur), _key_of(nxt), (None if final_out is None else _key_of(final_out))
    keys = [kc, kn]
    pair_i = 0
    for s in range(stages):
        h = 1 << s
        src = bufs[s % 2]
        dst = bufs[(s + 1) % 2]
        ksrc = keys[s % 2]
        kdst = keys[(s + 1) % 2]
        last = (s == stages - 1) and final_out is not None
        for blk in range(n // (2 * h)):
            for i in range(h):
                ia = blk * 2 * h + i
                ib = ia + h
                eng = "vector" if (pair_i % 2 == 0) else "gpsimd"
                pair_i += 1
                for sign, oi in ((ALU.add, ia), (ALU.subtract, ib)):
                    if last:
                        oap = final_out[:, oi, final_slice[0]:final_slice[1]]
                        okey = f"{kf}@{oi}"
                    else:
                        oap = dst[:, oi, :]
                        okey = f"{kdst}@{oi}"
                    p.exec(eng, lambda e, src=src, ia=ia, ib=ib, oap=oap, sign=sign:
                           e.tensor_tensor(oap, src[:, ia, :], src[:, ib, :], sign),
                           reads=[f"{ksrc}@{ia}", f"{ksrc}@{ib}"], writes=[okey])
    return bufs[stages % 2]


def build_qkv1(T, p=None, io=None):
    standalone = p is None
    if standalone:
        p = Prog("qkv1")
    _oldpfx = p.prefix
    if not standalone:
        p.prefix = "qkv1_" + p.prefix
    def _inp(name, shape, dtype):
        if io is not None and name in io:
            return io[name]
        return p.inp(name, shape, dtype)
    def _outp(name, shape, dtype):
        if io is not None and name in io:
            return io[name]
        return p.outp(name, shape, dtype)
    xn_fm = _inp("xn_fm", [E_DIM, T], BF16)
    Wqkv = _inp("Wqkv", [E_DIM, 3 * E_DIM], F32)
    bq_pm = _inp("bqkv_pm", [128, 48], F32)
    identbf = _inp("identbf", [128, 128], BF16)
    q_fm = _outp("q1_fm", [E_DIM, T], BF16)
    k_fm = _outp("k1_fm", [E_DIM, T], BF16)
    k_tm = _outp("k1_tm", [T, E_DIM], BF16)
    v_tm = _outp("v1_tm", [T, E_DIM], BF16)
    nech = E_DIM // 128
    TS = min(512, T)
    nts = T // TS
    nci_ = TS // 128

    idb = p.sb("idb", [128, 128], BF16)
    p.dma("sync", idb[:], identbf, writes=[idb])
    bq = p.sb("bq", [128, 48], F32)
    p.dma("sync", bq[:], bq_pm, writes=[bq])
    xnsb = p.sb("xnsb", [128, nech, T], BF16)
    xre = xn_fm.rearrange("(ec pp) t -> pp ec t", pp=128)
    for e0 in range(0, nech, 4):
        p.dma("sync", xnsb[:, e0:e0 + 4, :], xre[:, e0:e0 + 4, :], writes=[xnsb])
    wre = Wqkv.rearrange("(ec pp) o -> pp ec o", pp=128)
    q_re = q_fm.rearrange("(ec pp) t -> ec pp t", pp=128)
    k_re = k_fm.rearrange("(ec pp) t -> ec pp t", pp=128)

    wt = [p.sb(f"wt{i}", [128, nech, 128], BF16) for i in range(2)]
    ebuf_ = [p.sb(f"ebuf{i}", [128, TS], BF16) for i in range(2)]
    ttile_ = [p.sb(f"ttile{i}", [128, 128], BF16) for i in range(2)]
    ps_m = [p.ps(f"ps_m{i}", [128, TS]) for i in range(2)]
    ps_t = [p.ps(f"ps_t{i}", [128, 128], BF16) for i in range(2)]

    for oc in range(48):
        w = wt[oc % 2]
        for e0 in range(0, nech, 8):
            p.dma("gpsimd", w[:, e0:e0 + 8, :], wre[:, e0:e0 + 8, oc * 128:(oc + 1) * 128],
                  writes=[w])
        for ts in range(nts):
            pm = ps_m[ts % 2]
            ebuf = ebuf_[(oc * nts + ts) % 2]
            for ec in range(nech):
                p.exec("tensor", lambda e, ec=ec, ts=ts, w=w, pm=pm: e.matmul(
                    pm[:], w[:, ec, :], xnsb[:, ec, ts * TS:(ts + 1) * TS],
                    start=(ec == 0), stop=(ec == nech - 1)),
                    reads=[w, xnsb], writes=[pm])
            p.exec("vector", lambda e, oc=oc, pm=pm, ebuf=ebuf: e.tensor_scalar_add(
                ebuf[:], pm[:], bq[:, oc:oc + 1]), reads=[pm, bq], writes=[ebuf])
            if oc < 16:
                p.dma("gpsimd", q_re[oc, :, ts * TS:(ts + 1) * TS], ebuf[:], reads=[ebuf])
            elif oc < 32:
                p.dma("gpsimd", k_re[oc - 16, :, ts * TS:(ts + 1) * TS], ebuf[:], reads=[ebuf])
            if oc >= 16:
                tmout = k_tm if oc < 32 else v_tm
                for ci in range(nci_):
                    pt = ps_t[ci % 2]
                    ttile = ttile_[ci % 2]
                    p.exec("tensor", lambda e, ci=ci, pt=pt, ebuf=ebuf: e.transpose(
                        pt[:], ebuf[:, ci * 128:(ci + 1) * 128], idb[:]),
                        reads=[ebuf, idb], writes=[pt])
                    p.exec("vector", lambda e, pt=pt, ttile=ttile: e.tensor_copy(ttile[:], pt[:]),
                           reads=[pt], writes=[ttile])
                    r0 = ts * TS + ci * 128
                    p.dma("sync", tmout[r0:r0 + 128, (oc % 16) * 128:(oc % 16 + 1) * 128],
                          ttile[:], reads=[ttile])
    p.prefix = _oldpfx
    if standalone:
        p.fence()
        return p.emit_and_finalize()
    return None


def build_attn1(T, p=None, io=None):
    standalone = p is None
    if standalone:
        p = Prog("attn1")
    _oldpfx = p.prefix
    if not standalone:
        p.prefix = "attn1_" + p.prefix
    def _inp(name, shape, dtype):
        if io is not None and name in io:
            return io[name]
        return p.inp(name, shape, dtype)
    def _outp(name, shape, dtype):
        if io is not None and name in io:
            return io[name]
        return p.outp(name, shape, dtype)
    q_fm = _inp("q1_fm", [E_DIM, T], BF16)
    k_fm = _inp("k1_fm", [E_DIM, T], BF16)
    k_tm = _inp("k1_tm", [T, E_DIM], BF16)
    v_tm = _inp("v1_tm", [T, E_DIM], BF16)
    Wout = _inp("Wout", [E_DIM, E_DIM], F32)
    bout_row = _inp("bout_row", [1, E_DIM], F32)
    slopes_row = _inp("slopes_row", [1, H], F32)
    diffT = _inp("diffT64", [64, 64], F32)
    negiota = _inp("negiota64", [64, H], F32)
    negrev = _inp("negrev64", [64, H], F32)
    ones128f = _inp("ones128f", [1, 128], F32)
    onesbf = _inp("onesbf", [1, 128], BF16)
    identbf = _inp("identbf", [128, 128], BF16)
    attn_out = _outp("attn1_tm", [T, E_DIM], F32)
    BLK = 64
    nblk = T // BLK
    nech = E_DIM // 128

    idb = p.sb("idb", [128, 128], BF16)
    onb = p.sb("onb", [1, 128], BF16)
    onf = p.sb("onf", [1, 128], F32)
    slp = p.sb("slp", [1, H], F32)
    dfT = p.sb("dfT", [64, 64], F32)
    nio = p.sb("nio", [64, H], F32)
    nrv = p.sb("nrv", [64, H], F32)
    boutb = p.sb("boutb", [1, E_DIM], BF16)
    wsb = p.sb("wsb", [128, nech, E_DIM], BF16)
    for t, src in ((idb, identbf), (onb, onesbf), (onf, ones128f), (slp, slopes_row),
                   (dfT, diffT), (nio, negiota), (nrv, negrev)):
        p.dma("sync", t[:], src, writes=[t])
    p.dma("gpsimd", boutb[:], bout_row, writes=[boutb])
    load_weight_bf16(p, wsb, Wout, E_DIM, E_DIM)

    srep = p.sb("srep", [128, H], F32)
    nsrep = p.sb("nsrep", [128, H], F32)
    qd1 = p.sb("qd1", [64, H], F32)
    kd1 = p.sb("kd1", [64, H], F32)
    bd1 = p.sb("bd1", [128, H], F32)
    dT = p.sb("dT", [64, H, 64], F32)
    tmp16 = p.sb("tmp16", [128, H], F32)
    with p.scope():
        ps_r = p.ps("ps_r", [128, H])
        p.exec("tensor", lambda e: e.matmul(ps_r[:], onf[:], slp[:], start=True, stop=True),
               reads=[onf, slp], writes=[ps_r])
        p.exec("vector", lambda e: e.tensor_copy(srep[:], ps_r[:]), reads=[ps_r], writes=[srep])
    p.exec("vector", lambda e: e.tensor_scalar_mul(nsrep[:], srep[:], -1.0),
           reads=[srep], writes=[nsrep])
    p.exec("vector", lambda e: e.tensor_mul(tmp16[0:64, :], nio[:], srep[0:64, :]),
           reads=[nio, srep], writes=[tmp16])
    p.exec("scalar", lambda e: e.activation(qd1[:], tmp16[0:64, :], AF.Exp),
           reads=[tmp16], writes=[qd1])
    p.exec("vector", lambda e: e.tensor_mul(tmp16[0:64, :], nrv[:], srep[0:64, :]),
           reads=[nio, srep, qd1], writes=[tmp16])
    p.exec("scalar", lambda e: e.activation(kd1[:], tmp16[0:64, :], AF.Exp),
           reads=[tmp16], writes=[kd1])
    p.exec("vector", lambda e: e.tensor_scalar_mul(tmp16[:], srep[:], -64.0),
           reads=[srep, kd1], writes=[tmp16])
    p.exec("scalar", lambda e: e.activation(bd1[:], tmp16[:], AF.Exp),
           reads=[tmp16], writes=[bd1])
    for h in range(H):
        p.exec("scalar", lambda e, h=h: e.activation(
            dT[:, h, :], dfT[:], AF.Exp, scale=nsrep[0:64, h:h + 1]),
            reads=[dfT, nsrep], writes=[dT])

    qre = q_fm.rearrange("(h d) t -> d h t", d=128)
    kre = k_fm.rearrange("(h d) t -> d h t", d=128)

    ktm = [p.sb(f"ktm{i}", [64, E_DIM], BF16) for i in range(2)]
    vtm = [p.sb(f"vtm{i}", [64, E_DIM], BF16) for i in range(2)]
    qfm = [p.sb(f"qfm{i}", [128, H, 64], BF16) for i in range(2)]
    kfm = [p.sb(f"kfm{i}", [128, H, 64], BF16) for i in range(2)]
    kv32 = [[p.sb(f"kv32_{h}_{i}", [128, 128], F32) for i in range(2)] for h in range(H)]
    kvbf = [p.sb(f"kvbf_{h}", [128, 128], BF16) for h in range(H)]
    kdt = [p.sb(f"kdt{i}", [64, 128], BF16) for i in range(2)]
    smb = [p.sb(f"smb{i}", [64, 64], BF16) for i in range(2)]
    tmpo = [p.sb(f"tmpo{i}", [64, 128], F32) for i in range(2)]
    attn_b = p.sb("attn_b", [64, E_DIM], F32)
    anb = p.sb("anb", [64, E_DIM], BF16)
    afm = p.sb("afm", [128, nech, 64], BF16)
    outc = p.sb("outc", [64, E_DIM], F32)

    ps_st = p.ps("ps_st", [64, 64])
    ps_o = p.ps("ps_o", [64, 128])
    ps_i = p.ps("ps_i", [64, 128])
    ps_kv = p.ps("ps_kv", [128, 128])
    pstr = [p.ps(f"pstr{i}", [128, 64], BF16) for i in range(2)]
    ps_w = [p.ps(f"ps_w{i}", [64, 512]) for i in range(2)]

    for h in range(H):
        p.exec("vector", lambda e, h=h: e.memset(kv32[h][0][:], 0.0), writes=[kv32[h][0]])
        p.exec("gpsimd", lambda e, h=h: e.memset(kvbf[h][:], 0.0), writes=[kvbf[h]])

    for c in range(nblk):
        b = c % 2
        kt, vt, qf, kf = ktm[b], vtm[b], qfm[b], kfm[b]
        p.dma("sync", kt[:], k_tm[c * 64:(c + 1) * 64, :], writes=[kt])
        p.dma("sync", vt[:], v_tm[c * 64:(c + 1) * 64, :], writes=[vt])
        for h in range(H):
            p.dma("sync", qf[:, h, :], qre[:, h, c * 64:(c + 1) * 64], writes=[qf])
            p.dma("sync", kf[:, h, :], kre[:, h, c * 64:(c + 1) * 64], writes=[kf])
        for h in range(H):
            hb = h % 2
            kvp = kv32[h][c % 2]
            kvn = kv32[h][(c + 1) % 2]
            p.exec("vector", lambda e, h=h, hb=hb, kt=kt: e.tensor_scalar_mul(
                kdt[hb][:], kt[:, h * 128:(h + 1) * 128], kd1[:, h:h + 1]),
                reads=[kt, kd1], writes=[kdt[hb]])
            p.exec("tensor", lambda e, h=h, qf=qf, kf=kf: e.matmul(
                ps_st[:], kf[:, h, :], qf[:, h, :], start=True, stop=True),
                reads=[kf, qf], writes=[ps_st])
            p.exec("vector", lambda e, h=h, hb=hb: e.tensor_tensor(
                smb[hb][:], ps_st[:], dT[:, h, :], ALU.mult),
                reads=[ps_st, dT], writes=[smb[hb]])
            p.exec("tensor", lambda e, h=h, hb=hb, vt=vt: e.matmul(
                ps_o[:], smb[hb][:], vt[:, h * 128:(h + 1) * 128], start=True, stop=True),
                reads=[smb[hb], vt], writes=[ps_o])
            p.exec("tensor", lambda e, h=h, qf=qf: e.matmul(
                ps_i[:], qf[:, h, :], kvbf[h][:], start=True, stop=True),
                reads=[qf, kvbf[h]], writes=[ps_i])
            p.exec("scalar", lambda e, hb=hb: e.copy(tmpo[hb][:], ps_o[:]),
                   reads=[ps_o], writes=[tmpo[hb]])
            p.exec("vector", lambda e, h=h, hb=hb: e.scalar_tensor_tensor(
                attn_b[:, h * 128:(h + 1) * 128], ps_i[:], qd1[:, h:h + 1], tmpo[hb][:],
                ALU.mult, ALU.add),
                reads=[ps_i, qd1, tmpo[hb]], writes=[attn_b])
            p.exec("tensor", lambda e, h=h, hb=hb, vt=vt: e.matmul(
                ps_kv[:], kdt[hb][:], vt[:, h * 128:(h + 1) * 128], start=True, stop=True),
                reads=[kdt[hb], vt], writes=[ps_kv])
            p.exec("vector", lambda e, h=h, kvp=kvp, kvn=kvn: e.scalar_tensor_tensor(
                kvn[:], kvp[:], bd1[:, h:h + 1], ps_kv[:], ALU.mult, ALU.add),
                reads=[kvp, bd1, ps_kv], writes=[kvn])
            p.exec("scalar", lambda e, h=h, kvn=kvn: e.copy(kvbf[h][:], kvn[:]),
                   reads=[kvn], writes=[kvbf[h]])
        p.exec("vector", lambda e: e.tensor_copy(anb[:], attn_b[:]),
               reads=[attn_b], writes=[anb])
        for ec in range(nech):
            pb = pstr[ec % 2]
            p.exec("tensor", lambda e, ec=ec, pb=pb: e.transpose(
                pb[:], anb[:, ec * 128:(ec + 1) * 128], idb[0:64, 0:64]),
                reads=[anb, idb], writes=[pb])
            p.exec("vector", lambda e, ec=ec, pb=pb: e.tensor_copy(afm[:, ec, :], pb[:]),
                   reads=[pb], writes=[afm])
        for sl in range(4):
            pw = ps_w[sl % 2]
            for e_ in range(nech):
                p.exec("tensor", lambda en, sl=sl, e_=e_, pw=pw: en.matmul(
                    pw[:], afm[:, e_, :], wsb[:, e_, sl * 512:(sl + 1) * 512],
                    start=(e_ == 0), stop=False),
                    reads=[afm, wsb], writes=[pw])
            p.exec("tensor", lambda en, sl=sl, pw=pw: en.matmul(
                pw[:], onb[:, 0:64], boutb[:, sl * 512:(sl + 1) * 512],
                start=False, stop=True),
                reads=[onb, boutb], writes=[pw])
            p.exec("scalar", lambda en, sl=sl, pw=pw: en.copy(
                outc[:, sl * 512:(sl + 1) * 512], pw[:]), reads=[pw], writes=[outc])
        p.dma("gpsimd", attn_out[c * 64:(c + 1) * 64, :], outc[:], reads=[outc])
    p.prefix = _oldpfx
    if standalone:
        p.fence()
        return p.emit_and_finalize()
    return None


def build_ffn0(T, p=None, io=None):
    standalone = p is None
    if standalone:
        p = Prog("ffn0")
    _oldpfx = p.prefix
    if not standalone:
        p.prefix = "ffn0_" + p.prefix
    def _inp(name, shape, dtype):
        if io is not None and name in io:
            return io[name]
        return p.inp(name, shape, dtype)
    def _outp(name, shape, dtype):
        if io is not None and name in io:
            return io[name]
        return p.outp(name, shape, dtype)
    xn_fm = _inp("xn_fm", [E_DIM, T], BF16)
    W_upT = _inp("W_upT", [E_DIM, INTER], F32)
    W_gateT = _inp("W_gateT", [E_DIM, INTER], F32)
    W_downT = _inp("W_downT", [INTER, E_DIM], F32)
    su_ug = _inp("su_ug_pm", [128, 16], F32)
    su_dp = _inp("su_dp_pm", [128, 64], F32)
    wsc_up = _inp("wsc_up_pm", [128, 64], F32)
    wsc_gate = _inp("wsc_gate_pm", [128, 64], F32)
    wsc_dp = _inp("wsc_dp_pm", [128, 16], F32)
    h128in = _inp("h128bf", [128, 128], BF16)
    identf = _inp("identf32", [128, 128], F32)
    ffn_out = _outp("ffn0_tm", [T, E_DIM], F32)

    u_dram = p.dram("u_scr", [INTER, T], BF16)
    w_dram = p.dram("w_scr", [INTER, T], BF16)

    nech = E_DIM // 128
    nfch = INTER // 128

    h128sb = p.sb("h128sb", [128, 128], BF16)
    idf = p.sb("idf", [128, 128], F32)
    sug = p.sb("sug", [128, 16], F32)
    sud = p.sb("sud", [128, 64], F32)
    wup_s = p.sb("wup_s", [128, 64], F32)
    wgt_s = p.sb("wgt_s", [128, 64], F32)
    wdp_s = p.sb("wdp_s", [128, 16], F32)
    upscsu = p.sb("upscsu", [128, 64], F32)
    wdps = p.sb("wdps", [128, 16], F32)
    for t, src in ((h128sb, h128in), (idf, identf), (sug, su_ug), (sud, su_dp),
                   (wup_s, wsc_up), (wgt_s, wsc_gate), (wdp_s, wsc_dp)):
        p.dma("sync", t[:], src, writes=[t])
    p.exec("vector", lambda e: e.tensor_mul(upscsu[:], wup_s[:], sud[:]),
           reads=[wup_s, sud], writes=[upscsu])
    p.exec("vector", lambda e: e.tensor_scalar_mul(wdps[:], wdp_s[:], float(1.0 / np.sqrt(INTER))),
           reads=[wdp_s], writes=[wdps])

    xre = xn_fm.rearrange("(ec pp) t -> pp ec t", pp=128)
    wupre = W_upT.rearrange("(ec pp) f -> pp ec f", pp=128)
    wgtre = W_gateT.rearrange("(ec pp) f -> pp ec f", pp=128)
    wdnre = W_downT.rearrange("(fc pp) g -> pp fc g", pp=128)
    u_re = u_dram.rearrange("(fc pp) t -> pp fc t", pp=128)
    w_re = w_dram.rearrange("(fc pp) t -> pp fc t", pp=128)

    ab_scope = p.scope()
    ab_scope.__enter__()
    ybf = p.sb("ybf", [128, nech, T], BF16)
    with p.scope():
        suh = p.sb("suh", [128, 16, 128], BF16)
        for ec in range(nech):
            p.exec("vector", lambda e, ec=ec: e.tensor_scalar_mul(
                suh[:, ec, :], h128sb[:], sug[:, ec:ec + 1]),
                reads=[h128sb, sug], writes=[suh])
        TS = min(512, T)
        nts = T // TS
        xnb = [p.sb(f"xnb{i}", [128, nech, TS], BF16) for i in range(2)]
        yf = p.sb("yf", [128, nech, TS], F32)
        yg = p.sb("yg", [128, nech, TS], F32)
        ps_a = [p.ps(f"ps_a{i}", [128, TS]) for i in range(2)]
        for tsl in range(nts):
            xb = xnb[tsl % 2]
            for e0 in range(0, nech, 8):
                p.dma("sync", xb[:, e0:e0 + 8, :], xre[:, e0:e0 + 8, tsl * TS:(tsl + 1) * TS],
                      writes=[xb])
            for ec in range(nech):
                pa = ps_a[ec % 2]
                p.exec("tensor", lambda e, ec=ec, xb=xb, pa=pa: e.matmul(
                    pa[:], suh[:, ec, :], xb[:, ec, :], start=True, stop=True),
                    reads=[suh, xb], writes=[pa])
                p.exec("vector", lambda e, ec=ec, pa=pa: e.tensor_scalar_mul(
                    yf[:, ec, :], pa[:], float(1.0 / np.sqrt(E_DIM))),
                    reads=[pa], writes=[yf, f"{_key_of(yf)}@{ec}"])
            _butterfly(p, nech, yf, yg, final_out=ybf,
                       final_slice=(tsl * TS, (tsl + 1) * TS))

    with p.scope():
        TS = min(512, T)
        nts = T // TS
        wut = [p.sb(f"wut{i}", [128, nech, 128], BF16) for i in range(2)]
        wgt = [p.sb(f"wgt{i}", [128, nech, 128], BF16) for i in range(2)]
        sg = p.sb("sg", [128, TS], F32)
        ut = p.sb("ut", [128, TS], BF16)
        ps_up = p.ps("ps_up", [128, TS])
        ps_gt = p.ps("ps_gt", [128, TS])
        for fi in range(nfch):
            wu, wg = wut[fi % 2], wgt[fi % 2]
            for e0 in range(0, nech, 8):
                p.dma("gpsimd", wu[:, e0:e0 + 8, :],
                      wupre[:, e0:e0 + 8, fi * 128:(fi + 1) * 128], writes=[wu])
                p.dma("gpsimd", wg[:, e0:e0 + 8, :],
                      wgtre[:, e0:e0 + 8, fi * 128:(fi + 1) * 128], writes=[wg])
            for tsl in range(nts):
                for ec in range(nech):
                    p.exec("tensor", lambda e, ec=ec, tsl=tsl, wu=wu: e.matmul(
                        ps_up[:], wu[:, ec, :], ybf[:, ec, tsl * TS:(tsl + 1) * TS],
                        start=(ec == 0), stop=(ec == nech - 1)),
                        reads=[wu, f"{_key_of(ybf)}@{ec}"], writes=[ps_up])
                    p.exec("tensor", lambda e, ec=ec, tsl=tsl, wg=wg: e.matmul(
                        ps_gt[:], wg[:, ec, :], ybf[:, ec, tsl * TS:(tsl + 1) * TS],
                        start=(ec == 0), stop=(ec == nech - 1)),
                        reads=[wg, f"{_key_of(ybf)}@{ec}"], writes=[ps_gt])
                p.exec("scalar", lambda e, fi=fi: e.activation(
                    sg[:], ps_gt[:], AF.Silu, scale=wgt_s[:, fi:fi + 1]),
                    reads=[ps_gt, wgt_s], writes=[sg])
                p.exec("vector", lambda e, fi=fi: e.scalar_tensor_tensor(
                    ut[:], ps_up[:], upscsu[:, fi:fi + 1], sg[:], ALU.mult, ALU.mult),
                    reads=[ps_up, upscsu, sg], writes=[ut])
                p.dma("sync", u_re[:, fi, tsl * TS:(tsl + 1) * TS], ut[:], reads=[ut])
    ab_scope.__exit__(None, None, None)

    with p.scope():
        TC = min(256, T)
        ntc = T // TC
        ub = p.sb("ub", [128, nfch, TC], BF16)
        vb = p.sb("vb", [128, nfch, TC], F32)
        wt = p.sb("wt", [128, nfch, TC], BF16)
        ps_c = [p.ps(f"ps_c{i}", [128, TC]) for i in range(2)]
        kub, kvb, kwt = _key_of(ub), _key_of(vb), _key_of(wt)
        for tc in range(ntc):
            for f0 in range(0, nfch, 8):
                p.dma("sync", ub[:, f0:f0 + 8, :], u_re[:, f0:f0 + 8, tc * TC:(tc + 1) * TC],
                      writes=[ub] + [f"{kub}@{fc}" for fc in range(f0, f0 + 8)])
            for fc in range(nfch):
                pc = ps_c[fc % 2]
                p.exec("tensor", lambda e, fc=fc, pc=pc: e.matmul(
                    pc[:], h128sb[:], ub[:, fc, :], start=True, stop=True),
                    reads=[h128sb, f"{kub}@{fc}"], writes=[pc])
                p.exec("vector", lambda e, fc=fc, pc=pc: e.tensor_copy(vb[:, fc, :], pc[:]),
                       reads=[pc], writes=[vb, f"{kvb}@{fc}"])
            _butterfly(p, nfch, vb, ub, final_out=wt, final_slice=(0, TC))
            for f0 in range(0, nfch, 8):
                p.dma("gpsimd", w_re[:, f0:f0 + 8, tc * TC:(tc + 1) * TC], wt[:, f0:f0 + 8, :],
                      reads=[wt] + [f"{kwt}@{fc}" for fc in range(f0, f0 + 8)])

    with p.scope():
        TQ = min(1024, T)
        ntq = T // TQ
        nci = TQ // 128
        HW_ = min(512, TQ)
        nhalf = TQ // HW_
        wq = p.sb("wq", [128, nfch, TQ], BF16)
        wdt = [p.sb(f"wdt{i}", [128, 8, 128], BF16) for i in range(4)]
        dtile = [p.sb(f"dtile{i}", [128, TQ], F32) for i in range(2)]
        otile = [p.sb(f"otile{i}", [128, 128], F32) for i in range(4)]
        ps_d = [p.ps(f"ps_d{i}", [128, HW_]) for i in range(4)]
        ps_t = [p.ps(f"ps_t{i}", [128, 128]) for i in range(2)]
        kwq = _key_of(wq)
        nbat = 0
        for tq in range(ntq):
            for f0 in range(0, nfch, 8):
                p.dma("sync", wq[:, f0:f0 + 8, :], w_re[:, f0:f0 + 8, tq * TQ:(tq + 1) * TQ],
                      writes=[wq] + [f"{kwq}@{fc}" for fc in range(f0, f0 + 8)])
            for g in range(nech):
                pd = [ps_d[(g % 2) * 2], ps_d[(g % 2) * 2 + 1]]
                dt = dtile[g % 2]
                for f0 in range(0, nfch, 8):
                    wd = wdt[nbat % 4]
                    nbat += 1
                    p.dma("gpsimd", wd[:], wdnre[:, f0:f0 + 8, g * 128:(g + 1) * 128],
                          writes=[wd])
                    for k in range(8):
                        fc = f0 + k
                        for half in range(nhalf):
                            p.exec("tensor", lambda e, fc=fc, k=k, wd=wd, ph=pd[half], half=half: e.matmul(
                                ph[:], wd[:, k, :], wq[:, fc, half * HW_:(half + 1) * HW_],
                                start=(fc == 0), stop=(fc == nfch - 1)),
                                reads=[wd, f"{kwq}@{fc}"], writes=[pd[half]])
                for half in range(nhalf):
                    p.exec("vector", lambda e, g=g, ph=pd[half], dt=dt, half=half: e.tensor_scalar_mul(
                        dt[:, half * HW_:(half + 1) * HW_], ph[:], wdps[:, g:g + 1]),
                        reads=[pd[half], wdps], writes=[dt])
                for ci in range(nci):
                    pt = ps_t[ci % 2]
                    ot = otile[ci % 4]
                    p.exec("tensor", lambda e, ci=ci, pt=pt, dt=dt: e.transpose(
                        pt[:], dt[:, ci * 128:(ci + 1) * 128], idf[:]),
                        reads=[dt, idf], writes=[pt])
                    p.exec("vector", lambda e, ci=ci, pt=pt, ot=ot: e.tensor_copy(
                        ot[:], pt[:]), reads=[pt], writes=[ot])
                    p.dma("gpsimd", ffn_out[tq * TQ + ci * 128: tq * TQ + (ci + 1) * 128,
                                            g * 128:(g + 1) * 128],
                          ot[:], reads=[ot])
    p.prefix = _oldpfx
    if standalone:
        p.fence()
        return p.emit_and_finalize()
    return None


def build_ffn1(T, p=None, io=None):
    standalone = p is None
    if standalone:
        p = Prog("ffn1")
    _oldpfx = p.prefix
    if not standalone:
        p.prefix = "ffn1_" + p.prefix
    def _inp(name, shape, dtype):
        if io is not None and name in io:
            return io[name]
        return p.inp(name, shape, dtype)
    def _outp(name, shape, dtype):
        if io is not None and name in io:
            return io[name]
        return p.outp(name, shape, dtype)
    xn_fm = _inp("xn_fm", [E_DIM, T], BF16)
    w_gate = _inp("w_gate_p", [E_DIM, INTER], F32)
    w_up = _inp("w_up_p", [E_DIM, INTER], F32)
    w_down = _inp("w_down_p", [INTER, E_DIM], F32)
    identf = _inp("identf32", [128, 128], F32)
    ffn_out = _outp("ffn1_tm", [T, E_DIM], F32)

    h_dram = p.dram("h_scr", [INTER, T], F32)
    hs_dram = p.dram("hs_scr", [INTER, T], BF16)
    nech = E_DIM // 128
    nfch = INTER // 128

    idf = p.sb("idf", [128, 128], F32)
    p.dma("sync", idf[:], identf, writes=[idf])
    xre = xn_fm.rearrange("(ec pp) t -> pp ec t", pp=128)
    wgre = w_gate.rearrange("(ec pp) f -> pp ec f", pp=128)
    wure = w_up.rearrange("(ec pp) f -> pp ec f", pp=128)
    wdre = w_down.rearrange("(fc pp) g -> pp fc g", pp=128)
    h_re = h_dram.rearrange("(fc pp) t -> pp fc t", pp=128)
    hs_re = hs_dram.rearrange("(fc pp) t -> pp fc t", pp=128)

    with p.scope():
        TS = min(512, T)
        nts = T // TS
        xnsb = p.sb("xnsb", [128, nech, T], BF16)
        for e0 in range(0, nech, 4):
            p.dma("sync", xnsb[:, e0:e0 + 4, :], xre[:, e0:e0 + 4, :], writes=[xnsb])
        wut = [p.sb(f"wut{i}", [128, nech, 128], BF16) for i in range(2)]
        wgt = [p.sb(f"wgt{i}", [128, nech, 128], BF16) for i in range(2)]
        sg = p.sb("sg", [128, TS], F32)
        ht = p.sb("ht", [128, TS], F32)
        ps_up = p.ps("ps_up", [128, TS])
        ps_gt = p.ps("ps_gt", [128, TS])
        for fi in range(nfch):
            wu, wg = wut[fi % 2], wgt[fi % 2]
            for e0 in range(0, nech, 8):
                p.dma("gpsimd", wu[:, e0:e0 + 8, :],
                      wure[:, e0:e0 + 8, fi * 128:(fi + 1) * 128], writes=[wu])
                p.dma("gpsimd", wg[:, e0:e0 + 8, :],
                      wgre[:, e0:e0 + 8, fi * 128:(fi + 1) * 128], writes=[wg])
            for tsl in range(nts):
                for ec in range(nech):
                    p.exec("tensor", lambda e, ec=ec, tsl=tsl, wu=wu, TS=TS: e.matmul(
                        ps_up[:], wu[:, ec, :], xnsb[:, ec, tsl * TS:(tsl + 1) * TS],
                        start=(ec == 0), stop=(ec == nech - 1)),
                        reads=[wu, xnsb], writes=[ps_up])
                    p.exec("tensor", lambda e, ec=ec, tsl=tsl, wg=wg, TS=TS: e.matmul(
                        ps_gt[:], wg[:, ec, :], xnsb[:, ec, tsl * TS:(tsl + 1) * TS],
                        start=(ec == 0), stop=(ec == nech - 1)),
                        reads=[wg, xnsb], writes=[ps_gt])
                p.exec("scalar", lambda e: e.activation(sg[:], ps_gt[:], AF.Silu),
                       reads=[ps_gt], writes=[sg])
                p.exec("vector", lambda e: e.tensor_mul(ht[:], ps_up[:], sg[:]),
                       reads=[ps_up, sg], writes=[ht])
                p.dma("sync", h_re[:, fi, tsl * TS:(tsl + 1) * TS], ht[:], reads=[ht])

    with p.scope():
        TS = min(512, T)
        nts = T // TS
        h4_ = [p.sb(f"h4_{i}", [128, 4, TS], F32) for i in range(2)]
        a4_ = [p.sb(f"a4_{i}", [128, 4, TS], F32) for i in range(2)]
        b6_ = [p.sb(f"b6_{i}", [128, 6, TS], F32) for i in range(2)]
        t2_ = [p.sb(f"t2_{i}", [128, 2, TS], F32) for i in range(2)]
        hs4_ = [p.sb(f"hs4_{i}", [128, 4, TS], BF16) for i in range(2)]
        pairs = [(0, 1), (0, 2), (0, 3), (1, 2), (1, 3), (2, 3)]

        def _sparsify_group(h4, a4, b6, t2, hs4, fcs, tsl):
            for m, fc in enumerate(fcs):
                p.dma("sync", h4[:, m, :], h_re[:, fc, tsl * TS:(tsl + 1) * TS],
                      writes=[h4])
            for m in range(4):
                p.exec("scalar", lambda e, m=m: e.activation(
                    a4[:, m, :], h4[:, m, :], AF.Abs), reads=[h4], writes=[a4])
            for bi, (i, j) in enumerate(pairs):
                p.exec("vector", lambda e, bi=bi, i=i, j=j: e.tensor_tensor(
                    b6[:, bi, :], a4[:, i, :], a4[:, j, :], ALU.is_ge),
                    reads=[a4], writes=[b6])
            p.exec("vector", lambda e: e.tensor_add(t2[:, 0, :], b6[:, 0, :], b6[:, 1, :]),
                   reads=[b6], writes=[t2])
            p.exec("vector", lambda e: e.tensor_add(t2[:, 0, :], t2[:, 0, :], b6[:, 2, :]),
                   reads=[b6, t2], writes=[t2])
            p.exec("vector", lambda e: e.scalar_tensor_tensor(
                hs4[:, 0, :], t2[:, 0, :], 1.5, h4[:, 0, :], ALU.is_ge, ALU.mult),
                reads=[t2, h4], writes=[hs4])
            p.exec("vector", lambda e: e.tensor_add(t2[:, 1, :], b6[:, 3, :], b6[:, 4, :]),
                   reads=[b6], writes=[t2])
            p.exec("vector", lambda e: e.tensor_sub(t2[:, 1, :], t2[:, 1, :], b6[:, 0, :]),
                   reads=[b6, t2], writes=[t2])
            p.exec("vector", lambda e: e.scalar_tensor_tensor(
                hs4[:, 1, :], t2[:, 1, :], 0.5, h4[:, 1, :], ALU.is_ge, ALU.mult),
                reads=[t2, h4], writes=[hs4])
            p.exec("vector", lambda e: e.tensor_sub(t2[:, 0, :], b6[:, 5, :], b6[:, 1, :]),
                   reads=[b6], writes=[t2])
            p.exec("vector", lambda e: e.tensor_sub(t2[:, 0, :], t2[:, 0, :], b6[:, 3, :]),
                   reads=[b6, t2], writes=[t2])
            p.exec("vector", lambda e: e.scalar_tensor_tensor(
                hs4[:, 2, :], t2[:, 0, :], -0.5, h4[:, 2, :], ALU.is_ge, ALU.mult),
                reads=[t2, h4], writes=[hs4])
            p.exec("vector", lambda e: e.tensor_add(t2[:, 1, :], b6[:, 2, :], b6[:, 4, :]),
                   reads=[b6], writes=[t2])
            p.exec("vector", lambda e: e.tensor_add(t2[:, 1, :], t2[:, 1, :], b6[:, 5, :]),
                   reads=[b6, t2], writes=[t2])
            p.exec("vector", lambda e: e.scalar_tensor_tensor(
                hs4[:, 3, :], t2[:, 1, :], 1.5, h4[:, 3, :], ALU.is_le, ALU.mult),
                reads=[t2, h4], writes=[hs4])
            for m, fc in enumerate(fcs):
                p.dma("gpsimd", hs_re[:, fc, tsl * TS:(tsl + 1) * TS], hs4[:, m, :],
                      reads=[hs4])

        for tsl in range(nts):
            for grp in range(16):
                par = (tsl * 16 + grp) % 2
                _sparsify_group(h4_[par], a4_[par], b6_[par], t2_[par], hs4_[par],
                                [grp, grp + 16, grp + 32, grp + 48], tsl)

    with p.scope():
        TQ = min(1024, T)
        ntq = T // TQ
        nci = TQ // 128
        HW_ = min(512, TQ)
        nhalf = TQ // HW_
        wq = p.sb("wq", [128, nfch, TQ], BF16)
        wdt = [p.sb(f"wdt{i}", [128, 8, 128], BF16) for i in range(4)]
        dtile = [p.sb(f"dtile{i}", [128, TQ], F32) for i in range(2)]
        otile = [p.sb(f"otile{i}", [128, 128], F32) for i in range(4)]
        ps_d = [p.ps(f"ps_d{i}", [128, HW_]) for i in range(4)]
        ps_t = [p.ps(f"ps_t{i}", [128, 128]) for i in range(2)]
        kwq = _key_of(wq)
        nbat = 0
        for tq in range(ntq):
            for f0 in range(0, nfch, 8):
                p.dma("sync", wq[:, f0:f0 + 8, :], hs_re[:, f0:f0 + 8, tq * TQ:(tq + 1) * TQ],
                      writes=[wq] + [f"{kwq}@{fc}" for fc in range(f0, f0 + 8)])
            for g in range(nech):
                pd = [ps_d[(g % 2) * 2], ps_d[(g % 2) * 2 + 1]]
                dt = dtile[g % 2]
                for f0 in range(0, nfch, 8):
                    wd = wdt[nbat % 4]
                    nbat += 1
                    p.dma("gpsimd", wd[:], wdre[:, f0:f0 + 8, g * 128:(g + 1) * 128],
                          writes=[wd])
                    for k in range(8):
                        fc = f0 + k
                        for half in range(nhalf):
                            p.exec("tensor", lambda e, fc=fc, k=k, wd=wd, ph=pd[half]: e.matmul(
                                ph[:], wd[:, k, :], wq[:, fc, half * HW_:(half + 1) * HW_],
                                start=(fc == 0), stop=(fc == nfch - 1)),
                                reads=[wd, f"{kwq}@{fc}"], writes=[pd[half]])
                for half in range(nhalf):
                    p.exec("vector", lambda e, ph=pd[half], dt=dt, half=half: e.tensor_copy(
                        dt[:, half * HW_:(half + 1) * HW_], ph[:]),
                        reads=[pd[half]], writes=[dt])
                for ci in range(nci):
                    pt = ps_t[ci % 2]
                    ot = otile[ci % 4]
                    p.exec("tensor", lambda e, ci=ci, pt=pt, dt=dt: e.transpose(
                        pt[:], dt[:, ci * 128:(ci + 1) * 128], idf[:]),
                        reads=[dt, idf], writes=[pt])
                    p.exec("vector", lambda e, ci=ci, pt=pt, ot=ot: e.tensor_copy(
                        ot[:], pt[:]), reads=[pt], writes=[ot])
                    p.dma("gpsimd", ffn_out[tq * TQ + ci * 128: tq * TQ + (ci + 1) * 128,
                                            g * 128:(g + 1) * 128],
                          ot[:], reads=[ot])
    p.prefix = _oldpfx
    if standalone:
        p.fence()
        return p.emit_and_finalize()
    return None


def build_resln(T, with_ln=True, nadd=2, p=None, io=None):
    standalone = p is None
    if standalone:
        p = Prog("resln")
    _oldpfx = p.prefix
    if not standalone:
        p.prefix = "resln_" + p.prefix
    def _inp(name, shape, dtype):
        if io is not None and name in io:
            return io[name]
        return p.inp(name, shape, dtype)
    def _outp(name, shape, dtype):
        if io is not None and name in io:
            return io[name]
        return p.outp(name, shape, dtype)
    x = _inp("x", [T, E_DIM], F32)
    adds = [_inp(f"add{i}", [T, E_DIM], F32) for i in range(nadd)]
    x1_tm = _outp("x1_tm", [T, E_DIM], F32)
    if with_ln:
        identbf = _inp("identbf", [128, 128], BF16)
        constpm = _inp("constpm", [128, 2], F32)
        xn_fm = _outp("xn_fm", [E_DIM, T], BF16)
        idb = p.sb("idb", [128, 128], BF16)
        cp = p.sb("cp", [128, 2], F32)
        p.dma("sync", idb[:], identbf, writes=[idb])
        p.dma("sync", cp[:], constpm, writes=[cp])
        xn_re = xn_fm.rearrange("(ec pp) t -> ec pp t", pp=128)
    nch = T // 128
    nech = E_DIM // 128

    xin = [p.sb(f"xin{i}", [128, E_DIM], F32) for i in range(2)]
    ain = [p.sb(f"ain{i}", [128, E_DIM], F32) for i in range(2)]
    bin_ = [p.sb(f"bin{i}", [128, E_DIM], F32) for i in range(2)]
    x1b = [p.sb(f"x1b{i}", [128, E_DIM], F32) for i in range(2)]
    if with_ln:
        xsq = p.sb("xsq", [128, E_DIM], F32)
        stats = p.sb("stats", [128, 8], F32)
        xn_tm = p.sb("xn_tm", [128, E_DIM], BF16)
        tbuf = [p.sb(f"tbuf{i}", [128, nech, 128], BF16) for i in range(2)]
        pstr = [p.ps(f"pstr{i}", [128, 128], BF16) for i in range(2)]

    for c in range(nch):
        b = c % 2
        xb, ab, bb, x1 = xin[b], ain[b], bin_[b], x1b[b]
        p.dma("sync", xb[:], x[c * 128:(c + 1) * 128, :], writes=[xb])
        p.dma("sync", ab[:], adds[0][c * 128:(c + 1) * 128, :], writes=[ab])
        if nadd > 1:
            p.dma("sync", bb[:], adds[1][c * 128:(c + 1) * 128, :], writes=[bb])
        p.exec("vector", lambda e, xb=xb, ab=ab, x1=x1: e.tensor_add(x1[:], xb[:], ab[:]),
               reads=[xb, ab], writes=[x1])
        if nadd > 1:
            p.exec("vector", lambda e, bb=bb, x1=x1: e.tensor_add(x1[:], x1[:], bb[:]),
                   reads=[bb, x1], writes=[x1])
        p.dma("gpsimd", x1_tm[c * 128:(c + 1) * 128, :], x1[:], reads=[x1])
        if not with_ln:
            continue
        p.exec("scalar", lambda e, x1=x1: e.square(xsq[:], x1[:]), reads=[x1], writes=[xsq])
        p.exec("vector", lambda e, x1=x1: e.reduce_sum(stats[:, 0:1], x1[:], axis=mybir.AxisListType.X),
               reads=[x1], writes=[stats])
        p.exec("vector", lambda e: e.reduce_sum(stats[:, 1:2], xsq[:], axis=mybir.AxisListType.X),
               reads=[xsq], writes=[stats])
        p.exec("vector", lambda e: e.tensor_scalar_mul(stats[:, 2:3], stats[:, 0:1], -1.0 / E_DIM),
               reads=[stats], writes=[stats])
        p.exec("vector", lambda e: e.tensor_scalar_mul(stats[:, 3:4], stats[:, 1:2], 1.0 / E_DIM),
               reads=[stats], writes=[stats])
        p.exec("vector", lambda e: e.tensor_mul(stats[:, 4:5], stats[:, 2:3], stats[:, 2:3]),
               reads=[stats], writes=[stats])
        p.exec("vector", lambda e: e.tensor_sub(stats[:, 5:6], stats[:, 3:4], stats[:, 4:5]),
               reads=[stats], writes=[stats])
        p.exec("scalar", lambda e: e.activation(stats[:, 6:7], stats[:, 5:6], AF.Ln, bias=cp[:, 0:1]),
               reads=[stats, cp], writes=[stats])
        p.exec("scalar", lambda e: e.activation(stats[:, 7:8], stats[:, 6:7], AF.Exp, scale=cp[:, 1:2]),
               reads=[stats, cp], writes=[stats])
        p.exec("vector", lambda e: e.tensor_mul(stats[:, 4:5], stats[:, 2:3], stats[:, 7:8]),
               reads=[stats], writes=[stats])
        p.exec("vector", lambda e, x1=x1: e.tensor_scalar(
            xn_tm[:], x1[:], stats[:, 7:8], stats[:, 4:5], ALU.mult, ALU.add),
            reads=[x1, stats], writes=[xn_tm])
        tb = tbuf[b]
        for ec in range(nech):
            pb = pstr[ec % 2]
            p.exec("tensor", lambda e, ec=ec, pb=pb: e.transpose(
                pb[:], xn_tm[:, ec * 128:(ec + 1) * 128], idb[:]),
                reads=[xn_tm, idb], writes=[pb])
            p.exec("vector", lambda e, ec=ec, pb=pb, tb=tb: e.tensor_copy(tb[:, ec, :], pb[:]),
                   reads=[pb], writes=[tb])
            p.dma("gpsimd", xn_re[ec, :, c * 128:(c + 1) * 128], tb[:, ec, :],
                  reads=[tb])
    p.prefix = _oldpfx
    if standalone:
        p.fence()
        return p.emit_and_finalize()
    return None


def build_attn0(T, p=None, io=None):
    standalone = p is None
    if standalone:
        p = Prog("attn0")
    _oldpfx = p.prefix
    if not standalone:
        p.prefix = "attn0_" + p.prefix
    def _inp(name, shape, dtype):
        if io is not None and name in io:
            return io[name]
        return p.inp(name, shape, dtype)
    def _outp(name, shape, dtype):
        if io is not None and name in io:
            return io[name]
        return p.outp(name, shape, dtype)
    qr_fm = _inp("qr_fm", [E_DIM, T], BF16)
    kr_fm = _inp("kr_fm", [E_DIM, T], BF16)
    kr_tm = _inp("kr_tm", [T, E_DIM], BF16)
    v_tm = _inp("v_tm", [T, E_DIM], BF16)
    Wo = _inp("Wo", [E_DIM, E_DIM], F32)
    rms_w_row = _inp("rms_w_row", [1, E_DIM], F32)
    dmaskT = _inp("dmaskT", [H, 128, 128], F32)
    qdec = _inp("qdec", [128, H], F32)
    kdec = _inp("kdec", [128, H], F32)
    rC = _inp("rC", [128, H], F32)
    identbf = _inp("identbf", [128, 128], BF16)
    ones128f = _inp("ones128f", [1, 128], F32)
    constpm = _inp("constpm", [128, 2], F32)
    attn_out = _outp("attn0_tm", [T, E_DIM], F32)
    nch = T // 128
    nech = E_DIM // 128

    idb = p.sb("idb", [128, 128], BF16)
    cp = p.sb("cp", [128, 2], F32)
    wsb = p.sb("wsb", [128, nech, E_DIM], BF16)
    dm = p.sb("dm", [128, H, 128], F32)
    qd = p.sb("qd", [128, H], F32)
    kd = p.sb("kd", [128, H], F32)
    rc = p.sb("rc", [128, H], F32)
    ones_sb = p.sb("ones_sb", [1, 128], F32)
    rmsrow = p.sb("rmsrow", [1, E_DIM], F32)
    rmsrep = p.sb("rmsrep", [128, E_DIM], F32)

    p.dma("sync", idb[:], identbf, writes=[idb])
    p.dma("sync", cp[:], constpm, writes=[cp])
    for h_ in range(H):
        p.dma("sync", dm[:, h_, :], dmaskT[h_], writes=[dm])
    p.dma("sync", qd[:], qdec, writes=[qd])
    p.dma("sync", kd[:], kdec, writes=[kd])
    p.dma("sync", rc[:], rC, writes=[rc])
    p.dma("sync", ones_sb[:], ones128f, writes=[ones_sb])
    p.dma("sync", rmsrow[:], rms_w_row, writes=[rmsrow])
    load_weight_bf16(p, wsb, Wo, E_DIM, E_DIM)

    qre = qr_fm.rearrange("(h d) t -> d h t", d=128)
    kre = kr_fm.rearrange("(h d) t -> d h t", d=128)

    ktm = [p.sb(f"ktm{i}", [128, E_DIM], BF16) for i in range(2)]
    vtm = [p.sb(f"vtm{i}", [128, E_DIM], BF16) for i in range(2)]
    qfm = [p.sb(f"qfm{i}", [128, H, 128], BF16) for i in range(2)]
    kfm = [p.sb(f"kfm{i}", [128, H, 128], BF16) for i in range(2)]
    kv32 = [[p.sb(f"kv32_{h}_{i}", [128, 128], F32) for i in range(2)] for h in range(H)]
    kvbf = [p.sb(f"kvbf_{h}", [128, 128], BF16) for h in range(H)]
    kdt = [p.sb(f"kdt{i}", [128, 128], BF16) for i in range(2)]
    smb = [p.sb(f"smb{i}", [128, 128], BF16) for i in range(2)]
    tmpo = [p.sb(f"tmpo{i}", [128, 128], F32) for i in range(2)]
    attn_tm = p.sb("attn_tm", [128, E_DIM], F32)
    asq = p.sb("asq", [128, E_DIM], F32)
    st8 = p.sb("st8", [128, 8], F32)
    anb = p.sb("anb", [128, E_DIM], BF16)
    afm = p.sb("afm", [128, nech, 128], BF16)
    outc = p.sb("outc", [128, E_DIM], F32)

    ps_st = p.ps("ps_st", [128, 128])
    ps_o = p.ps("ps_o", [128, 128])
    ps_i = p.ps("ps_i", [128, 128])
    ps_kv = p.ps("ps_kv", [128, 128])
    pstr = [p.ps(f"pstr{i}", [128, 128], BF16) for i in range(2)]
    ps_w = [p.ps(f"ps_w{i}", [128, 512]) for i in range(2)]

    for sl in range(4):
        pw = ps_w[sl % 2]
        p.exec("tensor", lambda e, sl=sl, pw=pw: e.matmul(
            pw[:], ones_sb[:], rmsrow[:, sl * 512:(sl + 1) * 512], start=True, stop=True),
            reads=[ones_sb, rmsrow], writes=[pw])
        p.exec("vector", lambda e, sl=sl, pw=pw: e.tensor_copy(
            rmsrep[:, sl * 512:(sl + 1) * 512], pw[:]), reads=[pw], writes=[rmsrep])

    for h in range(H):
        p.exec("vector", lambda e, h=h: e.memset(kv32[h][0][:], 0.0), writes=[kv32[h][0]])
        p.exec("gpsimd", lambda e, h=h: e.memset(kvbf[h][:], 0.0), writes=[kvbf[h]])

    for c in range(nch):
        b = c % 2
        kt, vt, qf, kf = ktm[b], vtm[b], qfm[b], kfm[b]
        p.dma("sync", kt[:], kr_tm[c * 128:(c + 1) * 128, :], writes=[kt])
        p.dma("sync", vt[:], v_tm[c * 128:(c + 1) * 128, :], writes=[vt])
        for h in range(H):
            p.dma("sync", qf[:, h, :], qre[:, h, c * 128:(c + 1) * 128], writes=[qf])
            p.dma("sync", kf[:, h, :], kre[:, h, c * 128:(c + 1) * 128], writes=[kf])
        for h in range(H):
            hb = h % 2
            kvp = kv32[h][c % 2]
            kvn = kv32[h][(c + 1) % 2]
            p.exec("vector", lambda e, h=h, hb=hb, kt=kt: e.tensor_scalar_mul(
                kdt[hb][:], kt[:, h * 128:(h + 1) * 128], kd[:, h:h + 1]),
                reads=[kt, kd], writes=[kdt[hb]])
            p.exec("tensor", lambda e, h=h, qf=qf, kf=kf: e.matmul(
                ps_st[:], kf[:, h, :], qf[:, h, :], start=True, stop=True),
                reads=[kf, qf], writes=[ps_st])
            p.exec("vector", lambda e, h=h, hb=hb: e.tensor_tensor(
                smb[hb][:], ps_st[:], dm[:, h, :], ALU.mult),
                reads=[ps_st, dm], writes=[smb[hb]])
            p.exec("tensor", lambda e, h=h, hb=hb, vt=vt: e.matmul(
                ps_o[:], smb[hb][:], vt[:, h * 128:(h + 1) * 128], start=True, stop=True),
                reads=[smb[hb], vt], writes=[ps_o])
            p.exec("tensor", lambda e, h=h, qf=qf: e.matmul(
                ps_i[:], qf[:, h, :], kvbf[h][:], start=True, stop=True),
                reads=[qf, kvbf[h]], writes=[ps_i])
            p.exec("scalar", lambda e, hb=hb: e.copy(tmpo[hb][:], ps_o[:]),
                   reads=[ps_o], writes=[tmpo[hb]])
            p.exec("vector", lambda e, h=h, hb=hb: e.scalar_tensor_tensor(
                attn_tm[:, h * 128:(h + 1) * 128], ps_i[:], qd[:, h:h + 1], tmpo[hb][:],
                ALU.mult, ALU.add),
                reads=[ps_i, qd, tmpo[hb]], writes=[attn_tm])
            p.exec("tensor", lambda e, h=h, hb=hb, vt=vt: e.matmul(
                ps_kv[:], kdt[hb][:], vt[:, h * 128:(h + 1) * 128], start=True, stop=True),
                reads=[kdt[hb], vt], writes=[ps_kv])
            p.exec("vector", lambda e, h=h, kvp=kvp, kvn=kvn: e.scalar_tensor_tensor(
                kvn[:], kvp[:], rc[:, h:h + 1], ps_kv[:], ALU.mult, ALU.add),
                reads=[kvp, rc, ps_kv], writes=[kvn])
            p.exec("scalar", lambda e, h=h, kvn=kvn: e.copy(kvbf[h][:], kvn[:]),
                   reads=[kvn], writes=[kvbf[h]])
        p.exec("scalar", lambda e: e.square(asq[:], attn_tm[:]), reads=[attn_tm], writes=[asq])
        p.exec("vector", lambda e: e.reduce_sum(st8[:, 0:1], asq[:], axis=mybir.AxisListType.X),
               reads=[asq], writes=[st8])
        p.exec("vector", lambda e: e.tensor_scalar_mul(st8[:, 1:2], st8[:, 0:1], 1.0 / E_DIM),
               reads=[st8], writes=[st8])
        p.exec("scalar", lambda e: e.activation(st8[:, 2:3], st8[:, 1:2], AF.Ln, bias=cp[:, 0:1]),
               reads=[st8, cp], writes=[st8])
        p.exec("scalar", lambda e: e.activation(st8[:, 3:4], st8[:, 2:3], AF.Exp, scale=cp[:, 1:2]),
               reads=[st8, cp], writes=[st8])
        p.exec("vector", lambda e: e.scalar_tensor_tensor(
            anb[:], attn_tm[:], st8[:, 3:4], rmsrep[:], ALU.mult, ALU.mult),
            reads=[attn_tm, st8, rmsrep], writes=[anb])
        for ec in range(nech):
            pb = pstr[ec % 2]
            p.exec("tensor", lambda e, ec=ec, pb=pb: e.transpose(
                pb[:], anb[:, ec * 128:(ec + 1) * 128], idb[:]),
                reads=[anb, idb], writes=[pb])
            p.exec("vector", lambda e, ec=ec, pb=pb: e.tensor_copy(afm[:, ec, :], pb[:]),
                   reads=[pb], writes=[afm])
        for sl in range(4):
            pw = ps_w[sl % 2]
            for e_ in range(nech):
                p.exec("tensor", lambda en, sl=sl, e_=e_, pw=pw: en.matmul(
                    pw[:], afm[:, e_, :], wsb[:, e_, sl * 512:(sl + 1) * 512],
                    start=(e_ == 0), stop=(e_ == nech - 1)),
                    reads=[afm, wsb], writes=[pw])
            p.exec("scalar", lambda en, sl=sl, pw=pw: en.copy(
                outc[:, sl * 512:(sl + 1) * 512], pw[:]), reads=[pw], writes=[outc])
        p.dma("gpsimd", attn_out[c * 128:(c + 1) * 128, :], outc[:], reads=[outc])
    p.prefix = _oldpfx
    if standalone:
        p.fence()
        return p.emit_and_finalize()
    return None


def build_qkv0(T, p=None, io=None):
    standalone = p is None
    if standalone:
        p = Prog("qkv0")
    _oldpfx = p.prefix
    if not standalone:
        p.prefix = "qkv0_" + p.prefix
    def _inp(name, shape, dtype):
        if io is not None and name in io:
            return io[name]
        return p.inp(name, shape, dtype)
    def _outp(name, shape, dtype):
        if io is not None and name in io:
            return io[name]
        return p.outp(name, shape, dtype)
    xn_fm = _inp("xn_fm", [E_DIM, T], BF16)
    Ws = {n: _inp(n, [E_DIM, E_DIM], F32) for n in ("Wq", "Wk", "Wv")}
    rotc = _inp("rotc", [T // 128, 128, 64], F32)
    rots = _inp("rots", [T // 128, 128, 64], F32)
    identbf = _inp("identbf", [128, 128], BF16)
    outs = {
        "q": (_outp("qr_fm", [E_DIM, T], BF16), None),
        "k": (_outp("kr_fm", [E_DIM, T], BF16), _outp("kr_tm", [T, E_DIM], BF16)),
        "v": (None, _outp("v_tm", [T, E_DIM], BF16)),
    }
    nch = T // 128
    nech = E_DIM // 128

    idb = p.sb("idb", [128, 128], BF16)
    p.dma("sync", idb[:], identbf, writes=[idb])
    wsb = p.sb("wsb", [128, nech, E_DIM], BF16)
    xnc = [p.sb(f"xnc{i}", [128, nech, 128], BF16) for i in range(2)]
    pf32_ = [p.sb(f"pf32_{i}", [128, E_DIM], F32) for i in range(2)]
    tmpA_ = [p.sb(f"tmpA_{i}", [128, H, 64], F32) for i in range(2)]
    tmpB_ = [p.sb(f"tmpB_{i}", [128, H, 64], F32) for i in range(2)]
    ptm_ = [p.sb(f"ptm_{i}", [128, E_DIM], BF16) for i in range(2)]
    tbuf = [p.sb(f"tbuf{i}", [128, nech, 128], BF16) for i in range(2)]
    cosc = p.sb("cosc", [128, 64], F32)
    sinc = p.sb("sinc", [128, 64], F32)
    psa = [p.ps(f"psa{i}", [128, 512]) for i in range(4)]
    pstr = [p.ps(f"pstr{i}", [128, 128], BF16) for i in range(2)]

    xre = xn_fm.rearrange("(ec pp) t -> pp ec t", pp=128)

    for tname in ("q", "k", "v"):
        W = Ws["W" + tname]
        fm_out, tm_out = outs[tname]
        load_weight_bf16(p, wsb, W, E_DIM, E_DIM)
        fm_re = fm_out.rearrange("(ec pp) t -> ec pp t", pp=128) if fm_out is not None else None
        pending_tr = None
        for c in range(nch):
            xb = xnc[c % 2]
            pf32 = pf32_[c % 2]
            tmpA = tmpA_[c % 2]
            tmpB = tmpB_[c % 2]
            ptm = ptm_[c % 2]
            for e0 in range(0, nech, 8):
                p.dma("sync", xb[:, e0:e0 + 8, :], xre[:, e0:e0 + 8, c * 128:(c + 1) * 128],
                      writes=[xb])
            for e in range(nech):
                for sl in range(4):
                    p.exec("tensor", lambda en, e=e, sl=sl, xb=xb: en.matmul(
                        psa[sl][:], xb[:, e, :], wsb[:, e, sl * 512:(sl + 1) * 512],
                        start=(e == 0), stop=(e == nech - 1)),
                        reads=[xb, wsb], writes=[psa[sl]])
            if tname == "v":
                for sl in range(4):
                    p.exec("vector", lambda en, sl=sl, ptm=ptm: en.tensor_copy(
                        ptm[:, sl * 512:(sl + 1) * 512], psa[sl][:]),
                        reads=[psa[sl]], writes=[ptm])
            else:
                for sl in range(4):
                    p.exec("vector", lambda en, sl=sl, pf32=pf32: en.tensor_copy(
                        pf32[:, sl * 512:(sl + 1) * 512], psa[sl][:]),
                        reads=[psa[sl]], writes=[pf32])
                p.dma("sync", cosc[:], rotc[c], writes=[cosc])
                p.dma("sync", sinc[:], rots[c], writes=[sinc])
                v4 = pf32.ap().rearrange("pp (h two d) -> pp h two d", h=H, two=2)
                o4 = ptm.ap().rearrange("pp (h two d) -> pp h two d", h=H, two=2)
                cb = cosc.ap().rearrange("pp (o d) -> pp o d", o=1).broadcast_to([128, H, 64])
                sb_ = sinc.ap().rearrange("pp (o d) -> pp o d", o=1).broadcast_to([128, H, 64])
                x1 = v4[:, :, 0, :]
                x2 = v4[:, :, 1, :]
                p.exec("vector", lambda en, x1=x1, cb=cb, tmpA=tmpA: en.tensor_tensor(tmpA.ap(), x1, cb, ALU.mult),
                       reads=[pf32, cosc], writes=[tmpA])
                p.exec("vector", lambda en, x2=x2, sb_=sb_, tmpB=tmpB: en.tensor_tensor(tmpB.ap(), x2, sb_, ALU.mult),
                       reads=[pf32, sinc], writes=[tmpB])
                p.exec("vector", lambda en, o4=o4, tmpA=tmpA, tmpB=tmpB: en.tensor_tensor(
                    o4[:, :, 0, :], tmpA.ap(), tmpB.ap(), ALU.add),
                    reads=[tmpA, tmpB], writes=[ptm])
                p.exec("vector", lambda en, x2=x2, cb=cb, tmpA=tmpA: en.tensor_tensor(tmpA.ap(), x2, cb, ALU.mult),
                       reads=[pf32, cosc], writes=[tmpA])
                p.exec("vector", lambda en, x1=x1, sb_=sb_, tmpB=tmpB: en.tensor_tensor(tmpB.ap(), x1, sb_, ALU.mult),
                       reads=[pf32, sinc], writes=[tmpB])
                p.exec("vector", lambda en, o4=o4, tmpA=tmpA, tmpB=tmpB: en.tensor_tensor(
                    o4[:, :, 1, :], tmpA.ap(), tmpB.ap(), ALU.subtract),
                    reads=[tmpA, tmpB], writes=[ptm])
            if tm_out is not None:
                p.dma("gpsimd", tm_out[c * 128:(c + 1) * 128, :], ptm[:], reads=[ptm])

            def _emit_tr(c, ptm):
                tb = tbuf[c % 2]
                for ec in range(nech):
                    pb = pstr[ec % 2]
                    p.exec("tensor", lambda en, ec=ec, pb=pb, ptm=ptm: en.transpose(
                        pb[:], ptm[:, ec * 128:(ec + 1) * 128], idb[:]),
                        reads=[ptm, idb], writes=[pb])
                    p.exec("vector", lambda en, ec=ec, pb=pb, tb=tb: en.tensor_copy(
                        tb[:, ec, :], pb[:]), reads=[pb], writes=[tb])
                    p.dma("gpsimd", fm_re[ec, :, c * 128:(c + 1) * 128], tb[:, ec, :],
                          reads=[tb])

            pending_new = (c, ptm) if fm_out is not None else None
            if pending_tr is not None:
                _emit_tr(*pending_tr)
            pending_tr = pending_new
        if pending_tr is not None:
            _emit_tr(*pending_tr)
            pending_tr = None
    p.prefix = _oldpfx
    if standalone:
        p.fence()
        return p.emit_and_finalize()
    return None
def build_ln(T, xname="x", outname="xn_fm", p=None, io=None):
    standalone = p is None
    if standalone:
        p = Prog("ln")
    _oldpfx = p.prefix
    if not standalone:
        p.prefix = "ln_" + p.prefix
    def _inp(name, shape, dtype):
        if io is not None and name in io:
            return io[name]
        return p.inp(name, shape, dtype)
    def _outp(name, shape, dtype):
        if io is not None and name in io:
            return io[name]
        return p.outp(name, shape, dtype)
    x = _inp(xname, [T, E_DIM], F32)
    xn_fm = _outp(outname, [E_DIM, T], BF16)
    identbf = _inp("identbf", [128, 128], BF16)
    constpm = _inp("constpm", [128, 2], F32)
    nch = T // 128
    nech = E_DIM // 128

    xin = [p.sb(f"xin{i}", [128, E_DIM], F32) for i in range(2)]
    xsq = p.sb("xsq", [128, E_DIM], F32)
    stats = p.sb("stats", [128, 8], F32)
    xn_tm = p.sb("xn_tm", [128, E_DIM], BF16)
    tbuf = [p.sb(f"tbuf{i}", [128, nech, 128], BF16) for i in range(2)]
    idb = p.sb("idb", [128, 128], BF16)
    pstr = [p.ps(f"pstr{i}", [128, 128], BF16) for i in range(2)]

    cp = p.sb("cp", [128, 2], F32)
    p.dma("sync", cp[:], constpm, writes=[cp])
    p.dma("sync", idb[:], identbf, writes=[idb])
    xn_re = xn_fm.rearrange("(ec pp) t -> ec pp t", pp=128)

    for c in range(nch):
        b = c % 2
        xb = xin[b]
        p.dma("sync", xb[:], x[c * 128:(c + 1) * 128, :], writes=[xb])
        p.exec("scalar", lambda e, i=xb: e.square(xsq[:], i[:]), reads=[xb], writes=[xsq])
        p.exec("vector", lambda e, i=xb: e.reduce_sum(stats[:, 0:1], i[:], axis=mybir.AxisListType.X),
               reads=[xb], writes=[stats])
        p.exec("vector", lambda e: e.reduce_sum(stats[:, 1:2], xsq[:], axis=mybir.AxisListType.X),
               reads=[xsq], writes=[stats])
        p.exec("vector", lambda e: e.tensor_scalar_mul(stats[:, 2:3], stats[:, 0:1], -1.0 / E_DIM),
               reads=[stats], writes=[stats])
        p.exec("vector", lambda e: e.tensor_scalar_mul(stats[:, 3:4], stats[:, 1:2], 1.0 / E_DIM),
               reads=[stats], writes=[stats])
        p.exec("vector", lambda e: e.tensor_mul(stats[:, 4:5], stats[:, 2:3], stats[:, 2:3]),
               reads=[stats], writes=[stats])
        p.exec("vector", lambda e: e.tensor_sub(stats[:, 5:6], stats[:, 3:4], stats[:, 4:5]),
               reads=[stats], writes=[stats])
        p.exec("scalar", lambda e: e.activation(stats[:, 6:7], stats[:, 5:6], AF.Ln, bias=cp[:, 0:1]),
               reads=[stats, cp], writes=[stats])
        p.exec("scalar", lambda e: e.activation(stats[:, 7:8], stats[:, 6:7], AF.Exp, scale=cp[:, 1:2]),
               reads=[stats, cp], writes=[stats])
        p.exec("vector", lambda e: e.tensor_mul(stats[:, 4:5], stats[:, 2:3], stats[:, 7:8]),
               reads=[stats], writes=[stats])
        p.exec("vector", lambda e, i=xb: e.tensor_scalar(
            xn_tm[:], i[:], stats[:, 7:8], stats[:, 4:5], ALU.mult, ALU.add),
            reads=[xb, stats], writes=[xn_tm])
        tb = tbuf[b]
        for ec in range(nech):
            pb = pstr[ec % 2]
            p.exec("tensor", lambda e, ec=ec, pb=pb: e.transpose(
                pb[:], xn_tm[:, ec * 128:(ec + 1) * 128], idb[:]),
                reads=[xn_tm, idb], writes=[pb])
            p.exec("vector", lambda e, ec=ec, pb=pb, tb=tb: e.tensor_copy(tb[:, ec, :], pb[:]),
                   reads=[pb], writes=[tb])
            p.dma("gpsimd", xn_re[ec, :, c * 128:(c + 1) * 128], tb[:, ec, :],
                  reads=[tb])
    p.prefix = _oldpfx
    if standalone:
        p.fence()
        return p.emit_and_finalize()
    return None


# version 06
# speedup vs baseline: 1.6430x; 1.0281x over previous
import numpy as np
import ml_dtypes
from contextlib import ExitStack

import concourse.bass as bass
import concourse.mybir as mybir
from concourse.bass_utils import run_bass_kernel_spmd

F32 = mybir.dt.float32
BF16 = mybir.dt.bfloat16
AF = mybir.ActivationFunctionType
ALU = mybir.AluOpType
BF = ml_dtypes.bfloat16

E_DIM = 2048
H = 16
D = 128
INTER = 8192
EPS = 1e-5

ENGINES = ("sync", "scalar", "vector", "tensor", "gpsimd")
DMA_QUEUES = ("sync", "gpsimd")


class _Op:
    __slots__ = ("eng", "fn", "is_dma", "deps", "inc_needed", "sem_val", "sem_key", "idx")

    def __init__(self, eng, fn, is_dma, idx):
        self.eng = eng
        self.fn = fn
        self.is_dma = is_dma
        self.deps = set()
        self.inc_needed = False
        self.sem_val = None
        self.idx = idx


class Prog:

    def __init__(self, name="prog"):
        self.nc = bass.Bass("TRN2", target_bir_lowering=False)
        self.stack = ExitStack()
        self.ops = []
        self.last_writer = {}
        self.readers = {}
        self.names = set()
        self._barrier = ()
        self._scopes = []
        self._dma_recent = {}
        self.prefix = ""

    from contextlib import contextmanager as _cm

    @_cm
    def scope(self):
        st = ExitStack()
        self._scopes.append(st)
        try:
            yield
        finally:
            self._scopes.pop()
            st.close()
            self.barrier()

    def barrier(self):
        last = {}
        for op in self.ops:
            if not op.is_dma:
                last[op.eng] = op
        bar = set(last.values())
        for q, lst in self._dma_recent.items():
            bar.update(lst)
        self._barrier = tuple(bar)

    def inp(self, name, shape, dtype):
        return self.nc.dram_tensor(name, list(shape), dtype, kind="ExternalInput").ap()

    def outp(self, name, shape, dtype):
        return self.nc.dram_tensor(name, list(shape), dtype, kind="ExternalOutput").ap()

    def dram(self, name, shape, dtype):
        return self.nc.dram_tensor(self.prefix + name, list(shape), dtype, kind="Internal").ap()

    def sb(self, name, shape, dtype):
        name = self.prefix + name
        assert name not in self.names
        self.names.add(name)
        st = self._scopes[-1] if self._scopes else self.stack
        return st.enter_context(self.nc.sbuf_tensor(name, list(shape), dtype))

    def ps(self, name, shape, dtype=F32):
        name = self.prefix + name
        assert name not in self.names
        self.names.add(name)
        st = self._scopes[-1] if self._scopes else self.stack
        return st.enter_context(self.nc.psum_tensor(name, list(shape), dtype))

    def exec(self, eng, fn, reads=(), writes=(), is_dma=False):
        op = _Op(eng, fn, is_dma, len(self.ops))
        deps = set()
        for t in reads:
            w = self.last_writer.get(self._key(t))
            if w is not None:
                deps.add(w)
        for t in writes:
            k = self._key(t)
            deps.update(self.readers.get(k, ()))
            w = self.last_writer.get(k)
            if w is not None:
                deps.add(w)
        deps.update(self._barrier)
        op.deps = {d for d in deps if d is not op}
        self.ops.append(op)
        for t in reads:
            self.readers.setdefault(self._key(t), []).append(op)
        for t in writes:
            k = self._key(t)
            self.last_writer[k] = op
            self.readers[k] = []
        return op

    def dma(self, queue, out, in_, reads=(), writes=(), **kw):
        fn = lambda e, o=out, i=in_, kw=kw: e.dma_start(out=o, in_=i, **kw)
        reads = list(reads) + [in_]
        writes = list(writes) + [out]
        op = self.exec(queue, fn, reads=reads, writes=writes, is_dma=True)
        rec = self._dma_recent.setdefault(queue, [])
        rec.append(op)
        if len(rec) > self.DMA_SLOTS:
            rec.pop(0)
        return op

    @staticmethod
    def _key(t):
        n = getattr(t, "name", None)
        if n is None and hasattr(t, "tensor"):
            n = t.tensor.name
        return n if n is not None else t

    DMA_SLOTS = 8

    def emit_and_finalize(self):
        def _needs_sem(d, op):
            if d.eng == "tensor" and op.eng == "tensor" and not d.is_dma and not op.is_dma:
                return False
            return True

        for op in self.ops:
            for d in op.deps:
                if _needs_sem(d, op):
                    d.inc_needed = True
        for op in self.ops:
            if op.is_dma:
                op.inc_needed = True

        sems = {}
        counters = {}
        slot_rr = {}
        slot_prev = {}
        for op in self.ops:
            if not op.inc_needed:
                continue
            if op.is_dma:
                q = op.eng
                s = slot_rr.get(q, 0)
                slot_rr[q] = (s + 1) % self.DMA_SLOTS
                key = (q, "d", s)
                prev = slot_prev.get(key)
                if prev is not None:
                    op.deps.add(prev)
                slot_prev[key] = op
            else:
                key = (op.eng, "c", 0)
            if key not in sems:
                sems[key] = self.stack.enter_context(
                    self.nc.semaphore(f"s_{key[0]}_{key[1]}{key[2]}")
                )
            step = 16 if op.is_dma else 1
            counters[key] = counters.get(key, 0) + step
            op.sem_val = counters[key]
            op.sem_key = key
        self._sems = sems

        streams = {e: [] for e in ENGINES}
        floors = {e: {} for e in ENGINES}
        for op in self.ops:
            waits = {}
            fl = floors[op.eng]
            for d in op.deps:
                if not _needs_sem(d, op):
                    continue
                key = d.sem_key
                if fl.get(key, 0) >= d.sem_val:
                    continue
                waits[key] = max(waits.get(key, 0), d.sem_val)
            for key, val in waits.items():
                fl[key] = val
            streams[op.eng].append((op, sorted(waits.items(), key=lambda kv: str(kv[0]))))

        block = self.stack.enter_context(self.nc.Block())
        for e in ENGINES:
            if not streams[e]:
                continue

            def body(eng, ops=streams[e]):
                for op, waits in ops:
                    for (key, val) in waits:
                        eng.wait_ge(sems[key], val)
                    inst = op.fn(eng)
                    if op.inc_needed:
                        inst.then_inc(sems[op.sem_key], 16 if op.is_dma else 1)

            getattr(block, e)(body)
        self.stack.close()
        self.nc.finalize()
        return self.nc

    def fence(self, eng="gpsimd"):
        pend = set()
        for k, w in self.last_writer.items():
            pend.add(w)
        for k, rs in self.readers.items():
            pend.update(rs)

        def fn(e):
            return e.nop()

        op = _Op(eng, fn, False, len(self.ops))
        op.deps = {d for d in pend if d is not op}
        self.ops.append(op)
        return op


def cdiv(a, b):
    return (a + b - 1) // b


def prep_host(w, T):
    import math
    pr = {}
    pr["identbf"] = np.eye(128).astype(BF)
    pr["identf32"] = np.eye(128, dtype=np.float32)
    pr["h128bf"] = sylvester(128).astype(BF)
    pr["ones128f"] = np.ones((1, 128), np.float32)
    pr["onesbf"] = np.ones((1, 128)).astype(BF)
    pr["constpm"] = np.tile(np.array([[EPS, -0.5]], np.float32), (128, 1))
    freq = (1.0 / 1024.0) ** np.linspace(0.0, 1.0, 32, dtype=np.float32)
    freq = np.concatenate([freq, np.zeros(32, np.float32)])
    t = np.arange(T, dtype=np.float32)
    theta = t[:, None] * freq[None, :]
    pr["rotc"] = np.ascontiguousarray(np.cos(theta).astype(np.float32).reshape(T // 128, 128, 64))
    pr["rots"] = np.ascontiguousarray(np.sin(theta).astype(np.float32).reshape(T // 128, 128, 64))
    start = 2.0 ** (-2.0 ** (-(math.log2(H) - 3)))
    slopes0 = np.array([start * start ** i for i in range(H)], np.float64)
    r = np.exp(-slopes0)
    i = np.arange(128, dtype=np.float64)
    pr["qdec"] = (r[None, :] ** (i[:, None] + 1.0)).astype(np.float32)
    pr["kdec"] = (r[None, :] ** (127.0 - i[:, None])).astype(np.float32)
    pr["rC"] = np.tile((r ** 128.0).astype(np.float32)[None, :], (128, 1))
    diff = i[:, None] - i[None, :]
    dmaskT = np.zeros((H, 128, 128), np.float32)
    for h in range(H):
        dm = np.where(diff >= 0, r[h] ** np.maximum(diff, 0), 0.0)
        dmaskT[h] = dm.T.astype(np.float32)
    pr["dmaskT"] = dmaskT
    j64 = np.arange(64, dtype=np.float32)
    dT = j64[None, :] - j64[:, None]
    pr["diffT64"] = np.where(dT >= 0, dT, 1e6).astype(np.float32)
    pr["negiota64"] = np.tile(-j64[:, None], (1, H)).astype(np.float32)
    pr["negrev64"] = np.tile(-(64.0 - j64)[:, None], (1, H)).astype(np.float32)
    pr["Wq"] = w["L0_attn_Wq"]
    pr["Wk"] = w["L0_attn_Wk"]
    pr["Wv"] = w["L0_attn_Wv"]
    pr["Wo"] = w["L0_attn_Wo"]
    pr["rms_w_row"] = np.ascontiguousarray(w["L0_attn_rms_w"][None, :])
    pr["W_upT"] = np.ascontiguousarray(w["L0_ffn_W_up"].T)
    pr["W_gateT"] = np.ascontiguousarray(w["L0_ffn_W_gate"].T)
    pr["W_downT"] = np.ascontiguousarray(w["L0_ffn_W_down"].T)
    pr["su_ug_pm"] = np.ascontiguousarray(w["L0_ffn_SU_ug"].reshape(16, 128).T)
    pr["su_dp_pm"] = np.ascontiguousarray(w["L0_ffn_SU_dp"].reshape(64, 128).T)
    pr["wsc_up_pm"] = np.ascontiguousarray(w["L0_ffn_Wscale_ug"][:INTER].reshape(64, 128).T)
    pr["wsc_gate_pm"] = np.ascontiguousarray(w["L0_ffn_Wscale_ug"][INTER:].reshape(64, 128).T)
    pr["wsc_dp_pm"] = np.ascontiguousarray(w["L0_ffn_Wscale_dp"].reshape(16, 128).T)
    pr["Wqkv"] = w["L1_attn_Wqkv"]
    pr["bqkv_pm"] = np.ascontiguousarray(w["L1_attn_bqkv"].reshape(48, 128).T)
    pr["Wout"] = w["L1_attn_Wout"]
    pr["bout_row"] = np.ascontiguousarray(w["L1_attn_bout"][None, :])
    pr["slopes_row"] = np.ascontiguousarray(w["L1_attn_slopes"][None, :])
    fprime = np.arange(INTER)
    perm = 4 * (fprime % 2048) + fprime // 2048
    pr["w_gate_p"] = np.ascontiguousarray(w["L1_ffn_w_gate"][:, perm])
    pr["w_up_p"] = np.ascontiguousarray(w["L1_ffn_w_up"][:, perm])
    pr["w_down_p"] = np.ascontiguousarray(w["L1_ffn_w_down"][perm, :])
    return pr


def build_mega(T):
    p = Prog("mega")
    ext = {}
    for name, shape, dt in [
        ("x", [T, E_DIM], F32), ("identbf", [128, 128], BF16), ("constpm", [128, 2], F32),
        ("rotc", [T // 128, 128, 64], F32), ("rots", [T // 128, 128, 64], F32),
        ("Wq", [E_DIM, E_DIM], F32), ("Wk", [E_DIM, E_DIM], F32), ("Wv", [E_DIM, E_DIM], F32),
        ("Wo", [E_DIM, E_DIM], F32), ("rms_w_row", [1, E_DIM], F32),
        ("dmaskT", [H, 128, 128], F32), ("qdec", [128, H], F32), ("kdec", [128, H], F32),
        ("rC", [128, H], F32), ("ones128f", [1, 128], F32), ("onesbf", [1, 128], BF16),
        ("W_upT", [E_DIM, INTER], F32), ("W_gateT", [E_DIM, INTER], F32),
        ("W_downT", [INTER, E_DIM], F32),
        ("su_ug_pm", [128, 16], F32), ("su_dp_pm", [128, 64], F32),
        ("wsc_up_pm", [128, 64], F32), ("wsc_gate_pm", [128, 64], F32),
        ("wsc_dp_pm", [128, 16], F32),
        ("h128bf", [128, 128], BF16), ("identf32", [128, 128], F32),
        ("Wqkv", [E_DIM, 3 * E_DIM], F32), ("bqkv_pm", [128, 48], F32),
        ("Wout", [E_DIM, E_DIM], F32), ("bout_row", [1, E_DIM], F32),
        ("slopes_row", [1, H], F32), ("diffT64", [64, 64], F32),
        ("negiota64", [64, H], F32), ("negrev64", [64, H], F32),
        ("w_gate_p", [E_DIM, INTER], F32), ("w_up_p", [E_DIM, INTER], F32),
        ("w_down_p", [INTER, E_DIM], F32),
    ]:
        ext[name] = p.inp(name, shape, dt)
    out = p.outp("out_tm", [T, E_DIM], F32)
    d_xn0 = p.dram("xn0_a", [E_DIM, T], BF16)
    d_q0 = p.dram("q0_a", [E_DIM, T], BF16)
    d_k0 = p.dram("k0_a", [E_DIM, T], BF16)
    d_k0t = p.dram("k0t_a", [T, E_DIM], BF16)
    d_v0 = p.dram("v0_a", [T, E_DIM], BF16)
    d_attn0 = p.dram("attn0_a", [T, E_DIM], F32)
    d_ffn0 = p.dram("ffn0_a", [T, E_DIM], F32)
    d_x1 = p.dram("x1_a", [T, E_DIM], F32)
    d_xn1 = p.dram("xn1_a", [E_DIM, T], BF16)
    d_q1 = p.dram("q1_a", [E_DIM, T], BF16)
    d_k1 = p.dram("k1_a", [E_DIM, T], BF16)
    d_k1t = p.dram("k1t_a", [T, E_DIM], BF16)
    d_v1 = p.dram("v1_a", [T, E_DIM], BF16)
    d_attn1 = p.dram("attn1_a", [T, E_DIM], F32)
    d_ffn1 = p.dram("ffn1_a", [T, E_DIM], F32)

    with p.scope():
        build_ln(T, p=p, io={"x": ext["x"], "xn_fm": d_xn0,
                             "identbf": ext["identbf"], "constpm": ext["constpm"]})
    with p.scope():
        build_qkv0(T, p=p, io={"xn_fm": d_xn0, "Wq": ext["Wq"], "Wk": ext["Wk"],
                               "Wv": ext["Wv"], "rotc": ext["rotc"], "rots": ext["rots"],
                               "identbf": ext["identbf"], "qr_fm": d_q0, "kr_fm": d_k0,
                               "kr_tm": d_k0t, "v_tm": d_v0})
    with p.scope():
        build_attn0(T, p=p, io={"qr_fm": d_q0, "kr_fm": d_k0, "kr_tm": d_k0t, "v_tm": d_v0,
                                "Wo": ext["Wo"], "rms_w_row": ext["rms_w_row"],
                                "dmaskT": ext["dmaskT"], "qdec": ext["qdec"],
                                "kdec": ext["kdec"], "rC": ext["rC"],
                                "identbf": ext["identbf"], "ones128f": ext["ones128f"],
                                "constpm": ext["constpm"], "attn0_tm": d_attn0})
    with p.scope():
        build_ffn0(T, p=p, io={"xn_fm": d_xn0, "W_upT": ext["W_upT"],
                               "W_gateT": ext["W_gateT"], "W_downT": ext["W_downT"],
                               "su_ug_pm": ext["su_ug_pm"], "su_dp_pm": ext["su_dp_pm"],
                               "wsc_up_pm": ext["wsc_up_pm"], "wsc_gate_pm": ext["wsc_gate_pm"],
                               "wsc_dp_pm": ext["wsc_dp_pm"], "h128bf": ext["h128bf"],
                               "identf32": ext["identf32"], "ffn0_tm": d_ffn0})
    with p.scope():
        p.prefix = "r0_"
        build_resln(T, with_ln=True, p=p, io={"x": ext["x"], "add0": d_attn0,
                                              "add1": d_ffn0, "x1_tm": d_x1,
                                              "xn_fm": d_xn1, "identbf": ext["identbf"],
                                              "constpm": ext["constpm"]})
        p.prefix = ""
    with p.scope():
        build_qkv1(T, p=p, io={"xn_fm": d_xn1, "Wqkv": ext["Wqkv"],
                               "bqkv_pm": ext["bqkv_pm"], "identbf": ext["identbf"],
                               "q1_fm": d_q1, "k1_fm": d_k1, "k1_tm": d_k1t, "v1_tm": d_v1})
    with p.scope():
        build_attn1(T, p=p, io={"q1_fm": d_q1, "k1_fm": d_k1, "k1_tm": d_k1t, "v1_tm": d_v1,
                                "Wout": ext["Wout"], "bout_row": ext["bout_row"],
                                "slopes_row": ext["slopes_row"], "diffT64": ext["diffT64"],
                                "negiota64": ext["negiota64"], "negrev64": ext["negrev64"],
                                "ones128f": ext["ones128f"], "onesbf": ext["onesbf"],
                                "identbf": ext["identbf"], "attn1_tm": d_attn1})
    with p.scope():
        build_ffn1(T, p=p, io={"xn_fm": d_xn1, "w_gate_p": ext["w_gate_p"],
                               "w_up_p": ext["w_up_p"], "w_down_p": ext["w_down_p"],
                               "identf32": ext["identf32"], "ffn1_tm": d_ffn1})
    with p.scope():
        p.prefix = "r1_"
        build_resln(T, with_ln=False, p=p, io={"x": d_x1, "add0": d_attn1,
                                               "add1": d_ffn1, "x1_tm": out})
        p.prefix = ""
    p.fence()
    return p.emit_and_finalize()


def mega_in_map(pr, x_b):
    m = {k: pr[k] for k in (
        "identbf", "constpm", "rotc", "rots", "Wq", "Wk", "Wv", "Wo", "rms_w_row",
        "dmaskT", "qdec", "kdec", "rC", "ones128f", "onesbf", "W_upT", "W_gateT",
        "W_downT", "su_ug_pm", "su_dp_pm", "wsc_up_pm", "wsc_gate_pm", "wsc_dp_pm",
        "h128bf", "identf32", "Wqkv", "bqkv_pm", "Wout", "bout_row", "slopes_row",
        "diffT64", "negiota64", "negrev64", "w_gate_p", "w_up_p", "w_down_p")}
    m["x"] = np.ascontiguousarray(x_b)
    return m


_NC_CACHE = {}


def _get_nc(key, builder, T):
    if key not in _NC_CACHE:
        _NC_CACHE[key] = builder(T)
    return _NC_CACHE[key]


def forward(x_full, w, T, run):
    B = x_full.shape[0]
    pr = prep_host(w, T)

    def rep(d):
        return [d] * B

    def percore(key_arrs, shared):
        return [dict(shared, **ka) for ka in key_arrs]

    nc = _get_nc(("ln0", T), build_ln, T)
    outs = run(nc, percore([{"x": np.ascontiguousarray(x_full[b])} for b in range(B)],
                           {"identbf": pr["identbf"], "constpm": pr["constpm"]}))
    xn0 = [o["xn_fm"] for o in outs]
    nc = _get_nc(("qkv0", T), build_qkv0, T)
    outs = run(nc, percore([{"xn_fm": xn0[b]} for b in range(B)],
                           {"Wq": pr["Wq"], "Wk": pr["Wk"], "Wv": pr["Wv"],
                            "rotc": pr["rotc"], "rots": pr["rots"], "identbf": pr["identbf"]}))
    q0, k0, k0t, v0 = ([o[k] for o in outs] for k in ("qr_fm", "kr_fm", "kr_tm", "v_tm"))
    nc = _get_nc(("attn0", T), build_attn0, T)
    outs = run(nc, percore(
        [{"qr_fm": q0[b], "kr_fm": k0[b], "kr_tm": k0t[b], "v_tm": v0[b]} for b in range(B)],
        {"Wo": pr["Wo"], "rms_w_row": pr["rms_w_row"], "dmaskT": pr["dmaskT"],
         "qdec": pr["qdec"], "kdec": pr["kdec"], "rC": pr["rC"],
         "identbf": pr["identbf"], "ones128f": pr["ones128f"], "constpm": pr["constpm"]}))
    attn0 = [o["attn0_tm"] for o in outs]
    nc = _get_nc(("ffn0", T), build_ffn0, T)
    outs = run(nc, percore([{"xn_fm": xn0[b]} for b in range(B)],
                           {"W_upT": pr["W_upT"], "W_gateT": pr["W_gateT"],
                            "W_downT": pr["W_downT"], "su_ug_pm": pr["su_ug_pm"],
                            "su_dp_pm": pr["su_dp_pm"], "wsc_up_pm": pr["wsc_up_pm"],
                            "wsc_gate_pm": pr["wsc_gate_pm"], "wsc_dp_pm": pr["wsc_dp_pm"],
                            "h128bf": pr["h128bf"], "identf32": pr["identf32"]}))
    ffn0 = [o["ffn0_tm"] for o in outs]
    nc = _get_nc(("resln", T), lambda TT: build_resln(TT, with_ln=True), T)
    outs = run(nc, percore(
        [{"x": np.ascontiguousarray(x_full[b]), "add0": attn0[b], "add1": ffn0[b]}
         for b in range(B)],
        {"identbf": pr["identbf"], "constpm": pr["constpm"]}))
    x1 = [o["x1_tm"] for o in outs]
    xn1 = [o["xn_fm"] for o in outs]
    nc = _get_nc(("qkv1", T), build_qkv1, T)
    outs = run(nc, percore([{"xn_fm": xn1[b]} for b in range(B)],
                           {"Wqkv": pr["Wqkv"], "bqkv_pm": pr["bqkv_pm"],
                            "identbf": pr["identbf"]}))
    q1, k1, k1t, v1 = ([o[k] for o in outs] for k in ("q1_fm", "k1_fm", "k1_tm", "v1_tm"))
    nc = _get_nc(("attn1", T), build_attn1, T)
    outs = run(nc, percore(
        [{"q1_fm": q1[b], "k1_fm": k1[b], "k1_tm": k1t[b], "v1_tm": v1[b]} for b in range(B)],
        {"Wout": pr["Wout"], "bout_row": pr["bout_row"], "slopes_row": pr["slopes_row"],
         "diffT64": pr["diffT64"], "negiota64": pr["negiota64"], "negrev64": pr["negrev64"],
         "ones128f": pr["ones128f"], "onesbf": pr["onesbf"], "identbf": pr["identbf"]}))
    attn1 = [o["attn1_tm"] for o in outs]
    nc = _get_nc(("ffn1", T), build_ffn1, T)
    outs = run(nc, percore([{"xn_fm": xn1[b]} for b in range(B)],
                           {"w_gate_p": pr["w_gate_p"], "w_up_p": pr["w_up_p"],
                            "w_down_p": pr["w_down_p"], "identf32": pr["identf32"]}))
    ffn1 = [o["ffn1_tm"] for o in outs]
    nc = _get_nc(("res1", T), lambda TT: build_resln(TT, with_ln=False), T)
    outs = run(nc, [{"x": x1[b], "add0": attn1[b], "add1": ffn1[b]} for b in range(B)])
    return np.stack([o["x1_tm"] for o in outs], axis=0)


def kernel(**inputs):
    x = np.asarray(inputs["L0_attn_x"], np.float32)
    T = x.shape[1]
    B = x.shape[0]
    pr = prep_host(inputs, T)
    nc = _get_nc(("mega", T), build_mega, T)
    in_maps = [mega_in_map(pr, x[b]) for b in range(B)]
    res = run_bass_kernel_spmd(nc, in_maps, core_ids=list(range(B)))
    return np.stack([r["out_tm"] for r in res.results], axis=0)


def kernel_multilaunch(**inputs):
    x = np.asarray(inputs["L0_attn_x"], np.float32)
    T = x.shape[1]

    def run(nc, in_maps):
        res = run_bass_kernel_spmd(nc, in_maps, core_ids=list(range(len(in_maps))))
        return res.results

    return forward(x, inputs, T, run)


def sylvester(n):
    Hm = np.array([[1.0]], np.float32)
    while Hm.shape[0] < n:
        Hm = np.block([[Hm, Hm], [Hm, -Hm]])
    return Hm.astype(np.float32)


def load_weight_bf16(p, wsb, w_dram, rows, cols, name=None, ecs_per_dma=8):
    nech = rows // 128
    wre = w_dram.rearrange("(ec pp) c -> pp ec c", pp=128)
    for e0 in range(0, nech, ecs_per_dma):
        e1 = min(e0 + ecs_per_dma, nech)
        p.dma("gpsimd", wsb[:, e0:e1, :], wre[:, e0:e1, :], writes=[wsb])


def _key_of(t):
    return Prog._key(t)


def _butterfly(p, n, cur, nxt, final_out=None, final_slice=None, width=None, tag=""):
    import math as _m
    stages = int(_m.log2(n))
    bufs = [cur, nxt]
    kc, kn, kf = _key_of(cur), _key_of(nxt), (None if final_out is None else _key_of(final_out))
    keys = [kc, kn]
    pair_i = 0
    for s in range(stages):
        h = 1 << s
        src = bufs[s % 2]
        dst = bufs[(s + 1) % 2]
        ksrc = keys[s % 2]
        kdst = keys[(s + 1) % 2]
        last = (s == stages - 1) and final_out is not None
        for blk in range(n // (2 * h)):
            for i in range(h):
                ia = blk * 2 * h + i
                ib = ia + h
                eng = "vector" if (pair_i % 2 == 0) else "gpsimd"
                pair_i += 1
                for sign, oi in ((ALU.add, ia), (ALU.subtract, ib)):
                    if last:
                        oap = final_out[:, oi, final_slice[0]:final_slice[1]]
                        okey = f"{kf}@{oi}"
                    else:
                        oap = dst[:, oi, :]
                        okey = f"{kdst}@{oi}"
                    p.exec(eng, lambda e, src=src, ia=ia, ib=ib, oap=oap, sign=sign:
                           e.tensor_tensor(oap, src[:, ia, :], src[:, ib, :], sign),
                           reads=[f"{ksrc}@{ia}", f"{ksrc}@{ib}"], writes=[okey])
    return bufs[stages % 2]


def build_qkv1(T, p=None, io=None):
    standalone = p is None
    if standalone:
        p = Prog("qkv1")
    _oldpfx = p.prefix
    if not standalone:
        p.prefix = "qkv1_" + p.prefix
    def _inp(name, shape, dtype):
        if io is not None and name in io:
            return io[name]
        return p.inp(name, shape, dtype)
    def _outp(name, shape, dtype):
        if io is not None and name in io:
            return io[name]
        return p.outp(name, shape, dtype)
    xn_fm = _inp("xn_fm", [E_DIM, T], BF16)
    Wqkv = _inp("Wqkv", [E_DIM, 3 * E_DIM], F32)
    bq_pm = _inp("bqkv_pm", [128, 48], F32)
    identbf = _inp("identbf", [128, 128], BF16)
    q_fm = _outp("q1_fm", [E_DIM, T], BF16)
    k_fm = _outp("k1_fm", [E_DIM, T], BF16)
    k_tm = _outp("k1_tm", [T, E_DIM], BF16)
    v_tm = _outp("v1_tm", [T, E_DIM], BF16)
    nech = E_DIM // 128
    TS = min(512, T)
    nts = T // TS
    nci_ = TS // 128

    idb = p.sb("idb", [128, 128], BF16)
    p.dma("sync", idb[:], identbf, writes=[idb])
    bq = p.sb("bq", [128, 48], F32)
    p.dma("sync", bq[:], bq_pm, writes=[bq])
    xnsb = p.sb("xnsb", [128, nech, T], BF16)
    xre = xn_fm.rearrange("(ec pp) t -> pp ec t", pp=128)
    for e0 in range(0, nech, 4):
        p.dma("sync", xnsb[:, e0:e0 + 4, :], xre[:, e0:e0 + 4, :], writes=[xnsb])
    wre = Wqkv.rearrange("(ec pp) o -> pp ec o", pp=128)
    q_re = q_fm.rearrange("(ec pp) t -> ec pp t", pp=128)
    k_re = k_fm.rearrange("(ec pp) t -> ec pp t", pp=128)

    wt = [p.sb(f"wt{i}", [128, nech, 128], BF16) for i in range(2)]
    ebuf_ = [p.sb(f"ebuf{i}", [128, TS], BF16) for i in range(2)]
    ttile_ = [p.sb(f"ttile{i}", [128, 128], BF16) for i in range(2)]
    ps_m = [p.ps(f"ps_m{i}", [128, TS]) for i in range(2)]
    ps_t = [p.ps(f"ps_t{i}", [128, 128], BF16) for i in range(2)]

    for oc in range(48):
        w = wt[oc % 2]
        for e0 in range(0, nech, 8):
            p.dma("gpsimd", w[:, e0:e0 + 8, :], wre[:, e0:e0 + 8, oc * 128:(oc + 1) * 128],
                  writes=[w])
        for ts in range(nts):
            pm = ps_m[ts % 2]
            ebuf = ebuf_[(oc * nts + ts) % 2]
            for ec in range(nech):
                p.exec("tensor", lambda e, ec=ec, ts=ts, w=w, pm=pm: e.matmul(
                    pm[:], w[:, ec, :], xnsb[:, ec, ts * TS:(ts + 1) * TS],
                    start=(ec == 0), stop=(ec == nech - 1)),
                    reads=[w, xnsb], writes=[pm])
            p.exec("vector", lambda e, oc=oc, pm=pm, ebuf=ebuf: e.tensor_scalar_add(
                ebuf[:], pm[:], bq[:, oc:oc + 1]), reads=[pm, bq], writes=[ebuf])
            if oc < 16:
                p.dma("gpsimd", q_re[oc, :, ts * TS:(ts + 1) * TS], ebuf[:], reads=[ebuf])
            elif oc < 32:
                p.dma("gpsimd", k_re[oc - 16, :, ts * TS:(ts + 1) * TS], ebuf[:], reads=[ebuf])
            if oc >= 16:
                tmout = k_tm if oc < 32 else v_tm
                for ci in range(nci_):
                    pt = ps_t[ci % 2]
                    ttile = ttile_[ci % 2]
                    p.exec("tensor", lambda e, ci=ci, pt=pt, ebuf=ebuf: e.transpose(
                        pt[:], ebuf[:, ci * 128:(ci + 1) * 128], idb[:]),
                        reads=[ebuf, idb], writes=[pt])
                    p.exec("vector", lambda e, pt=pt, ttile=ttile: e.tensor_copy(ttile[:], pt[:]),
                           reads=[pt], writes=[ttile])
                    r0 = ts * TS + ci * 128
                    p.dma("sync", tmout[r0:r0 + 128, (oc % 16) * 128:(oc % 16 + 1) * 128],
                          ttile[:], reads=[ttile])
    p.prefix = _oldpfx
    if standalone:
        p.fence()
        return p.emit_and_finalize()
    return None


def build_attn1(T, p=None, io=None):
    standalone = p is None
    if standalone:
        p = Prog("attn1")
    _oldpfx = p.prefix
    if not standalone:
        p.prefix = "attn1_" + p.prefix
    def _inp(name, shape, dtype):
        if io is not None and name in io:
            return io[name]
        return p.inp(name, shape, dtype)
    def _outp(name, shape, dtype):
        if io is not None and name in io:
            return io[name]
        return p.outp(name, shape, dtype)
    q_fm = _inp("q1_fm", [E_DIM, T], BF16)
    k_fm = _inp("k1_fm", [E_DIM, T], BF16)
    k_tm = _inp("k1_tm", [T, E_DIM], BF16)
    v_tm = _inp("v1_tm", [T, E_DIM], BF16)
    Wout = _inp("Wout", [E_DIM, E_DIM], F32)
    bout_row = _inp("bout_row", [1, E_DIM], F32)
    slopes_row = _inp("slopes_row", [1, H], F32)
    diffT = _inp("diffT64", [64, 64], F32)
    negiota = _inp("negiota64", [64, H], F32)
    negrev = _inp("negrev64", [64, H], F32)
    ones128f = _inp("ones128f", [1, 128], F32)
    onesbf = _inp("onesbf", [1, 128], BF16)
    identbf = _inp("identbf", [128, 128], BF16)
    attn_out = _outp("attn1_tm", [T, E_DIM], F32)
    BLK = 64
    nblk = T // BLK
    nech = E_DIM // 128

    idb = p.sb("idb", [128, 128], BF16)
    onb = p.sb("onb", [1, 128], BF16)
    onf = p.sb("onf", [1, 128], F32)
    slp = p.sb("slp", [1, H], F32)
    dfT = p.sb("dfT", [64, 64], F32)
    nio = p.sb("nio", [64, H], F32)
    nrv = p.sb("nrv", [64, H], F32)
    boutb = p.sb("boutb", [1, E_DIM], BF16)
    wsb = p.sb("wsb", [128, nech, E_DIM], BF16)
    for t, src in ((idb, identbf), (onb, onesbf), (onf, ones128f), (slp, slopes_row),
                   (dfT, diffT), (nio, negiota), (nrv, negrev)):
        p.dma("sync", t[:], src, writes=[t])
    p.dma("gpsimd", boutb[:], bout_row, writes=[boutb])
    load_weight_bf16(p, wsb, Wout, E_DIM, E_DIM)

    srep = p.sb("srep", [128, H], F32)
    nsrep = p.sb("nsrep", [128, H], F32)
    qd1 = p.sb("qd1", [64, H], F32)
    kd1 = p.sb("kd1", [64, H], F32)
    bd1 = p.sb("bd1", [128, H], F32)
    dT = p.sb("dT", [64, H, 64], F32)
    tmp16 = p.sb("tmp16", [128, H], F32)
    with p.scope():
        ps_r = p.ps("ps_r", [128, H])
        p.exec("tensor", lambda e: e.matmul(ps_r[:], onf[:], slp[:], start=True, stop=True),
               reads=[onf, slp], writes=[ps_r])
        p.exec("vector", lambda e: e.tensor_copy(srep[:], ps_r[:]), reads=[ps_r], writes=[srep])
    p.exec("vector", lambda e: e.tensor_scalar_mul(nsrep[:], srep[:], -1.0),
           reads=[srep], writes=[nsrep])
    p.exec("vector", lambda e: e.tensor_mul(tmp16[0:64, :], nio[:], srep[0:64, :]),
           reads=[nio, srep], writes=[tmp16])
    p.exec("scalar", lambda e: e.activation(qd1[:], tmp16[0:64, :], AF.Exp),
           reads=[tmp16], writes=[qd1])
    p.exec("vector", lambda e: e.tensor_mul(tmp16[0:64, :], nrv[:], srep[0:64, :]),
           reads=[nio, srep, qd1], writes=[tmp16])
    p.exec("scalar", lambda e: e.activation(kd1[:], tmp16[0:64, :], AF.Exp),
           reads=[tmp16], writes=[kd1])
    p.exec("vector", lambda e: e.tensor_scalar_mul(tmp16[:], srep[:], -64.0),
           reads=[srep, kd1], writes=[tmp16])
    p.exec("scalar", lambda e: e.activation(bd1[:], tmp16[:], AF.Exp),
           reads=[tmp16], writes=[bd1])
    for h in range(H):
        p.exec("scalar", lambda e, h=h: e.activation(
            dT[:, h, :], dfT[:], AF.Exp, scale=nsrep[0:64, h:h + 1]),
            reads=[dfT, nsrep], writes=[dT])

    qre = q_fm.rearrange("(h d) t -> d h t", d=128)
    kre = k_fm.rearrange("(h d) t -> d h t", d=128)

    ktm = [p.sb(f"ktm{i}", [64, E_DIM], BF16) for i in range(2)]
    vtm = [p.sb(f"vtm{i}", [64, E_DIM], BF16) for i in range(2)]
    qfm = [p.sb(f"qfm{i}", [128, H, 64], BF16) for i in range(2)]
    kfm = [p.sb(f"kfm{i}", [128, H, 64], BF16) for i in range(2)]
    kv32 = [[p.sb(f"kv32_{h}_{i}", [128, 128], F32) for i in range(2)] for h in range(H)]
    kvbf = [p.sb(f"kvbf_{h}", [128, 128], BF16) for h in range(H)]
    kdt = [p.sb(f"kdt{i}", [64, 128], BF16) for i in range(2)]
    smb = [p.sb(f"smb{i}", [64, 64], BF16) for i in range(2)]
    tmpo = [p.sb(f"tmpo{i}", [64, 128], F32) for i in range(2)]
    attn_b = p.sb("attn_b", [64, E_DIM], F32)
    anb = p.sb("anb", [64, E_DIM], BF16)
    afm = p.sb("afm", [128, nech, 64], BF16)
    outc = p.sb("outc", [64, E_DIM], F32)

    ps_st = p.ps("ps_st", [64, 64])
    ps_o = p.ps("ps_o", [64, 128])
    ps_i = p.ps("ps_i", [64, 128])
    ps_kv = p.ps("ps_kv", [128, 128])
    pstr = [p.ps(f"pstr{i}", [128, 64], BF16) for i in range(2)]
    ps_w = [p.ps(f"ps_w{i}", [64, 512]) for i in range(2)]

    for h in range(H):
        p.exec("vector", lambda e, h=h: e.memset(kv32[h][0][:], 0.0), writes=[kv32[h][0]])
        p.exec("gpsimd", lambda e, h=h: e.memset(kvbf[h][:], 0.0), writes=[kvbf[h]])

    for c in range(nblk):
        b = c % 2
        kt, vt, qf, kf = ktm[b], vtm[b], qfm[b], kfm[b]
        p.dma("sync", kt[:], k_tm[c * 64:(c + 1) * 64, :], writes=[kt])
        p.dma("sync", vt[:], v_tm[c * 64:(c + 1) * 64, :], writes=[vt])
        for h in range(H):
            p.dma("sync", qf[:, h, :], qre[:, h, c * 64:(c + 1) * 64], writes=[qf])
            p.dma("sync", kf[:, h, :], kre[:, h, c * 64:(c + 1) * 64], writes=[kf])
        for h in range(H):
            hb = h % 2
            kvp = kv32[h][c % 2]
            kvn = kv32[h][(c + 1) % 2]
            p.exec("vector", lambda e, h=h, hb=hb, kt=kt: e.tensor_scalar_mul(
                kdt[hb][:], kt[:, h * 128:(h + 1) * 128], kd1[:, h:h + 1]),
                reads=[kt, kd1], writes=[kdt[hb]])
            p.exec("tensor", lambda e, h=h, qf=qf, kf=kf: e.matmul(
                ps_st[:], kf[:, h, :], qf[:, h, :], start=True, stop=True),
                reads=[kf, qf], writes=[ps_st])
            p.exec("vector", lambda e, h=h, hb=hb: e.tensor_tensor(
                smb[hb][:], ps_st[:], dT[:, h, :], ALU.mult),
                reads=[ps_st, dT], writes=[smb[hb]])
            p.exec("tensor", lambda e, h=h, hb=hb, vt=vt: e.matmul(
                ps_o[:], smb[hb][:], vt[:, h * 128:(h + 1) * 128], start=True, stop=True),
                reads=[smb[hb], vt], writes=[ps_o])
            p.exec("tensor", lambda e, h=h, qf=qf: e.matmul(
                ps_i[:], qf[:, h, :], kvbf[h][:], start=True, stop=True),
                reads=[qf, kvbf[h]], writes=[ps_i])
            p.exec("scalar", lambda e, hb=hb: e.copy(tmpo[hb][:], ps_o[:]),
                   reads=[ps_o], writes=[tmpo[hb]])
            p.exec("vector", lambda e, h=h, hb=hb: e.scalar_tensor_tensor(
                attn_b[:, h * 128:(h + 1) * 128], ps_i[:], qd1[:, h:h + 1], tmpo[hb][:],
                ALU.mult, ALU.add),
                reads=[ps_i, qd1, tmpo[hb]], writes=[attn_b])
            p.exec("tensor", lambda e, h=h, hb=hb, vt=vt: e.matmul(
                ps_kv[:], kdt[hb][:], vt[:, h * 128:(h + 1) * 128], start=True, stop=True),
                reads=[kdt[hb], vt], writes=[ps_kv])
            p.exec("vector", lambda e, h=h, kvp=kvp, kvn=kvn: e.scalar_tensor_tensor(
                kvn[:], kvp[:], bd1[:, h:h + 1], ps_kv[:], ALU.mult, ALU.add),
                reads=[kvp, bd1, ps_kv], writes=[kvn])
            p.exec("scalar", lambda e, h=h, kvn=kvn: e.copy(kvbf[h][:], kvn[:]),
                   reads=[kvn], writes=[kvbf[h]])
        p.exec("vector", lambda e: e.tensor_copy(anb[:], attn_b[:]),
               reads=[attn_b], writes=[anb])
        for ec in range(nech):
            pb = pstr[ec % 2]
            p.exec("tensor", lambda e, ec=ec, pb=pb: e.transpose(
                pb[:], anb[:, ec * 128:(ec + 1) * 128], idb[0:64, 0:64]),
                reads=[anb, idb], writes=[pb])
            p.exec("vector", lambda e, ec=ec, pb=pb: e.tensor_copy(afm[:, ec, :], pb[:]),
                   reads=[pb], writes=[afm])
        for sl in range(4):
            pw = ps_w[sl % 2]
            for e_ in range(nech):
                p.exec("tensor", lambda en, sl=sl, e_=e_, pw=pw: en.matmul(
                    pw[:], afm[:, e_, :], wsb[:, e_, sl * 512:(sl + 1) * 512],
                    start=(e_ == 0), stop=False),
                    reads=[afm, wsb], writes=[pw])
            p.exec("tensor", lambda en, sl=sl, pw=pw: en.matmul(
                pw[:], onb[:, 0:64], boutb[:, sl * 512:(sl + 1) * 512],
                start=False, stop=True),
                reads=[onb, boutb], writes=[pw])
            p.exec("scalar", lambda en, sl=sl, pw=pw: en.copy(
                outc[:, sl * 512:(sl + 1) * 512], pw[:]), reads=[pw], writes=[outc])
        p.dma("gpsimd", attn_out[c * 64:(c + 1) * 64, :], outc[:], reads=[outc])
    p.prefix = _oldpfx
    if standalone:
        p.fence()
        return p.emit_and_finalize()
    return None


def build_ffn0(T, p=None, io=None):
    standalone = p is None
    if standalone:
        p = Prog("ffn0")
    _oldpfx = p.prefix
    if not standalone:
        p.prefix = "ffn0_" + p.prefix
    def _inp(name, shape, dtype):
        if io is not None and name in io:
            return io[name]
        return p.inp(name, shape, dtype)
    def _outp(name, shape, dtype):
        if io is not None and name in io:
            return io[name]
        return p.outp(name, shape, dtype)
    xn_fm = _inp("xn_fm", [E_DIM, T], BF16)
    W_upT = _inp("W_upT", [E_DIM, INTER], F32)
    W_gateT = _inp("W_gateT", [E_DIM, INTER], F32)
    W_downT = _inp("W_downT", [INTER, E_DIM], F32)
    su_ug = _inp("su_ug_pm", [128, 16], F32)
    su_dp = _inp("su_dp_pm", [128, 64], F32)
    wsc_up = _inp("wsc_up_pm", [128, 64], F32)
    wsc_gate = _inp("wsc_gate_pm", [128, 64], F32)
    wsc_dp = _inp("wsc_dp_pm", [128, 16], F32)
    h128in = _inp("h128bf", [128, 128], BF16)
    identf = _inp("identf32", [128, 128], F32)
    ffn_out = _outp("ffn0_tm", [T, E_DIM], F32)

    u_dram = p.dram("u_scr", [INTER, T], BF16)
    w_dram = p.dram("w_scr", [INTER, T], BF16)

    nech = E_DIM // 128
    nfch = INTER // 128

    h128sb = p.sb("h128sb", [128, 128], BF16)
    idf = p.sb("idf", [128, 128], F32)
    sug = p.sb("sug", [128, 16], F32)
    sud = p.sb("sud", [128, 64], F32)
    wup_s = p.sb("wup_s", [128, 64], F32)
    wgt_s = p.sb("wgt_s", [128, 64], F32)
    wdp_s = p.sb("wdp_s", [128, 16], F32)
    upscsu = p.sb("upscsu", [128, 64], F32)
    wdps = p.sb("wdps", [128, 16], F32)
    for t, src in ((h128sb, h128in), (idf, identf), (sug, su_ug), (sud, su_dp),
                   (wup_s, wsc_up), (wgt_s, wsc_gate), (wdp_s, wsc_dp)):
        p.dma("sync", t[:], src, writes=[t])
    p.exec("vector", lambda e: e.tensor_mul(upscsu[:], wup_s[:], sud[:]),
           reads=[wup_s, sud], writes=[upscsu])
    p.exec("vector", lambda e: e.tensor_scalar_mul(wdps[:], wdp_s[:], float(1.0 / np.sqrt(INTER))),
           reads=[wdp_s], writes=[wdps])

    xre = xn_fm.rearrange("(ec pp) t -> pp ec t", pp=128)
    wupre = W_upT.rearrange("(ec pp) f -> pp ec f", pp=128)
    wgtre = W_gateT.rearrange("(ec pp) f -> pp ec f", pp=128)
    wdnre = W_downT.rearrange("(fc pp) g -> pp fc g", pp=128)
    u_re = u_dram.rearrange("(fc pp) t -> pp fc t", pp=128)
    w_re = w_dram.rearrange("(fc pp) t -> pp fc t", pp=128)

    ab_scope = p.scope()
    ab_scope.__enter__()
    ybf = p.sb("ybf", [128, nech, T], BF16)
    with p.scope():
        suh = p.sb("suh", [128, 16, 128], BF16)
        for ec in range(nech):
            p.exec("vector", lambda e, ec=ec: e.tensor_scalar_mul(
                suh[:, ec, :], h128sb[:], sug[:, ec:ec + 1]),
                reads=[h128sb, sug], writes=[suh])
        TS = min(512, T)
        nts = T // TS
        xnb = [p.sb(f"xnb{i}", [128, nech, TS], BF16) for i in range(2)]
        yf = p.sb("yf", [128, nech, TS], F32)
        yg = p.sb("yg", [128, nech, TS], F32)
        ps_a = [p.ps(f"ps_a{i}", [128, TS]) for i in range(2)]
        for tsl in range(nts):
            xb = xnb[tsl % 2]
            for e0 in range(0, nech, 8):
                p.dma("sync", xb[:, e0:e0 + 8, :], xre[:, e0:e0 + 8, tsl * TS:(tsl + 1) * TS],
                      writes=[xb])
            for ec in range(nech):
                pa = ps_a[ec % 2]
                p.exec("tensor", lambda e, ec=ec, xb=xb, pa=pa: e.matmul(
                    pa[:], suh[:, ec, :], xb[:, ec, :], start=True, stop=True),
                    reads=[suh, xb], writes=[pa])
                p.exec("vector", lambda e, ec=ec, pa=pa: e.tensor_scalar_mul(
                    yf[:, ec, :], pa[:], float(1.0 / np.sqrt(E_DIM))),
                    reads=[pa], writes=[yf, f"{_key_of(yf)}@{ec}"])
            _butterfly(p, nech, yf, yg, final_out=ybf,
                       final_slice=(tsl * TS, (tsl + 1) * TS))

    with p.scope():
        TS = min(512, T)
        nts = T // TS
        wut = [p.sb(f"wut{i}", [128, nech, 128], BF16) for i in range(2)]
        wgt = [p.sb(f"wgt{i}", [128, nech, 128], BF16) for i in range(2)]
        sg = p.sb("sg", [128, TS], F32)
        ut = p.sb("ut", [128, TS], BF16)
        ps_up = p.ps("ps_up", [128, TS])
        ps_gt = p.ps("ps_gt", [128, TS])
        for fi in range(nfch):
            wu, wg = wut[fi % 2], wgt[fi % 2]
            for e0 in range(0, nech, 8):
                p.dma("gpsimd", wu[:, e0:e0 + 8, :],
                      wupre[:, e0:e0 + 8, fi * 128:(fi + 1) * 128], writes=[wu])
                p.dma("gpsimd", wg[:, e0:e0 + 8, :],
                      wgtre[:, e0:e0 + 8, fi * 128:(fi + 1) * 128], writes=[wg])
            for tsl in range(nts):
                for ec in range(nech):
                    p.exec("tensor", lambda e, ec=ec, tsl=tsl, wu=wu: e.matmul(
                        ps_up[:], wu[:, ec, :], ybf[:, ec, tsl * TS:(tsl + 1) * TS],
                        start=(ec == 0), stop=(ec == nech - 1)),
                        reads=[wu, f"{_key_of(ybf)}@{ec}"], writes=[ps_up])
                    p.exec("tensor", lambda e, ec=ec, tsl=tsl, wg=wg: e.matmul(
                        ps_gt[:], wg[:, ec, :], ybf[:, ec, tsl * TS:(tsl + 1) * TS],
                        start=(ec == 0), stop=(ec == nech - 1)),
                        reads=[wg, f"{_key_of(ybf)}@{ec}"], writes=[ps_gt])
                p.exec("scalar", lambda e, fi=fi: e.activation(
                    sg[:], ps_gt[:], AF.Silu, scale=wgt_s[:, fi:fi + 1]),
                    reads=[ps_gt, wgt_s], writes=[sg])
                p.exec("vector", lambda e, fi=fi: e.scalar_tensor_tensor(
                    ut[:], ps_up[:], upscsu[:, fi:fi + 1], sg[:], ALU.mult, ALU.mult),
                    reads=[ps_up, upscsu, sg], writes=[ut])
                p.dma("sync", u_re[:, fi, tsl * TS:(tsl + 1) * TS], ut[:], reads=[ut])
    ab_scope.__exit__(None, None, None)

    with p.scope():
        TC = min(256, T)
        ntc = T // TC
        ub = p.sb("ub", [128, nfch, TC], BF16)
        vb = p.sb("vb", [128, nfch, TC], F32)
        wt = p.sb("wt", [128, nfch, TC], BF16)
        ps_c = [p.ps(f"ps_c{i}", [128, TC]) for i in range(2)]
        kub, kvb, kwt = _key_of(ub), _key_of(vb), _key_of(wt)
        for tc in range(ntc):
            for f0 in range(0, nfch, 8):
                p.dma("sync", ub[:, f0:f0 + 8, :], u_re[:, f0:f0 + 8, tc * TC:(tc + 1) * TC],
                      writes=[ub] + [f"{kub}@{fc}" for fc in range(f0, f0 + 8)])
            for fc in range(nfch):
                pc = ps_c[fc % 2]
                p.exec("tensor", lambda e, fc=fc, pc=pc: e.matmul(
                    pc[:], h128sb[:], ub[:, fc, :], start=True, stop=True),
                    reads=[h128sb, f"{kub}@{fc}"], writes=[pc])
                p.exec("vector", lambda e, fc=fc, pc=pc: e.tensor_copy(vb[:, fc, :], pc[:]),
                       reads=[pc], writes=[vb, f"{kvb}@{fc}"])
            _butterfly(p, nfch, vb, ub, final_out=wt, final_slice=(0, TC))
            for f0 in range(0, nfch, 8):
                p.dma("gpsimd", w_re[:, f0:f0 + 8, tc * TC:(tc + 1) * TC], wt[:, f0:f0 + 8, :],
                      reads=[wt] + [f"{kwt}@{fc}" for fc in range(f0, f0 + 8)])

    with p.scope():
        TQ = min(1024, T)
        ntq = T // TQ
        nci = TQ // 128
        HW_ = min(512, TQ)
        nhalf = TQ // HW_
        wq = p.sb("wq", [128, nfch, TQ], BF16)
        wdt = [p.sb(f"wdt{i}", [128, 8, 128], BF16) for i in range(4)]
        dtile = [p.sb(f"dtile{i}", [128, TQ], F32) for i in range(2)]
        otile = [p.sb(f"otile{i}", [128, 128], F32) for i in range(4)]
        ps_d = [p.ps(f"ps_d{i}", [128, HW_]) for i in range(4)]
        ps_t = [p.ps(f"ps_t{i}", [128, 128]) for i in range(2)]
        kwq = _key_of(wq)
        nbat = 0
        for tq in range(ntq):
            for f0 in range(0, nfch, 8):
                p.dma("sync", wq[:, f0:f0 + 8, :], w_re[:, f0:f0 + 8, tq * TQ:(tq + 1) * TQ],
                      writes=[wq] + [f"{kwq}@{fc}" for fc in range(f0, f0 + 8)])
            for g in range(nech):
                pd = [ps_d[(g % 2) * 2], ps_d[(g % 2) * 2 + 1]]
                dt = dtile[g % 2]
                for f0 in range(0, nfch, 8):
                    wd = wdt[nbat % 4]
                    nbat += 1
                    p.dma("gpsimd", wd[:], wdnre[:, f0:f0 + 8, g * 128:(g + 1) * 128],
                          writes=[wd])
                    for k in range(8):
                        fc = f0 + k
                        for half in range(nhalf):
                            p.exec("tensor", lambda e, fc=fc, k=k, wd=wd, ph=pd[half], half=half: e.matmul(
                                ph[:], wd[:, k, :], wq[:, fc, half * HW_:(half + 1) * HW_],
                                start=(fc == 0), stop=(fc == nfch - 1)),
                                reads=[wd, f"{kwq}@{fc}"], writes=[pd[half]])
                for half in range(nhalf):
                    p.exec("vector", lambda e, g=g, ph=pd[half], dt=dt, half=half: e.tensor_scalar_mul(
                        dt[:, half * HW_:(half + 1) * HW_], ph[:], wdps[:, g:g + 1]),
                        reads=[pd[half], wdps], writes=[dt])
                for ci in range(nci):
                    pt = ps_t[ci % 2]
                    ot = otile[ci % 4]
                    p.exec("tensor", lambda e, ci=ci, pt=pt, dt=dt: e.transpose(
                        pt[:], dt[:, ci * 128:(ci + 1) * 128], idf[:]),
                        reads=[dt, idf], writes=[pt])
                    p.exec("vector", lambda e, ci=ci, pt=pt, ot=ot: e.tensor_copy(
                        ot[:], pt[:]), reads=[pt], writes=[ot])
                    p.dma("gpsimd", ffn_out[tq * TQ + ci * 128: tq * TQ + (ci + 1) * 128,
                                            g * 128:(g + 1) * 128],
                          ot[:], reads=[ot])
    p.prefix = _oldpfx
    if standalone:
        p.fence()
        return p.emit_and_finalize()
    return None


def build_ffn1(T, p=None, io=None):
    standalone = p is None
    if standalone:
        p = Prog("ffn1")
    _oldpfx = p.prefix
    if not standalone:
        p.prefix = "ffn1_" + p.prefix
    def _inp(name, shape, dtype):
        if io is not None and name in io:
            return io[name]
        return p.inp(name, shape, dtype)
    def _outp(name, shape, dtype):
        if io is not None and name in io:
            return io[name]
        return p.outp(name, shape, dtype)
    xn_fm = _inp("xn_fm", [E_DIM, T], BF16)
    w_gate = _inp("w_gate_p", [E_DIM, INTER], F32)
    w_up = _inp("w_up_p", [E_DIM, INTER], F32)
    w_down = _inp("w_down_p", [INTER, E_DIM], F32)
    identf = _inp("identf32", [128, 128], F32)
    ffn_out = _outp("ffn1_tm", [T, E_DIM], F32)

    h_dram = p.dram("h_scr", [INTER, T], F32)
    hs_dram = p.dram("hs_scr", [INTER, T], BF16)
    nech = E_DIM // 128
    nfch = INTER // 128

    idf = p.sb("idf", [128, 128], F32)
    p.dma("sync", idf[:], identf, writes=[idf])
    xre = xn_fm.rearrange("(ec pp) t -> pp ec t", pp=128)
    wgre = w_gate.rearrange("(ec pp) f -> pp ec f", pp=128)
    wure = w_up.rearrange("(ec pp) f -> pp ec f", pp=128)
    wdre = w_down.rearrange("(fc pp) g -> pp fc g", pp=128)
    h_re = h_dram.rearrange("(fc pp) t -> pp fc t", pp=128)
    hs_re = hs_dram.rearrange("(fc pp) t -> pp fc t", pp=128)

    with p.scope():
        TS = min(512, T)
        nts = T // TS
        xnsb = p.sb("xnsb", [128, nech, T], BF16)
        for e0 in range(0, nech, 4):
            p.dma("sync", xnsb[:, e0:e0 + 4, :], xre[:, e0:e0 + 4, :], writes=[xnsb])
        wut = [p.sb(f"wut{i}", [128, nech, 128], BF16) for i in range(2)]
        wgt = [p.sb(f"wgt{i}", [128, nech, 128], BF16) for i in range(2)]
        sg = p.sb("sg", [128, TS], F32)
        ht = p.sb("ht", [128, TS], F32)
        ps_up = p.ps("ps_up", [128, TS])
        ps_gt = p.ps("ps_gt", [128, TS])
        for fi in range(nfch):
            wu, wg = wut[fi % 2], wgt[fi % 2]
            for e0 in range(0, nech, 8):
                p.dma("gpsimd", wu[:, e0:e0 + 8, :],
                      wure[:, e0:e0 + 8, fi * 128:(fi + 1) * 128], writes=[wu])
                p.dma("gpsimd", wg[:, e0:e0 + 8, :],
                      wgre[:, e0:e0 + 8, fi * 128:(fi + 1) * 128], writes=[wg])
            for tsl in range(nts):
                for ec in range(nech):
                    p.exec("tensor", lambda e, ec=ec, tsl=tsl, wu=wu, TS=TS: e.matmul(
                        ps_up[:], wu[:, ec, :], xnsb[:, ec, tsl * TS:(tsl + 1) * TS],
                        start=(ec == 0), stop=(ec == nech - 1)),
                        reads=[wu, xnsb], writes=[ps_up])
                    p.exec("tensor", lambda e, ec=ec, tsl=tsl, wg=wg, TS=TS: e.matmul(
                        ps_gt[:], wg[:, ec, :], xnsb[:, ec, tsl * TS:(tsl + 1) * TS],
                        start=(ec == 0), stop=(ec == nech - 1)),
                        reads=[wg, xnsb], writes=[ps_gt])
                p.exec("scalar", lambda e: e.activation(sg[:], ps_gt[:], AF.Silu),
                       reads=[ps_gt], writes=[sg])
                p.exec("vector", lambda e: e.tensor_mul(ht[:], ps_up[:], sg[:]),
                       reads=[ps_up, sg], writes=[ht])
                p.dma("sync", h_re[:, fi, tsl * TS:(tsl + 1) * TS], ht[:], reads=[ht])

    with p.scope():
        TS = min(512, T)
        nts = T // TS
        h4_ = [p.sb(f"h4_{i}", [128, 4, TS], F32) for i in range(2)]
        a4_ = [p.sb(f"a4_{i}", [128, 4, TS], F32) for i in range(2)]
        b6_ = [p.sb(f"b6_{i}", [128, 6, TS], F32) for i in range(2)]
        t2_ = [p.sb(f"t2_{i}", [128, 2, TS], F32) for i in range(2)]
        hs4_ = [p.sb(f"hs4_{i}", [128, 4, TS], BF16) for i in range(2)]
        pairs = [(0, 1), (0, 2), (0, 3), (1, 2), (1, 3), (2, 3)]

        def _sparsify_group(h4, a4, b6, t2, hs4, fcs, tsl):
            for m, fc in enumerate(fcs):
                p.dma("sync", h4[:, m, :], h_re[:, fc, tsl * TS:(tsl + 1) * TS],
                      writes=[h4])
            for m in range(4):
                p.exec("scalar", lambda e, m=m: e.activation(
                    a4[:, m, :], h4[:, m, :], AF.Abs), reads=[h4], writes=[a4])
            for bi, (i, j) in enumerate(pairs):
                p.exec("vector", lambda e, bi=bi, i=i, j=j: e.tensor_tensor(
                    b6[:, bi, :], a4[:, i, :], a4[:, j, :], ALU.is_ge),
                    reads=[a4], writes=[b6])
            p.exec("vector", lambda e: e.tensor_add(t2[:, 0, :], b6[:, 0, :], b6[:, 1, :]),
                   reads=[b6], writes=[t2])
            p.exec("vector", lambda e: e.tensor_add(t2[:, 0, :], t2[:, 0, :], b6[:, 2, :]),
                   reads=[b6, t2], writes=[t2])
            p.exec("vector", lambda e: e.scalar_tensor_tensor(
                hs4[:, 0, :], t2[:, 0, :], 1.5, h4[:, 0, :], ALU.is_ge, ALU.mult),
                reads=[t2, h4], writes=[hs4])
            p.exec("vector", lambda e: e.tensor_add(t2[:, 1, :], b6[:, 3, :], b6[:, 4, :]),
                   reads=[b6], writes=[t2])
            p.exec("vector", lambda e: e.tensor_sub(t2[:, 1, :], t2[:, 1, :], b6[:, 0, :]),
                   reads=[b6, t2], writes=[t2])
            p.exec("vector", lambda e: e.scalar_tensor_tensor(
                hs4[:, 1, :], t2[:, 1, :], 0.5, h4[:, 1, :], ALU.is_ge, ALU.mult),
                reads=[t2, h4], writes=[hs4])
            p.exec("vector", lambda e: e.tensor_sub(t2[:, 0, :], b6[:, 5, :], b6[:, 1, :]),
                   reads=[b6], writes=[t2])
            p.exec("vector", lambda e: e.tensor_sub(t2[:, 0, :], t2[:, 0, :], b6[:, 3, :]),
                   reads=[b6, t2], writes=[t2])
            p.exec("vector", lambda e: e.scalar_tensor_tensor(
                hs4[:, 2, :], t2[:, 0, :], -0.5, h4[:, 2, :], ALU.is_ge, ALU.mult),
                reads=[t2, h4], writes=[hs4])
            p.exec("vector", lambda e: e.tensor_add(t2[:, 1, :], b6[:, 2, :], b6[:, 4, :]),
                   reads=[b6], writes=[t2])
            p.exec("vector", lambda e: e.tensor_add(t2[:, 1, :], t2[:, 1, :], b6[:, 5, :]),
                   reads=[b6, t2], writes=[t2])
            p.exec("vector", lambda e: e.scalar_tensor_tensor(
                hs4[:, 3, :], t2[:, 1, :], 1.5, h4[:, 3, :], ALU.is_le, ALU.mult),
                reads=[t2, h4], writes=[hs4])
            for m, fc in enumerate(fcs):
                p.dma("gpsimd", hs_re[:, fc, tsl * TS:(tsl + 1) * TS], hs4[:, m, :],
                      reads=[hs4])

        for tsl in range(nts):
            for grp in range(16):
                par = (tsl * 16 + grp) % 2
                _sparsify_group(h4_[par], a4_[par], b6_[par], t2_[par], hs4_[par],
                                [grp, grp + 16, grp + 32, grp + 48], tsl)

    with p.scope():
        TQ = min(1024, T)
        ntq = T // TQ
        nci = TQ // 128
        HW_ = min(512, TQ)
        nhalf = TQ // HW_
        wq = p.sb("wq", [128, nfch, TQ], BF16)
        wdt = [p.sb(f"wdt{i}", [128, 8, 128], BF16) for i in range(4)]
        dtile = [p.sb(f"dtile{i}", [128, TQ], F32) for i in range(2)]
        otile = [p.sb(f"otile{i}", [128, 128], F32) for i in range(4)]
        ps_d = [p.ps(f"ps_d{i}", [128, HW_]) for i in range(4)]
        ps_t = [p.ps(f"ps_t{i}", [128, 128]) for i in range(2)]
        kwq = _key_of(wq)
        nbat = 0
        for tq in range(ntq):
            for f0 in range(0, nfch, 8):
                p.dma("sync", wq[:, f0:f0 + 8, :], hs_re[:, f0:f0 + 8, tq * TQ:(tq + 1) * TQ],
                      writes=[wq] + [f"{kwq}@{fc}" for fc in range(f0, f0 + 8)])
            for g in range(nech):
                pd = [ps_d[(g % 2) * 2], ps_d[(g % 2) * 2 + 1]]
                dt = dtile[g % 2]
                for f0 in range(0, nfch, 8):
                    wd = wdt[nbat % 4]
                    nbat += 1
                    p.dma("gpsimd", wd[:], wdre[:, f0:f0 + 8, g * 128:(g + 1) * 128],
                          writes=[wd])
                    for k in range(8):
                        fc = f0 + k
                        for half in range(nhalf):
                            p.exec("tensor", lambda e, fc=fc, k=k, wd=wd, ph=pd[half], half=half: e.matmul(
                                ph[:], wd[:, k, :], wq[:, fc, half * HW_:(half + 1) * HW_],
                                start=(fc == 0), stop=(fc == nfch - 1)),
                                reads=[wd, f"{kwq}@{fc}"], writes=[pd[half]])
                for half in range(nhalf):
                    p.exec("vector", lambda e, ph=pd[half], dt=dt, half=half: e.tensor_copy(
                        dt[:, half * HW_:(half + 1) * HW_], ph[:]),
                        reads=[pd[half]], writes=[dt])
                for ci in range(nci):
                    pt = ps_t[ci % 2]
                    ot = otile[ci % 4]
                    p.exec("tensor", lambda e, ci=ci, pt=pt, dt=dt: e.transpose(
                        pt[:], dt[:, ci * 128:(ci + 1) * 128], idf[:]),
                        reads=[dt, idf], writes=[pt])
                    p.exec("vector", lambda e, ci=ci, pt=pt, ot=ot: e.tensor_copy(
                        ot[:], pt[:]), reads=[pt], writes=[ot])
                    p.dma("gpsimd", ffn_out[tq * TQ + ci * 128: tq * TQ + (ci + 1) * 128,
                                            g * 128:(g + 1) * 128],
                          ot[:], reads=[ot])
    p.prefix = _oldpfx
    if standalone:
        p.fence()
        return p.emit_and_finalize()
    return None


def build_resln(T, with_ln=True, nadd=2, p=None, io=None):
    standalone = p is None
    if standalone:
        p = Prog("resln")
    _oldpfx = p.prefix
    if not standalone:
        p.prefix = "resln_" + p.prefix
    def _inp(name, shape, dtype):
        if io is not None and name in io:
            return io[name]
        return p.inp(name, shape, dtype)
    def _outp(name, shape, dtype):
        if io is not None and name in io:
            return io[name]
        return p.outp(name, shape, dtype)
    x = _inp("x", [T, E_DIM], F32)
    adds = [_inp(f"add{i}", [T, E_DIM], F32) for i in range(nadd)]
    x1_tm = _outp("x1_tm", [T, E_DIM], F32)
    if with_ln:
        identbf = _inp("identbf", [128, 128], BF16)
        constpm = _inp("constpm", [128, 2], F32)
        xn_fm = _outp("xn_fm", [E_DIM, T], BF16)
        idb = p.sb("idb", [128, 128], BF16)
        cp = p.sb("cp", [128, 2], F32)
        p.dma("sync", idb[:], identbf, writes=[idb])
        p.dma("sync", cp[:], constpm, writes=[cp])
        xn_re = xn_fm.rearrange("(ec pp) t -> ec pp t", pp=128)
    nch = T // 128
    nech = E_DIM // 128

    xin = [p.sb(f"xin{i}", [128, E_DIM], F32) for i in range(2)]
    ain = [p.sb(f"ain{i}", [128, E_DIM], F32) for i in range(2)]
    bin_ = [p.sb(f"bin{i}", [128, E_DIM], F32) for i in range(2)]
    x1b = [p.sb(f"x1b{i}", [128, E_DIM], F32) for i in range(2)]
    if with_ln:
        xsq = p.sb("xsq", [128, E_DIM], F32)
        stats = p.sb("stats", [128, 8], F32)
        xn_tm = p.sb("xn_tm", [128, E_DIM], BF16)
        tbuf = [p.sb(f"tbuf{i}", [128, nech, 128], BF16) for i in range(2)]
        pstr = [p.ps(f"pstr{i}", [128, 128], BF16) for i in range(2)]

    for c in range(nch):
        b = c % 2
        xb, ab, bb, x1 = xin[b], ain[b], bin_[b], x1b[b]
        p.dma("sync", xb[:], x[c * 128:(c + 1) * 128, :], writes=[xb])
        p.dma("sync", ab[:], adds[0][c * 128:(c + 1) * 128, :], writes=[ab])
        if nadd > 1:
            p.dma("sync", bb[:], adds[1][c * 128:(c + 1) * 128, :], writes=[bb])
        p.exec("vector", lambda e, xb=xb, ab=ab, x1=x1: e.tensor_add(x1[:], xb[:], ab[:]),
               reads=[xb, ab], writes=[x1])
        if nadd > 1:
            p.exec("vector", lambda e, bb=bb, x1=x1: e.tensor_add(x1[:], x1[:], bb[:]),
                   reads=[bb, x1], writes=[x1])
        p.dma("gpsimd", x1_tm[c * 128:(c + 1) * 128, :], x1[:], reads=[x1])
        if not with_ln:
            continue
        p.exec("scalar", lambda e, x1=x1: e.square(xsq[:], x1[:]), reads=[x1], writes=[xsq])
        p.exec("vector", lambda e, x1=x1: e.reduce_sum(stats[:, 0:1], x1[:], axis=mybir.AxisListType.X),
               reads=[x1], writes=[stats])
        p.exec("vector", lambda e: e.reduce_sum(stats[:, 1:2], xsq[:], axis=mybir.AxisListType.X),
               reads=[xsq], writes=[stats])
        p.exec("vector", lambda e: e.tensor_scalar_mul(stats[:, 2:3], stats[:, 0:1], -1.0 / E_DIM),
               reads=[stats], writes=[stats])
        p.exec("vector", lambda e: e.tensor_scalar_mul(stats[:, 3:4], stats[:, 1:2], 1.0 / E_DIM),
               reads=[stats], writes=[stats])
        p.exec("vector", lambda e: e.tensor_mul(stats[:, 4:5], stats[:, 2:3], stats[:, 2:3]),
               reads=[stats], writes=[stats])
        p.exec("vector", lambda e: e.tensor_sub(stats[:, 5:6], stats[:, 3:4], stats[:, 4:5]),
               reads=[stats], writes=[stats])
        p.exec("scalar", lambda e: e.activation(stats[:, 6:7], stats[:, 5:6], AF.Ln, bias=cp[:, 0:1]),
               reads=[stats, cp], writes=[stats])
        p.exec("scalar", lambda e: e.activation(stats[:, 7:8], stats[:, 6:7], AF.Exp, scale=cp[:, 1:2]),
               reads=[stats, cp], writes=[stats])
        p.exec("vector", lambda e: e.tensor_mul(stats[:, 4:5], stats[:, 2:3], stats[:, 7:8]),
               reads=[stats], writes=[stats])
        p.exec("vector", lambda e, x1=x1: e.tensor_scalar(
            xn_tm[:], x1[:], stats[:, 7:8], stats[:, 4:5], ALU.mult, ALU.add),
            reads=[x1, stats], writes=[xn_tm])
        tb = tbuf[b]
        for ec in range(nech):
            pb = pstr[ec % 2]
            p.exec("tensor", lambda e, ec=ec, pb=pb: e.transpose(
                pb[:], xn_tm[:, ec * 128:(ec + 1) * 128], idb[:]),
                reads=[xn_tm, idb], writes=[pb])
            p.exec("vector", lambda e, ec=ec, pb=pb, tb=tb: e.tensor_copy(tb[:, ec, :], pb[:]),
                   reads=[pb], writes=[tb])
            p.dma("gpsimd", xn_re[ec, :, c * 128:(c + 1) * 128], tb[:, ec, :],
                  reads=[tb])
    p.prefix = _oldpfx
    if standalone:
        p.fence()
        return p.emit_and_finalize()
    return None


def build_attn0(T, p=None, io=None):
    standalone = p is None
    if standalone:
        p = Prog("attn0")
    _oldpfx = p.prefix
    if not standalone:
        p.prefix = "attn0_" + p.prefix
    def _inp(name, shape, dtype):
        if io is not None and name in io:
            return io[name]
        return p.inp(name, shape, dtype)
    def _outp(name, shape, dtype):
        if io is not None and name in io:
            return io[name]
        return p.outp(name, shape, dtype)
    qr_fm = _inp("qr_fm", [E_DIM, T], BF16)
    kr_fm = _inp("kr_fm", [E_DIM, T], BF16)
    kr_tm = _inp("kr_tm", [T, E_DIM], BF16)
    v_tm = _inp("v_tm", [T, E_DIM], BF16)
    Wo = _inp("Wo", [E_DIM, E_DIM], F32)
    rms_w_row = _inp("rms_w_row", [1, E_DIM], F32)
    dmaskT = _inp("dmaskT", [H, 128, 128], F32)
    qdec = _inp("qdec", [128, H], F32)
    kdec = _inp("kdec", [128, H], F32)
    rC = _inp("rC", [128, H], F32)
    identbf = _inp("identbf", [128, 128], BF16)
    ones128f = _inp("ones128f", [1, 128], F32)
    constpm = _inp("constpm", [128, 2], F32)
    attn_out = _outp("attn0_tm", [T, E_DIM], F32)
    nch = T // 128
    nech = E_DIM // 128

    idb = p.sb("idb", [128, 128], BF16)
    cp = p.sb("cp", [128, 2], F32)
    wsb = p.sb("wsb", [128, nech, E_DIM], BF16)
    dm = p.sb("dm", [128, H, 128], F32)
    qd = p.sb("qd", [128, H], F32)
    kd = p.sb("kd", [128, H], F32)
    rc = p.sb("rc", [128, H], F32)
    ones_sb = p.sb("ones_sb", [1, 128], F32)
    rmsrow = p.sb("rmsrow", [1, E_DIM], F32)
    rmsrep = p.sb("rmsrep", [128, E_DIM], F32)

    p.dma("sync", idb[:], identbf, writes=[idb])
    p.dma("sync", cp[:], constpm, writes=[cp])
    for h_ in range(H):
        p.dma("sync", dm[:, h_, :], dmaskT[h_], writes=[dm])
    p.dma("sync", qd[:], qdec, writes=[qd])
    p.dma("sync", kd[:], kdec, writes=[kd])
    p.dma("sync", rc[:], rC, writes=[rc])
    p.dma("sync", ones_sb[:], ones128f, writes=[ones_sb])
    p.dma("sync", rmsrow[:], rms_w_row, writes=[rmsrow])
    load_weight_bf16(p, wsb, Wo, E_DIM, E_DIM)

    qre = qr_fm.rearrange("(h d) t -> d h t", d=128)
    kre = kr_fm.rearrange("(h d) t -> d h t", d=128)

    ktm = [p.sb(f"ktm{i}", [128, E_DIM], BF16) for i in range(2)]
    vtm = [p.sb(f"vtm{i}", [128, E_DIM], BF16) for i in range(2)]
    qfm = [p.sb(f"qfm{i}", [128, H, 128], BF16) for i in range(2)]
    kfm = [p.sb(f"kfm{i}", [128, H, 128], BF16) for i in range(2)]
    kv32 = [[p.sb(f"kv32_{h}_{i}", [128, 128], F32) for i in range(2)] for h in range(H)]
    kvbf = [p.sb(f"kvbf_{h}", [128, 128], BF16) for h in range(H)]
    kdt = [p.sb(f"kdt{i}", [128, 128], BF16) for i in range(2)]
    smb = [p.sb(f"smb{i}", [128, 128], BF16) for i in range(2)]
    tmpo = [p.sb(f"tmpo{i}", [128, 128], F32) for i in range(2)]
    attn_tm = p.sb("attn_tm", [128, E_DIM], F32)
    asq = p.sb("asq", [128, E_DIM], F32)
    st8 = p.sb("st8", [128, 8], F32)
    anb = p.sb("anb", [128, E_DIM], BF16)
    afm = p.sb("afm", [128, nech, 128], BF16)
    outc = p.sb("outc", [128, E_DIM], F32)

    ps_st = p.ps("ps_st", [128, 128])
    ps_o = p.ps("ps_o", [128, 128])
    ps_i = p.ps("ps_i", [128, 128])
    ps_kv = p.ps("ps_kv", [128, 128])
    pstr = [p.ps(f"pstr{i}", [128, 128], BF16) for i in range(2)]
    ps_w = [p.ps(f"ps_w{i}", [128, 512]) for i in range(2)]

    for sl in range(4):
        pw = ps_w[sl % 2]
        p.exec("tensor", lambda e, sl=sl, pw=pw: e.matmul(
            pw[:], ones_sb[:], rmsrow[:, sl * 512:(sl + 1) * 512], start=True, stop=True),
            reads=[ones_sb, rmsrow], writes=[pw])
        p.exec("vector", lambda e, sl=sl, pw=pw: e.tensor_copy(
            rmsrep[:, sl * 512:(sl + 1) * 512], pw[:]), reads=[pw], writes=[rmsrep])

    for h in range(H):
        p.exec("vector", lambda e, h=h: e.memset(kv32[h][0][:], 0.0), writes=[kv32[h][0]])
        p.exec("gpsimd", lambda e, h=h: e.memset(kvbf[h][:], 0.0), writes=[kvbf[h]])

    for c in range(nch):
        b = c % 2
        kt, vt, qf, kf = ktm[b], vtm[b], qfm[b], kfm[b]
        p.dma("sync", kt[:], kr_tm[c * 128:(c + 1) * 128, :], writes=[kt])
        p.dma("sync", vt[:], v_tm[c * 128:(c + 1) * 128, :], writes=[vt])
        for h in range(H):
            p.dma("sync", qf[:, h, :], qre[:, h, c * 128:(c + 1) * 128], writes=[qf])
            p.dma("sync", kf[:, h, :], kre[:, h, c * 128:(c + 1) * 128], writes=[kf])
        for h in range(H):
            hb = h % 2
            kvp = kv32[h][c % 2]
            kvn = kv32[h][(c + 1) % 2]
            p.exec("vector", lambda e, h=h, hb=hb, kt=kt: e.tensor_scalar_mul(
                kdt[hb][:], kt[:, h * 128:(h + 1) * 128], kd[:, h:h + 1]),
                reads=[kt, kd], writes=[kdt[hb]])
            p.exec("tensor", lambda e, h=h, qf=qf, kf=kf: e.matmul(
                ps_st[:], kf[:, h, :], qf[:, h, :], start=True, stop=True),
                reads=[kf, qf], writes=[ps_st])
            p.exec("vector", lambda e, h=h, hb=hb: e.tensor_tensor(
                smb[hb][:], ps_st[:], dm[:, h, :], ALU.mult),
                reads=[ps_st, dm], writes=[smb[hb]])
            p.exec("tensor", lambda e, h=h, hb=hb, vt=vt: e.matmul(
                ps_o[:], smb[hb][:], vt[:, h * 128:(h + 1) * 128], start=True, stop=True),
                reads=[smb[hb], vt], writes=[ps_o])
            p.exec("tensor", lambda e, h=h, qf=qf: e.matmul(
                ps_i[:], qf[:, h, :], kvbf[h][:], start=True, stop=True),
                reads=[qf, kvbf[h]], writes=[ps_i])
            p.exec("scalar", lambda e, hb=hb: e.copy(tmpo[hb][:], ps_o[:]),
                   reads=[ps_o], writes=[tmpo[hb]])
            p.exec("vector", lambda e, h=h, hb=hb: e.scalar_tensor_tensor(
                attn_tm[:, h * 128:(h + 1) * 128], ps_i[:], qd[:, h:h + 1], tmpo[hb][:],
                ALU.mult, ALU.add),
                reads=[ps_i, qd, tmpo[hb]], writes=[attn_tm])
            p.exec("tensor", lambda e, h=h, hb=hb, vt=vt: e.matmul(
                ps_kv[:], kdt[hb][:], vt[:, h * 128:(h + 1) * 128], start=True, stop=True),
                reads=[kdt[hb], vt], writes=[ps_kv])
            p.exec("vector", lambda e, h=h, kvp=kvp, kvn=kvn: e.scalar_tensor_tensor(
                kvn[:], kvp[:], rc[:, h:h + 1], ps_kv[:], ALU.mult, ALU.add),
                reads=[kvp, rc, ps_kv], writes=[kvn])
            p.exec("scalar", lambda e, h=h, kvn=kvn: e.copy(kvbf[h][:], kvn[:]),
                   reads=[kvn], writes=[kvbf[h]])
        p.exec("scalar", lambda e: e.square(asq[:], attn_tm[:]), reads=[attn_tm], writes=[asq])
        p.exec("vector", lambda e: e.reduce_sum(st8[:, 0:1], asq[:], axis=mybir.AxisListType.X),
               reads=[asq], writes=[st8])
        p.exec("vector", lambda e: e.tensor_scalar_mul(st8[:, 1:2], st8[:, 0:1], 1.0 / E_DIM),
               reads=[st8], writes=[st8])
        p.exec("scalar", lambda e: e.activation(st8[:, 2:3], st8[:, 1:2], AF.Ln, bias=cp[:, 0:1]),
               reads=[st8, cp], writes=[st8])
        p.exec("scalar", lambda e: e.activation(st8[:, 3:4], st8[:, 2:3], AF.Exp, scale=cp[:, 1:2]),
               reads=[st8, cp], writes=[st8])
        p.exec("vector", lambda e: e.scalar_tensor_tensor(
            anb[:], attn_tm[:], st8[:, 3:4], rmsrep[:], ALU.mult, ALU.mult),
            reads=[attn_tm, st8, rmsrep], writes=[anb])
        for ec in range(nech):
            pb = pstr[ec % 2]
            p.exec("tensor", lambda e, ec=ec, pb=pb: e.transpose(
                pb[:], anb[:, ec * 128:(ec + 1) * 128], idb[:]),
                reads=[anb, idb], writes=[pb])
            p.exec("vector", lambda e, ec=ec, pb=pb: e.tensor_copy(afm[:, ec, :], pb[:]),
                   reads=[pb], writes=[afm])
        for sl in range(4):
            pw = ps_w[sl % 2]
            for e_ in range(nech):
                p.exec("tensor", lambda en, sl=sl, e_=e_, pw=pw: en.matmul(
                    pw[:], afm[:, e_, :], wsb[:, e_, sl * 512:(sl + 1) * 512],
                    start=(e_ == 0), stop=(e_ == nech - 1)),
                    reads=[afm, wsb], writes=[pw])
            p.exec("scalar", lambda en, sl=sl, pw=pw: en.copy(
                outc[:, sl * 512:(sl + 1) * 512], pw[:]), reads=[pw], writes=[outc])
        p.dma("gpsimd", attn_out[c * 128:(c + 1) * 128, :], outc[:], reads=[outc])
    p.prefix = _oldpfx
    if standalone:
        p.fence()
        return p.emit_and_finalize()
    return None


def build_qkv0(T, p=None, io=None):
    standalone = p is None
    if standalone:
        p = Prog("qkv0")
    _oldpfx = p.prefix
    if not standalone:
        p.prefix = "qkv0_" + p.prefix
    def _inp(name, shape, dtype):
        if io is not None and name in io:
            return io[name]
        return p.inp(name, shape, dtype)
    def _outp(name, shape, dtype):
        if io is not None and name in io:
            return io[name]
        return p.outp(name, shape, dtype)
    xn_fm = _inp("xn_fm", [E_DIM, T], BF16)
    Ws = {n: _inp(n, [E_DIM, E_DIM], F32) for n in ("Wq", "Wk", "Wv")}
    rotc = _inp("rotc", [T // 128, 128, 64], F32)
    rots = _inp("rots", [T // 128, 128, 64], F32)
    identbf = _inp("identbf", [128, 128], BF16)
    outs = {
        "q": (_outp("qr_fm", [E_DIM, T], BF16), None),
        "k": (_outp("kr_fm", [E_DIM, T], BF16), _outp("kr_tm", [T, E_DIM], BF16)),
        "v": (None, _outp("v_tm", [T, E_DIM], BF16)),
    }
    nch = T // 128
    nech = E_DIM // 128

    idb = p.sb("idb", [128, 128], BF16)
    p.dma("sync", idb[:], identbf, writes=[idb])
    wsb = p.sb("wsb", [128, nech, E_DIM], BF16)
    xnc = [p.sb(f"xnc{i}", [128, nech, 128], BF16) for i in range(2)]
    pf32_ = [p.sb(f"pf32_{i}", [128, E_DIM], F32) for i in range(2)]
    tmpA_ = [p.sb(f"tmpA_{i}", [128, H, 64], F32) for i in range(2)]
    tmpB_ = [p.sb(f"tmpB_{i}", [128, H, 64], F32) for i in range(2)]
    ptm_ = [p.sb(f"ptm_{i}", [128, E_DIM], BF16) for i in range(2)]
    tbuf = [p.sb(f"tbuf{i}", [128, nech, 128], BF16) for i in range(2)]
    cosc = p.sb("cosc", [128, 64], F32)
    sinc = p.sb("sinc", [128, 64], F32)
    psa = [p.ps(f"psa{i}", [128, 512]) for i in range(4)]
    pstr = [p.ps(f"pstr{i}", [128, 128], BF16) for i in range(2)]

    xre = xn_fm.rearrange("(ec pp) t -> pp ec t", pp=128)

    for tname in ("q", "k", "v"):
        W = Ws["W" + tname]
        fm_out, tm_out = outs[tname]
        load_weight_bf16(p, wsb, W, E_DIM, E_DIM)
        fm_re = fm_out.rearrange("(ec pp) t -> ec pp t", pp=128) if fm_out is not None else None
        pending_tr = None
        for c in range(nch):
            xb = xnc[c % 2]
            pf32 = pf32_[c % 2]
            tmpA = tmpA_[c % 2]
            tmpB = tmpB_[c % 2]
            ptm = ptm_[c % 2]
            for e0 in range(0, nech, 8):
                p.dma("sync", xb[:, e0:e0 + 8, :], xre[:, e0:e0 + 8, c * 128:(c + 1) * 128],
                      writes=[xb])
            for e in range(nech):
                for sl in range(4):
                    p.exec("tensor", lambda en, e=e, sl=sl, xb=xb: en.matmul(
                        psa[sl][:], xb[:, e, :], wsb[:, e, sl * 512:(sl + 1) * 512],
                        start=(e == 0), stop=(e == nech - 1)),
                        reads=[xb, wsb], writes=[psa[sl]])
            if tname == "v":
                for sl in range(4):
                    p.exec("vector", lambda en, sl=sl, ptm=ptm: en.tensor_copy(
                        ptm[:, sl * 512:(sl + 1) * 512], psa[sl][:]),
                        reads=[psa[sl]], writes=[ptm])
            else:
                for sl in range(4):
                    p.exec("vector", lambda en, sl=sl, pf32=pf32: en.tensor_copy(
                        pf32[:, sl * 512:(sl + 1) * 512], psa[sl][:]),
                        reads=[psa[sl]], writes=[pf32])
                p.dma("sync", cosc[:], rotc[c], writes=[cosc])
                p.dma("sync", sinc[:], rots[c], writes=[sinc])
                v4 = pf32.ap().rearrange("pp (h two d) -> pp h two d", h=H, two=2)
                o4 = ptm.ap().rearrange("pp (h two d) -> pp h two d", h=H, two=2)
                cb = cosc.ap().rearrange("pp (o d) -> pp o d", o=1).broadcast_to([128, H, 64])
                sb_ = sinc.ap().rearrange("pp (o d) -> pp o d", o=1).broadcast_to([128, H, 64])
                x1 = v4[:, :, 0, :]
                x2 = v4[:, :, 1, :]
                p.exec("vector", lambda en, x1=x1, cb=cb, tmpA=tmpA: en.tensor_tensor(tmpA.ap(), x1, cb, ALU.mult),
                       reads=[pf32, cosc], writes=[tmpA])
                p.exec("vector", lambda en, x2=x2, sb_=sb_, tmpB=tmpB: en.tensor_tensor(tmpB.ap(), x2, sb_, ALU.mult),
                       reads=[pf32, sinc], writes=[tmpB])
                p.exec("vector", lambda en, o4=o4, tmpA=tmpA, tmpB=tmpB: en.tensor_tensor(
                    o4[:, :, 0, :], tmpA.ap(), tmpB.ap(), ALU.add),
                    reads=[tmpA, tmpB], writes=[ptm])
                p.exec("vector", lambda en, x2=x2, cb=cb, tmpA=tmpA: en.tensor_tensor(tmpA.ap(), x2, cb, ALU.mult),
                       reads=[pf32, cosc], writes=[tmpA])
                p.exec("vector", lambda en, x1=x1, sb_=sb_, tmpB=tmpB: en.tensor_tensor(tmpB.ap(), x1, sb_, ALU.mult),
                       reads=[pf32, sinc], writes=[tmpB])
                p.exec("vector", lambda en, o4=o4, tmpA=tmpA, tmpB=tmpB: en.tensor_tensor(
                    o4[:, :, 1, :], tmpA.ap(), tmpB.ap(), ALU.subtract),
                    reads=[tmpA, tmpB], writes=[ptm])
            if tm_out is not None:
                p.dma("gpsimd", tm_out[c * 128:(c + 1) * 128, :], ptm[:], reads=[ptm])

            def _emit_tr(c, ptm):
                tb = tbuf[c % 2]
                for ec in range(nech):
                    pb = pstr[ec % 2]
                    p.exec("tensor", lambda en, ec=ec, pb=pb, ptm=ptm: en.transpose(
                        pb[:], ptm[:, ec * 128:(ec + 1) * 128], idb[:]),
                        reads=[ptm, idb], writes=[pb])
                    p.exec("vector", lambda en, ec=ec, pb=pb, tb=tb: en.tensor_copy(
                        tb[:, ec, :], pb[:]), reads=[pb], writes=[tb])
                    p.dma("gpsimd", fm_re[ec, :, c * 128:(c + 1) * 128], tb[:, ec, :],
                          reads=[tb])

            pending_new = (c, ptm) if fm_out is not None else None
            if pending_tr is not None:
                _emit_tr(*pending_tr)
            pending_tr = pending_new
        if pending_tr is not None:
            _emit_tr(*pending_tr)
            pending_tr = None
    p.prefix = _oldpfx
    if standalone:
        p.fence()
        return p.emit_and_finalize()
    return None
def build_ln(T, xname="x", outname="xn_fm", p=None, io=None):
    standalone = p is None
    if standalone:
        p = Prog("ln")
    _oldpfx = p.prefix
    if not standalone:
        p.prefix = "ln_" + p.prefix
    def _inp(name, shape, dtype):
        if io is not None and name in io:
            return io[name]
        return p.inp(name, shape, dtype)
    def _outp(name, shape, dtype):
        if io is not None and name in io:
            return io[name]
        return p.outp(name, shape, dtype)
    x = _inp(xname, [T, E_DIM], F32)
    xn_fm = _outp(outname, [E_DIM, T], BF16)
    identbf = _inp("identbf", [128, 128], BF16)
    constpm = _inp("constpm", [128, 2], F32)
    nch = T // 128
    nech = E_DIM // 128

    xin = [p.sb(f"xin{i}", [128, E_DIM], F32) for i in range(2)]
    xsq = p.sb("xsq", [128, E_DIM], F32)
    stats = p.sb("stats", [128, 8], F32)
    xn_tm = p.sb("xn_tm", [128, E_DIM], BF16)
    tbuf = [p.sb(f"tbuf{i}", [128, nech, 128], BF16) for i in range(2)]
    idb = p.sb("idb", [128, 128], BF16)
    pstr = [p.ps(f"pstr{i}", [128, 128], BF16) for i in range(2)]

    cp = p.sb("cp", [128, 2], F32)
    p.dma("sync", cp[:], constpm, writes=[cp])
    p.dma("sync", idb[:], identbf, writes=[idb])
    xn_re = xn_fm.rearrange("(ec pp) t -> ec pp t", pp=128)

    for c in range(nch):
        b = c % 2
        xb = xin[b]
        p.dma("sync", xb[:], x[c * 128:(c + 1) * 128, :], writes=[xb])
        p.exec("scalar", lambda e, i=xb: e.square(xsq[:], i[:]), reads=[xb], writes=[xsq])
        p.exec("vector", lambda e, i=xb: e.reduce_sum(stats[:, 0:1], i[:], axis=mybir.AxisListType.X),
               reads=[xb], writes=[stats])
        p.exec("vector", lambda e: e.reduce_sum(stats[:, 1:2], xsq[:], axis=mybir.AxisListType.X),
               reads=[xsq], writes=[stats])
        p.exec("vector", lambda e: e.tensor_scalar_mul(stats[:, 2:3], stats[:, 0:1], -1.0 / E_DIM),
               reads=[stats], writes=[stats])
        p.exec("vector", lambda e: e.tensor_scalar_mul(stats[:, 3:4], stats[:, 1:2], 1.0 / E_DIM),
               reads=[stats], writes=[stats])
        p.exec("vector", lambda e: e.tensor_mul(stats[:, 4:5], stats[:, 2:3], stats[:, 2:3]),
               reads=[stats], writes=[stats])
        p.exec("vector", lambda e: e.tensor_sub(stats[:, 5:6], stats[:, 3:4], stats[:, 4:5]),
               reads=[stats], writes=[stats])
        p.exec("scalar", lambda e: e.activation(stats[:, 6:7], stats[:, 5:6], AF.Ln, bias=cp[:, 0:1]),
               reads=[stats, cp], writes=[stats])
        p.exec("scalar", lambda e: e.activation(stats[:, 7:8], stats[:, 6:7], AF.Exp, scale=cp[:, 1:2]),
               reads=[stats, cp], writes=[stats])
        p.exec("vector", lambda e: e.tensor_mul(stats[:, 4:5], stats[:, 2:3], stats[:, 7:8]),
               reads=[stats], writes=[stats])
        p.exec("vector", lambda e, i=xb: e.tensor_scalar(
            xn_tm[:], i[:], stats[:, 7:8], stats[:, 4:5], ALU.mult, ALU.add),
            reads=[xb, stats], writes=[xn_tm])
        tb = tbuf[b]
        for ec in range(nech):
            pb = pstr[ec % 2]
            p.exec("tensor", lambda e, ec=ec, pb=pb: e.transpose(
                pb[:], xn_tm[:, ec * 128:(ec + 1) * 128], idb[:]),
                reads=[xn_tm, idb], writes=[pb])
            p.exec("vector", lambda e, ec=ec, pb=pb, tb=tb: e.tensor_copy(tb[:, ec, :], pb[:]),
                   reads=[pb], writes=[tb])
            p.dma("gpsimd", xn_re[ec, :, c * 128:(c + 1) * 128], tb[:, ec, :],
                  reads=[tb])
    p.prefix = _oldpfx
    if standalone:
        p.fence()
        return p.emit_and_finalize()
    return None


# version 08
# speedup vs baseline: 1.6810x; 1.0231x over previous
import numpy as np
import ml_dtypes
from contextlib import ExitStack

import concourse.bass as bass
import concourse.mybir as mybir
from concourse.bass_utils import run_bass_kernel_spmd

F32 = mybir.dt.float32
BF16 = mybir.dt.bfloat16
AF = mybir.ActivationFunctionType
ALU = mybir.AluOpType
BF = ml_dtypes.bfloat16

E_DIM = 2048
H = 16
D = 128
INTER = 8192
EPS = 1e-5

ENGINES = ("sync", "scalar", "vector", "tensor", "gpsimd")
DMA_QUEUES = ("sync", "gpsimd")


class _Op:
    __slots__ = ("eng", "fn", "is_dma", "deps", "inc_needed", "sem_val", "sem_key", "idx")

    def __init__(self, eng, fn, is_dma, idx):
        self.eng = eng
        self.fn = fn
        self.is_dma = is_dma
        self.deps = set()
        self.inc_needed = False
        self.sem_val = None
        self.idx = idx


class Prog:

    def __init__(self, name="prog"):
        self.nc = bass.Bass("TRN2", target_bir_lowering=False)
        self.stack = ExitStack()
        self.ops = []
        self.last_writer = {}
        self.readers = {}
        self.names = set()
        self._barrier = ()
        self._scopes = []
        self._dma_recent = {}
        self.prefix = ""

    from contextlib import contextmanager as _cm

    @_cm
    def scope(self):
        st = ExitStack()
        self._scopes.append(st)
        try:
            yield
        finally:
            self._scopes.pop()
            st.close()
            self.barrier()

    def barrier(self):
        last = {}
        for op in self.ops:
            if not op.is_dma:
                last[op.eng] = op
        bar = set(last.values())
        for q, lst in self._dma_recent.items():
            bar.update(lst)
        self._barrier = tuple(bar)

    def inp(self, name, shape, dtype):
        return self.nc.dram_tensor(name, list(shape), dtype, kind="ExternalInput").ap()

    def outp(self, name, shape, dtype):
        return self.nc.dram_tensor(name, list(shape), dtype, kind="ExternalOutput").ap()

    def dram(self, name, shape, dtype):
        return self.nc.dram_tensor(self.prefix + name, list(shape), dtype, kind="Internal").ap()

    def sb(self, name, shape, dtype):
        name = self.prefix + name
        assert name not in self.names
        self.names.add(name)
        st = self._scopes[-1] if self._scopes else self.stack
        return st.enter_context(self.nc.sbuf_tensor(name, list(shape), dtype))

    def ps(self, name, shape, dtype=F32):
        name = self.prefix + name
        assert name not in self.names
        self.names.add(name)
        st = self._scopes[-1] if self._scopes else self.stack
        return st.enter_context(self.nc.psum_tensor(name, list(shape), dtype))

    def exec(self, eng, fn, reads=(), writes=(), is_dma=False):
        op = _Op(eng, fn, is_dma, len(self.ops))
        deps = set()
        for t in reads:
            w = self.last_writer.get(self._key(t))
            if w is not None:
                deps.add(w)
        for t in writes:
            k = self._key(t)
            deps.update(self.readers.get(k, ()))
            w = self.last_writer.get(k)
            if w is not None:
                deps.add(w)
        deps.update(self._barrier)
        op.deps = {d for d in deps if d is not op}
        self.ops.append(op)
        for t in reads:
            self.readers.setdefault(self._key(t), []).append(op)
        for t in writes:
            k = self._key(t)
            self.last_writer[k] = op
            self.readers[k] = []
        return op

    def dma(self, queue, out, in_, reads=(), writes=(), **kw):
        fn = lambda e, o=out, i=in_, kw=kw: e.dma_start(out=o, in_=i, **kw)
        reads = list(reads) + [in_]
        writes = list(writes) + [out]
        op = self.exec(queue, fn, reads=reads, writes=writes, is_dma=True)
        rec = self._dma_recent.setdefault(queue, [])
        rec.append(op)
        if len(rec) > self.DMA_SLOTS:
            rec.pop(0)
        return op

    @staticmethod
    def _key(t):
        n = getattr(t, "name", None)
        if n is None and hasattr(t, "tensor"):
            n = t.tensor.name
        return n if n is not None else t

    DMA_SLOTS = 8

    def emit_and_finalize(self):
        def _needs_sem(d, op):
            if d.eng == "tensor" and op.eng == "tensor" and not d.is_dma and not op.is_dma:
                return False
            return True

        for op in self.ops:
            for d in op.deps:
                if _needs_sem(d, op):
                    d.inc_needed = True
        for op in self.ops:
            if op.is_dma:
                op.inc_needed = True

        sems = {}
        counters = {}
        slot_rr = {}
        slot_prev = {}
        for op in self.ops:
            if not op.inc_needed:
                continue
            if op.is_dma:
                q = op.eng
                s = slot_rr.get(q, 0)
                slot_rr[q] = (s + 1) % self.DMA_SLOTS
                key = (q, "d", s)
                prev = slot_prev.get(key)
                if prev is not None:
                    op.deps.add(prev)
                slot_prev[key] = op
            else:
                key = (op.eng, "c", 0)
            if key not in sems:
                sems[key] = self.stack.enter_context(
                    self.nc.semaphore(f"s_{key[0]}_{key[1]}{key[2]}")
                )
            step = 16 if op.is_dma else 1
            counters[key] = counters.get(key, 0) + step
            op.sem_val = counters[key]
            op.sem_key = key
        self._sems = sems

        streams = {e: [] for e in ENGINES}
        floors = {e: {} for e in ENGINES}
        for op in self.ops:
            waits = {}
            fl = floors[op.eng]
            for d in op.deps:
                if not _needs_sem(d, op):
                    continue
                key = d.sem_key
                if fl.get(key, 0) >= d.sem_val:
                    continue
                waits[key] = max(waits.get(key, 0), d.sem_val)
            for key, val in waits.items():
                fl[key] = val
            streams[op.eng].append((op, sorted(waits.items(), key=lambda kv: str(kv[0]))))

        block = self.stack.enter_context(self.nc.Block())
        for e in ENGINES:
            if not streams[e]:
                continue

            def body(eng, ops=streams[e]):
                for op, waits in ops:
                    for (key, val) in waits:
                        eng.wait_ge(sems[key], val)
                    inst = op.fn(eng)
                    if op.inc_needed:
                        inst.then_inc(sems[op.sem_key], 16 if op.is_dma else 1)

            getattr(block, e)(body)
        self.stack.close()
        self.nc.finalize()
        return self.nc

    def fence(self, eng="gpsimd"):
        pend = set()
        for k, w in self.last_writer.items():
            pend.add(w)
        for k, rs in self.readers.items():
            pend.update(rs)

        def fn(e):
            return e.nop()

        op = _Op(eng, fn, False, len(self.ops))
        op.deps = {d for d in pend if d is not op}
        self.ops.append(op)
        return op


def cdiv(a, b):
    return (a + b - 1) // b


def prep_host(w, T):
    import math
    pr = {}
    pr["identbf"] = np.eye(128).astype(BF)
    pr["identf32"] = np.eye(128, dtype=np.float32)
    pr["h128bf"] = sylvester(128).astype(BF)
    pr["ones128f"] = np.ones((1, 128), np.float32)
    pr["onesbf"] = np.ones((1, 128)).astype(BF)
    pr["constpm"] = np.tile(np.array([[EPS, -0.5]], np.float32), (128, 1))
    freq = (1.0 / 1024.0) ** np.linspace(0.0, 1.0, 32, dtype=np.float32)
    freq = np.concatenate([freq, np.zeros(32, np.float32)])
    t = np.arange(T, dtype=np.float32)
    theta = t[:, None] * freq[None, :]
    pr["rotc"] = np.ascontiguousarray(np.cos(theta).astype(np.float32).reshape(T // 128, 128, 64))
    pr["rots"] = np.ascontiguousarray(np.sin(theta).astype(np.float32).reshape(T // 128, 128, 64))
    start = 2.0 ** (-2.0 ** (-(math.log2(H) - 3)))
    slopes0 = np.array([start * start ** i for i in range(H)], np.float64)
    r = np.exp(-slopes0)
    i = np.arange(128, dtype=np.float64)
    pr["qdec"] = (r[None, :] ** (i[:, None] + 1.0)).astype(np.float32)
    pr["kdec"] = (r[None, :] ** (127.0 - i[:, None])).astype(np.float32)
    pr["rC"] = np.tile((r ** 128.0).astype(np.float32)[None, :], (128, 1))
    diff = i[:, None] - i[None, :]
    dmaskT = np.zeros((H, 128, 128), np.float32)
    for h in range(H):
        dm = np.where(diff >= 0, r[h] ** np.maximum(diff, 0), 0.0)
        dmaskT[h] = dm.T.astype(np.float32)
    pr["dmaskT"] = dmaskT
    j64 = np.arange(64, dtype=np.float32)
    dT = j64[None, :] - j64[:, None]
    pr["diffT64"] = np.where(dT >= 0, dT, 1e6).astype(np.float32)
    pr["negiota64"] = np.tile(-j64[:, None], (1, H)).astype(np.float32)
    pr["negrev64"] = np.tile(-(64.0 - j64)[:, None], (1, H)).astype(np.float32)
    pr["Wq"] = w["L0_attn_Wq"]
    pr["Wk"] = w["L0_attn_Wk"]
    pr["Wv"] = w["L0_attn_Wv"]
    pr["Wo"] = w["L0_attn_Wo"]
    pr["rms_w_row"] = np.ascontiguousarray(w["L0_attn_rms_w"][None, :])
    pr["W_upT"] = np.ascontiguousarray(w["L0_ffn_W_up"].T)
    pr["W_gateT"] = np.ascontiguousarray(w["L0_ffn_W_gate"].T)
    pr["W_downT"] = np.ascontiguousarray(w["L0_ffn_W_down"].T)
    pr["su_ug_pm"] = np.ascontiguousarray(w["L0_ffn_SU_ug"].reshape(16, 128).T)
    pr["su_dp_pm"] = np.ascontiguousarray(w["L0_ffn_SU_dp"].reshape(64, 128).T)
    pr["wsc_up_pm"] = np.ascontiguousarray(w["L0_ffn_Wscale_ug"][:INTER].reshape(64, 128).T)
    pr["wsc_gate_pm"] = np.ascontiguousarray(w["L0_ffn_Wscale_ug"][INTER:].reshape(64, 128).T)
    pr["wsc_dp_pm"] = np.ascontiguousarray(w["L0_ffn_Wscale_dp"].reshape(16, 128).T)
    pr["Wqkv"] = w["L1_attn_Wqkv"]
    pr["bqkv_pm"] = np.ascontiguousarray(w["L1_attn_bqkv"].reshape(48, 128).T)
    pr["Wout"] = w["L1_attn_Wout"]
    pr["bout_row"] = np.ascontiguousarray(w["L1_attn_bout"][None, :])
    pr["slopes_row"] = np.ascontiguousarray(w["L1_attn_slopes"][None, :])
    fprime = np.arange(INTER)
    perm = 4 * (fprime % 2048) + fprime // 2048
    pr["w_gate_p"] = np.ascontiguousarray(w["L1_ffn_w_gate"][:, perm])
    pr["w_up_p"] = np.ascontiguousarray(w["L1_ffn_w_up"][:, perm])
    pr["w_down_p"] = np.ascontiguousarray(w["L1_ffn_w_down"][perm, :])
    return pr


def build_mega(T):
    p = Prog("mega")
    ext = {}
    for name, shape, dt in [
        ("x", [T, E_DIM], F32), ("identbf", [128, 128], BF16), ("constpm", [128, 2], F32),
        ("rotc", [T // 128, 128, 64], F32), ("rots", [T // 128, 128, 64], F32),
        ("Wq", [E_DIM, E_DIM], F32), ("Wk", [E_DIM, E_DIM], F32), ("Wv", [E_DIM, E_DIM], F32),
        ("Wo", [E_DIM, E_DIM], F32), ("rms_w_row", [1, E_DIM], F32),
        ("dmaskT", [H, 128, 128], F32), ("qdec", [128, H], F32), ("kdec", [128, H], F32),
        ("rC", [128, H], F32), ("ones128f", [1, 128], F32), ("onesbf", [1, 128], BF16),
        ("W_upT", [E_DIM, INTER], F32), ("W_gateT", [E_DIM, INTER], F32),
        ("W_downT", [INTER, E_DIM], F32),
        ("su_ug_pm", [128, 16], F32), ("su_dp_pm", [128, 64], F32),
        ("wsc_up_pm", [128, 64], F32), ("wsc_gate_pm", [128, 64], F32),
        ("wsc_dp_pm", [128, 16], F32),
        ("h128bf", [128, 128], BF16), ("identf32", [128, 128], F32),
        ("Wqkv", [E_DIM, 3 * E_DIM], F32), ("bqkv_pm", [128, 48], F32),
        ("Wout", [E_DIM, E_DIM], F32), ("bout_row", [1, E_DIM], F32),
        ("slopes_row", [1, H], F32), ("diffT64", [64, 64], F32),
        ("negiota64", [64, H], F32), ("negrev64", [64, H], F32),
        ("w_gate_p", [E_DIM, INTER], F32), ("w_up_p", [E_DIM, INTER], F32),
        ("w_down_p", [INTER, E_DIM], F32),
    ]:
        ext[name] = p.inp(name, shape, dt)
    out = p.outp("out_tm", [T, E_DIM], F32)
    d_xn0 = p.dram("xn0_a", [E_DIM, T], BF16)
    d_q0 = p.dram("q0_a", [E_DIM, T], BF16)
    d_k0 = p.dram("k0_a", [E_DIM, T], BF16)
    d_k0t = p.dram("k0t_a", [T, E_DIM], BF16)
    d_v0 = p.dram("v0_a", [T, E_DIM], BF16)
    d_attn0 = p.dram("attn0_a", [T, E_DIM], F32)
    d_ffn0 = p.dram("ffn0_a", [T, E_DIM], F32)
    d_x1 = p.dram("x1_a", [T, E_DIM], F32)
    d_xn1 = p.dram("xn1_a", [E_DIM, T], BF16)
    d_q1 = p.dram("q1_a", [E_DIM, T], BF16)
    d_k1 = p.dram("k1_a", [E_DIM, T], BF16)
    d_k1t = p.dram("k1t_a", [T, E_DIM], BF16)
    d_v1 = p.dram("v1_a", [T, E_DIM], BF16)
    d_attn1 = p.dram("attn1_a", [T, E_DIM], F32)
    d_ffn1 = p.dram("ffn1_a", [T, E_DIM], F32)

    with p.scope():
        build_ln(T, p=p, io={"x": ext["x"], "xn_fm": d_xn0,
                             "identbf": ext["identbf"], "constpm": ext["constpm"]})
    with p.scope():
        build_qkv0(T, p=p, io={"xn_fm": d_xn0, "Wq": ext["Wq"], "Wk": ext["Wk"],
                               "Wv": ext["Wv"], "rotc": ext["rotc"], "rots": ext["rots"],
                               "identbf": ext["identbf"], "qr_fm": d_q0, "kr_fm": d_k0,
                               "kr_tm": d_k0t, "v_tm": d_v0})
    with p.scope():
        build_attn0(T, p=p, io={"qr_fm": d_q0, "kr_fm": d_k0, "kr_tm": d_k0t, "v_tm": d_v0,
                                "Wo": ext["Wo"], "rms_w_row": ext["rms_w_row"],
                                "dmaskT": ext["dmaskT"], "qdec": ext["qdec"],
                                "kdec": ext["kdec"], "rC": ext["rC"],
                                "identbf": ext["identbf"], "ones128f": ext["ones128f"],
                                "constpm": ext["constpm"], "attn0_tm": d_attn0})
    with p.scope():
        build_ffn0(T, p=p, io={"xn_fm": d_xn0, "W_upT": ext["W_upT"],
                               "W_gateT": ext["W_gateT"], "W_downT": ext["W_downT"],
                               "su_ug_pm": ext["su_ug_pm"], "su_dp_pm": ext["su_dp_pm"],
                               "wsc_up_pm": ext["wsc_up_pm"], "wsc_gate_pm": ext["wsc_gate_pm"],
                               "wsc_dp_pm": ext["wsc_dp_pm"], "h128bf": ext["h128bf"],
                               "identf32": ext["identf32"], "ffn0_tm": d_ffn0})
    with p.scope():
        p.prefix = "r0_"
        build_resln(T, with_ln=True, p=p, io={"x": ext["x"], "add0": d_attn0,
                                              "add1": d_ffn0, "x1_tm": d_x1,
                                              "xn_fm": d_xn1, "identbf": ext["identbf"],
                                              "constpm": ext["constpm"]})
        p.prefix = ""
    with p.scope():
        build_qkv1(T, p=p, io={"xn_fm": d_xn1, "Wqkv": ext["Wqkv"],
                               "bqkv_pm": ext["bqkv_pm"], "identbf": ext["identbf"],
                               "q1_fm": d_q1, "k1_fm": d_k1, "k1_tm": d_k1t, "v1_tm": d_v1})
    with p.scope():
        build_attn1(T, p=p, io={"q1_fm": d_q1, "k1_fm": d_k1, "k1_tm": d_k1t, "v1_tm": d_v1,
                                "Wout": ext["Wout"], "bout_row": ext["bout_row"],
                                "slopes_row": ext["slopes_row"], "diffT64": ext["diffT64"],
                                "negiota64": ext["negiota64"], "negrev64": ext["negrev64"],
                                "ones128f": ext["ones128f"], "onesbf": ext["onesbf"],
                                "identbf": ext["identbf"], "attn1_tm": d_attn1})
    with p.scope():
        build_ffn1(T, p=p, io={"xn_fm": d_xn1, "w_gate_p": ext["w_gate_p"],
                               "w_up_p": ext["w_up_p"], "w_down_p": ext["w_down_p"],
                               "identf32": ext["identf32"], "ffn1_tm": d_ffn1})
    with p.scope():
        p.prefix = "r1_"
        build_resln(T, with_ln=False, p=p, io={"x": d_x1, "add0": d_attn1,
                                               "add1": d_ffn1, "x1_tm": out})
        p.prefix = ""
    p.fence()
    return p.emit_and_finalize()


def mega_in_map(pr, x_b):
    m = {k: pr[k] for k in (
        "identbf", "constpm", "rotc", "rots", "Wq", "Wk", "Wv", "Wo", "rms_w_row",
        "dmaskT", "qdec", "kdec", "rC", "ones128f", "onesbf", "W_upT", "W_gateT",
        "W_downT", "su_ug_pm", "su_dp_pm", "wsc_up_pm", "wsc_gate_pm", "wsc_dp_pm",
        "h128bf", "identf32", "Wqkv", "bqkv_pm", "Wout", "bout_row", "slopes_row",
        "diffT64", "negiota64", "negrev64", "w_gate_p", "w_up_p", "w_down_p")}
    m["x"] = np.ascontiguousarray(x_b)
    return m


_NC_CACHE = {}


def _get_nc(key, builder, T):
    if key not in _NC_CACHE:
        _NC_CACHE[key] = builder(T)
    return _NC_CACHE[key]


def forward(x_full, w, T, run):
    B = x_full.shape[0]
    pr = prep_host(w, T)

    def rep(d):
        return [d] * B

    def percore(key_arrs, shared):
        return [dict(shared, **ka) for ka in key_arrs]

    nc = _get_nc(("ln0", T), build_ln, T)
    outs = run(nc, percore([{"x": np.ascontiguousarray(x_full[b])} for b in range(B)],
                           {"identbf": pr["identbf"], "constpm": pr["constpm"]}))
    xn0 = [o["xn_fm"] for o in outs]
    nc = _get_nc(("qkv0", T), build_qkv0, T)
    outs = run(nc, percore([{"xn_fm": xn0[b]} for b in range(B)],
                           {"Wq": pr["Wq"], "Wk": pr["Wk"], "Wv": pr["Wv"],
                            "rotc": pr["rotc"], "rots": pr["rots"], "identbf": pr["identbf"]}))
    q0, k0, k0t, v0 = ([o[k] for o in outs] for k in ("qr_fm", "kr_fm", "kr_tm", "v_tm"))
    nc = _get_nc(("attn0", T), build_attn0, T)
    outs = run(nc, percore(
        [{"qr_fm": q0[b], "kr_fm": k0[b], "kr_tm": k0t[b], "v_tm": v0[b]} for b in range(B)],
        {"Wo": pr["Wo"], "rms_w_row": pr["rms_w_row"], "dmaskT": pr["dmaskT"],
         "qdec": pr["qdec"], "kdec": pr["kdec"], "rC": pr["rC"],
         "identbf": pr["identbf"], "ones128f": pr["ones128f"], "constpm": pr["constpm"]}))
    attn0 = [o["attn0_tm"] for o in outs]
    nc = _get_nc(("ffn0", T), build_ffn0, T)
    outs = run(nc, percore([{"xn_fm": xn0[b]} for b in range(B)],
                           {"W_upT": pr["W_upT"], "W_gateT": pr["W_gateT"],
                            "W_downT": pr["W_downT"], "su_ug_pm": pr["su_ug_pm"],
                            "su_dp_pm": pr["su_dp_pm"], "wsc_up_pm": pr["wsc_up_pm"],
                            "wsc_gate_pm": pr["wsc_gate_pm"], "wsc_dp_pm": pr["wsc_dp_pm"],
                            "h128bf": pr["h128bf"], "identf32": pr["identf32"]}))
    ffn0 = [o["ffn0_tm"] for o in outs]
    nc = _get_nc(("resln", T), lambda TT: build_resln(TT, with_ln=True), T)
    outs = run(nc, percore(
        [{"x": np.ascontiguousarray(x_full[b]), "add0": attn0[b], "add1": ffn0[b]}
         for b in range(B)],
        {"identbf": pr["identbf"], "constpm": pr["constpm"]}))
    x1 = [o["x1_tm"] for o in outs]
    xn1 = [o["xn_fm"] for o in outs]
    nc = _get_nc(("qkv1", T), build_qkv1, T)
    outs = run(nc, percore([{"xn_fm": xn1[b]} for b in range(B)],
                           {"Wqkv": pr["Wqkv"], "bqkv_pm": pr["bqkv_pm"],
                            "identbf": pr["identbf"]}))
    q1, k1, k1t, v1 = ([o[k] for o in outs] for k in ("q1_fm", "k1_fm", "k1_tm", "v1_tm"))
    nc = _get_nc(("attn1", T), build_attn1, T)
    outs = run(nc, percore(
        [{"q1_fm": q1[b], "k1_fm": k1[b], "k1_tm": k1t[b], "v1_tm": v1[b]} for b in range(B)],
        {"Wout": pr["Wout"], "bout_row": pr["bout_row"], "slopes_row": pr["slopes_row"],
         "diffT64": pr["diffT64"], "negiota64": pr["negiota64"], "negrev64": pr["negrev64"],
         "ones128f": pr["ones128f"], "onesbf": pr["onesbf"], "identbf": pr["identbf"]}))
    attn1 = [o["attn1_tm"] for o in outs]
    nc = _get_nc(("ffn1", T), build_ffn1, T)
    outs = run(nc, percore([{"xn_fm": xn1[b]} for b in range(B)],
                           {"w_gate_p": pr["w_gate_p"], "w_up_p": pr["w_up_p"],
                            "w_down_p": pr["w_down_p"], "identf32": pr["identf32"]}))
    ffn1 = [o["ffn1_tm"] for o in outs]
    nc = _get_nc(("res1", T), lambda TT: build_resln(TT, with_ln=False), T)
    outs = run(nc, [{"x": x1[b], "add0": attn1[b], "add1": ffn1[b]} for b in range(B)])
    return np.stack([o["x1_tm"] for o in outs], axis=0)


def kernel(**inputs):
    x = np.asarray(inputs["L0_attn_x"], np.float32)
    T = x.shape[1]
    B = x.shape[0]
    pr = prep_host(inputs, T)
    nc = _get_nc(("mega", T), build_mega, T)
    in_maps = [mega_in_map(pr, x[b]) for b in range(B)]
    res = run_bass_kernel_spmd(nc, in_maps, core_ids=list(range(B)))
    return np.stack([r["out_tm"] for r in res.results], axis=0)


def kernel_multilaunch(**inputs):
    x = np.asarray(inputs["L0_attn_x"], np.float32)
    T = x.shape[1]

    def run(nc, in_maps):
        res = run_bass_kernel_spmd(nc, in_maps, core_ids=list(range(len(in_maps))))
        return res.results

    return forward(x, inputs, T, run)


def sylvester(n):
    Hm = np.array([[1.0]], np.float32)
    while Hm.shape[0] < n:
        Hm = np.block([[Hm, Hm], [Hm, -Hm]])
    return Hm.astype(np.float32)


def load_weight_bf16(p, wsb, w_dram, rows, cols, name=None, ecs_per_dma=8):
    nech = rows // 128
    wre = w_dram.rearrange("(ec pp) c -> pp ec c", pp=128)
    for e0 in range(0, nech, ecs_per_dma):
        e1 = min(e0 + ecs_per_dma, nech)
        p.dma("gpsimd", wsb[:, e0:e1, :], wre[:, e0:e1, :], writes=[wsb])


def _key_of(t):
    return Prog._key(t)


def _butterfly(p, n, cur, nxt, final_out=None, final_slice=None, width=None, tag=""):
    import math as _m
    stages = int(_m.log2(n))
    bufs = [cur, nxt]
    kc, kn, kf = _key_of(cur), _key_of(nxt), (None if final_out is None else _key_of(final_out))
    keys = [kc, kn]
    pair_i = 0
    for s in range(stages):
        h = 1 << s
        src = bufs[s % 2]
        dst = bufs[(s + 1) % 2]
        ksrc = keys[s % 2]
        kdst = keys[(s + 1) % 2]
        last = (s == stages - 1) and final_out is not None
        for blk in range(n // (2 * h)):
            for i in range(h):
                ia = blk * 2 * h + i
                ib = ia + h
                eng = "vector" if (pair_i % 2 == 0) else "gpsimd"
                pair_i += 1
                for sign, oi in ((ALU.add, ia), (ALU.subtract, ib)):
                    if last:
                        oap = final_out[:, oi, final_slice[0]:final_slice[1]]
                        okey = f"{kf}@{oi}"
                    else:
                        oap = dst[:, oi, :]
                        okey = f"{kdst}@{oi}"
                    p.exec(eng, lambda e, src=src, ia=ia, ib=ib, oap=oap, sign=sign:
                           e.tensor_tensor(oap, src[:, ia, :], src[:, ib, :], sign),
                           reads=[f"{ksrc}@{ia}", f"{ksrc}@{ib}"], writes=[okey])
    return bufs[stages % 2]


def build_qkv1(T, p=None, io=None):
    standalone = p is None
    if standalone:
        p = Prog("qkv1")
    _oldpfx = p.prefix
    if not standalone:
        p.prefix = "qkv1_" + p.prefix
    def _inp(name, shape, dtype):
        if io is not None and name in io:
            return io[name]
        return p.inp(name, shape, dtype)
    def _outp(name, shape, dtype):
        if io is not None and name in io:
            return io[name]
        return p.outp(name, shape, dtype)
    xn_fm = _inp("xn_fm", [E_DIM, T], BF16)
    Wqkv = _inp("Wqkv", [E_DIM, 3 * E_DIM], F32)
    bq_pm = _inp("bqkv_pm", [128, 48], F32)
    identbf = _inp("identbf", [128, 128], BF16)
    q_fm = _outp("q1_fm", [E_DIM, T], BF16)
    k_fm = _outp("k1_fm", [E_DIM, T], BF16)
    k_tm = _outp("k1_tm", [T, E_DIM], BF16)
    v_tm = _outp("v1_tm", [T, E_DIM], BF16)
    nech = E_DIM // 128
    TS = min(512, T)
    nts = T // TS
    nci_ = TS // 128

    idb = p.sb("idb", [128, 128], BF16)
    p.dma("sync", idb[:], identbf, writes=[idb])
    bq = p.sb("bq", [128, 48], F32)
    p.dma("sync", bq[:], bq_pm, writes=[bq])
    xnsb = p.sb("xnsb", [128, nech, T], BF16)
    xre = xn_fm.rearrange("(ec pp) t -> pp ec t", pp=128)
    for e0 in range(0, nech, 4):
        p.dma("sync", xnsb[:, e0:e0 + 4, :], xre[:, e0:e0 + 4, :], writes=[xnsb])
    wre = Wqkv.rearrange("(ec pp) o -> pp ec o", pp=128)
    q_re = q_fm.rearrange("(ec pp) t -> ec pp t", pp=128)
    k_re = k_fm.rearrange("(ec pp) t -> ec pp t", pp=128)

    wt = [p.sb(f"wt{i}", [128, nech, 128], BF16) for i in range(2)]
    ebuf_ = [p.sb(f"ebuf{i}", [128, TS], BF16) for i in range(2)]
    ttile_ = [p.sb(f"ttile{i}", [128, 128], BF16) for i in range(2)]
    ps_m = [p.ps(f"ps_m{i}", [128, TS]) for i in range(2)]
    ps_t = [p.ps(f"ps_t{i}", [128, 128], BF16) for i in range(2)]

    for oc in range(48):
        w = wt[oc % 2]
        for e0 in range(0, nech, 8):
            p.dma("gpsimd", w[:, e0:e0 + 8, :], wre[:, e0:e0 + 8, oc * 128:(oc + 1) * 128],
                  writes=[w])
        for ts in range(nts):
            pm = ps_m[ts % 2]
            ebuf = ebuf_[(oc * nts + ts) % 2]
            for ec in range(nech):
                p.exec("tensor", lambda e, ec=ec, ts=ts, w=w, pm=pm: e.matmul(
                    pm[:], w[:, ec, :], xnsb[:, ec, ts * TS:(ts + 1) * TS],
                    start=(ec == 0), stop=(ec == nech - 1)),
                    reads=[w, xnsb], writes=[pm])
            p.exec("vector", lambda e, oc=oc, pm=pm, ebuf=ebuf: e.tensor_scalar_add(
                ebuf[:], pm[:], bq[:, oc:oc + 1]), reads=[pm, bq], writes=[ebuf])
            if oc < 16:
                p.dma("gpsimd", q_re[oc, :, ts * TS:(ts + 1) * TS], ebuf[:], reads=[ebuf])
            elif oc < 32:
                p.dma("gpsimd", k_re[oc - 16, :, ts * TS:(ts + 1) * TS], ebuf[:], reads=[ebuf])
            if oc >= 16:
                tmout = k_tm if oc < 32 else v_tm
                for ci in range(nci_):
                    pt = ps_t[ci % 2]
                    ttile = ttile_[ci % 2]
                    p.exec("tensor", lambda e, ci=ci, pt=pt, ebuf=ebuf: e.transpose(
                        pt[:], ebuf[:, ci * 128:(ci + 1) * 128], idb[:]),
                        reads=[ebuf, idb], writes=[pt])
                    p.exec("vector", lambda e, pt=pt, ttile=ttile: e.tensor_copy(ttile[:], pt[:]),
                           reads=[pt], writes=[ttile])
                    r0 = ts * TS + ci * 128
                    p.dma("sync", tmout[r0:r0 + 128, (oc % 16) * 128:(oc % 16 + 1) * 128],
                          ttile[:], reads=[ttile])
    p.prefix = _oldpfx
    if standalone:
        p.fence()
        return p.emit_and_finalize()
    return None


def build_attn1(T, p=None, io=None):
    standalone = p is None
    if standalone:
        p = Prog("attn1")
    _oldpfx = p.prefix
    if not standalone:
        p.prefix = "attn1_" + p.prefix
    def _inp(name, shape, dtype):
        if io is not None and name in io:
            return io[name]
        return p.inp(name, shape, dtype)
    def _outp(name, shape, dtype):
        if io is not None and name in io:
            return io[name]
        return p.outp(name, shape, dtype)
    q_fm = _inp("q1_fm", [E_DIM, T], BF16)
    k_fm = _inp("k1_fm", [E_DIM, T], BF16)
    k_tm = _inp("k1_tm", [T, E_DIM], BF16)
    v_tm = _inp("v1_tm", [T, E_DIM], BF16)
    Wout = _inp("Wout", [E_DIM, E_DIM], F32)
    bout_row = _inp("bout_row", [1, E_DIM], F32)
    slopes_row = _inp("slopes_row", [1, H], F32)
    diffT = _inp("diffT64", [64, 64], F32)
    negiota = _inp("negiota64", [64, H], F32)
    negrev = _inp("negrev64", [64, H], F32)
    ones128f = _inp("ones128f", [1, 128], F32)
    onesbf = _inp("onesbf", [1, 128], BF16)
    identbf = _inp("identbf", [128, 128], BF16)
    attn_out = _outp("attn1_tm", [T, E_DIM], F32)
    BLK = 64
    nblk = T // BLK
    nech = E_DIM // 128

    idb = p.sb("idb", [128, 128], BF16)
    onb = p.sb("onb", [1, 128], BF16)
    onf = p.sb("onf", [1, 128], F32)
    slp = p.sb("slp", [1, H], F32)
    dfT = p.sb("dfT", [64, 64], F32)
    nio = p.sb("nio", [64, H], F32)
    nrv = p.sb("nrv", [64, H], F32)
    boutb = p.sb("boutb", [1, E_DIM], BF16)
    wsb = p.sb("wsb", [128, nech, E_DIM], BF16)
    for t, src in ((idb, identbf), (onb, onesbf), (onf, ones128f), (slp, slopes_row),
                   (dfT, diffT), (nio, negiota), (nrv, negrev)):
        p.dma("sync", t[:], src, writes=[t])
    p.dma("gpsimd", boutb[:], bout_row, writes=[boutb])
    load_weight_bf16(p, wsb, Wout, E_DIM, E_DIM)

    srep = p.sb("srep", [128, H], F32)
    nsrep = p.sb("nsrep", [128, H], F32)
    qd1 = p.sb("qd1", [64, H], F32)
    kd1 = p.sb("kd1", [64, H], F32)
    bd1 = p.sb("bd1", [128, H], F32)
    dT = p.sb("dT", [64, H, 64], F32)
    tmp16 = p.sb("tmp16", [128, H], F32)
    with p.scope():
        ps_r = p.ps("ps_r", [128, H])
        p.exec("tensor", lambda e: e.matmul(ps_r[:], onf[:], slp[:], start=True, stop=True),
               reads=[onf, slp], writes=[ps_r])
        p.exec("vector", lambda e: e.tensor_copy(srep[:], ps_r[:]), reads=[ps_r], writes=[srep])
    p.exec("vector", lambda e: e.tensor_scalar_mul(nsrep[:], srep[:], -1.0),
           reads=[srep], writes=[nsrep])
    p.exec("vector", lambda e: e.tensor_mul(tmp16[0:64, :], nio[:], srep[0:64, :]),
           reads=[nio, srep], writes=[tmp16])
    p.exec("scalar", lambda e: e.activation(qd1[:], tmp16[0:64, :], AF.Exp),
           reads=[tmp16], writes=[qd1])
    p.exec("vector", lambda e: e.tensor_mul(tmp16[0:64, :], nrv[:], srep[0:64, :]),
           reads=[nio, srep, qd1], writes=[tmp16])
    p.exec("scalar", lambda e: e.activation(kd1[:], tmp16[0:64, :], AF.Exp),
           reads=[tmp16], writes=[kd1])
    p.exec("vector", lambda e: e.tensor_scalar_mul(tmp16[:], srep[:], -64.0),
           reads=[srep, kd1], writes=[tmp16])
    p.exec("scalar", lambda e: e.activation(bd1[:], tmp16[:], AF.Exp),
           reads=[tmp16], writes=[bd1])
    for h in range(H):
        p.exec("scalar", lambda e, h=h: e.activation(
            dT[:, h, :], dfT[:], AF.Exp, scale=nsrep[0:64, h:h + 1]),
            reads=[dfT, nsrep], writes=[dT])

    qre = q_fm.rearrange("(h d) t -> d h t", d=128)
    kre = k_fm.rearrange("(h d) t -> d h t", d=128)

    GB = min(4, T // 64)
    ktm = [p.sb(f"ktm{i}", [64, E_DIM], BF16) for i in range(2)]
    vtm = [p.sb(f"vtm{i}", [64, E_DIM], BF16) for i in range(2)]
    qfm = [p.sb(f"qfm{i}", [128, H, GB * 64], BF16) for i in range(2)]
    kfm = [p.sb(f"kfm{i}", [128, H, GB * 64], BF16) for i in range(2)]
    kv32 = [[p.sb(f"kv32_{h}_{i}", [128, 128], F32) for i in range(2)] for h in range(H)]
    kvbf = [p.sb(f"kvbf_{h}", [128, 128], BF16) for h in range(H)]
    kdt = [p.sb(f"kdt{i}", [64, 128], BF16) for i in range(2)]
    smb = [p.sb(f"smb{i}", [64, 64], BF16) for i in range(2)]
    tmpo = [p.sb(f"tmpo{i}", [64, 128], F32) for i in range(2)]
    attn_b = p.sb("attn_b", [64, E_DIM], F32)
    anb = p.sb("anb", [64, E_DIM], BF16)
    afm = p.sb("afm", [128, nech, 64], BF16)
    outc = p.sb("outc", [64, E_DIM], F32)

    ps_st = p.ps("ps_st", [64, 64])
    ps_o = p.ps("ps_o", [64, 128])
    ps_i = p.ps("ps_i", [64, 128])
    ps_kv = p.ps("ps_kv", [128, 128])
    pstr = [p.ps(f"pstr{i}", [128, 64], BF16) for i in range(2)]
    ps_w = [p.ps(f"ps_w{i}", [64, 512]) for i in range(2)]

    for h in range(H):
        p.exec("vector", lambda e, h=h: e.memset(kv32[h][0][:], 0.0), writes=[kv32[h][0]])
        p.exec("gpsimd", lambda e, h=h: e.memset(kvbf[h][:], 0.0), writes=[kvbf[h]])

    for c in range(nblk):
        b = c % 2
        gb = (c // GB) % 2
        bb = c % GB
        kt, vt, qf, kf = ktm[b], vtm[b], qfm[gb], kfm[gb]
        p.dma("sync", kt[:], k_tm[c * 64:(c + 1) * 64, :], writes=[kt])
        p.dma("sync", vt[:], v_tm[c * 64:(c + 1) * 64, :], writes=[vt])
        if bb == 0:
            g0 = c // GB
            span = (g0 * GB * 64, (g0 + 1) * GB * 64)
            for h0 in range(0, H, 8):
                p.dma("sync", qf[:, h0:h0 + 8, :], qre[:, h0:h0 + 8, span[0]:span[1]],
                      writes=[qf])
                p.dma("sync", kf[:, h0:h0 + 8, :], kre[:, h0:h0 + 8, span[0]:span[1]],
                      writes=[kf])
        for h in range(H):
            hb = h % 2
            kvp = kv32[h][c % 2]
            kvn = kv32[h][(c + 1) % 2]
            qsl = qf[:, h, bb * 64:(bb + 1) * 64]
            ksl = kf[:, h, bb * 64:(bb + 1) * 64]
            p.exec("vector", lambda e, h=h, hb=hb, kt=kt: e.tensor_scalar_mul(
                kdt[hb][:], kt[:, h * 128:(h + 1) * 128], kd1[:, h:h + 1]),
                reads=[kt, kd1], writes=[kdt[hb]])
            p.exec("tensor", lambda e, qsl=qsl, ksl=ksl: e.matmul(
                ps_st[:], ksl, qsl, start=True, stop=True),
                reads=[kf, qf], writes=[ps_st])
            p.exec("vector", lambda e, h=h, hb=hb: e.tensor_tensor(
                smb[hb][:], ps_st[:], dT[:, h, :], ALU.mult),
                reads=[ps_st, dT], writes=[smb[hb]])
            p.exec("tensor", lambda e, h=h, hb=hb, vt=vt: e.matmul(
                ps_o[:], smb[hb][:], vt[:, h * 128:(h + 1) * 128], start=True, stop=True),
                reads=[smb[hb], vt], writes=[ps_o])
            p.exec("tensor", lambda e, qsl=qsl, h=h: e.matmul(
                ps_i[:], qsl, kvbf[h][:], start=True, stop=True),
                reads=[qf, kvbf[h]], writes=[ps_i])
            p.exec("scalar", lambda e, hb=hb: e.copy(tmpo[hb][:], ps_o[:]),
                   reads=[ps_o], writes=[tmpo[hb]])
            p.exec("vector", lambda e, h=h, hb=hb: e.scalar_tensor_tensor(
                attn_b[:, h * 128:(h + 1) * 128], ps_i[:], qd1[:, h:h + 1], tmpo[hb][:],
                ALU.mult, ALU.add),
                reads=[ps_i, qd1, tmpo[hb]], writes=[attn_b])
            p.exec("tensor", lambda e, h=h, hb=hb, vt=vt: e.matmul(
                ps_kv[:], kdt[hb][:], vt[:, h * 128:(h + 1) * 128], start=True, stop=True),
                reads=[kdt[hb], vt], writes=[ps_kv])
            p.exec("vector", lambda e, h=h, kvp=kvp, kvn=kvn: e.scalar_tensor_tensor(
                kvn[:], kvp[:], bd1[:, h:h + 1], ps_kv[:], ALU.mult, ALU.add),
                reads=[kvp, bd1, ps_kv], writes=[kvn])
            p.exec("scalar", lambda e, h=h, kvn=kvn: e.copy(kvbf[h][:], kvn[:]),
                   reads=[kvn], writes=[kvbf[h]])
        p.exec("vector", lambda e: e.tensor_copy(anb[:], attn_b[:]),
               reads=[attn_b], writes=[anb])
        for ec in range(nech):
            pb = pstr[ec % 2]
            p.exec("tensor", lambda e, ec=ec, pb=pb: e.transpose(
                pb[:], anb[:, ec * 128:(ec + 1) * 128], idb[0:64, 0:64]),
                reads=[anb, idb], writes=[pb])
            p.exec("vector", lambda e, ec=ec, pb=pb: e.tensor_copy(afm[:, ec, :], pb[:]),
                   reads=[pb], writes=[afm])
        for sl in range(4):
            pw = ps_w[sl % 2]
            for e_ in range(nech):
                p.exec("tensor", lambda en, sl=sl, e_=e_, pw=pw: en.matmul(
                    pw[:], afm[:, e_, :], wsb[:, e_, sl * 512:(sl + 1) * 512],
                    start=(e_ == 0), stop=False),
                    reads=[afm, wsb], writes=[pw])
            p.exec("tensor", lambda en, sl=sl, pw=pw: en.matmul(
                pw[:], onb[:, 0:64], boutb[:, sl * 512:(sl + 1) * 512],
                start=False, stop=True),
                reads=[onb, boutb], writes=[pw])
            p.exec("scalar", lambda en, sl=sl, pw=pw: en.copy(
                outc[:, sl * 512:(sl + 1) * 512], pw[:]), reads=[pw], writes=[outc])
        p.dma("gpsimd", attn_out[c * 64:(c + 1) * 64, :], outc[:], reads=[outc])
    p.prefix = _oldpfx
    if standalone:
        p.fence()
        return p.emit_and_finalize()
    return None


def build_ffn0(T, p=None, io=None):
    standalone = p is None
    if standalone:
        p = Prog("ffn0")
    _oldpfx = p.prefix
    if not standalone:
        p.prefix = "ffn0_" + p.prefix
    def _inp(name, shape, dtype):
        if io is not None and name in io:
            return io[name]
        return p.inp(name, shape, dtype)
    def _outp(name, shape, dtype):
        if io is not None and name in io:
            return io[name]
        return p.outp(name, shape, dtype)
    xn_fm = _inp("xn_fm", [E_DIM, T], BF16)
    W_upT = _inp("W_upT", [E_DIM, INTER], F32)
    W_gateT = _inp("W_gateT", [E_DIM, INTER], F32)
    W_downT = _inp("W_downT", [INTER, E_DIM], F32)
    su_ug = _inp("su_ug_pm", [128, 16], F32)
    su_dp = _inp("su_dp_pm", [128, 64], F32)
    wsc_up = _inp("wsc_up_pm", [128, 64], F32)
    wsc_gate = _inp("wsc_gate_pm", [128, 64], F32)
    wsc_dp = _inp("wsc_dp_pm", [128, 16], F32)
    h128in = _inp("h128bf", [128, 128], BF16)
    identf = _inp("identf32", [128, 128], F32)
    ffn_out = _outp("ffn0_tm", [T, E_DIM], F32)

    u_dram = p.dram("u_scr", [INTER, T], BF16)
    w_dram = p.dram("w_scr", [INTER, T], BF16)

    nech = E_DIM // 128
    nfch = INTER // 128

    h128sb = p.sb("h128sb", [128, 128], BF16)
    idf = p.sb("idf", [128, 128], F32)
    sug = p.sb("sug", [128, 16], F32)
    sud = p.sb("sud", [128, 64], F32)
    wup_s = p.sb("wup_s", [128, 64], F32)
    wgt_s = p.sb("wgt_s", [128, 64], F32)
    wdp_s = p.sb("wdp_s", [128, 16], F32)
    upscsu = p.sb("upscsu", [128, 64], F32)
    wdps = p.sb("wdps", [128, 16], F32)
    for t, src in ((h128sb, h128in), (idf, identf), (sug, su_ug), (sud, su_dp),
                   (wup_s, wsc_up), (wgt_s, wsc_gate), (wdp_s, wsc_dp)):
        p.dma("sync", t[:], src, writes=[t])
    p.exec("vector", lambda e: e.tensor_mul(upscsu[:], wup_s[:], sud[:]),
           reads=[wup_s, sud], writes=[upscsu])
    p.exec("vector", lambda e: e.tensor_scalar_mul(wdps[:], wdp_s[:], float(1.0 / np.sqrt(INTER))),
           reads=[wdp_s], writes=[wdps])

    xre = xn_fm.rearrange("(ec pp) t -> pp ec t", pp=128)
    wupre = W_upT.rearrange("(ec pp) f -> pp ec f", pp=128)
    wgtre = W_gateT.rearrange("(ec pp) f -> pp ec f", pp=128)
    wdnre = W_downT.rearrange("(fc pp) g -> pp fc g", pp=128)
    u_re = u_dram.rearrange("(fc pp) t -> pp fc t", pp=128)
    w_re = w_dram.rearrange("(fc pp) t -> pp fc t", pp=128)

    ab_scope = p.scope()
    ab_scope.__enter__()
    ybf = p.sb("ybf", [128, nech, T], BF16)
    with p.scope():
        suh = p.sb("suh", [128, 16, 128], BF16)
        for ec in range(nech):
            p.exec("vector", lambda e, ec=ec: e.tensor_scalar_mul(
                suh[:, ec, :], h128sb[:], sug[:, ec:ec + 1]),
                reads=[h128sb, sug], writes=[suh])
        TS = min(512, T)
        nts = T // TS
        xnb = [p.sb(f"xnb{i}", [128, nech, TS], BF16) for i in range(2)]
        yf = p.sb("yf", [128, nech, TS], F32)
        yg = p.sb("yg", [128, nech, TS], F32)
        ps_a = [p.ps(f"ps_a{i}", [128, TS]) for i in range(2)]
        for tsl in range(nts):
            xb = xnb[tsl % 2]
            for e0 in range(0, nech, 8):
                p.dma("sync", xb[:, e0:e0 + 8, :], xre[:, e0:e0 + 8, tsl * TS:(tsl + 1) * TS],
                      writes=[xb])
            for ec in range(nech):
                pa = ps_a[ec % 2]
                p.exec("tensor", lambda e, ec=ec, xb=xb, pa=pa: e.matmul(
                    pa[:], suh[:, ec, :], xb[:, ec, :], start=True, stop=True),
                    reads=[suh, xb], writes=[pa])
                p.exec("vector", lambda e, ec=ec, pa=pa: e.tensor_scalar_mul(
                    yf[:, ec, :], pa[:], float(1.0 / np.sqrt(E_DIM))),
                    reads=[pa], writes=[yf, f"{_key_of(yf)}@{ec}"])
            _butterfly(p, nech, yf, yg, final_out=ybf,
                       final_slice=(tsl * TS, (tsl + 1) * TS))

    with p.scope():
        TS = min(512, T)
        nts = T // TS
        wut = [p.sb(f"wut{i}", [128, nech, 128], BF16) for i in range(2)]
        wgt = [p.sb(f"wgt{i}", [128, nech, 128], BF16) for i in range(2)]
        sg = p.sb("sg", [128, TS], F32)
        ut = p.sb("ut", [128, TS], BF16)
        ps_up = p.ps("ps_up", [128, TS])
        ps_gt = p.ps("ps_gt", [128, TS])
        for fi in range(nfch):
            wu, wg = wut[fi % 2], wgt[fi % 2]
            for e0 in range(0, nech, 8):
                p.dma("gpsimd", wu[:, e0:e0 + 8, :],
                      wupre[:, e0:e0 + 8, fi * 128:(fi + 1) * 128], writes=[wu])
                p.dma("gpsimd", wg[:, e0:e0 + 8, :],
                      wgtre[:, e0:e0 + 8, fi * 128:(fi + 1) * 128], writes=[wg])
            for tsl in range(nts):
                for ec in range(nech):
                    p.exec("tensor", lambda e, ec=ec, tsl=tsl, wu=wu: e.matmul(
                        ps_up[:], wu[:, ec, :], ybf[:, ec, tsl * TS:(tsl + 1) * TS],
                        start=(ec == 0), stop=(ec == nech - 1)),
                        reads=[wu, f"{_key_of(ybf)}@{ec}"], writes=[ps_up])
                    p.exec("tensor", lambda e, ec=ec, tsl=tsl, wg=wg: e.matmul(
                        ps_gt[:], wg[:, ec, :], ybf[:, ec, tsl * TS:(tsl + 1) * TS],
                        start=(ec == 0), stop=(ec == nech - 1)),
                        reads=[wg, f"{_key_of(ybf)}@{ec}"], writes=[ps_gt])
                p.exec("scalar", lambda e, fi=fi: e.activation(
                    sg[:], ps_gt[:], AF.Silu, scale=wgt_s[:, fi:fi + 1]),
                    reads=[ps_gt, wgt_s], writes=[sg])
                p.exec("vector", lambda e, fi=fi: e.scalar_tensor_tensor(
                    ut[:], ps_up[:], upscsu[:, fi:fi + 1], sg[:], ALU.mult, ALU.mult),
                    reads=[ps_up, upscsu, sg], writes=[ut])
                p.dma("sync", u_re[:, fi, tsl * TS:(tsl + 1) * TS], ut[:], reads=[ut])
    ab_scope.__exit__(None, None, None)

    with p.scope():
        TC = min(256, T)
        ntc = T // TC
        ub = p.sb("ub", [128, nfch, TC], BF16)
        vb = p.sb("vb", [128, nfch, TC], F32)
        wt = p.sb("wt", [128, nfch, TC], BF16)
        ps_c = [p.ps(f"ps_c{i}", [128, TC]) for i in range(2)]
        kub, kvb, kwt = _key_of(ub), _key_of(vb), _key_of(wt)
        for tc in range(ntc):
            for f0 in range(0, nfch, 8):
                p.dma("sync", ub[:, f0:f0 + 8, :], u_re[:, f0:f0 + 8, tc * TC:(tc + 1) * TC],
                      writes=[ub] + [f"{kub}@{fc}" for fc in range(f0, f0 + 8)])
            for fc in range(nfch):
                pc = ps_c[fc % 2]
                p.exec("tensor", lambda e, fc=fc, pc=pc: e.matmul(
                    pc[:], h128sb[:], ub[:, fc, :], start=True, stop=True),
                    reads=[h128sb, f"{kub}@{fc}"], writes=[pc])
                p.exec("vector", lambda e, fc=fc, pc=pc: e.tensor_copy(vb[:, fc, :], pc[:]),
                       reads=[pc], writes=[vb, f"{kvb}@{fc}"])
            _butterfly(p, nfch, vb, ub, final_out=wt, final_slice=(0, TC))
            for f0 in range(0, nfch, 8):
                p.dma("gpsimd", w_re[:, f0:f0 + 8, tc * TC:(tc + 1) * TC], wt[:, f0:f0 + 8, :],
                      reads=[wt] + [f"{kwt}@{fc}" for fc in range(f0, f0 + 8)])

    with p.scope():
        TQ = min(1024, T)
        ntq = T // TQ
        nci = TQ // 128
        HW_ = min(512, TQ)
        nhalf = TQ // HW_
        wq = p.sb("wq", [128, nfch, TQ], BF16)
        wdt = [p.sb(f"wdt{i}", [128, 8, 128], BF16) for i in range(4)]
        dtile = [p.sb(f"dtile{i}", [128, TQ], F32) for i in range(2)]
        otile = [p.sb(f"otile{i}", [128, 128], F32) for i in range(4)]
        ps_d = [p.ps(f"ps_d{i}", [128, HW_]) for i in range(4)]
        ps_t = [p.ps(f"ps_t{i}", [128, 128]) for i in range(2)]
        kwq = _key_of(wq)
        nbat = 0
        for tq in range(ntq):
            for f0 in range(0, nfch, 8):
                p.dma("sync", wq[:, f0:f0 + 8, :], w_re[:, f0:f0 + 8, tq * TQ:(tq + 1) * TQ],
                      writes=[wq] + [f"{kwq}@{fc}" for fc in range(f0, f0 + 8)])
            for g in range(nech):
                pd = [ps_d[(g % 2) * 2], ps_d[(g % 2) * 2 + 1]]
                dt = dtile[g % 2]
                for f0 in range(0, nfch, 8):
                    wd = wdt[nbat % 4]
                    nbat += 1
                    p.dma("gpsimd", wd[:], wdnre[:, f0:f0 + 8, g * 128:(g + 1) * 128],
                          writes=[wd])
                    for k in range(8):
                        fc = f0 + k
                        for half in range(nhalf):
                            p.exec("tensor", lambda e, fc=fc, k=k, wd=wd, ph=pd[half], half=half: e.matmul(
                                ph[:], wd[:, k, :], wq[:, fc, half * HW_:(half + 1) * HW_],
                                start=(fc == 0), stop=(fc == nfch - 1)),
                                reads=[wd, f"{kwq}@{fc}"], writes=[pd[half]])
                for half in range(nhalf):
                    p.exec("vector", lambda e, g=g, ph=pd[half], dt=dt, half=half: e.tensor_scalar_mul(
                        dt[:, half * HW_:(half + 1) * HW_], ph[:], wdps[:, g:g + 1]),
                        reads=[pd[half], wdps], writes=[dt])
                for ci in range(nci):
                    pt = ps_t[ci % 2]
                    ot = otile[ci % 4]
                    p.exec("tensor", lambda e, ci=ci, pt=pt, dt=dt: e.transpose(
                        pt[:], dt[:, ci * 128:(ci + 1) * 128], idf[:]),
                        reads=[dt, idf], writes=[pt])
                    p.exec("vector", lambda e, ci=ci, pt=pt, ot=ot: e.tensor_copy(
                        ot[:], pt[:]), reads=[pt], writes=[ot])
                    p.dma("gpsimd", ffn_out[tq * TQ + ci * 128: tq * TQ + (ci + 1) * 128,
                                            g * 128:(g + 1) * 128],
                          ot[:], reads=[ot])
    p.prefix = _oldpfx
    if standalone:
        p.fence()
        return p.emit_and_finalize()
    return None


def build_ffn1(T, p=None, io=None):
    standalone = p is None
    if standalone:
        p = Prog("ffn1")
    _oldpfx = p.prefix
    if not standalone:
        p.prefix = "ffn1_" + p.prefix
    def _inp(name, shape, dtype):
        if io is not None and name in io:
            return io[name]
        return p.inp(name, shape, dtype)
    def _outp(name, shape, dtype):
        if io is not None and name in io:
            return io[name]
        return p.outp(name, shape, dtype)
    xn_fm = _inp("xn_fm", [E_DIM, T], BF16)
    w_gate = _inp("w_gate_p", [E_DIM, INTER], F32)
    w_up = _inp("w_up_p", [E_DIM, INTER], F32)
    w_down = _inp("w_down_p", [INTER, E_DIM], F32)
    identf = _inp("identf32", [128, 128], F32)
    ffn_out = _outp("ffn1_tm", [T, E_DIM], F32)

    h_dram = p.dram("h_scr", [INTER, T], F32)
    hs_dram = p.dram("hs_scr", [INTER, T], BF16)
    nech = E_DIM // 128
    nfch = INTER // 128

    idf = p.sb("idf", [128, 128], F32)
    p.dma("sync", idf[:], identf, writes=[idf])
    xre = xn_fm.rearrange("(ec pp) t -> pp ec t", pp=128)
    wgre = w_gate.rearrange("(ec pp) f -> pp ec f", pp=128)
    wure = w_up.rearrange("(ec pp) f -> pp ec f", pp=128)
    wdre = w_down.rearrange("(fc pp) g -> pp fc g", pp=128)
    h_re = h_dram.rearrange("(fc pp) t -> pp fc t", pp=128)
    hs_re = hs_dram.rearrange("(fc pp) t -> pp fc t", pp=128)

    with p.scope():
        TS = min(512, T)
        nts = T // TS
        xnsb = p.sb("xnsb", [128, nech, T], BF16)
        for e0 in range(0, nech, 4):
            p.dma("sync", xnsb[:, e0:e0 + 4, :], xre[:, e0:e0 + 4, :], writes=[xnsb])
        wut = [p.sb(f"wut{i}", [128, nech, 128], BF16) for i in range(2)]
        wgt = [p.sb(f"wgt{i}", [128, nech, 128], BF16) for i in range(2)]
        sg = p.sb("sg", [128, TS], F32)
        ht = p.sb("ht", [128, TS], F32)
        ps_up = p.ps("ps_up", [128, TS])
        ps_gt = p.ps("ps_gt", [128, TS])
        for fi in range(nfch):
            wu, wg = wut[fi % 2], wgt[fi % 2]
            for e0 in range(0, nech, 8):
                p.dma("gpsimd", wu[:, e0:e0 + 8, :],
                      wure[:, e0:e0 + 8, fi * 128:(fi + 1) * 128], writes=[wu])
                p.dma("gpsimd", wg[:, e0:e0 + 8, :],
                      wgre[:, e0:e0 + 8, fi * 128:(fi + 1) * 128], writes=[wg])
            for tsl in range(nts):
                for ec in range(nech):
                    p.exec("tensor", lambda e, ec=ec, tsl=tsl, wu=wu, TS=TS: e.matmul(
                        ps_up[:], wu[:, ec, :], xnsb[:, ec, tsl * TS:(tsl + 1) * TS],
                        start=(ec == 0), stop=(ec == nech - 1)),
                        reads=[wu, xnsb], writes=[ps_up])
                    p.exec("tensor", lambda e, ec=ec, tsl=tsl, wg=wg, TS=TS: e.matmul(
                        ps_gt[:], wg[:, ec, :], xnsb[:, ec, tsl * TS:(tsl + 1) * TS],
                        start=(ec == 0), stop=(ec == nech - 1)),
                        reads=[wg, xnsb], writes=[ps_gt])
                p.exec("scalar", lambda e: e.activation(sg[:], ps_gt[:], AF.Silu),
                       reads=[ps_gt], writes=[sg])
                p.exec("vector", lambda e: e.tensor_mul(ht[:], ps_up[:], sg[:]),
                       reads=[ps_up, sg], writes=[ht])
                p.dma("sync", h_re[:, fi, tsl * TS:(tsl + 1) * TS], ht[:], reads=[ht])

    with p.scope():
        TS = min(512, T)
        nts = T // TS
        h4_ = [p.sb(f"h4_{i}", [128, 4, TS], F32) for i in range(2)]
        a4_ = [p.sb(f"a4_{i}", [128, 4, TS], F32) for i in range(2)]
        b6_ = [p.sb(f"b6_{i}", [128, 6, TS], F32) for i in range(2)]
        t2_ = [p.sb(f"t2_{i}", [128, 2, TS], F32) for i in range(2)]
        hs4_ = [p.sb(f"hs4_{i}", [128, 4, TS], BF16) for i in range(2)]
        pairs = [(0, 1), (0, 2), (0, 3), (1, 2), (1, 3), (2, 3)]

        def _sparsify_group(h4, a4, b6, t2, hs4, fcs, tsl, veng):
            for m, fc in enumerate(fcs):
                p.dma("sync", h4[:, m, :], h_re[:, fc, tsl * TS:(tsl + 1) * TS],
                      writes=[h4])
            for m in range(4):
                p.exec("scalar", lambda e, m=m: e.activation(
                    a4[:, m, :], h4[:, m, :], AF.Abs), reads=[h4], writes=[a4])
            for bi, (i, j) in enumerate(pairs):
                p.exec("vector", lambda e, bi=bi, i=i, j=j: e.tensor_tensor(
                    b6[:, bi, :], a4[:, i, :], a4[:, j, :], ALU.is_ge),
                    reads=[a4], writes=[b6])
            p.exec(veng, lambda e: e.tensor_add(t2[:, 0, :], b6[:, 0, :], b6[:, 1, :]),
                   reads=[b6], writes=[t2])
            p.exec(veng, lambda e: e.tensor_add(t2[:, 0, :], t2[:, 0, :], b6[:, 2, :]),
                   reads=[b6, t2], writes=[t2])
            p.exec("vector", lambda e: e.scalar_tensor_tensor(
                hs4[:, 0, :], t2[:, 0, :], 1.5, h4[:, 0, :], ALU.is_ge, ALU.mult),
                reads=[t2, h4], writes=[hs4])
            p.exec(veng, lambda e: e.tensor_add(t2[:, 1, :], b6[:, 3, :], b6[:, 4, :]),
                   reads=[b6], writes=[t2])
            p.exec(veng, lambda e: e.tensor_sub(t2[:, 1, :], t2[:, 1, :], b6[:, 0, :]),
                   reads=[b6, t2], writes=[t2])
            p.exec("vector", lambda e: e.scalar_tensor_tensor(
                hs4[:, 1, :], t2[:, 1, :], 0.5, h4[:, 1, :], ALU.is_ge, ALU.mult),
                reads=[t2, h4], writes=[hs4])
            p.exec(veng, lambda e: e.tensor_sub(t2[:, 0, :], b6[:, 5, :], b6[:, 1, :]),
                   reads=[b6], writes=[t2])
            p.exec(veng, lambda e: e.tensor_sub(t2[:, 0, :], t2[:, 0, :], b6[:, 3, :]),
                   reads=[b6, t2], writes=[t2])
            p.exec("vector", lambda e: e.scalar_tensor_tensor(
                hs4[:, 2, :], t2[:, 0, :], -0.5, h4[:, 2, :], ALU.is_ge, ALU.mult),
                reads=[t2, h4], writes=[hs4])
            p.exec(veng, lambda e: e.tensor_add(t2[:, 1, :], b6[:, 2, :], b6[:, 4, :]),
                   reads=[b6], writes=[t2])
            p.exec(veng, lambda e: e.tensor_add(t2[:, 1, :], t2[:, 1, :], b6[:, 5, :]),
                   reads=[b6, t2], writes=[t2])
            p.exec("vector", lambda e: e.scalar_tensor_tensor(
                hs4[:, 3, :], t2[:, 1, :], 1.5, h4[:, 3, :], ALU.is_le, ALU.mult),
                reads=[t2, h4], writes=[hs4])
            for m, fc in enumerate(fcs):
                p.dma("gpsimd", hs_re[:, fc, tsl * TS:(tsl + 1) * TS], hs4[:, m, :],
                      reads=[hs4])

        for tsl in range(nts):
            for grp in range(16):
                par = (tsl * 16 + grp) % 2
                _sparsify_group(h4_[par], a4_[par], b6_[par], t2_[par], hs4_[par],
                                [grp, grp + 16, grp + 32, grp + 48], tsl,
                                "vector" if par == 0 else "gpsimd")

    with p.scope():
        TQ = min(1024, T)
        ntq = T // TQ
        nci = TQ // 128
        HW_ = min(512, TQ)
        nhalf = TQ // HW_
        wq = p.sb("wq", [128, nfch, TQ], BF16)
        wdt = [p.sb(f"wdt{i}", [128, 8, 128], BF16) for i in range(4)]
        dtile = [p.sb(f"dtile{i}", [128, TQ], F32) for i in range(2)]
        otile = [p.sb(f"otile{i}", [128, 128], F32) for i in range(4)]
        ps_d = [p.ps(f"ps_d{i}", [128, HW_]) for i in range(4)]
        ps_t = [p.ps(f"ps_t{i}", [128, 128]) for i in range(2)]
        kwq = _key_of(wq)
        nbat = 0
        for tq in range(ntq):
            for f0 in range(0, nfch, 8):
                p.dma("sync", wq[:, f0:f0 + 8, :], hs_re[:, f0:f0 + 8, tq * TQ:(tq + 1) * TQ],
                      writes=[wq] + [f"{kwq}@{fc}" for fc in range(f0, f0 + 8)])
            for g in range(nech):
                pd = [ps_d[(g % 2) * 2], ps_d[(g % 2) * 2 + 1]]
                dt = dtile[g % 2]
                for f0 in range(0, nfch, 8):
                    wd = wdt[nbat % 4]
                    nbat += 1
                    p.dma("gpsimd", wd[:], wdre[:, f0:f0 + 8, g * 128:(g + 1) * 128],
                          writes=[wd])
                    for k in range(8):
                        fc = f0 + k
                        for half in range(nhalf):
                            p.exec("tensor", lambda e, fc=fc, k=k, wd=wd, ph=pd[half], half=half: e.matmul(
                                ph[:], wd[:, k, :], wq[:, fc, half * HW_:(half + 1) * HW_],
                                start=(fc == 0), stop=(fc == nfch - 1)),
                                reads=[wd, f"{kwq}@{fc}"], writes=[pd[half]])
                for half in range(nhalf):
                    p.exec("vector", lambda e, ph=pd[half], dt=dt, half=half: e.tensor_copy(
                        dt[:, half * HW_:(half + 1) * HW_], ph[:]),
                        reads=[pd[half]], writes=[dt])
                for ci in range(nci):
                    pt = ps_t[ci % 2]
                    ot = otile[ci % 4]
                    p.exec("tensor", lambda e, ci=ci, pt=pt, dt=dt: e.transpose(
                        pt[:], dt[:, ci * 128:(ci + 1) * 128], idf[:]),
                        reads=[dt, idf], writes=[pt])
                    p.exec("vector", lambda e, ci=ci, pt=pt, ot=ot: e.tensor_copy(
                        ot[:], pt[:]), reads=[pt], writes=[ot])
                    p.dma("gpsimd", ffn_out[tq * TQ + ci * 128: tq * TQ + (ci + 1) * 128,
                                            g * 128:(g + 1) * 128],
                          ot[:], reads=[ot])
    p.prefix = _oldpfx
    if standalone:
        p.fence()
        return p.emit_and_finalize()
    return None


def build_resln(T, with_ln=True, nadd=2, p=None, io=None):
    standalone = p is None
    if standalone:
        p = Prog("resln")
    _oldpfx = p.prefix
    if not standalone:
        p.prefix = "resln_" + p.prefix
    def _inp(name, shape, dtype):
        if io is not None and name in io:
            return io[name]
        return p.inp(name, shape, dtype)
    def _outp(name, shape, dtype):
        if io is not None and name in io:
            return io[name]
        return p.outp(name, shape, dtype)
    x = _inp("x", [T, E_DIM], F32)
    adds = [_inp(f"add{i}", [T, E_DIM], F32) for i in range(nadd)]
    x1_tm = _outp("x1_tm", [T, E_DIM], F32)
    if with_ln:
        identbf = _inp("identbf", [128, 128], BF16)
        constpm = _inp("constpm", [128, 2], F32)
        xn_fm = _outp("xn_fm", [E_DIM, T], BF16)
        idb = p.sb("idb", [128, 128], BF16)
        cp = p.sb("cp", [128, 2], F32)
        p.dma("sync", idb[:], identbf, writes=[idb])
        p.dma("sync", cp[:], constpm, writes=[cp])
        xn_re = xn_fm.rearrange("(ec pp) t -> ec pp t", pp=128)
    nch = T // 128
    nech = E_DIM // 128

    xin = [p.sb(f"xin{i}", [128, E_DIM], F32) for i in range(2)]
    ain = [p.sb(f"ain{i}", [128, E_DIM], F32) for i in range(2)]
    bin_ = [p.sb(f"bin{i}", [128, E_DIM], F32) for i in range(2)]
    x1b = [p.sb(f"x1b{i}", [128, E_DIM], F32) for i in range(2)]
    if with_ln:
        xsq = p.sb("xsq", [128, E_DIM], F32)
        stats = p.sb("stats", [128, 8], F32)
        xn_tm = p.sb("xn_tm", [128, E_DIM], BF16)
        tbuf = [p.sb(f"tbuf{i}", [128, nech, 128], BF16) for i in range(2)]
        pstr = [p.ps(f"pstr{i}", [128, 128], BF16) for i in range(2)]

    for c in range(nch):
        b = c % 2
        xb, ab, bb, x1 = xin[b], ain[b], bin_[b], x1b[b]
        p.dma("sync", xb[:], x[c * 128:(c + 1) * 128, :], writes=[xb])
        p.dma("sync", ab[:], adds[0][c * 128:(c + 1) * 128, :], writes=[ab])
        if nadd > 1:
            p.dma("sync", bb[:], adds[1][c * 128:(c + 1) * 128, :], writes=[bb])
        p.exec("vector", lambda e, xb=xb, ab=ab, x1=x1: e.tensor_add(x1[:], xb[:], ab[:]),
               reads=[xb, ab], writes=[x1])
        if nadd > 1:
            p.exec("vector", lambda e, bb=bb, x1=x1: e.tensor_add(x1[:], x1[:], bb[:]),
                   reads=[bb, x1], writes=[x1])
        p.dma("gpsimd", x1_tm[c * 128:(c + 1) * 128, :], x1[:], reads=[x1])
        if not with_ln:
            continue
        p.exec("scalar", lambda e, x1=x1: e.square(xsq[:], x1[:]), reads=[x1], writes=[xsq])
        p.exec("vector", lambda e, x1=x1: e.reduce_sum(stats[:, 0:1], x1[:], axis=mybir.AxisListType.X),
               reads=[x1], writes=[stats])
        p.exec("vector", lambda e: e.reduce_sum(stats[:, 1:2], xsq[:], axis=mybir.AxisListType.X),
               reads=[xsq], writes=[stats])
        p.exec("vector", lambda e: e.tensor_scalar_mul(stats[:, 2:3], stats[:, 0:1], -1.0 / E_DIM),
               reads=[stats], writes=[stats])
        p.exec("vector", lambda e: e.tensor_scalar_mul(stats[:, 3:4], stats[:, 1:2], 1.0 / E_DIM),
               reads=[stats], writes=[stats])
        p.exec("vector", lambda e: e.tensor_mul(stats[:, 4:5], stats[:, 2:3], stats[:, 2:3]),
               reads=[stats], writes=[stats])
        p.exec("vector", lambda e: e.tensor_sub(stats[:, 5:6], stats[:, 3:4], stats[:, 4:5]),
               reads=[stats], writes=[stats])
        p.exec("scalar", lambda e: e.activation(stats[:, 6:7], stats[:, 5:6], AF.Ln, bias=cp[:, 0:1]),
               reads=[stats, cp], writes=[stats])
        p.exec("scalar", lambda e: e.activation(stats[:, 7:8], stats[:, 6:7], AF.Exp, scale=cp[:, 1:2]),
               reads=[stats, cp], writes=[stats])
        p.exec("vector", lambda e: e.tensor_mul(stats[:, 4:5], stats[:, 2:3], stats[:, 7:8]),
               reads=[stats], writes=[stats])
        p.exec("vector", lambda e, x1=x1: e.tensor_scalar(
            xn_tm[:], x1[:], stats[:, 7:8], stats[:, 4:5], ALU.mult, ALU.add),
            reads=[x1, stats], writes=[xn_tm])
        tb = tbuf[b]
        for ec in range(nech):
            pb = pstr[ec % 2]
            p.exec("tensor", lambda e, ec=ec, pb=pb: e.transpose(
                pb[:], xn_tm[:, ec * 128:(ec + 1) * 128], idb[:]),
                reads=[xn_tm, idb], writes=[pb])
            p.exec("vector", lambda e, ec=ec, pb=pb, tb=tb: e.tensor_copy(tb[:, ec, :], pb[:]),
                   reads=[pb], writes=[tb])
            p.dma("gpsimd", xn_re[ec, :, c * 128:(c + 1) * 128], tb[:, ec, :],
                  reads=[tb])
    p.prefix = _oldpfx
    if standalone:
        p.fence()
        return p.emit_and_finalize()
    return None


def build_attn0(T, p=None, io=None):
    standalone = p is None
    if standalone:
        p = Prog("attn0")
    _oldpfx = p.prefix
    if not standalone:
        p.prefix = "attn0_" + p.prefix
    def _inp(name, shape, dtype):
        if io is not None and name in io:
            return io[name]
        return p.inp(name, shape, dtype)
    def _outp(name, shape, dtype):
        if io is not None and name in io:
            return io[name]
        return p.outp(name, shape, dtype)
    qr_fm = _inp("qr_fm", [E_DIM, T], BF16)
    kr_fm = _inp("kr_fm", [E_DIM, T], BF16)
    kr_tm = _inp("kr_tm", [T, E_DIM], BF16)
    v_tm = _inp("v_tm", [T, E_DIM], BF16)
    Wo = _inp("Wo", [E_DIM, E_DIM], F32)
    rms_w_row = _inp("rms_w_row", [1, E_DIM], F32)
    dmaskT = _inp("dmaskT", [H, 128, 128], F32)
    qdec = _inp("qdec", [128, H], F32)
    kdec = _inp("kdec", [128, H], F32)
    rC = _inp("rC", [128, H], F32)
    identbf = _inp("identbf", [128, 128], BF16)
    ones128f = _inp("ones128f", [1, 128], F32)
    constpm = _inp("constpm", [128, 2], F32)
    attn_out = _outp("attn0_tm", [T, E_DIM], F32)
    nch = T // 128
    nech = E_DIM // 128

    idb = p.sb("idb", [128, 128], BF16)
    cp = p.sb("cp", [128, 2], F32)
    wsb = p.sb("wsb", [128, nech, E_DIM], BF16)
    dm = p.sb("dm", [128, H, 128], F32)
    qd = p.sb("qd", [128, H], F32)
    kd = p.sb("kd", [128, H], F32)
    rc = p.sb("rc", [128, H], F32)
    ones_sb = p.sb("ones_sb", [1, 128], F32)
    rmsrow = p.sb("rmsrow", [1, E_DIM], F32)
    rmsrep = p.sb("rmsrep", [128, E_DIM], F32)

    p.dma("sync", idb[:], identbf, writes=[idb])
    p.dma("sync", cp[:], constpm, writes=[cp])
    for h_ in range(H):
        p.dma("sync", dm[:, h_, :], dmaskT[h_], writes=[dm])
    p.dma("sync", qd[:], qdec, writes=[qd])
    p.dma("sync", kd[:], kdec, writes=[kd])
    p.dma("sync", rc[:], rC, writes=[rc])
    p.dma("sync", ones_sb[:], ones128f, writes=[ones_sb])
    p.dma("sync", rmsrow[:], rms_w_row, writes=[rmsrow])
    load_weight_bf16(p, wsb, Wo, E_DIM, E_DIM)

    qre = qr_fm.rearrange("(h d) t -> d h t", d=128)
    kre = kr_fm.rearrange("(h d) t -> d h t", d=128)

    GC = min(2, T // 128)
    ktm = [p.sb(f"ktm{i}", [128, E_DIM], BF16) for i in range(2)]
    vtm = [p.sb(f"vtm{i}", [128, E_DIM], BF16) for i in range(2)]
    qfm = [p.sb(f"qfm{i}", [128, H, GC * 128], BF16) for i in range(2)]
    kfm = [p.sb(f"kfm{i}", [128, H, GC * 128], BF16) for i in range(2)]
    kv32 = [[p.sb(f"kv32_{h}_{i}", [128, 128], F32) for i in range(2)] for h in range(H)]
    kvbf = [p.sb(f"kvbf_{h}", [128, 128], BF16) for h in range(H)]
    kdt = [p.sb(f"kdt{i}", [128, 128], BF16) for i in range(2)]
    smb = [p.sb(f"smb{i}", [128, 128], BF16) for i in range(2)]
    tmpo = [p.sb(f"tmpo{i}", [128, 128], F32) for i in range(2)]
    attn_tm = p.sb("attn_tm", [128, E_DIM], F32)
    asq = p.sb("asq", [128, E_DIM], F32)
    st8 = p.sb("st8", [128, 8], F32)
    anb = p.sb("anb", [128, E_DIM], BF16)
    afm = p.sb("afm", [128, nech, 128], BF16)
    outc = p.sb("outc", [128, E_DIM], F32)

    ps_st = p.ps("ps_st", [128, 128])
    ps_o = p.ps("ps_o", [128, 128])
    ps_i = p.ps("ps_i", [128, 128])
    ps_kv = p.ps("ps_kv", [128, 128])
    pstr = [p.ps(f"pstr{i}", [128, 128], BF16) for i in range(2)]
    ps_w = [p.ps(f"ps_w{i}", [128, 512]) for i in range(2)]

    for sl in range(4):
        pw = ps_w[sl % 2]
        p.exec("tensor", lambda e, sl=sl, pw=pw: e.matmul(
            pw[:], ones_sb[:], rmsrow[:, sl * 512:(sl + 1) * 512], start=True, stop=True),
            reads=[ones_sb, rmsrow], writes=[pw])
        p.exec("vector", lambda e, sl=sl, pw=pw: e.tensor_copy(
            rmsrep[:, sl * 512:(sl + 1) * 512], pw[:]), reads=[pw], writes=[rmsrep])

    for h in range(H):
        p.exec("vector", lambda e, h=h: e.memset(kv32[h][0][:], 0.0), writes=[kv32[h][0]])
        p.exec("gpsimd", lambda e, h=h: e.memset(kvbf[h][:], 0.0), writes=[kvbf[h]])

    for c in range(nch):
        b = c % 2
        gb = (c // GC) % 2
        cc = c % GC
        kt, vt, qf, kf = ktm[b], vtm[b], qfm[gb], kfm[gb]
        p.dma("sync", kt[:], kr_tm[c * 128:(c + 1) * 128, :], writes=[kt])
        p.dma("sync", vt[:], v_tm[c * 128:(c + 1) * 128, :], writes=[vt])
        if cc == 0:
            c4 = c // GC
            span = (c4 * GC * 128, (c4 + 1) * GC * 128)
            for h0 in range(0, H, 8):
                p.dma("sync", qf[:, h0:h0 + 8, :], qre[:, h0:h0 + 8, span[0]:span[1]],
                      writes=[qf])
                p.dma("sync", kf[:, h0:h0 + 8, :], kre[:, h0:h0 + 8, span[0]:span[1]],
                      writes=[kf])
        for h in range(H):
            hb = h % 2
            kvp = kv32[h][c % 2]
            kvn = kv32[h][(c + 1) % 2]
            qsl = qf[:, h, cc * 128:(cc + 1) * 128]
            ksl = kf[:, h, cc * 128:(cc + 1) * 128]
            p.exec("vector", lambda e, h=h, hb=hb, kt=kt: e.tensor_scalar_mul(
                kdt[hb][:], kt[:, h * 128:(h + 1) * 128], kd[:, h:h + 1]),
                reads=[kt, kd], writes=[kdt[hb]])
            p.exec("tensor", lambda e, qsl=qsl, ksl=ksl: e.matmul(
                ps_st[:], ksl, qsl, start=True, stop=True),
                reads=[kf, qf], writes=[ps_st])
            p.exec("vector", lambda e, h=h, hb=hb: e.tensor_tensor(
                smb[hb][:], ps_st[:], dm[:, h, :], ALU.mult),
                reads=[ps_st, dm], writes=[smb[hb]])
            p.exec("tensor", lambda e, h=h, hb=hb, vt=vt: e.matmul(
                ps_o[:], smb[hb][:], vt[:, h * 128:(h + 1) * 128], start=True, stop=True),
                reads=[smb[hb], vt], writes=[ps_o])
            p.exec("tensor", lambda e, qsl=qsl, h=h: e.matmul(
                ps_i[:], qsl, kvbf[h][:], start=True, stop=True),
                reads=[qf, kvbf[h]], writes=[ps_i])
            p.exec("scalar", lambda e, hb=hb: e.copy(tmpo[hb][:], ps_o[:]),
                   reads=[ps_o], writes=[tmpo[hb]])
            p.exec("vector", lambda e, h=h, hb=hb: e.scalar_tensor_tensor(
                attn_tm[:, h * 128:(h + 1) * 128], ps_i[:], qd[:, h:h + 1], tmpo[hb][:],
                ALU.mult, ALU.add),
                reads=[ps_i, qd, tmpo[hb]], writes=[attn_tm])
            p.exec("tensor", lambda e, h=h, hb=hb, vt=vt: e.matmul(
                ps_kv[:], kdt[hb][:], vt[:, h * 128:(h + 1) * 128], start=True, stop=True),
                reads=[kdt[hb], vt], writes=[ps_kv])
            p.exec("vector", lambda e, h=h, kvp=kvp, kvn=kvn: e.scalar_tensor_tensor(
                kvn[:], kvp[:], rc[:, h:h + 1], ps_kv[:], ALU.mult, ALU.add),
                reads=[kvp, rc, ps_kv], writes=[kvn])
            p.exec("scalar", lambda e, h=h, kvn=kvn: e.copy(kvbf[h][:], kvn[:]),
                   reads=[kvn], writes=[kvbf[h]])
        p.exec("scalar", lambda e: e.square(asq[:], attn_tm[:]), reads=[attn_tm], writes=[asq])
        p.exec("vector", lambda e: e.reduce_sum(st8[:, 0:1], asq[:], axis=mybir.AxisListType.X),
               reads=[asq], writes=[st8])
        p.exec("vector", lambda e: e.tensor_scalar_mul(st8[:, 1:2], st8[:, 0:1], 1.0 / E_DIM),
               reads=[st8], writes=[st8])
        p.exec("scalar", lambda e: e.activation(st8[:, 2:3], st8[:, 1:2], AF.Ln, bias=cp[:, 0:1]),
               reads=[st8, cp], writes=[st8])
        p.exec("scalar", lambda e: e.activation(st8[:, 3:4], st8[:, 2:3], AF.Exp, scale=cp[:, 1:2]),
               reads=[st8, cp], writes=[st8])
        p.exec("vector", lambda e: e.scalar_tensor_tensor(
            anb[:], attn_tm[:], st8[:, 3:4], rmsrep[:], ALU.mult, ALU.mult),
            reads=[attn_tm, st8, rmsrep], writes=[anb])
        for ec in range(nech):
            pb = pstr[ec % 2]
            p.exec("tensor", lambda e, ec=ec, pb=pb: e.transpose(
                pb[:], anb[:, ec * 128:(ec + 1) * 128], idb[:]),
                reads=[anb, idb], writes=[pb])
            p.exec("vector", lambda e, ec=ec, pb=pb: e.tensor_copy(afm[:, ec, :], pb[:]),
                   reads=[pb], writes=[afm])
        for sl in range(4):
            pw = ps_w[sl % 2]
            for e_ in range(nech):
                p.exec("tensor", lambda en, sl=sl, e_=e_, pw=pw: en.matmul(
                    pw[:], afm[:, e_, :], wsb[:, e_, sl * 512:(sl + 1) * 512],
                    start=(e_ == 0), stop=(e_ == nech - 1)),
                    reads=[afm, wsb], writes=[pw])
            p.exec("scalar", lambda en, sl=sl, pw=pw: en.copy(
                outc[:, sl * 512:(sl + 1) * 512], pw[:]), reads=[pw], writes=[outc])
        p.dma("gpsimd", attn_out[c * 128:(c + 1) * 128, :], outc[:], reads=[outc])
    p.prefix = _oldpfx
    if standalone:
        p.fence()
        return p.emit_and_finalize()
    return None


def build_qkv0(T, p=None, io=None):
    standalone = p is None
    if standalone:
        p = Prog("qkv0")
    _oldpfx = p.prefix
    if not standalone:
        p.prefix = "qkv0_" + p.prefix
    def _inp(name, shape, dtype):
        if io is not None and name in io:
            return io[name]
        return p.inp(name, shape, dtype)
    def _outp(name, shape, dtype):
        if io is not None and name in io:
            return io[name]
        return p.outp(name, shape, dtype)
    xn_fm = _inp("xn_fm", [E_DIM, T], BF16)
    Ws = {n: _inp(n, [E_DIM, E_DIM], F32) for n in ("Wq", "Wk", "Wv")}
    rotc = _inp("rotc", [T // 128, 128, 64], F32)
    rots = _inp("rots", [T // 128, 128, 64], F32)
    identbf = _inp("identbf", [128, 128], BF16)
    outs = {
        "q": (_outp("qr_fm", [E_DIM, T], BF16), None),
        "k": (_outp("kr_fm", [E_DIM, T], BF16), _outp("kr_tm", [T, E_DIM], BF16)),
        "v": (None, _outp("v_tm", [T, E_DIM], BF16)),
    }
    nch = T // 128
    nech = E_DIM // 128

    GC = min(4, T // 128)
    idb = p.sb("idb", [128, 128], BF16)
    p.dma("sync", idb[:], identbf, writes=[idb])
    wsb = p.sb("wsb", [128, nech, E_DIM], BF16)
    xnc = [p.sb(f"xnc{i}", [128, nech, GC * 128], BF16) for i in range(2)]
    pf32_ = [p.sb(f"pf32_{i}", [128, E_DIM], F32) for i in range(2)]
    tmpA_ = [p.sb(f"tmpA_{i}", [128, H, 64], F32) for i in range(2)]
    tmpB_ = [p.sb(f"tmpB_{i}", [128, H, 64], F32) for i in range(2)]
    ptm_ = [p.sb(f"ptm_{i}", [128, E_DIM], BF16) for i in range(2)]
    tbuf = [p.sb(f"tbuf{i}", [128, nech, GC * 128], BF16) for i in range(2)]
    cos4 = p.sb("cos4", [128, GC, 64], F32)
    sin4 = p.sb("sin4", [128, GC, 64], F32)
    psa = [p.ps(f"psa{i}", [128, 512]) for i in range(4)]
    pstr = [p.ps(f"pstr{i}", [128, 128], BF16) for i in range(2)]

    xre = xn_fm.rearrange("(ec pp) t -> pp ec t", pp=128)
    rotc_re = rotc.rearrange("c pp d -> pp c d")
    rots_re = rots.rearrange("c pp d -> pp c d")

    for tname in ("q", "k", "v"):
        W = Ws["W" + tname]
        fm_out, tm_out = outs[tname]
        load_weight_bf16(p, wsb, W, E_DIM, E_DIM)
        fm_re = fm_out.rearrange("(ec pp) t -> ec pp t", pp=128) if fm_out is not None else None
        kxn = _key_of(xnc[0])

        def _emit_tr(c, ptm):
            tb = tbuf[(c // GC) % 2]
            cc = c % GC
            for ec in range(nech):
                pb = pstr[ec % 2]
                p.exec("tensor", lambda en, ec=ec, pb=pb, ptm=ptm: en.transpose(
                    pb[:], ptm[:, ec * 128:(ec + 1) * 128], idb[:]),
                    reads=[ptm, idb], writes=[pb])
                p.exec("vector", lambda en, ec=ec, pb=pb, tb=tb, cc=cc: en.tensor_copy(
                    tb[:, ec, cc * 128:(cc + 1) * 128], pb[:]), reads=[pb], writes=[tb])
            if cc == GC - 1:
                c4 = c // GC
                for e0 in range(0, nech, 8):
                    p.dma("gpsimd", fm_re[e0:e0 + 8, :, c4 * GC * 128:(c4 + 1) * GC * 128]
                          .rearrange("ec pp t -> pp ec t"),
                          tb[:, e0:e0 + 8, :], reads=[tb])

        pending_tr = None
        for c in range(nch):
            xb = xnc[(c // GC) % 2]
            pf32 = pf32_[c % 2]
            tmpA = tmpA_[c % 2]
            tmpB = tmpB_[c % 2]
            ptm = ptm_[c % 2]
            cc = c % GC
            if cc == 0:
                c4 = c // GC
                for e0 in range(0, nech, 8):
                    p.dma("sync", xb[:, e0:e0 + 8, :],
                          xre[:, e0:e0 + 8, c4 * GC * 128:(c4 + 1) * GC * 128], writes=[xb])
                if tname != "v":
                    p.dma("sync", cos4[:], rotc_re[:, c4 * GC:(c4 + 1) * GC, :], writes=[cos4])
                    p.dma("sync", sin4[:], rots_re[:, c4 * GC:(c4 + 1) * GC, :], writes=[sin4])
            for e in range(nech):
                for sl in range(4):
                    p.exec("tensor", lambda en, e=e, sl=sl, xb=xb, cc=cc: en.matmul(
                        psa[sl][:], xb[:, e, cc * 128:(cc + 1) * 128],
                        wsb[:, e, sl * 512:(sl + 1) * 512],
                        start=(e == 0), stop=(e == nech - 1)),
                        reads=[xb, wsb], writes=[psa[sl]])
            if tname == "v":
                for sl in range(4):
                    p.exec("vector", lambda en, sl=sl, ptm=ptm: en.tensor_copy(
                        ptm[:, sl * 512:(sl + 1) * 512], psa[sl][:]),
                        reads=[psa[sl]], writes=[ptm])
            else:
                for sl in range(4):
                    p.exec("vector", lambda en, sl=sl, pf32=pf32: en.tensor_copy(
                        pf32[:, sl * 512:(sl + 1) * 512], psa[sl][:]),
                        reads=[psa[sl]], writes=[pf32])
                v4 = pf32.ap().rearrange("pp (h two d) -> pp h two d", h=H, two=2)
                o4 = ptm.ap().rearrange("pp (h two d) -> pp h two d", h=H, two=2)
                cb = cos4.ap()[:, cc:cc + 1, :].broadcast_to([128, H, 64])
                sb_ = sin4.ap()[:, cc:cc + 1, :].broadcast_to([128, H, 64])
                x1 = v4[:, :, 0, :]
                x2 = v4[:, :, 1, :]
                p.exec("vector", lambda en, x1=x1, cb=cb, tmpA=tmpA: en.tensor_tensor(tmpA.ap(), x1, cb, ALU.mult),
                       reads=[pf32, cos4], writes=[tmpA])
                p.exec("vector", lambda en, x2=x2, sb_=sb_, tmpB=tmpB: en.tensor_tensor(tmpB.ap(), x2, sb_, ALU.mult),
                       reads=[pf32, sin4], writes=[tmpB])
                p.exec("vector", lambda en, o4=o4, tmpA=tmpA, tmpB=tmpB: en.tensor_tensor(
                    o4[:, :, 0, :], tmpA.ap(), tmpB.ap(), ALU.add),
                    reads=[tmpA, tmpB], writes=[ptm])
                p.exec("vector", lambda en, x2=x2, cb=cb, tmpA=tmpA: en.tensor_tensor(tmpA.ap(), x2, cb, ALU.mult),
                       reads=[pf32, cos4], writes=[tmpA])
                p.exec("vector", lambda en, x1=x1, sb_=sb_, tmpB=tmpB: en.tensor_tensor(tmpB.ap(), x1, sb_, ALU.mult),
                       reads=[pf32, sin4], writes=[tmpB])
                p.exec("vector", lambda en, o4=o4, tmpA=tmpA, tmpB=tmpB: en.tensor_tensor(
                    o4[:, :, 1, :], tmpA.ap(), tmpB.ap(), ALU.subtract),
                    reads=[tmpA, tmpB], writes=[ptm])
            if tm_out is not None:
                p.dma("gpsimd", tm_out[c * 128:(c + 1) * 128, :], ptm[:], reads=[ptm])
            pending_new = (c, ptm) if fm_out is not None else None
            if pending_tr is not None:
                _emit_tr(*pending_tr)
            pending_tr = pending_new
        if pending_tr is not None:
            _emit_tr(*pending_tr)
            pending_tr = None
    p.prefix = _oldpfx
    if standalone:
        p.fence()
        return p.emit_and_finalize()
    return None
def build_ln(T, xname="x", outname="xn_fm", p=None, io=None):
    standalone = p is None
    if standalone:
        p = Prog("ln")
    _oldpfx = p.prefix
    if not standalone:
        p.prefix = "ln_" + p.prefix
    def _inp(name, shape, dtype):
        if io is not None and name in io:
            return io[name]
        return p.inp(name, shape, dtype)
    def _outp(name, shape, dtype):
        if io is not None and name in io:
            return io[name]
        return p.outp(name, shape, dtype)
    x = _inp(xname, [T, E_DIM], F32)
    xn_fm = _outp(outname, [E_DIM, T], BF16)
    identbf = _inp("identbf", [128, 128], BF16)
    constpm = _inp("constpm", [128, 2], F32)
    nch = T // 128
    nech = E_DIM // 128

    xin = [p.sb(f"xin{i}", [128, E_DIM], F32) for i in range(2)]
    xsq = p.sb("xsq", [128, E_DIM], F32)
    stats = p.sb("stats", [128, 8], F32)
    xn_tm = p.sb("xn_tm", [128, E_DIM], BF16)
    tbuf = [p.sb(f"tbuf{i}", [128, nech, 128], BF16) for i in range(2)]
    idb = p.sb("idb", [128, 128], BF16)
    pstr = [p.ps(f"pstr{i}", [128, 128], BF16) for i in range(2)]

    cp = p.sb("cp", [128, 2], F32)
    p.dma("sync", cp[:], constpm, writes=[cp])
    p.dma("sync", idb[:], identbf, writes=[idb])
    xn_re = xn_fm.rearrange("(ec pp) t -> ec pp t", pp=128)

    for c in range(nch):
        b = c % 2
        xb = xin[b]
        p.dma("sync", xb[:], x[c * 128:(c + 1) * 128, :], writes=[xb])
        p.exec("scalar", lambda e, i=xb: e.square(xsq[:], i[:]), reads=[xb], writes=[xsq])
        p.exec("vector", lambda e, i=xb: e.reduce_sum(stats[:, 0:1], i[:], axis=mybir.AxisListType.X),
               reads=[xb], writes=[stats])
        p.exec("vector", lambda e: e.reduce_sum(stats[:, 1:2], xsq[:], axis=mybir.AxisListType.X),
               reads=[xsq], writes=[stats])
        p.exec("vector", lambda e: e.tensor_scalar_mul(stats[:, 2:3], stats[:, 0:1], -1.0 / E_DIM),
               reads=[stats], writes=[stats])
        p.exec("vector", lambda e: e.tensor_scalar_mul(stats[:, 3:4], stats[:, 1:2], 1.0 / E_DIM),
               reads=[stats], writes=[stats])
        p.exec("vector", lambda e: e.tensor_mul(stats[:, 4:5], stats[:, 2:3], stats[:, 2:3]),
               reads=[stats], writes=[stats])
        p.exec("vector", lambda e: e.tensor_sub(stats[:, 5:6], stats[:, 3:4], stats[:, 4:5]),
               reads=[stats], writes=[stats])
        p.exec("scalar", lambda e: e.activation(stats[:, 6:7], stats[:, 5:6], AF.Ln, bias=cp[:, 0:1]),
               reads=[stats, cp], writes=[stats])
        p.exec("scalar", lambda e: e.activation(stats[:, 7:8], stats[:, 6:7], AF.Exp, scale=cp[:, 1:2]),
               reads=[stats, cp], writes=[stats])
        p.exec("vector", lambda e: e.tensor_mul(stats[:, 4:5], stats[:, 2:3], stats[:, 7:8]),
               reads=[stats], writes=[stats])
        p.exec("vector", lambda e, i=xb: e.tensor_scalar(
            xn_tm[:], i[:], stats[:, 7:8], stats[:, 4:5], ALU.mult, ALU.add),
            reads=[xb, stats], writes=[xn_tm])
        tb = tbuf[b]
        for ec in range(nech):
            pb = pstr[ec % 2]
            p.exec("tensor", lambda e, ec=ec, pb=pb: e.transpose(
                pb[:], xn_tm[:, ec * 128:(ec + 1) * 128], idb[:]),
                reads=[xn_tm, idb], writes=[pb])
            p.exec("vector", lambda e, ec=ec, pb=pb, tb=tb: e.tensor_copy(tb[:, ec, :], pb[:]),
                   reads=[pb], writes=[tb])
            p.dma("gpsimd", xn_re[ec, :, c * 128:(c + 1) * 128], tb[:, ec, :],
                  reads=[tb])
    p.prefix = _oldpfx
    if standalone:
        p.fence()
        return p.emit_and_finalize()
    return None
